# version 03
# baseline (speedup 1.0000x reference)
import numpy as np
from contextlib import ExitStack

import concourse.bass as bass
import concourse.mybir as mybir

F32 = mybir.dt.float32
BF16 = mybir.dt.bfloat16
AF = mybir.ActivationFunctionType
ALU = mybir.AluOpType
AX = mybir.AxisListType

NEG = -1e30


def _ap(x):
    try:
        return x[:]
    except Exception:
        return x



ENGINES = ("pe", "v", "sc", "gp", "sy")


class Em:

    def __init__(self, nc):
        self.nc = nc
        self.streams = {e: [] for e in ENGINES}
        self.cnt = {}
        self.waited = {e: {} for e in ENGINES}
        self.sems = {}

    def ev(self, sem):
        return (sem, self.cnt.get(sem, 0))

    AUTOCHAIN = {"v": "sv", "sc": "ssc", "gp": "sgp"}

    def emit(self, eng, fn, waits=(), inc=None):
        waits = [w for w in waits if w is not None]
        ac = self.AUTOCHAIN.get(eng)
        if ac is not None:
            waits = list(waits) + [(ac, self.cnt.get(ac, 0))]
            if inc is None:
                inc = (ac, 1)
        real = []
        for (s, v) in waits:
            if v > self.waited[eng].get(s, 0):
                self.waited[eng][s] = v
                real.append((s, v))
        evt = None
        if inc is not None:
            s, a = inc
            self.cnt[s] = self.cnt.get(s, 0) + a
            evt = (s, self.cnt[s])
        self.streams[eng].append((real, fn, inc))
        return evt

    def pe(self, fn, waits=(), inc=False):
        return self.emit("pe", fn, waits, ("spe", 1) if inc else None)

    def v(self, fn, waits=(), inc=False):
        return self.emit("v", fn, waits, ("sv", 1) if inc else None)

    def sc(self, fn, waits=(), inc=False):
        return self.emit("sc", fn, waits, ("ssc", 1) if inc else None)

    def gp(self, fn, waits=(), inc=False):
        return self.emit("gp", fn, waits, ("sgp", 1) if inc else None)

    def phase_barrier(self):
        waits = [(s, v) for s, v in self.cnt.items() if v > 0]
        for eng in ENGINES:
            fn = (lambda e: e.nop())
            self.emit(eng, fn, waits=list(waits))

    def dma_sy(self, sem, out, in_, waits=()):
        return self.emit("sy", lambda e: e.dma_start(out=out, in_=in_), waits, (sem, 16))

    def dma_gp(self, sem, out, in_, waits=(), **kw):
        return self.emit("gp", lambda e: e.dma_start(out=out, in_=in_, **kw), waits, (sem, 16))

    def play(self, block):
        eng_attr = {"pe": "tensor", "v": "vector", "sc": "scalar", "gp": "gpsimd", "sy": "sync"}
        for name in ENGINES:
            stream = self.streams[name]

            def body(eng, stream=stream):
                for waits, fn, inc in stream:
                    for s, v in waits:
                        eng.wait_ge(self.sems[s], v)
                    ins = fn(eng)
                    if inc is not None:
                        ins.then_inc(self.sems[inc[0]], inc[1])

            getattr(block, eng_attr[name])(body)


class Cfg:
    def __init__(self, SEQ=2048, NSEQ=2):
        self.SEQ = SEQ
        self.NSEQ = NSEQ
        self.NT = SEQ * NSEQ
        self.D = 1024
        self.NC = 8
        self.HEADS = 16
        self.HD = 64
        self.WIN = 128
        self.G = 64
        self.I = 2816
        self.NI = 22
        self.E = 8
        self.F = 2048
        self.NF = 16
        self.TT = 512
        self.NTILE = self.NT // self.TT
        self.WPT = self.TT // self.WIN
        self.NWIN = SEQ // self.WIN
        self.LW = 1024




class Bufs:
    pass


def alloc_bufs(nc, ctx, cfg, parts):
    b = Bufs()
    sb = lambda name, shape, dt: ctx.enter_context(nc.sbuf_tensor(name, shape, dt))
    ps = lambda name, shape, dt: ctx.enter_context(nc.psum_tensor(name, shape, dt))
    c = cfg

    b.x = sb("x_sb", [128, c.NC * c.NT], BF16)

    b.ident = sb("ident_bf", [128, 128], BF16)
    b.ones_bf = sb("ones_bf", [128, 128], BF16)
    b.ident32 = sb("ident_f32", [128, 128], F32) if "moe" in parts else None
    b.ones32 = None
    if any(p in parts for p in ("l0a", "l0f", "l1a")):
        b.onesc32 = sb("onesc32", [128, 1], F32)
        b.onescbf = sb("onescbf", [128, 1], BF16)
        b.onesr32 = sb("onesr32", [1, 128], F32)
    else:
        b.onesc32 = None

    b.pA = [ps(f"pA{i}", [128, 512], F32) for i in range(4)]
    b.pCtx = ps("pCtx", [128, 1024], F32)
    b.pT = [ps("pTa", [128, 256], BF16), ps("pTb", [128, 256], BF16)]

    b.xin = None

    b.acc = sb("acc_fm", [128, c.NC * c.TT], F32)
    if any(p in parts for p in ("l0a", "l0f", "l1a")):
        b.sq = sb("sq_bf", [128, 2 * c.TT], BF16)
        b.strow = sb("strow", [1, 4 * c.TT], F32)

    if "moe" in parts:
        b.w1e = sb("w1e", [128, c.NC * c.F], BF16)
        b.w2e = sb("w2e", [128, c.NC * c.F], BF16)
        b.w3s = [sb(f"w3s{i}", [128, c.NF * 128], BF16)[:] for i in range(2)]
        b.scores = sb("scores_m", [8, c.NT], BF16)
        b.g_fm = b.scores
        b.sel = sb("sel_m", [8, 8 * 128], BF16)
        b.hid = sb("hid_m", [128, c.NF * 512], BF16)
        b.gB = sb("gB_m", [128, 512], F32)
        b.msil = sb("msil", [128, 512], BF16)
        b.mt1 = sb("mt1", [128, 512], BF16)

        b.mst = sb("mst", [128, 12], F32)
        b.sctm = sb("sctm", [128, 8], BF16)
        b.top8 = sb("top8", [128, 8], BF16)
        b.rep = sb("rep", [128, 8], BF16)
        b.ms2 = sb("ms2", [128, 8], BF16)
        b.ms3 = sb("ms3", [128, 8], BF16)
        b.mk1 = sb("mk1", [128, 8], F32)
        b.mk2 = sb("mk2", [128, 8], F32)
        b.gtm = sb("gtm", [128, 8], F32)
        b.gbf = sb("gbf", [128, 8], BF16)
        b.wgT = sb("wgT", [128, c.NC * 8], BF16)
        hv = b.hid[:].bitcast(F32)
        b.sqf = hv[:, 0:c.D]
        b.wg_sb = hv[0:8, c.D:2 * c.D]
        b.otm = hv[:, 2 * c.D:3 * c.D]
        b.wload = b.hid[:].bitcast(BF16)[:, 4 * c.D:6 * c.D]
        gv = b.gB[:].bitcast(BF16)
        b.wtb = [gv[:, 0:128], gv[:, 128:256]]


    if "l0a" in parts or "l1a" in parts or "moe" in parts:
        nw = 4 if "l0a" in parts else 0
        b.wT = [sb(f"wT{i}", [128, c.NC * c.D], BF16) for i in range(nw)]
        if "moe" not in parts:
            b.wload = sb("wload", [128, c.NC // 4 * c.D], BF16)
    if "l0a" in parts:
        b.qb = sb("qb", [128, c.NC * c.TT], BF16)
        b.kb = sb("kb", [128, c.NC * c.TT], BF16)
        b.vb = sb("vb", [128, c.NC * c.TT], BF16)

    if "l1a" in parts:
        b.kS = sb("kS", [128, c.NC * c.SEQ], BF16)
        b.vS = sb("vS", [128, (c.SEQ // 128) * c.D], BF16)
        b.qg = sb("qg", [128, c.NC * 512], BF16)
        b.vtmp = sb("vtmp", [128, 512], BF16)
        b.cmask = sb("cmask", [128, 4 * 512], BF16)
        b.probs1 = sb("probs1", [128, 4 * 512], BF16)
        b.probsT1 = sb("probsT1", [128, 16 * 128], BF16)
        b.nm1 = sb("nm1", [128, 16], F32)
        b.wstg = [sb(f"wstg{i}", [128, c.NC * 128], BF16)[:] for i in range(2)]
        b.wstg_free = [None, None]
        b.wload_free = None
        b.wstr = [sb(f"wstr{i}", [128, c.NC * 128], BF16) for i in range(2)]

    if "l0f" in parts:
        b.wq_in = sb("wq_in", [128, c.I], F32)
        b.wtb = [sb(f"wtb{i}", [128, 128], BF16) for i in range(2)]
        b.qs_xn = sb("qs_xn", [128, c.LW], F32)
        b.qs_g = sb("qs_g", [128, c.LW], F32)
        b.qs_a = sb("qs_a", [128, c.LW], F32)
        b.qs_c = sb("qs_c", [128, c.LW], F32)
        b.qs_sc = sb("qs_sc", [128, 3 * (c.LW // 16)], F32)
        b.qs_h = sb("qs_h", [128, c.LW], F32)
        b.xtm32 = None
        b.xqtm = None
        b.xq_fm = sb("xq_fm", [128, c.NC * c.TT], BF16)
        b.hb = sb("hb", [128, 4 * c.I], BF16)
        b.hq_tm = sb("hq_tm", [128, c.I], BF16)
        b.hq_fm = sb("hq_fm", [128, c.NI * c.TT], BF16)
        b.wg_s = [sb("wg_s0", [128, c.NC * 512], BF16)] * 2
        b.wup_s = [sb("wup_s0", [128, c.NC * 512], BF16)] * 2
        b.wd_s = [sb("wd_s0", [128, c.NI * 128], BF16)] * 2
        b.silu = sb("silu_b", [128, 512], BF16)
        b.wq_out = b.hq_tm
        wiv = b.wq_in[:].bitcast(BF16)
        b.xin = [wiv[:, 0:c.D], wiv[:, c.D:2 * c.D]]
        b.xtm32 = b.wq_in[:][:, 0:c.D]
        b.xqtm = wiv[:, 2 * c.D:3 * c.D]

    if "l0a" in parts:
        b.kg = [sb(f"kg{s}", [128, c.NC * c.G], BF16) for s in range(c.NSEQ)]
        b.vgT = [sb(f"vgT{s}", [c.G, c.D], BF16) for s in range(c.NSEQ)]
        b.vwT = sb("vwT", [128, c.D], BF16)
        b.expb = [sb(f"expb{i}", [128, c.G + c.WIN], F32) for i in range(2)]
        b.nm = sb("nm_sums", [128, 8], F32)
        b.probs = [sb(f"probs{i}", [128, c.G + c.WIN], BF16) for i in range(2)]
        b.probsT = [sb(f"probsT{i}", [128, 256], BF16) for i in range(2)]
        b.ctxb = sb("ctxb", [128, c.NC * c.TT], BF16)

    if b.xin is None and "moe" in parts:
        w1v = b.w1e[:]
        b.xin = [w1v[:, 0:c.D], w1v[:, c.D:2 * c.D]]
    if b.xin is None:
        b.xin = [sb(f"xin{i}", [128, c.D], BF16)[:] for i in range(2)]
    return b




class Carver:
    def __init__(self, base_ap, limit_bytes):
        self.base = base_ap
        self.off = 0
        self.limit = limit_bytes

    def take(self, dt, cols, parts=128):
        sz = 4 if dt == F32 else 2
        off = (self.off + 31) // 32 * 32
        self.off = off + cols * sz
        assert self.off <= self.limit, f"arena overflow: {self.off} > {self.limit}"
        return self.base.bitcast(dt)[0:parts, off // sz: off // sz + cols]


def alloc_bufs_fused(nc, ctx, cfg):
    b = Bufs()
    c = cfg
    sb = lambda name, shape, dt: ctx.enter_context(nc.sbuf_tensor(name, shape, dt))
    ps = lambda name, shape, dt: ctx.enter_context(nc.psum_tensor(name, shape, dt))

    b.x = sb("x_sb", [128, c.NC * c.NT], BF16)
    b.acc = sb("acc_fm", [128, c.NC * c.TT], F32)
    b.sq = sb("sq_bf", [128, 2 * c.TT], BF16)
    b.strow = sb("strow", [1, 4 * c.TT], F32)
    b.ident = sb("ident_bf", [128, 128], BF16)
    b.ident32 = sb("ident_f32", [128, 128], F32)
    b.ones_bf = sb("ones_bf", [128, 128], BF16)
    b.ones32 = None
    b.onesc32 = sb("onesc32", [128, 1], F32)
    b.onescbf = sb("onescbf", [128, 1], BF16)
    b.onesr32 = sb("onesr32", [1, 128], F32)
    b.xin = [sb(f"xin{i}", [128, c.D], BF16)[:] for i in range(2)]

    b.pA = [ps(f"pA{i}", [128, 512], F32) for i in range(4)]
    b.pCtx = ps("pCtx", [128, 1024], F32)
    b.pT = [ps("pTa", [128, 256], BF16), ps("pTb", [128, 256], BF16)]

    rem = nc.sbuf_bytes_remaining - 64
    AR = sb("arena", [128, rem // 2 // 32 * 32], BF16)[:]
    arena_bytes = AR.shape[1] * 2
    b.ph = {}

    cv = Carver(AR, arena_bytes)
    d = {}
    d["wT"] = [cv.take(BF16, c.NC * c.D) for _ in range(4)]
    d["qb"] = cv.take(BF16, c.NC * c.TT)
    d["kb"] = cv.take(BF16, c.NC * c.TT)
    d["vb"] = cv.take(BF16, c.NC * c.TT)
    d["ctxb"] = cv.take(BF16, c.NC * c.TT)
    d["wload"] = d["ctxb"][:, 0:c.NC // 4 * c.D]
    d["kg"] = [cv.take(BF16, c.NC * c.G) for _ in range(c.NSEQ)]
    d["vgT"] = [cv.take(BF16, c.D, parts=c.G) for _ in range(c.NSEQ)]
    d["vwT"] = cv.take(BF16, c.D)
    d["expb"] = [cv.take(F32, c.G + c.WIN) for _ in range(2)]
    d["probs"] = [cv.take(BF16, c.G + c.WIN) for _ in range(2)]
    d["probsT"] = [cv.take(BF16, 256) for _ in range(2)]
    d["nm"] = cv.take(F32, 8)
    d["x32st"] = cv.take(F32, c.D)
    b.ph["l0a"] = d

    cv = Carver(AR, arena_bytes)
    d = {}
    d["qs_xn"] = cv.take(F32, c.LW)
    d["qs_g"] = cv.take(F32, c.LW)
    d["qs_a"] = cv.take(F32, c.LW)
    d["qs_c"] = cv.take(F32, c.LW)
    d["qs_sc"] = cv.take(F32, 3 * (c.LW // 16))
    d["qs_h"] = cv.take(F32, c.LW)
    d["xtm32"] = cv.take(F32, c.D)
    d["xqtm"] = cv.take(BF16, c.D)
    d["xq_fm"] = cv.take(BF16, c.NC * c.TT)
    hb = cv.take(BF16, 4 * c.I)
    d["hb"] = hb
    d["wq_in"] = hb.bitcast(F32)[:, 0:c.I]
    d["hq_tm"] = cv.take(BF16, c.I)
    d["wq_out"] = d["hq_tm"]
    d["hq_fm"] = cv.take(BF16, c.NI * c.TT)
    wg0 = cv.take(BF16, c.NC * 512)
    wu0 = cv.take(BF16, c.NC * 512)
    wd0 = cv.take(BF16, c.NI * 128)
    d["wg_s"] = [wg0, wg0]
    d["wup_s"] = [wu0, wu0]
    d["wd_s"] = [wd0, wd0]
    d["silu"] = cv.take(BF16, 512)
    d["wtb"] = [cv.take(BF16, 128) for _ in range(2)]
    b.ph["l0f"] = d

    cv = Carver(AR, arena_bytes)
    d = {}
    d["kS"] = cv.take(BF16, c.NC * c.SEQ)
    d["vS"] = cv.take(BF16, (c.SEQ // 128) * c.D)
    d["qg"] = cv.take(BF16, c.NC * 512)
    d["vtmp"] = cv.take(BF16, 512)
    d["cmask"] = cv.take(BF16, 4 * 512)
    d["probs1"] = cv.take(BF16, 4 * 512)
    d["probsT1"] = cv.take(BF16, 16 * 128)
    d["nm1"] = cv.take(F32, 16)
    d["wstr"] = [cv.take(BF16, c.NC * 128) for _ in range(2)]
    d["wstg"] = [cv.take(BF16, c.NC * 128) for _ in range(2)]
    d["wstg_free"] = [None, None]
    d["wload"] = cv.take(BF16, c.NC // 4 * c.D)
    d["wload_free"] = None
    b.ph["l1a"] = d

    cv = Carver(AR, arena_bytes)
    d = {}
    d["w1e"] = cv.take(BF16, c.NC * c.F)
    d["w2e"] = cv.take(BF16, c.NC * c.F)
    d["w3s"] = [cv.take(BF16, c.NF * 128) for _ in range(2)]
    hid = cv.take(BF16, c.NF * 512)
    d["hid"] = hid
    hv = hid.bitcast(F32)
    d["sqf"] = hv[:, 0:c.D]
    d["wg_sb"] = hv[0:8, c.D:2 * c.D]
    d["otm"] = hv[:, 2 * c.D:3 * c.D]
    R = cv.take(BF16, 3 * c.NC // 4 * c.D)
    d["scores"] = R[0:8, 0:c.NT]
    d["g_fm"] = d["scores"]
    d["wload"] = R[:, 0:c.NC // 4 * c.D]
    d["wload_free"] = None
    d["wstg"] = [R[:, c.NC // 4 * c.D:2 * (c.NC // 4) * c.D],
                 R[:, 2 * (c.NC // 4) * c.D:3 * (c.NC // 4) * c.D]]
    d["wstg_free"] = [None, None]
    d["gld"] = cv.take(BF16, 512, parts=8)
    d["sel"] = cv.take(BF16, 8 * 128, parts=8)
    gB = cv.take(F32, 512)
    d["gB"] = gB
    d["msil"] = cv.take(BF16, 512)
    d["mt1"] = cv.take(BF16, 512)
    d["mst"] = cv.take(F32, 12)
    d["sctm"] = cv.take(BF16, 8)
    d["top8"] = cv.take(BF16, 8)
    d["rep"] = cv.take(BF16, 8)
    d["ms2"] = cv.take(BF16, 8)
    d["ms3"] = cv.take(BF16, 8)
    d["mk1"] = cv.take(F32, 8)
    d["mk2"] = cv.take(F32, 8)
    d["gtm"] = cv.take(F32, 8)
    d["gbf"] = cv.take(BF16, 8)
    d["wgT"] = cv.take(BF16, c.NC * 8)
    b.ph["moe"] = d
    return b


def phase_bufs(b, name):
    nb = Bufs()
    nb.__dict__.update({k: v for k, v in b.__dict__.items() if k != "ph"})
    nb.__dict__.update(b.ph[name])
    return nb




def emit_consts(em, b, cfg):
    e1 = em.gp(lambda e: e.memset(b.ones_bf[:], 1.0), inc=True)
    em.gp(lambda e: e.affine_select(
        out=b.ident[:], in_=b.ones_bf[:], pattern=[[-1, 128]],
        compare_op=ALU.is_equal, fill=0.0, base=0, channel_multiplier=1), waits=[e1], inc=True)
    if b.ident32 is not None:
        e2 = em.v(lambda e: e.memset(b.acc[:, 0:128], 1.0), inc=True)
        em.gp(lambda e: e.affine_select(
            out=b.ident32[:], in_=b.acc[:, 0:128], pattern=[[-1, 128]],
            compare_op=ALU.is_equal, fill=0.0, base=0, channel_multiplier=1), waits=[e2], inc=True)
    if getattr(b, "onesc32", None) is not None:
        em.v(lambda e: e.memset(b.onesc32[:], 1.0), inc=True)
        em.v(lambda e: e.memset(b.onescbf[:], 1.0), inc=True)
        em.v(lambda e: e.memset(b.onesr32[:], 1.0), inc=True)
    return em.gp(lambda e: e.engine_nop(), waits=[em.ev("sv"), em.ev("sgp")], inc=True)


def emit_ingest(em, b, cfg, x_in):
    c = cfg
    xflat = x_in.rearrange("s t d -> (s t) d")
    tile_ev = []
    prev_use = [None, None]
    for ti in range(c.NT // 128):
        buf = ti % 2
        w = [prev_use[buf]] if prev_use[buf] else []
        ld = em.dma_gp(f"dxi{buf}", b.xin[buf], xflat[ti * 128:(ti + 1) * 128, :], waits=w)
        evs = None
        for ch in range(c.NC):
            slot = ch % 2
            te = em.pe(lambda e, ch=ch, buf=buf, slot=slot: e.transpose(
                b.pT[slot][:, 0:128],
                b.xin[buf][:, ch * 128:(ch + 1) * 128], b.ident[:]),
                waits=[ld, em.ev("ssc")], inc=True)
            evs = em.sc(lambda e, ch=ch, ti=ti, slot=slot: e.copy(
                b.x[:, ch * c.NT + ti * 128: ch * c.NT + (ti + 1) * 128],
                b.pT[slot][:, 0:128]),
                waits=[te], inc=True)
        prev_use[buf] = evs
        if (ti + 1) % (c.TT // 128) == 0:
            tile_ev.append(evs)
    return tile_ev


def _consts_wait(em, ce):
    return [ce]


def emit_weight_load_T(em, b, cfg, w_dram, dst, ce, ready_after=()):
    c = cfg
    O = w_dram.shape[0]
    D = w_dram.shape[1]
    noc = O // 128
    ndc = D // 128
    hoc = max(1, (c.NC // 4 * c.D) // D)
    wre = w_dram.rearrange("(oc p) d -> p oc d", p=128)
    last = None
    prev_reads = list(ready_after)
    for og in range(0, noc, hoc):
        og_n = min(hoc, noc - og)
        ld = em.dma_gp("dwl", b.wload[:, 0:og_n * D].rearrange("p (oc d) -> p oc d", d=D),
                       wre[:, og:og + og_n, :], waits=prev_reads)
        te = None
        for oc in range(og, og + og_n):
            for dc in range(ndc):
                slot = (oc * ndc + dc) % 2
                te = em.pe(lambda e, oc=oc, dc=dc, slot=slot, og=og: e.transpose(
                    b.pT[slot][:, 0:128],
                    b.wload[:, (oc - og) * D + dc * 128: (oc - og) * D + (dc + 1) * 128],
                    b.ident[:]),
                    waits=[ld, ce, em.ev("ssc")], inc=True)
                last = em.sc(lambda e, oc=oc, dc=dc, slot=slot: e.copy(
                    dst[:, dc * O + oc * 128: dc * O + (oc + 1) * 128],
                    b.pT[slot][:, 0:128]),
                    waits=[te], inc=True)
        prev_reads = [te]
    return last




def emit_ln_fm(em, b, cfg, ti, acc_ev, prev_ln_ev=None, extra_waits=(), keep_acc=False):
    c = cfg
    T = c.TT
    s1 = None
    for ch in range(c.NC):
        s1 = em.pe(lambda e, ch=ch: e.matmul(
            b.pA[2][0:1, 0:T], b.onesc32[:], b.acc[:, ch * T:(ch + 1) * T],
            start=(ch == 0), stop=(ch == c.NC - 1)),
            waits=[acc_ev] + ([prev_ln_ev] if prev_ln_ev else [])
                  + (list(extra_waits) if ch == 0 else []),
            inc=(ch == c.NC - 1))
    s2 = None
    sq_evs = [None, None]
    for ch in range(c.NC):
        sl = ch % 2
        sq_evs[sl] = em.sc(lambda e, ch=ch, sl=sl: e.activation(
            b.sq[:, sl * T:(sl + 1) * T], b.acc[:, ch * T:(ch + 1) * T],
            AF.Square, bias=0.0, scale=1.0),
            waits=[acc_ev, s2], inc=True)
        s2 = em.pe(lambda e, ch=ch, sl=sl: e.matmul(
            b.pA[3][0:1, 0:T], b.onescbf[:], b.sq[:, sl * T:(sl + 1) * T],
            start=(ch == 0), stop=(ch == c.NC - 1)),
            waits=[sq_evs[sl]] + ([prev_ln_ev] if prev_ln_ev and ch == 0 else []),
            inc=True)
    st = b.strow
    t1 = st[0:1, 0:T]
    m_ = st[0:1, T:2 * T]
    t2 = st[0:1, 2 * T:3 * T]
    mr = st[0:1, 3 * T:4 * T]
    em.v(lambda e: e.tensor_scalar(m_, b.pA[2][0:1, 0:T], 1.0 / c.D, None, ALU.mult),
         waits=[s1], inc=True)
    em.v(lambda e: e.tensor_scalar(t1, b.pA[3][0:1, 0:T], 1.0 / c.D, None, ALU.mult),
         waits=[s2], inc=True)
    em.v(lambda e: e.tensor_tensor(t2, m_, m_, ALU.mult), inc=True)
    em.v(lambda e: e.tensor_sub(t1, t1, t2), inc=True)
    ve = em.v(lambda e: e.tensor_scalar(t1, t1, 1e-5, None, ALU.add), inc=True)
    se = em.sc(lambda e: e.activation(t2, t1, AF.Sqrt, bias=0.0, scale=1.0),
               waits=[ve], inc=True)
    re_ = em.v(lambda e: e.reciprocal(t1, t2), waits=[se], inc=True)
    mre = em.v(lambda e: e.tensor_tensor(mr, m_, t1, ALU.mult), inc=True)
    rstd = t1
    bc1 = em.pe(lambda e: e.matmul(b.pA[2][:, 0:T], b.onesr32[:], rstd,
                                   start=True, stop=True), waits=[re_, mre], inc=True)
    bc2 = em.pe(lambda e: e.matmul(b.pA[3][:, 0:T], b.onesr32[:], mr,
                                   start=True, stop=True), waits=[mre], inc=True)
    out_ev = None
    for ch in range(c.NC):
        em.v(lambda e, ch=ch: e.tensor_tensor(
            b.acc[:, ch * T:(ch + 1) * T],
            b.acc[:, ch * T:(ch + 1) * T], b.pA[2][:, 0:T], ALU.mult),
            waits=[bc1], inc=True)
        if keep_acc:
            em.v(lambda e, ch=ch: e.tensor_tensor(
                b.acc[:, ch * T:(ch + 1) * T],
                b.acc[:, ch * T:(ch + 1) * T],
                b.pA[3][:, 0:T], ALU.subtract),
                waits=[bc2], inc=True)
            out_ev = em.v(lambda e, ch=ch, ti=ti: e.tensor_copy(
                b.x[:, ch * c.NT + ti * T: ch * c.NT + (ti + 1) * T],
                b.acc[:, ch * T:(ch + 1) * T]), inc=True)
        else:
            out_ev = em.v(lambda e, ch=ch, ti=ti: e.tensor_tensor(
                b.x[:, ch * c.NT + ti * T: ch * c.NT + (ti + 1) * T],
                b.acc[:, ch * T:(ch + 1) * T],
                b.pA[3][:, 0:T], ALU.subtract),
                waits=[bc2], inc=True)
    return out_ev




def emit_egress(em, b, cfg, x_out, waits):
    c = cfg
    eg = [b.acc[:, 0:c.D], b.acc[:, c.D:2 * c.D]]
    xflat = x_out.rearrange("s t d -> (s t) d")
    dmas = []
    prev_dma = [None, None]
    for ti in range(c.NT // 128):
        sl = ti % 2
        ev = None
        for ch in range(c.NC):
            slot = ch % 2
            te = em.pe(lambda e, ch=ch, ti=ti, slot=slot: e.transpose(
                b.pT[slot][:, 0:128],
                b.x[:, ch * c.NT + ti * 128: ch * c.NT + (ti + 1) * 128],
                b.ident[:]),
                waits=list(waits) + [em.ev("ssc")], inc=True)
            ev = em.sc(lambda e, ch=ch, slot=slot, sl=sl: e.copy(
                eg[sl][:, ch * 128:(ch + 1) * 128],
                b.pT[slot][:, 0:128]),
                waits=[te, prev_dma[sl]], inc=True)
        dma = em.dma_sy(f"dout{sl}", xflat[ti * 128:(ti + 1) * 128, :],
                        eg[sl], waits=[ev])
        prev_dma[sl] = dma
        dmas.append(dma)
    return dmas




def emit_l0_attn(em, b, cfg, ce, x_tile_evs, prev_ln_ev=None, x32_dram=None):
    c = cfg
    T = c.TT
    wq_d, wk_d, wv_d, wo_d = b.l0a_w
    wq_ev = emit_weight_load_T(em, b, cfg, wq_d, b.wT[0][:], ce)
    wk_ev = emit_weight_load_T(em, b, cfg, wk_d, b.wT[1][:], ce, ready_after=[wq_ev])
    wv_ev = emit_weight_load_T(em, b, cfg, wv_d, b.wT[2][:], ce, ready_after=[wk_ev])
    wo_ev = emit_weight_load_T(em, b, cfg, wo_d, b.wT[3][:], ce, ready_after=[wv_ev])

    tiles_per_seq = c.SEQ // T
    ln_evs = []
    kg_ev = [None] * c.NSEQ
    vg_ev = [None] * c.NSEQ
    pSc_free = [None, None]
    probs_free = [None, None]
    expb_free = [None, None]
    probsT_free = [None, None]
    pT_free = [None, None]
    pCtx_free = None
    acc_free = None
    x32d = None
    prev_qkv_evac = None

    for ti in range(c.NTILE):
        s = ti // tiles_per_seq
        xev = x_tile_evs[ti]
        evac_evs = []
        for dst_i, (dst, wti, wev) in enumerate(
                ((b.qb, 0, wq_ev), (b.kb, 1, wk_ev), (b.vb, 2, wv_ev))):
            for oc in range(c.NC):
                bank = (dst_i * c.NC + oc) % 2
                mm = None
                for dc in range(c.NC):
                    w = [xev, wev]
                    if dc == 0:
                        w.append(pSc_free[bank] or em.ev("sv"))
                    mm = em.pe(lambda e, wti=wti, oc=oc, dc=dc, bank=bank, ti=ti: e.matmul(
                        b.pA[bank][:, 0:T],
                        b.wT[wti][:, dc * c.D + oc * 128: dc * c.D + (oc + 1) * 128],
                        b.x[:, dc * c.NT + ti * T: dc * c.NT + (ti + 1) * T],
                        start=(dc == 0), stop=(dc == c.NC - 1)),
                        waits=w, inc=(dc == c.NC - 1))
                if dst_i == 0:
                    ev = em.v(lambda e, oc=oc, bank=bank: e.tensor_scalar(
                        b.qb[:, oc * T:(oc + 1) * T], b.pA[bank][:, 0:T],
                        0.125, None, ALU.mult), waits=[mm], inc=True)
                else:
                    ev = em.sc(lambda e, dst=dst, oc=oc, bank=bank: e.copy(
                        dst[:, oc * T:(oc + 1) * T], b.pA[bank][:, 0:T]),
                        waits=[mm], inc=True)
                pSc_free[bank] = ev
                evac_evs.append(ev)
        qkv_ev = evac_evs[-1]
        qkv_all = [em.ev("sv"), em.ev("ssc")]

        for w4 in range(c.WPT):
            widx = ti * c.WPT + w4
            wi = widx % (c.SEQ // c.WIN)
            t0 = w4 * c.WIN
            vw_evs = []
            for chk in range(c.NC):
                slot = chk % 2
                te = em.pe(lambda e, chk=chk, slot=slot, t0=t0: e.transpose(
                    b.pT[slot][:, 0:128],
                    b.vb[:, chk * T + t0: chk * T + t0 + c.WIN],
                    b.ident[:]),
                    waits=qkv_all + [pT_free[slot] or em.ev("ssc")], inc=True)
                ev = em.sc(lambda e, chk=chk, slot=slot: e.copy(
                    b.vwT[:, chk * 128:(chk + 1) * 128],
                    b.pT[slot][:, 0:128]),
                    waits=[te], inc=True)
                pT_free[slot] = ev
                vw_evs.append(ev)
            vw_ev = vw_evs[-1]

            if wi == 0:
                kg_ev[s] = em.v(lambda e, s=s, t0=t0: e.tensor_copy(
                    b.kg[s][:].rearrange("p (c g) -> p c g", g=c.G),
                    b.kb[:].rearrange("p (c t) -> p c t", t=T)[:, :, t0:t0 + c.G]),
                    waits=qkv_all, inc=True)
                gevs = []
                for chk in range(c.NC):
                    slot = chk % 2
                    te = em.pe(lambda e, chk=chk, slot=slot, t0=t0: e.transpose(
                        b.pT[slot][0:c.G, 0:128],
                        b.vb[:, chk * T + t0: chk * T + t0 + c.G],
                        b.ident[:]),
                        waits=qkv_all + [pT_free[slot] or em.ev("ssc")], inc=True)
                    ev = em.sc(lambda e, chk=chk, slot=slot, s=s: e.copy(
                        b.vgT[s][0:c.G, chk * 128:(chk + 1) * 128],
                        b.pT[slot][0:c.G, 0:128]),
                        waits=[te], inc=True)
                    pT_free[slot] = ev
                    gevs.append(ev)
                vg_ev[s] = gevs[-1]

            ctx_mm_evs = []
            for h in range(c.HEADS):
                dc = h // 2
                r0 = (h % 2) * 64
                bank = h % 2
                hb = h % 2
                w0 = [qkv_ev, kg_ev[s], pSc_free[bank] or em.ev("sv")]
                em.pe(lambda e, dc=dc, r0=r0, bank=bank, s=s, t0=t0: e.matmul(
                    b.pA[bank][:, 0:c.G],
                    b.qb[r0:r0 + 64, dc * T + t0: dc * T + t0 + c.WIN],
                    b.kg[s][r0:r0 + 64, dc * c.G: (dc + 1) * c.G],
                    start=True, stop=True), waits=w0, inc=False)
                sc_mm = em.pe(lambda e, dc=dc, r0=r0, bank=bank, t0=t0: e.matmul(
                    b.pA[bank][:, c.G:c.G + c.WIN],
                    b.qb[r0:r0 + 64, dc * T + t0: dc * T + t0 + c.WIN],
                    b.kb[r0:r0 + 64, dc * T + t0: dc * T + t0 + c.WIN],
                    start=True, stop=True), inc=True)
                nmx = em.v(lambda e, bank=bank, hb=hb: e.reduce_max(
                    b.nm[:, hb:hb + 1], b.pA[bank][:, 0:c.G + c.WIN],
                    axis=AX.X, negate=True), waits=[sc_mm], inc=True)
                ex = em.sc(lambda e, bank=bank, hb=hb: e.activation(
                    b.expb[hb][:], b.pA[bank][:, 0:c.G + c.WIN], AF.Exp,
                    bias=b.nm[:, hb:hb + 1], scale=1.0,
                    accum_out=b.nm[:, 2 + hb:3 + hb]),
                    waits=[nmx, expb_free[hb] or em.ev("sv")], inc=True)
                pSc_free[bank] = ex
                rc = em.v(lambda e, hb=hb: e.reciprocal(
                    b.nm[:, 4 + hb:5 + hb], b.nm[:, 2 + hb:3 + hb]), waits=[ex], inc=True)
                pn = em.v(lambda e, hb=hb: e.tensor_scalar(
                    b.probs[hb][:], b.expb[hb][:], b.nm[:, 4 + hb:5 + hb], None,
                    ALU.mult), waits=[rc, probs_free[hb] or em.ev("spe")], inc=True)
                expb_free[hb] = pn
                tl = em.pe(lambda e, hb=hb: e.transpose(
                    b.pT[0][:, 0:128], b.probs[hb][:, c.G:c.G + c.WIN], b.ident[:]),
                    waits=[pn, pT_free[0] or em.ev("ssc")], inc=True)
                tg = em.pe(lambda e, hb=hb: e.transpose(
                    b.pT[1][0:c.G, 0:128], b.probs[hb][:, 0:c.G], b.ident[:]),
                    waits=[pn, pT_free[1] or em.ev("ssc")], inc=True)
                pe1 = em.sc(lambda e, hb=hb: e.copy(
                    b.probsT[hb][:, 0:128], b.pT[0][:, 0:128]),
                    waits=[tl, probsT_free[hb] or em.ev("spe")], inc=True)
                pe2 = em.sc(lambda e, hb=hb: e.copy(
                    b.probsT[hb][0:c.G, 128:128 + c.WIN], b.pT[1][0:c.G, 0:128]),
                    waits=[tg], inc=True)
                pT_free[0] = pe1
                pT_free[1] = pe2
                probs_free[hb] = pe2
                cw = [pe1, pe2, vw_ev, vg_ev[s]]
                if pCtx_free is not None and h == 0:
                    cw.append(pCtx_free)
                em.pe(lambda e, h=h, dc=dc, r0=r0, hb=hb, s=s: e.matmul(
                    b.pCtx[r0:r0 + 64, dc * 128:(dc + 1) * 128],
                    b.vgT[s][:, h * 64:(h + 1) * 64],
                    b.probsT[hb][0:c.G, 128:128 + c.WIN],
                    start=True, stop=False), waits=cw, inc=False)
                cme = em.pe(lambda e, h=h, dc=dc, r0=r0, hb=hb: e.matmul(
                    b.pCtx[r0:r0 + 64, dc * 128:(dc + 1) * 128],
                    b.vwT[:, h * 64:(h + 1) * 64],
                    b.probsT[hb][:, 0:128],
                    start=False, stop=True), inc=True)
                probsT_free[hb] = cme
                ctx_mm_evs.append(cme)
            ctx_ev = None
            for chk in range(c.NC):
                ctx_ev = em.v(lambda e, chk=chk, t0=t0: e.tensor_copy(
                    b.ctxb[:, chk * T + t0: chk * T + t0 + c.WIN],
                    b.pCtx[:, chk * 128:(chk + 1) * 128]),
                    waits=[ctx_mm_evs[-1]], inc=True)
            pCtx_free = ctx_ev

        acc_ev = None
        for oc in range(c.NC):
            bank = oc % 2
            mm = None
            for dc in range(c.NC):
                w = [pCtx_free, wo_ev]
                if dc == 0:
                    w.append(pSc_free[bank] or em.ev("sv"))
                mm = em.pe(lambda e, oc=oc, dc=dc, bank=bank: e.matmul(
                    b.pA[bank][:, 0:T],
                    b.wT[3][:, dc * c.D + oc * 128: dc * c.D + (oc + 1) * 128],
                    b.ctxb[:, dc * T:(dc + 1) * T],
                    start=(dc == 0), stop=(dc == c.NC - 1)),
                    waits=w, inc=(dc == c.NC - 1))
            acc_ev = em.v(lambda e, oc=oc, bank=bank, ti=ti: e.tensor_tensor(
                b.acc[:, oc * T:(oc + 1) * T], b.pA[bank][:, 0:T],
                b.x[:, oc * c.NT + ti * T: oc * c.NT + (ti + 1) * T], ALU.add),
                waits=[mm] + ([acc_free] if oc == 0 else []), inc=True)
            pSc_free[bank] = acc_ev
        ln_ev = emit_ln_fm(em, b, cfg, ti, acc_ev, prev_ln_ev,
                           keep_acc=(x32_dram is not None))
        prev_ln_ev = ln_ev
        if x32_dram is not None:
            for sub in range(4):
                ev = None
                for ch in range(c.NC):
                    slot = ch % 2
                    ptv = b.pT[slot][:].bitcast(F32)
                    te = em.pe(lambda e, ch=ch, sub=sub, slot=slot, ptv=ptv: e.transpose(
                        ptv[:, 0:128],
                        b.acc[:, ch * T + sub * 128: ch * T + sub * 128 + 128],
                        b.ident32[:]),
                        waits=[ln_ev, em.ev("ssc")], inc=True)
                    ev = em.sc(lambda e, ch=ch, slot=slot, ptv=ptv: e.copy(
                        b.x32st[:, ch * 128:(ch + 1) * 128], ptv[:, 0:128]),
                        waits=[te, x32d], inc=True)
                r0 = ti * T + sub * 128
                x32d = em.dma_sy("dx32", x32_dram[r0:r0 + 128, :], b.x32st[:],
                                 waits=[ev])
            acc_free = em.ev("spe")
            ln_ev = x32d
        else:
            acc_free = None
        ln_evs.append(ln_ev)
    return ln_evs




def build_program(cfg, parts):
    c = cfg
    nc = bass.Bass("TRN2", target_bir_lowering=False)
    x_in = nc.dram_tensor("x_in", [c.NSEQ, c.SEQ, c.D], F32, kind="ExternalInput").ap()
    x_out = nc.dram_tensor("x_out", [c.NSEQ, c.SEQ, c.D], F32, kind="ExternalOutput").ap()

    fused = len(parts) > 1 or parts == ["moe"]
    with ExitStack() as ctx:
        if fused:
            b = alloc_bufs_fused(nc, ctx, cfg)
        else:
            b = alloc_bufs(nc, ctx, cfg, parts)
        if "l0a" in parts:
            b.l0a_w = tuple(
                nc.dram_tensor(n, [c.D, c.D], F32, kind="ExternalInput").ap()
                for n in ("l0_wq", "l0_wk", "l0_wv", "l0_wo"))
        if "l1a" in parts:
            b.l1a_w = tuple(
                nc.dram_tensor(n, [c.D, c.D], F32, kind="ExternalInput").ap()
                for n in ("l1_wq", "l1_wk", "l1_wv"))
            b.l1a_wT = tuple(
                nc.dram_tensor(n, [128, c.NC, c.D], BF16).ap()
                for n in ("l1_wqT", "l1_wkT", "l1_wvT"))
        if "moe" in parts:
            b.moe_w = (
                nc.dram_tensor("l1_wg", [c.E, c.D], F32, kind="ExternalInput").ap(),
                nc.dram_tensor("l1_w1", [c.E, c.F, c.D], F32, kind="ExternalInput").ap(),
                nc.dram_tensor("l1_w2", [c.E, c.F, c.D], F32, kind="ExternalInput").ap(),
                nc.dram_tensor("l1_w3", [c.E, c.D, c.F], F32, kind="ExternalInput").ap(),
            )
            b.moe_w1T = nc.dram_tensor("m_w1T", [c.E, 128, c.NC, c.F], BF16).ap()
            b.moe_w2T = nc.dram_tensor("m_w2T", [c.E, 128, c.NC, c.F], BF16).ap()
            b.moe_w3T = nc.dram_tensor("m_w3T", [c.E, 128, c.NF, c.D], BF16).ap()
            b.accm = nc.dram_tensor("m_acc", [128, c.NC, c.NT], F32).ap()
        if "l0f" in parts:
            b.l0f_w = (
                nc.dram_tensor("l0_wgate", [c.I, c.D], F32, kind="ExternalInput").ap(),
                nc.dram_tensor("l0_wup", [c.I, c.D], F32, kind="ExternalInput").ap(),
                nc.dram_tensor("l0_wdown", [c.D, c.I], F32, kind="ExternalInput").ap(),
            )
            b.l0f_wgT = nc.dram_tensor("l0f_wgT", [128, c.NC, c.I], BF16).ap()
            b.l0f_wupT = nc.dram_tensor("l0f_wupT", [128, c.NC, c.I], BF16).ap()
            b.l0f_wdT = nc.dram_tensor("l0f_wdT", [128, c.NI, c.D], BF16).ap()
        if fused:
            b.x32s = nc.dram_tensor("x32s", [c.NT, c.D], F32).ap()
        if "moe" in parts:
            b.g_dram = nc.dram_tensor("g_dram", [8, c.NT], BF16).ap()
        em = Em(nc)

        bp = phase_bufs(b, "l0a") if fused else b
        ce = emit_consts(em, bp, cfg)
        tile_evs = emit_ingest(em, bp, cfg, x_in)
        final_evs = tile_evs
        if "l0a" in parts:
            final_evs = emit_l0_attn(em, bp, cfg, ce, tile_evs,
                                     x32_dram=b.x32s if fused else None)
        if "l0f" in parts:
            if fused:
                em.phase_barrier()
                bp = phase_bufs(b, "l0f")
                x32 = b.x32s
            else:
                x32 = x_in if parts[0] == "l0f" else None
            final_evs = emit_l0_ffn(em, bp, cfg, ce, final_evs, x32_dram=x32)
        if "l1a" in parts:
            if fused:
                em.phase_barrier()
                bp = phase_bufs(b, "l1a")
            final_evs = emit_l1_attn(em, bp, cfg, ce, final_evs)
        if "moe" in parts:
            if fused:
                em.phase_barrier()
                bp = phase_bufs(b, "moe")
            out_dmas = emit_l1_moe(em, bp, cfg, ce, final_evs, x_out)
        else:
            out_dmas = emit_egress(em, bp, cfg, x_out, [final_evs[-1]] if final_evs else [])
        finals = {}
        for s, v in out_dmas:
            finals[s] = max(finals.get(s, 0), v)
        em.emit("sy", lambda e: e.nop(), waits=sorted(finals.items()))

        for s in sorted(em.cnt.keys()):
            em.sems[s] = ctx.enter_context(nc.semaphore(s))
        block = ctx.enter_context(nc.Block())
        em.play(block)
    return nc




def emit_quant(em, b, cfg, src, dst, W, waits, eng="v", qs=None):
    c = cfg
    E = em.v if eng == "v" else em.gp
    q_xn, q_g, q_a, q_c, q_sc = qs if qs is not None else (
        b.qs_xn, b.qs_g, b.qs_a, b.qs_c, b.qs_sc)
    ev = None
    for lo in range(0, W, c.LW):
        wsub = min(c.LW, W - lo)
        nb = wsub // 16
        s3 = src[:, lo:lo + wsub].rearrange("p (a b) -> p a b", b=16)
        sc16 = q_sc[:, 0:nb]
        scl = q_sc[:, nb:2 * nb]
        rscl = q_sc[:, 2 * nb:3 * nb]
        xn = q_xn[:, 0:wsub]
        xn3 = xn.rearrange("p (a b) -> p a b", b=16)
        g = q_g[:, 0:wsub]
        a = q_a[:, 0:wsub]
        cc = q_c[:, 0:wsub]
        rev = E(lambda e, s3=s3, sc16=sc16: e.reduce_max(
            sc16, s3, axis=AX.X, apply_absolute_value=True), waits=list(waits))
        sev2 = em.v(lambda e, scl=scl, sc16=sc16: e.tensor_scalar(
            scl, sc16, 1.0 / 6.0, 1e-12, ALU.mult, ALU.max), waits=[rev])
        rcev = em.v(lambda e, rscl=rscl, scl=scl: e.reciprocal(rscl, scl))
        vev = E(lambda e, xn3=xn3, s3=s3, rscl=rscl, nb=nb: e.tensor_tensor(
            xn3, s3, rscl.unsqueeze(2).broadcast_to((128, nb, 16)), ALU.mult),
            waits=[rcev])
        em.sc(lambda e, g=g, xn=xn: e.sign(g, xn), waits=[vev])
        sev = em.sc(lambda e, a=a, xn=xn: e.activation(a, xn, AF.Abs, bias=0.0, scale=1.0))
        E(lambda e, cc=cc, a=a: e.tensor_scalar(cc, a, 0.25, None, ALU.is_gt),
          waits=[sev])
        for mid in (0.75, 1.25, 1.75):
            E(lambda e, cc=cc, a=a, mid=mid: e.scalar_tensor_tensor(
                cc, a, mid, cc, ALU.is_gt, ALU.add))
        E(lambda e, xn=xn, a=a: e.tensor_scalar(xn, a, 2.5, None, ALU.is_gt))
        E(lambda e, xn=xn, a=a: e.scalar_tensor_tensor(
            xn, a, 3.5, xn, ALU.is_gt, ALU.add))
        E(lambda e, cc=cc, xn=xn: e.scalar_tensor_tensor(
            cc, cc, 0.5, xn, ALU.mult, ALU.add))
        E(lambda e, a=a: e.tensor_scalar(a, a, 5.0, 2.0, ALU.is_gt, ALU.mult))
        E(lambda e, cc=cc, a=a: e.tensor_tensor(cc, cc, a, ALU.add))
        gev = E(lambda e, cc=cc, g=g: e.tensor_tensor(cc, cc, g, ALU.mult))
        cc3 = cc.rearrange("p (a b) -> p a b", b=16)
        d3 = dst[:, lo:lo + wsub].rearrange("p (a b) -> p a b", b=16)
        ev = E(lambda e, d3=d3, cc3=cc3, scl=scl, nb=nb: e.tensor_tensor(
            d3, cc3, scl.unsqueeze(2).broadcast_to((128, nb, 16)), ALU.mult),
            waits=[gev])
    return ev




def emit_l0f_weight_prep(em, b, cfg, ce, after=()):
    c = cfg
    jobs = (
        (b.l0f_w[0], b.l0f_wgT, c.I, c.D),
        (b.l0f_w[1], b.l0f_wupT, c.I, c.D),
        (b.l0f_w[2], b.l0f_wdT, c.D, c.I),
    )
    prev_ld_reads = None
    wtb_free = [None, None]
    for (w_d, wT_d, R, W) in jobs:
        nrc = R // 128
        nwc = W // 128
        wre = w_d.rearrange("(rc p) d -> p rc d", p=128)
        for rc in range(nrc):
            ld = em.dma_gp("dwqin", b.wq_in[:, 0:W], wre[:, rc, :],
                           waits=[prev_ld_reads, ce] + list(after))
            qev = emit_quant(em, b, cfg, b.wq_in[:, 0:W], b.wq_out[:, 0:W], W, [ld])
            prev_ld_reads = qev
            for wc in range(nwc):
                slot = wc % 2
                te = em.pe(lambda e, wc=wc, slot=slot: e.transpose(
                    b.pT[slot][:, 0:128],
                    b.wq_out[:, wc * 128:(wc + 1) * 128], b.ident[:]),
                    waits=[qev, em.ev("ssc"), ce], inc=True)
                cev = em.sc(lambda e, slot=slot: e.copy(
                    b.wtb[slot][:], b.pT[slot][:, 0:128]),
                    waits=[te, wtb_free[slot]], inc=True)
                dma = em.dma_gp(f"dwtb{slot}", wT_d[:, wc, rc * 128:(rc + 1) * 128],
                                b.wtb[slot][:], waits=[cev])
                wtb_free[slot] = dma
    return [em.ev("dwtb0"), em.ev("dwtb1")]


def emit_l0_ffn(em, b, cfg, ce, x_tile_evs, prev_ln_ev=None, x32_dram=None):
    c = cfg
    T = c.TT
    if x32_dram is None:
        xflat32 = None
    elif len(x32_dram.shape) == 3:
        xflat32 = x32_dram.rearrange("s t d -> (s t) d")
    else:
        xflat32 = x32_dram
    prep_evs = emit_l0f_weight_prep(em, b, cfg, ce, after=[x_tile_evs[-1]])
    NFB = (c.I + 511) // 512
    ln_evs = []
    pa_free = [None, None, None, None]
    pT_free = [None, None]
    xq_rd = None
    hb_rd = [None] * 4
    hqfm_rd = None
    wg_free = [None, None]
    wd_free = [None, None]
    silu_free = None
    xtm_free = None
    xqtm_free = None
    hqtm_free = None

    for ti in range(c.NTILE):
        xev = x_tile_evs[ti]
        xq_ev = None
        for sub in range(4):
            if xflat32 is not None:
                r0 = ti * T + sub * 128
                ev32 = em.dma_gp("dxq32", b.xtm32, xflat32[r0:r0 + 128, :],
                                 waits=[xev, xtm_free] + prep_evs)
            else:
                ev32 = None
                for ch in range(c.NC):
                    slot = ch % 2
                    te = em.pe(lambda e, ch=ch, sub=sub, ti=ti, slot=slot: e.transpose(
                        b.pT[slot][:, 0:128],
                        b.x[:, ch * c.NT + ti * T + sub * 128: ch * c.NT + ti * T + sub * 128 + 128],
                        b.ident[:]),
                        waits=[xev, em.ev("ssc"), ce], inc=True)
                    ev32 = em.sc(lambda e, ch=ch, slot=slot: e.copy(
                        b.xtm32[:, ch * 128:(ch + 1) * 128], b.pT[slot][:, 0:128]),
                        waits=[te, xtm_free], inc=True)
            qev = emit_quant(em, b, cfg, b.xtm32, b.xqtm, c.D,
                             [ev32, xqtm_free])
            xtm_free = qev
            for ch in range(c.NC):
                slot = ch % 2
                te = em.pe(lambda e, ch=ch, slot=slot: e.transpose(
                    b.pT[slot][:, 0:128],
                    b.xqtm[:, ch * 128:(ch + 1) * 128], b.ident[:]),
                    waits=[qev, em.ev("ssc")], inc=True)
                xq_ev = em.sc(lambda e, ch=ch, sub=sub, slot=slot: e.copy(
                    b.xq_fm[:, ch * T + sub * 128: ch * T + sub * 128 + 128],
                    b.pT[slot][:, 0:128]),
                    waits=[te, xq_rd], inc=True)
            xqtm_free = xq_ev
        h_ev = None
        for fb in range(NFB):
            fw = min(512, c.I - fb * 512)
            gsl = 0
            wgl = em.dma_gp(f"dwg{gsl}",
                            b.wg_s[gsl][:, 0:c.NC * fw].rearrange("p (dc f) -> p dc f", f=fw),
                            b.l0f_wgT[:, :, fb * 512: fb * 512 + fw],
                            waits=[wg_free[gsl]] + prep_evs)
            wul = em.dma_gp(f"dwu{gsl}",
                            b.wup_s[gsl][:, 0:c.NC * fw].rearrange("p (dc f) -> p dc f", f=fw),
                            b.l0f_wupT[:, :, fb * 512: fb * 512 + fw],
                            waits=[wg_free[gsl]] + prep_evs)
            last_mm = [None, None]
            for sub in range(4):
                bsel = (fb * 4 + sub) % 2
                bg, bu = (0, 1) if bsel == 0 else (2, 3)
                mmg = None
                mmu = None
                for dc in range(c.NC):
                    w = [xq_ev, wgl]
                    if dc == 0:
                        w.append(pa_free[bg])
                    mmg = em.pe(lambda e, dc=dc, sub=sub, fw=fw, gsl=gsl, bg=bg: e.matmul(
                        b.pA[bg][:, 0:fw],
                        b.xq_fm[:, dc * T + sub * 128: dc * T + sub * 128 + 128],
                        b.wg_s[gsl][:, dc * fw: (dc + 1) * fw],
                        start=(dc == 0), stop=(dc == c.NC - 1)),
                        waits=w, inc=(dc == c.NC - 1))
                for dc in range(c.NC):
                    w = [xq_ev, wul]
                    if dc == 0:
                        w.append(pa_free[bu])
                    mmu = em.pe(lambda e, dc=dc, sub=sub, fw=fw, gsl=gsl, bu=bu: e.matmul(
                        b.pA[bu][:, 0:fw],
                        b.xq_fm[:, dc * T + sub * 128: dc * T + sub * 128 + 128],
                        b.wup_s[gsl][:, dc * fw: (dc + 1) * fw],
                        start=(dc == 0), stop=(dc == c.NC - 1)),
                        waits=w, inc=(dc == c.NC - 1))
                sev = em.sc(lambda e, fw=fw, bg=bg: e.activation(
                    b.silu[:, 0:fw], b.pA[bg][:, 0:fw], AF.Silu, bias=0.0, scale=1.0),
                    waits=[mmg, silu_free], inc=True)
                pa_free[bg] = sev
                h_ev = em.v(lambda e, sub=sub, fb=fb, fw=fw, bu=bu: e.tensor_tensor(
                    b.hb[:, sub * c.I + fb * 512: sub * c.I + fb * 512 + fw],
                    b.silu[:, 0:fw], b.pA[bu][:, 0:fw], ALU.mult),
                    waits=[mmu, sev, hb_rd[sub]], inc=True)
                pa_free[bu] = h_ev
                silu_free = h_ev
                last_mm[0] = mmg
                last_mm[1] = mmu
            wg_free[gsl] = last_mm[1]
        hq_ev = None
        for sub in range(4):
            qevs = []
            for lo in range(0, c.I, c.LW):
                wsub = min(c.LW, c.I - lo)
                pc = em.v(lambda e, sub=sub, lo=lo, wsub=wsub: e.tensor_copy(
                    b.qs_h[:, 0:wsub], b.hb[:, sub * c.I + lo: sub * c.I + lo + wsub]),
                    waits=[h_ev, hqtm_free])
                qevs.append(emit_quant(em, b, cfg, b.qs_h[:, 0:wsub],
                                       b.hq_tm[:, lo:lo + wsub], wsub, [pc]))
            hb_rd[sub] = qevs[-1]
            qev = qevs[-1]
            for ic in range(c.NI):
                slot = ic % 2
                te = em.pe(lambda e, ic=ic, slot=slot: e.transpose(
                    b.pT[slot][:, 0:128],
                    b.hq_tm[:, ic * 128:(ic + 1) * 128], b.ident[:]),
                    waits=[qev, em.ev("ssc")], inc=True)
                hq_ev = em.sc(lambda e, ic=ic, sub=sub, slot=slot: e.copy(
                    b.hq_fm[:, ic * T + sub * 128: ic * T + sub * 128 + 128],
                    b.pT[slot][:, 0:128]),
                    waits=[te, hqfm_rd], inc=True)
            hqtm_free = hq_ev
        acc_ev = None
        down_mm = None
        for oc in range(c.NC):
            dsl = 0
            wdl = em.dma_gp(f"dwd{dsl}",
                            b.wd_s[dsl][:].rearrange("p (ic o) -> p ic o", o=128),
                            b.l0f_wdT[:, :, oc * 128:(oc + 1) * 128],
                            waits=[wd_free[dsl]] + prep_evs)
            bank = oc % 2
            for ic in range(c.NI):
                w = [hq_ev, wdl]
                if ic == 0:
                    w.append(pa_free[bank])
                down_mm = em.pe(lambda e, oc=oc, ic=ic, dsl=dsl, bank=bank: e.matmul(
                    b.pA[bank][:, 0:T],
                    b.wd_s[dsl][:, ic * 128:(ic + 1) * 128],
                    b.hq_fm[:, ic * T:(ic + 1) * T],
                    start=(ic == 0), stop=(ic == c.NI - 1)),
                    waits=w, inc=(ic == c.NI - 1))
            wd_free[dsl] = down_mm
            acc_ev = em.v(lambda e, oc=oc, bank=bank, ti=ti: e.tensor_tensor(
                b.acc[:, oc * T:(oc + 1) * T], b.pA[bank][:, 0:T],
                b.x[:, oc * c.NT + ti * T: oc * c.NT + (ti + 1) * T], ALU.add),
                waits=[down_mm], inc=True)
            pa_free[bank] = acc_ev
        hqfm_rd = down_mm
        xq_rd = last_mm[1]
        ln_ev = emit_ln_fm(em, b, cfg, ti, acc_ev, prev_ln_ev,
                           extra_waits=[pa_free[2], pa_free[3]])
        pa_free[2] = ln_ev
        pa_free[3] = ln_ev
        prev_ln_ev = ln_ev
        ln_evs.append(ln_ev)
    return ln_evs


class PrepJob:

    def __init__(self, em, b, cfg, w_dram, wT_dram, ce, after=()):
        self.em, self.b, self.cfg = em, b, cfg
        self.w = w_dram
        self.wT = wT_dram
        self.ce = ce
        self.O = w_dram.shape[0]
        self.D = w_dram.shape[1]
        self.noc = self.O // 128
        self.ndc = self.D // 128
        self.hoc = max(1, (cfg.NC // 4 * cfg.D) // self.D)
        self.wre = w_dram.rearrange("(oc p) d -> p oc d", p=128)
        self.extra = list(after)
        self.og = 0

    def done(self):
        return self.og >= self.noc

    def emit_chunk(self):
        em, b, c = self.em, self.b, self.cfg
        og = self.og
        og_n = min(self.hoc, self.noc - og)
        self.og += og_n
        D = self.D
        ld = em.dma_gp("dwl", b.wload[:, 0:og_n * D].rearrange("p (oc d) -> p oc d", d=D),
                       self.wre[:, og:og + og_n, :],
                       waits=self.extra + [b.wload_free, self.ce])
        self.extra = []
        te = None
        for oc in range(og, og + og_n):
            ssl = oc % 2
            evs = None
            for dc in range(self.ndc):
                slot = dc % 2
                te = em.pe(lambda e, oc=oc, dc=dc, slot=slot, og=og, D=D: e.transpose(
                    b.pT[slot][:, 0:128],
                    b.wload[:, (oc - og) * D + dc * 128: (oc - og) * D + (dc + 1) * 128],
                    b.ident[:]),
                    waits=[ld, self.ce, em.ev("ssc")], inc=True)
                evs = em.sc(lambda e, dc=dc, slot=slot, ssl=ssl: e.copy(
                    b.wstg[ssl][:, dc * 128:(dc + 1) * 128],
                    b.pT[slot][:, 0:128]),
                    waits=[te, b.wstg_free[ssl]], inc=True)
            dma = em.dma_gp(f"dstg{ssl}",
                            self.wT[:, :, oc * 128:(oc + 1) * 128],
                            b.wstg[ssl][:, 0:self.ndc * 128].rearrange(
                                "p (dc m) -> p dc m", m=128),
                            waits=[evs])
            b.wstg_free[ssl] = dma
        b.wload_free = te


def emit_weight_prep_dram(em, b, cfg, w_dram, wT_dram, ce, after=()):
    job = PrepJob(em, b, cfg, w_dram, wT_dram, ce, after)
    while not job.done():
        job.emit_chunk()
    return [em.ev("dstg0"), em.ev("dstg1")]




def emit_l1_attn(em, b, cfg, ce, x_tile_evs, prev_ln_ev=None):
    c = cfg
    T = c.TT
    pq = emit_weight_prep_dram(em, b, cfg, b.l1a_w[0], b.l1a_wT[0], ce)
    pk = emit_weight_prep_dram(em, b, cfg, b.l1a_w[1], b.l1a_wT[1], ce)
    pv = emit_weight_prep_dram(em, b, cfg, b.l1a_w[2], b.l1a_wT[2], ce)
    prep_evs = pv

    mz = em.gp(lambda e: e.memset(b.cmask[:], 0.0), waits=[ce], inc=True)
    mask_ev = mz
    for o in range(4):
        mask_ev = em.gp(lambda e, o=o: e.affine_select(
            out=b.cmask[:, o * 512:(o + 1) * 512], in_=b.cmask[:, o * 512:(o + 1) * 512],
            pattern=[[-1, 512]], compare_op=ALU.is_ge, fill=NEG,
            base=128 * o, channel_multiplier=1), waits=[mz], inc=True)

    ngrp = c.SEQ // T
    ntpg = T // 128
    ln_evs = []
    pa_free = [None, None, None, None]
    pT_free = [None, None]
    pCtx_free = None
    probsT_free = [None, None]
    probs_free = None
    kv_free = None
    qg_free = None
    vtmp_free = None
    wstr_free = [None, None]

    def proj_stream(wT_dram, oc, wsl, extra):
        return em.dma_gp(f"dws{wsl}",
                         b.wstr[wsl][:].rearrange("p (dc m) -> p dc m", m=128),
                         wT_dram[:, :, oc * 128:(oc + 1) * 128],
                         waits=[wstr_free[wsl]] + prep_evs + list(extra))

    for s in range(c.NSEQ):
        base = s * c.SEQ
        kv_ev = None
        for tg in range(ngrp):
            ti_glob = (base + tg * T) // T
            xev = x_tile_evs[ti_glob]
            for oc in range(c.NC):
                bank = oc % 2
                wsl = oc % 2
                wl = proj_stream(b.l1a_wT[1], oc, wsl, [])
                mm = None
                for dc in range(c.NC):
                    w = [xev, wl]
                    if dc == 0:
                        w += [pa_free[bank], kv_free]
                    mm = em.pe(lambda e, oc=oc, dc=dc, bank=bank, tg=tg, base=base, wsl=wsl: e.matmul(
                        b.pA[bank][:, 0:T],
                        b.wstr[wsl][:, dc * 128:(dc + 1) * 128],
                        b.x[:, dc * c.NT + base + tg * T: dc * c.NT + base + (tg + 1) * T],
                        start=(dc == 0), stop=(dc == c.NC - 1)),
                        waits=w, inc=(dc == c.NC - 1))
                wstr_free[wsl] = mm
                kv_ev = em.sc(lambda e, oc=oc, bank=bank, tg=tg, base=base: e.copy(
                    b.kS[:, oc * c.SEQ + tg * T: oc * c.SEQ + (tg + 1) * T],
                    b.pA[bank][:, 0:T]), waits=[mm], inc=True)
                pa_free[bank] = kv_ev
            for oc in range(c.NC):
                bank = oc % 2
                wsl = oc % 2
                wl = proj_stream(b.l1a_wT[2], oc, wsl, [])
                mm = None
                for dc in range(c.NC):
                    w = [xev, wl]
                    if dc == 0:
                        w += [pa_free[bank], kv_free]
                    mm = em.pe(lambda e, oc=oc, dc=dc, bank=bank, tg=tg, base=base, wsl=wsl: e.matmul(
                        b.pA[bank][:, 0:T],
                        b.wstr[wsl][:, dc * 128:(dc + 1) * 128],
                        b.x[:, dc * c.NT + base + tg * T: dc * c.NT + base + (tg + 1) * T],
                        start=(dc == 0), stop=(dc == c.NC - 1)),
                        waits=w, inc=(dc == c.NC - 1))
                wstr_free[wsl] = mm
                vt = em.v(lambda e, bank=bank: e.tensor_copy(
                    b.vtmp[:], b.pA[bank][:, 0:T]), waits=[mm, vtmp_free], inc=True)
                pa_free[bank] = vt
                for sj in range(4):
                    slot = sj % 2
                    te = em.pe(lambda e, sj=sj, slot=slot: e.transpose(
                        b.pT[slot][:, 0:128], b.vtmp[:, sj * 128:(sj + 1) * 128],
                        b.ident[:]), waits=[vt, pT_free[slot] or em.ev("ssc")], inc=True)
                    ev = em.sc(lambda e, oc=oc, tg=tg, sj=sj, slot=slot: e.copy(
                        b.vS[:, (tg * 4 + sj) * c.D + oc * 128: (tg * 4 + sj) * c.D + (oc + 1) * 128],
                        b.pT[slot][:, 0:128]), waits=[te], inc=True)
                    pT_free[slot] = ev
                    vtmp_free = ev
                kv_ev = vtmp_free

        for tg in range(ngrp):
            ti_glob = (base + tg * T) // T
            xev = x_tile_evs[ti_glob]
            q_ev = None
            for oc in range(c.NC):
                bank = oc % 2
                wsl = oc % 2
                wl = proj_stream(b.l1a_wT[0], oc, wsl, [])
                mm = None
                for dc in range(c.NC):
                    w = [xev, wl]
                    if dc == 0:
                        w += [pa_free[bank], qg_free]
                    mm = em.pe(lambda e, oc=oc, dc=dc, bank=bank, tg=tg, base=base, wsl=wsl: e.matmul(
                        b.pA[bank][:, 0:T],
                        b.wstr[wsl][:, dc * 128:(dc + 1) * 128],
                        b.x[:, dc * c.NT + base + tg * T: dc * c.NT + base + (tg + 1) * T],
                        start=(dc == 0), stop=(dc == c.NC - 1)),
                        waits=w, inc=(dc == c.NC - 1))
                wstr_free[wsl] = mm
                q_ev = em.v(lambda e, oc=oc, bank=bank: e.tensor_scalar(
                    b.qg[:, oc * T:(oc + 1) * T], b.pA[bank][:, 0:T],
                    1.0 / 32.0, None, ALU.mult), waits=[mm], inc=True)
                pa_free[bank] = q_ev

            for qt in range(ntpg):
                i = tg * ntpg + qt
                nblk = i // 4 + 1
                exp_evs = []
                for j in range(nblk):
                    diag = (j == nblk - 1)
                    mmS = None
                    for dc in range(c.NC):
                        w = [q_ev, kv_ev]
                        if dc == 0:
                            w.append(pa_free[j])
                        mmS = em.pe(lambda e, dc=dc, j=j, qt=qt: e.matmul(
                            b.pA[j][:, 0:T],
                            b.qg[:, dc * T + qt * 128: dc * T + qt * 128 + 128],
                            b.kS[:, dc * c.SEQ + j * T: dc * c.SEQ + (j + 1) * T],
                            start=(dc == 0), stop=(dc == c.NC - 1)),
                            waits=w, inc=(dc == c.NC - 1))
                    if diag:
                        mmS = em.v(lambda e, j=j, i=i: e.tensor_tensor(
                            b.pA[j][:, 0:T], b.pA[j][:, 0:T],
                            b.cmask[:, (i % 4) * 512:(i % 4) * 512 + 512], ALU.add),
                            waits=[mmS, mask_ev], inc=True)
                    em.v(lambda e, j=j: e.reduce_max(
                        b.nm1[:, j:j + 1], b.pA[j][:, 0:T], axis=AX.X, negate=True),
                        waits=[mmS], inc=True)
                cm = em.v(lambda e: e.tensor_copy(b.nm1[:, 4:5], b.nm1[:, 0:1]), inc=True)
                for j in range(1, nblk):
                    cm = em.v(lambda e, j=j: e.tensor_tensor(
                        b.nm1[:, 4:5], b.nm1[:, 4:5], b.nm1[:, j:j + 1], ALU.min),
                        inc=True)
                for j in range(nblk):
                    ex = em.sc(lambda e, j=j: e.activation(
                        b.probs1[:, j * T:(j + 1) * T], b.pA[j][:, 0:T], AF.Exp,
                        bias=b.nm1[:, 4:5], scale=1.0,
                        accum_out=b.nm1[:, 5 + j:6 + j]),
                        waits=[cm, probs_free], inc=True)
                    pa_free[j] = ex
                    exp_evs.append(ex)
                sm = em.v(lambda e: e.tensor_copy(b.nm1[:, 9:10], b.nm1[:, 5:6]),
                          waits=[exp_evs[0]], inc=True)
                for j in range(1, nblk):
                    sm = em.v(lambda e, j=j: e.tensor_tensor(
                        b.nm1[:, 9:10], b.nm1[:, 9:10], b.nm1[:, 5 + j:6 + j], ALU.add),
                        waits=[exp_evs[j]], inc=True)
                rc = em.v(lambda e: e.reciprocal(b.nm1[:, 10:11], b.nm1[:, 9:10]),
                          waits=[sm], inc=True)
                pt_ev = None
                for j in range(nblk):
                    pn = em.v(lambda e, j=j: e.tensor_scalar(
                        b.probs1[:, j * T:(j + 1) * T], b.probs1[:, j * T:(j + 1) * T],
                        b.nm1[:, 10:11], None, ALU.mult),
                        waits=[rc, exp_evs[j]], inc=True)
                    for sj in range(4):
                        slot = sj % 2
                        te = em.pe(lambda e, j=j, sj=sj, slot=slot: e.transpose(
                            b.pT[slot][:, 0:128],
                            b.probs1[:, j * T + sj * 128: j * T + sj * 128 + 128],
                            b.ident[:]),
                            waits=[pn, pT_free[slot] or em.ev("ssc"), probsT_free[0]], inc=True)
                        pT_free[slot] = em.sc(lambda e, j=j, sj=sj, slot=slot: e.copy(
                            b.probsT1[:, (j * 4 + sj) * 128:(j * 4 + sj) * 128 + 128],
                            b.pT[slot][:, 0:128]),
                            waits=[te], inc=True)
                pt_ev = pT_free[(nblk * 4 - 1) % 2]
                ctx_mm = None
                for ch in range(c.NC):
                    for j in range(nblk):
                        for sj in range(4):
                            w = [pt_ev, kv_ev]
                            if ch == 0 and j == 0 and sj == 0:
                                w.append(pCtx_free)
                            ctx_mm = em.pe(lambda e, ch=ch, sj=sj, j=j, nblk=nblk: e.matmul(
                                b.pCtx[:, ch * 128:(ch + 1) * 128],
                                b.vS[:, (j * 4 + sj) * c.D + ch * 128: (j * 4 + sj) * c.D + (ch + 1) * 128],
                                b.probsT1[:, (j * 4 + sj) * 128:(j * 4 + sj) * 128 + 128],
                                start=(j == 0 and sj == 0), stop=(j == nblk - 1 and sj == 3)),
                                waits=w, inc=(ch == c.NC - 1 and j == nblk - 1 and sj == 3))
                probsT_free = [ctx_mm, ctx_mm]
                probs_free = pt_ev
                acc_ev = None
                for ch in range(c.NC):
                    acc_ev = em.v(lambda e, ch=ch, qt=qt, i=i, base=base: e.tensor_tensor(
                        b.acc[:, ch * T + qt * 128: ch * T + qt * 128 + 128],
                        b.pCtx[:, ch * 128:(ch + 1) * 128],
                        b.x[:, ch * c.NT + base + i * 128: ch * c.NT + base + i * 128 + 128],
                        ALU.add), waits=[ctx_mm], inc=True)
                pCtx_free = acc_ev
            ln_ev = emit_ln_fm(em, b, cfg, ti_glob, acc_ev, prev_ln_ev,
                               extra_waits=[pa_free[2], pa_free[3]])
            pa_free[2] = ln_ev
            pa_free[3] = ln_ev
            prev_ln_ev = ln_ev
            ln_evs.append(ln_ev)
            qg_free = em.ev("spe")
        kv_free = em.ev("spe")
    return ln_evs




def emit_l1_moe(em, b, cfg, ce, x_tile_evs, x_out):
    c = cfg
    T = c.TT
    F = c.F
    xoflat = x_out.rearrange("s t d -> (s t) d")

    wgl = em.dma_gp("dwg_in", b.wg_sb, b.moe_w[0], waits=[ce])
    wgc = em.v(lambda e: e.tensor_copy(b.scores[0:8, 0:c.D], b.wg_sb),
               waits=[wgl], inc=True)
    wgt_ev = None
    for dc in range(c.NC):
        slot = dc % 2
        te = em.pe(lambda e, dc=dc, slot=slot: e.transpose(
            b.pT[slot][:, 0:8], b.scores[0:8, dc * 128:(dc + 1) * 128],
            b.ident[0:8, 0:8]), waits=[wgc, ce, em.ev("ssc")], inc=True)
        wgt_ev = em.sc(lambda e, dc=dc, slot=slot: e.copy(
            b.wgT[:, dc * 8:(dc + 1) * 8], b.pT[slot][0:128, 0:8]),
            waits=[te], inc=True)

    pa_free = [None, None, None, None]
    sc_ev = None
    for ti in range(c.NTILE):
        bank = ti % 2
        mm = None
        for dc in range(c.NC):
            w = [x_tile_evs[ti], wgt_ev]
            if dc == 0:
                w.append(pa_free[bank])
            mm = em.pe(lambda e, ti=ti, dc=dc, bank=bank: e.matmul(
                b.pA[bank][0:8, 0:T],
                b.wgT[:, dc * 8:(dc + 1) * 8],
                b.x[:, dc * c.NT + ti * T: dc * c.NT + (ti + 1) * T],
                start=(dc == 0), stop=(dc == c.NC - 1)),
                waits=w, inc=(dc == c.NC - 1))
        sc_ev = em.sc(lambda e, ti=ti, bank=bank: e.copy(
            b.scores[0:8, ti * T:(ti + 1) * T], b.pA[bank][0:8, 0:T]),
            waits=[mm], inc=True)
        pa_free[bank] = sc_ev

    pT_free = [None, None]
    g_ev = None
    for t in range(c.NT // 128):
        slot = t % 2
        te = em.pe(lambda e, t=t, slot=slot: e.transpose(
            b.pT[slot][:, 0:8], b.scores[0:8, t * 128:(t + 1) * 128],
            b.ident[0:8, 0:8]), waits=[sc_ev, pT_free[slot] or em.ev("ssc")], inc=True)
        ev = em.sc(lambda e, slot=slot: e.copy(
            b.sctm[:], b.pT[slot][0:128, 0:8]), waits=[te, g_ev], inc=True)
        pT_free[slot] = ev
        em.v(lambda e: e.max(out=b.top8[:], in_=b.sctm[:]), waits=[ev])
        em.v(lambda e: e.memset(b.rep[:], NEG))
        em.v(lambda e: e.tensor_copy(b.rep[:, 0:1], b.top8[:, 0:1]))
        em.v(lambda e: e.match_replace(b.ms2[:], b.rep[:], b.sctm[:], -2e30))
        em.v(lambda e: e.tensor_tensor(b.mk1[:], b.sctm[:], b.ms2[:], ALU.not_equal))
        em.v(lambda e: e.tensor_copy(b.rep[:, 0:1], b.top8[:, 1:2]))
        em.v(lambda e: e.match_replace(b.ms3[:], b.rep[:], b.ms2[:], -2e30))
        em.v(lambda e: e.tensor_tensor(b.mk2[:], b.ms2[:], b.ms3[:], ALU.not_equal))
        dv = em.v(lambda e: e.tensor_sub(b.mst[:, 0:1], b.top8[:, 0:1], b.top8[:, 1:2]))
        p1 = em.sc(lambda e: e.activation(b.mst[:, 1:2], b.mst[:, 0:1], AF.Sigmoid,
                                          bias=0.0, scale=1.0), waits=[dv], inc=True)
        p2 = em.sc(lambda e: e.activation(b.mst[:, 2:3], b.mst[:, 0:1], AF.Sigmoid,
                                          bias=0.0, scale=-1.0), inc=True)
        em.v(lambda e: e.tensor_scalar(b.gtm[:], b.mk1[:], b.mst[:, 1:2], None, ALU.mult),
             waits=[p1, p2])
        em.v(lambda e: e.scalar_tensor_tensor(b.gtm[:], b.mk2[:], b.mst[:, 2:3],
                                              b.gtm[:], ALU.mult, ALU.add))
        gc = em.v(lambda e: e.tensor_copy(b.gbf[:], b.gtm[:]))
        tb = em.pe(lambda e, t=t, slot=slot: e.transpose(
            b.pT[slot][0:8, 128:256], b.gbf[:], b.ident[:]),
            waits=[gc, pT_free[slot]], inc=True)
        g_ev = em.sc(lambda e, t=t, slot=slot: e.copy(
            b.g_fm[0:8, t * 128:(t + 1) * 128], b.pT[slot][0:8, 128:256]),
            waits=[tb], inc=True)
        pT_free[slot] = g_ev
    g_dma = em.dma_gp("dgout", b.g_dram, b.g_fm[0:8, :], waits=[g_ev])

    sel_ev = None
    for e_ in range(c.E):
        sel_ev = em.gp(lambda e, e_=e_: e.affine_select(
            out=b.sel[0:8, e_ * 128:(e_ + 1) * 128], in_=b.ones_bf[0:8, 0:128],
            pattern=[[0, 128]], compare_op=ALU.is_equal, fill=0.0,
            base=-e_, channel_multiplier=1), waits=[ce], inc=True)

    jobs = []
    for e in range(c.E):
        jobs.append([
            PrepJob(em, b, cfg, b.moe_w[1][e], b.moe_w1T[e], ce, after=[g_dma]),
            PrepJob(em, b, cfg, b.moe_w[2][e], b.moe_w2T[e], ce),
            PrepJob(em, b, cfg, b.moe_w[3][e], b.moe_w3T[e], ce),
        ])
    prep_done = {}

    def emit_prep(e, n_chunks):
        cnt = 0
        for j in jobs[e]:
            while not j.done() and (n_chunks < 0 or cnt < n_chunks):
                j.emit_chunk()
                cnt += 1
        if all(j.done() for j in jobs[e]) and e not in prep_done:
            prep_done[e] = [em.ev("dstg0"), em.ev("dstg1")]

    emit_prep(0, -1)

    w_rd = [None, None]
    w3_free = [None, None]
    pctx_free = None
    hid_rd = None
    stage_dma = None
    msil_free = None
    mt1_free = None
    gB_free = None
    gld_free = None
    acc_dmas = [None] * c.NTILE
    for e in range(c.E):
        pe_evs = prep_done[e]
        l1 = em.dma_gp("dw1e", b.w1e[:].rearrange("p (dc f) -> p dc f", f=F),
                       b.moe_w1T[e],
                       waits=[w_rd[0], x_tile_evs[-1]] + pe_evs)
        l2 = em.dma_gp("dw2e", b.w2e[:].rearrange("p (dc f) -> p dc f", f=F),
                       b.moe_w2T[e], waits=[w_rd[1]] + pe_evs)
        for ti in range(c.NTILE):
            gl = em.dma_gp("dgld", b.gld, b.g_dram[0:8, ti * T:(ti + 1) * T],
                           waits=[gld_free, g_dma])
            gmm = em.pe(lambda e_, e=e, ti=ti: e_.matmul(
                b.pCtx[:, 0:T], b.sel[0:8, e * 128:(e + 1) * 128],
                b.gld, start=True, stop=True),
                waits=[gl, sel_ev, pctx_free], inc=True)
            gld_free = gmm
            gbe = em.v(lambda e_: e_.tensor_copy(b.gB[:], b.pCtx[:, 0:T]),
                       waits=[gmm, gB_free], inc=True)
            pctx_free = gbe
            h_ev = None
            mmh1_last = None
            for fc in range(c.NF):
                bs = fc % 2
                bh1, bh2 = (0, 1) if bs == 0 else (2, 3)
                mmh1 = None
                mmh2 = None
                for dc in range(c.NC):
                    w = [l1, x_tile_evs[ti]]
                    if dc == 0:
                        w.append(pa_free[bh1])
                    mmh1 = em.pe(lambda e_, fc=fc, dc=dc, ti=ti, bh1=bh1: e_.matmul(
                        b.pA[bh1][:, 0:T],
                        b.w1e[:, dc * F + fc * 128: dc * F + fc * 128 + 128],
                        b.x[:, dc * c.NT + ti * T: dc * c.NT + (ti + 1) * T],
                        start=(dc == 0), stop=(dc == c.NC - 1)),
                        waits=w, inc=(dc == c.NC - 1))
                for dc in range(c.NC):
                    w = [l2]
                    if dc == 0:
                        w.append(pa_free[bh2])
                    mmh2 = em.pe(lambda e_, fc=fc, dc=dc, ti=ti, bh2=bh2: e_.matmul(
                        b.pA[bh2][:, 0:T],
                        b.w2e[:, dc * F + fc * 128: dc * F + fc * 128 + 128],
                        b.x[:, dc * c.NT + ti * T: dc * c.NT + (ti + 1) * T],
                        start=(dc == 0), stop=(dc == c.NC - 1)),
                        waits=w, inc=(dc == c.NC - 1))
                sev = em.sc(lambda e_, bh1=bh1: e_.activation(
                    b.msil[:], b.pA[bh1][:, 0:T], AF.Silu, bias=0.0, scale=1.0),
                    waits=[mmh1, msil_free], inc=True)
                pa_free[bh1] = sev
                gpe = em.gp(lambda e_: e_.tensor_tensor(
                    b.mt1[:], b.msil[:], b.gB[:], ALU.mult),
                    waits=[sev, gbe, mt1_free], inc=True)
                msil_free = gpe
                h_ev = em.v(lambda e_, fc=fc, bh2=bh2: e_.tensor_tensor(
                    b.hid[:, fc * T:(fc + 1) * T], b.mt1[:], b.pA[bh2][:, 0:T],
                    ALU.mult), waits=[gpe, mmh2, hid_rd], inc=True)
                pa_free[bh2] = h_ev
                mt1_free = h_ev
                mmh1_last = mmh1
            gB_free = gpe
            down_last = None
            stg_ev = None
            for oc in range(c.NC):
                bank = oc % 4
                w3sl = oc % 2
                l3 = em.dma_gp(f"dw3s{w3sl}",
                               b.w3s[w3sl].rearrange("p (fc m) -> p fc m", m=128),
                               b.moe_w3T[e][:, :, oc * 128:(oc + 1) * 128],
                               waits=[w3_free[w3sl]] + pe_evs)
                mm = None
                for fc in range(c.NF):
                    w = [h_ev, l3]
                    if fc == 0:
                        w.append(pa_free[bank])
                    mm = em.pe(lambda e_, oc=oc, fc=fc, bank=bank, w3sl=w3sl: e_.matmul(
                        b.pA[bank][:, 0:T],
                        b.w3s[w3sl][:, fc * 128:(fc + 1) * 128],
                        b.hid[:, fc * T:(fc + 1) * T],
                        start=(fc == 0), stop=(fc == c.NF - 1)),
                        waits=w, inc=(fc == c.NF - 1))
                w3_free[w3sl] = mm
                down_last = mm
                if e == 0:
                    stg_ev = em.v(lambda e_, oc=oc, ti=ti, bank=bank: e_.tensor_tensor(
                        b.acc[:, oc * T:(oc + 1) * T], b.pA[bank][:, 0:T],
                        b.x[:, oc * c.NT + ti * T: oc * c.NT + (ti + 1) * T], ALU.add),
                        waits=[mm, stage_dma], inc=True)
                else:
                    stg_ev = em.v(lambda e_, oc=oc, bank=bank: e_.tensor_copy(
                        b.acc[:, oc * T:(oc + 1) * T], b.pA[bank][:, 0:T]),
                        waits=[mm, stage_dma], inc=True)
                pa_free[bank] = stg_ev
            hid_rd = down_last
            dma = em.dma_gp(f"dacc{ti}",
                            b.accm[:, :, ti * T:(ti + 1) * T],
                            b.acc[:, 0:c.NC * T].rearrange("p (ch t) -> p ch t", t=T),
                            waits=[stg_ev, acc_dmas[ti]],
                            accum_op=ALU.add if e > 0 else ALU.bypass)
            acc_dmas[ti] = dma
            stage_dma = dma
            if e + 1 < c.E:
                emit_prep(e + 1, 3)
        if e + 1 < c.E:
            emit_prep(e + 1, -1)
        w_rd = [mmh1_last, h_ev]

    out_dmas = []
    rb_free = None
    otm_dma = None
    for ti in range(c.NTILE):
        rb = em.dma_gp("drb", b.acc[:, 0:c.NC * T].rearrange("p (ch t) -> p ch t", t=T),
                       b.accm[:, :, ti * T:(ti + 1) * T],
                       waits=list(acc_dmas) + [rb_free])
        for sub in range(4):
            ev = None
            for ch in range(c.NC):
                slot = ch % 2
                ptv = b.pT[slot][:].bitcast(F32)
                te = em.pe(lambda e_, ch=ch, sub=sub, slot=slot, ptv=ptv: e_.transpose(
                    ptv[:, 0:128],
                    b.acc[:, ch * T + sub * 128: ch * T + sub * 128 + 128],
                    b.ident32[:]),
                    waits=[rb, em.ev("ssc"), ce], inc=True)
                ev = em.sc(lambda e_, ch=ch, slot=slot, ptv=ptv: e_.copy(
                    b.otm[:, ch * 128:(ch + 1) * 128], ptv[:, 0:128]),
                    waits=[te, otm_dma], inc=True)
            rb_free = ev
            s1 = em.v(lambda e_: e_.reduce_sum(b.mst[:, 3:4], b.otm, axis=AX.X),
                      waits=[ev], inc=True)
            s2 = em.sc(lambda e_: e_.activation(b.sqf, b.otm, AF.Square,
                                                bias=0.0, scale=1.0,
                                                accum_out=b.mst[:, 4:5]),
                       waits=[ev], inc=True)
            em.v(lambda e_: e_.tensor_scalar(b.mst[:, 5:6], b.mst[:, 3:4],
                                             1.0 / c.D, None, ALU.mult), waits=[s1])
            em.v(lambda e_: e_.tensor_scalar(b.mst[:, 6:7], b.mst[:, 4:5],
                                             1.0 / c.D, None, ALU.mult), waits=[s2])
            em.v(lambda e_: e_.tensor_tensor(b.mst[:, 7:8], b.mst[:, 5:6],
                                             b.mst[:, 5:6], ALU.mult))
            em.v(lambda e_: e_.tensor_sub(b.mst[:, 6:7], b.mst[:, 6:7], b.mst[:, 7:8]))
            vv = em.v(lambda e_: e_.tensor_scalar(b.mst[:, 6:7], b.mst[:, 6:7],
                                                  1e-5, None, ALU.add))
            sq = em.sc(lambda e_: e_.activation(b.mst[:, 7:8], b.mst[:, 6:7], AF.Sqrt,
                                                bias=0.0, scale=1.0), waits=[vv], inc=True)
            rr = em.v(lambda e_: e_.reciprocal(b.mst[:, 8:9], b.mst[:, 7:8]),
                      waits=[sq])
            ap = em.v(lambda e_: e_.tensor_scalar(
                b.otm, b.otm, b.mst[:, 5:6], b.mst[:, 8:9],
                ALU.subtract, ALU.mult), waits=[rr])
            r0 = ti * T + sub * 128
            otm_dma = em.dma_sy("dout0", xoflat[r0:r0 + 128, :], b.otm, waits=[ap])
            out_dmas.append(otm_dma)
    return out_dmas



from concourse.bass_utils import run_bass_kernel_spmd

_PROG_CACHE = {}


def _get_prog():
    if "fused" not in _PROG_CACHE:
        cfg = Cfg(SEQ=2048, NSEQ=2)
        _PROG_CACHE["fused"] = build_program(cfg, parts=["l0a", "l0f", "l1a", "moe"])
    return _PROG_CACHE["fused"]


def kernel(L0_attn_x, L0_attn_Wq, L0_attn_Wk, L0_attn_Wv, L0_attn_Wo,
           L0_ffn_w_gate, L0_ffn_w_up, L0_ffn_w_down,
           L1_attn_Wq, L1_attn_Wk, L1_attn_Wv,
           L1_ffn_Wg, L1_ffn_W1, L1_ffn_W2, L1_ffn_W3):
    x = np.ascontiguousarray(np.asarray(L0_attn_x, dtype=np.float32))
    f32 = lambda a: np.ascontiguousarray(np.asarray(a, dtype=np.float32))
    wmap = dict(
        l0_wq=f32(L0_attn_Wq), l0_wk=f32(L0_attn_Wk),
        l0_wv=f32(L0_attn_Wv), l0_wo=f32(L0_attn_Wo),
        l0_wgate=f32(L0_ffn_w_gate), l0_wup=f32(L0_ffn_w_up),
        l0_wdown=f32(L0_ffn_w_down),
        l1_wq=f32(L1_attn_Wq), l1_wk=f32(L1_attn_Wk), l1_wv=f32(L1_attn_Wv),
        l1_wg=f32(L1_ffn_Wg), l1_w1=f32(L1_ffn_W1),
        l1_w2=f32(L1_ffn_W2), l1_w3=f32(L1_ffn_W3))
    nc = _get_prog()
    in_maps = []
    for c in range(8):
        m = dict(x_in=x[2 * c:2 * c + 2])
        m.update(wmap)
        in_maps.append(m)
    res = run_bass_kernel_spmd(nc, in_maps, core_ids=list(range(8)))
    return np.concatenate([np.asarray(r["x_out"]) for r in res.results], axis=0)


# version 04
# speedup vs baseline: 1.0250x; 1.0250x over previous
import numpy as np
from contextlib import ExitStack

import concourse.bass as bass
import concourse.mybir as mybir

F32 = mybir.dt.float32
BF16 = mybir.dt.bfloat16
AF = mybir.ActivationFunctionType
ALU = mybir.AluOpType
AX = mybir.AxisListType

NEG = -1e30


def _ap(x):
    try:
        return x[:]
    except Exception:
        return x



ENGINES = ("pe", "v", "sc", "gp", "sy")


class Em:

    def __init__(self, nc):
        self.nc = nc
        self.streams = {e: [] for e in ENGINES}
        self.cnt = {}
        self.waited = {e: {} for e in ENGINES}
        self.sems = {}

    def ev(self, sem):
        return (sem, self.cnt.get(sem, 0))

    AUTOCHAIN = {"v": "sv", "sc": "ssc", "gp": "sgp"}

    def emit(self, eng, fn, waits=(), inc=None):
        waits = [w for w in waits if w is not None]
        ac = self.AUTOCHAIN.get(eng)
        if ac is not None:
            waits = list(waits) + [(ac, self.cnt.get(ac, 0))]
            if inc is None:
                inc = (ac, 1)
        real = []
        for (s, v) in waits:
            if v > self.waited[eng].get(s, 0):
                self.waited[eng][s] = v
                real.append((s, v))
        evt = None
        if inc is not None:
            s, a = inc
            self.cnt[s] = self.cnt.get(s, 0) + a
            evt = (s, self.cnt[s])
        self.streams[eng].append((real, fn, inc))
        return evt

    def pe(self, fn, waits=(), inc=False):
        return self.emit("pe", fn, waits, ("spe", 1) if inc else None)

    def v(self, fn, waits=(), inc=False):
        return self.emit("v", fn, waits, ("sv", 1) if inc else None)

    def sc(self, fn, waits=(), inc=False):
        return self.emit("sc", fn, waits, ("ssc", 1) if inc else None)

    def gp(self, fn, waits=(), inc=False):
        return self.emit("gp", fn, waits, ("sgp", 1) if inc else None)

    def phase_barrier(self):
        waits = [(s, v) for s, v in self.cnt.items() if v > 0]
        for eng in ENGINES:
            fn = (lambda e: e.nop())
            self.emit(eng, fn, waits=list(waits))

    def dma_sy(self, sem, out, in_, waits=()):
        return self.emit("sy", lambda e: e.dma_start(out=out, in_=in_), waits, (sem, 16))

    def dma_gp(self, sem, out, in_, waits=(), **kw):
        return self.emit("gp", lambda e: e.dma_start(out=out, in_=in_, **kw), waits, (sem, 16))

    def play(self, block):
        eng_attr = {"pe": "tensor", "v": "vector", "sc": "scalar", "gp": "gpsimd", "sy": "sync"}
        for name in ENGINES:
            stream = self.streams[name]

            def body(eng, stream=stream):
                for waits, fn, inc in stream:
                    for s, v in waits:
                        eng.wait_ge(self.sems[s], v)
                    ins = fn(eng)
                    if inc is not None:
                        ins.then_inc(self.sems[inc[0]], inc[1])

            getattr(block, eng_attr[name])(body)


class Cfg:
    def __init__(self, SEQ=2048, NSEQ=2):
        self.SEQ = SEQ
        self.NSEQ = NSEQ
        self.NT = SEQ * NSEQ
        self.D = 1024
        self.NC = 8
        self.HEADS = 16
        self.HD = 64
        self.WIN = 128
        self.G = 64
        self.I = 2816
        self.NI = 22
        self.E = 8
        self.F = 2048
        self.NF = 16
        self.TT = 512
        self.NTILE = self.NT // self.TT
        self.WPT = self.TT // self.WIN
        self.NWIN = SEQ // self.WIN
        self.LW = 512




class Bufs:
    pass


def alloc_bufs(nc, ctx, cfg, parts):
    b = Bufs()
    sb = lambda name, shape, dt: ctx.enter_context(nc.sbuf_tensor(name, shape, dt))
    ps = lambda name, shape, dt: ctx.enter_context(nc.psum_tensor(name, shape, dt))
    c = cfg

    b.x = sb("x_sb", [128, c.NC * c.NT], BF16)

    b.ident = sb("ident_bf", [128, 128], BF16)
    b.ones_bf = sb("ones_bf", [128, 128], BF16)
    b.ident32 = sb("ident_f32", [128, 128], F32) if "moe" in parts else None
    b.ones32 = None
    if any(p in parts for p in ("l0a", "l0f", "l1a")):
        b.onesc32 = sb("onesc32", [128, 1], F32)
        b.onescbf = sb("onescbf", [128, 1], BF16)
        b.onesr32 = sb("onesr32", [1, 128], F32)
    else:
        b.onesc32 = None

    b.pA = [ps(f"pA{i}", [128, 512], F32) for i in range(4)]
    b.pCtx = ps("pCtx", [128, 1024], F32)
    b.pT = [ps("pTa", [128, 256], BF16), ps("pTb", [128, 256], BF16)]

    b.xin = None

    b.acc = sb("acc_fm", [128, c.NC * c.TT], F32)
    if any(p in parts for p in ("l0a", "l0f", "l1a")):
        b.sq = sb("sq_bf", [128, 2 * c.TT], BF16)
        b.strow = sb("strow", [1, 3 * c.TT], F32)

    if "moe" in parts:
        b.w1e = sb("w1e", [128, c.NC * c.F], BF16)
        b.w2e = sb("w2e", [128, c.NC * c.F], BF16)
        b.w3s = [sb(f"w3s{i}", [128, c.NF * 128], BF16)[:] for i in range(2)]
        b.scores = sb("scores_m", [8, c.NT], BF16)
        b.g_fm = b.scores
        b.sel = sb("sel_m", [8, 8 * 128], BF16)
        b.hid = sb("hid_m", [128, c.NF * 512], BF16)
        b.gB = sb("gB_m", [128, 512], F32)
        b.msil = sb("msil", [128, 512], BF16)
        b.mt1 = sb("mt1", [128, 512], BF16)

        b.mst = sb("mst", [128, 12], F32)
        b.sctm = sb("sctm", [128, 8], BF16)
        b.top8 = sb("top8", [128, 8], BF16)
        b.rep = sb("rep", [128, 8], BF16)
        b.ms2 = sb("ms2", [128, 8], BF16)
        b.ms3 = sb("ms3", [128, 8], BF16)
        b.mk1 = sb("mk1", [128, 8], F32)
        b.mk2 = sb("mk2", [128, 8], F32)
        b.gtm = sb("gtm", [128, 8], F32)
        b.gbf = sb("gbf", [128, 8], BF16)
        b.wgT = sb("wgT", [128, c.NC * 8], BF16)
        hv = b.hid[:].bitcast(F32)
        b.sqf = hv[:, 0:c.D]
        b.wg_sb = hv[0:8, c.D:2 * c.D]
        b.otm = hv[:, 2 * c.D:3 * c.D]
        b.wload = b.hid[:].bitcast(BF16)[:, 4 * c.D:6 * c.D]
        gv = b.gB[:].bitcast(BF16)
        b.wtb = [gv[:, 0:128], gv[:, 128:256]]


    if "l0a" in parts or "l1a" in parts or "moe" in parts:
        nw = 4 if "l0a" in parts else 0
        b.wT = [sb(f"wT{i}", [128, c.NC * c.D], BF16) for i in range(nw)]
        if "moe" not in parts:
            b.wload = sb("wload", [128, c.NC // 4 * c.D], BF16)
    if "l0a" in parts:
        b.qb = sb("qb", [128, c.NC * c.TT], BF16)
        b.kb = sb("kb", [128, c.NC * c.TT], BF16)
        b.vb = sb("vb", [128, c.NC * c.TT], BF16)

    if "l1a" in parts:
        b.kS = sb("kS", [128, c.NC * c.SEQ], BF16)
        b.vS = sb("vS", [128, (c.SEQ // 128) * c.D], BF16)
        b.qg = sb("qg", [128, c.NC * 512], BF16)
        b.vtmp = sb("vtmp", [128, 512], BF16)
        b.cmask = sb("cmask", [128, 4 * 512], BF16)
        b.probs1 = sb("probs1", [128, 4 * 512], BF16)
        b.probsT1 = sb("probsT1", [128, 16 * 128], BF16)
        b.nm1 = sb("nm1", [128, 16], F32)
        b.wstg = [sb(f"wstg{i}", [128, c.NC * 128], BF16)[:] for i in range(2)]
        b.wstg_free = [None, None]
        b.wload2 = [sb(f"wload2_{i}", [128, c.NC // 4 * c.D], BF16)[:] for i in range(2)]
        b.wload_free = [None, None]
        b.wload_par = 0
        b.pT_free = [None, None]
        b.wstr = [sb(f"wstr{i}", [128, c.NC * 128], BF16) for i in range(2)]

    if "l0f" in parts:
        b.wq_in = sb("wq_in", [128, c.I], F32)
        b.wq_in_b = sb("wq_in_b", [128, c.I], F32)
        b.wtb = [sb(f"wtb{i}", [128, 128], BF16) for i in range(2)]
        b.qsets = []
        for i in range(2):
            b.qsets.append((sb(f"qs_xn{i}", [128, c.LW], F32)[:],
                            sb(f"qs_g{i}", [128, c.LW], F32)[:],
                            sb(f"qs_a{i}", [128, c.LW], F32)[:],
                            sb(f"qs_c{i}", [128, c.LW], F32)[:],
                            sb(f"qs_sc{i}", [128, 4 * (c.LW // 16)], F32)[:]))
        b.qs_h2 = [sb(f"qs_h{i}", [128, c.LW], F32)[:] for i in range(2)]
        b.xtm32 = None
        b.xqtm = None
        b.xq_fm = sb("xq_fm", [128, c.NC * c.TT], BF16)
        b.hb = sb("hb", [128, 4 * c.I], BF16)
        b.hq_tm = sb("hq_tm", [128, c.I], BF16)
        b.hq_fm = sb("hq_fm", [128, c.NI * c.TT], BF16)
        b.wg_s = [sb("wg_s0", [128, c.NC * 512], BF16)] * 2
        b.wup_s = [sb("wup_s0", [128, c.NC * 512], BF16)] * 2
        b.wd_s = [sb("wd_s0", [128, c.NI * 128], BF16)] * 2
        b.silu = sb("silu_b", [128, 512], BF16)
        b.wq_in2 = [b.wq_in[:], b.wq_in_b[:]]
        wiv = b.wq_in[:].bitcast(BF16)
        wiv2 = b.wq_in_b[:].bitcast(BF16)
        b.xin = [wiv[:, 0:c.D], wiv[:, c.D:2 * c.D]]
        b.xtm32_2 = [b.wq_in[:][:, 0:c.D], b.wq_in_b[:][:, 0:c.D]]
        b.xqtm2 = [wiv[:, 2 * c.D:3 * c.D], wiv2[:, 2 * c.D:3 * c.D]]
        hqv = b.hq_fm[:]
        b.wstg = [hqv[:, 0:c.NI * 128], hqv[:, c.NI * 128:2 * c.NI * 128]]
        b.wstg_free = [None, None]
        b.wq_out2 = [b.hq_tm[:], hqv[:, 2 * c.NI * 128:2 * c.NI * 128 + c.I]]

    if "l0a" in parts:
        b.kg = [sb(f"kg{s}", [128, c.NC * c.G], BF16) for s in range(c.NSEQ)]
        b.vgT = [sb(f"vgT{s}", [c.G, c.D], BF16) for s in range(c.NSEQ)]
        b.vwT = sb("vwT", [128, c.D], BF16)
        b.expb = [sb(f"expb{i}", [128, c.G + c.WIN], F32) for i in range(2)]
        b.nm = sb("nm_sums", [128, 8], F32)
        b.probs = [sb(f"probs{i}", [128, c.G + c.WIN], BF16) for i in range(2)]
        b.probsT = [sb(f"probsT{i}", [128, 256], BF16) for i in range(2)]
        b.ctxb = sb("ctxb", [128, c.NC * c.TT], BF16)

    if b.xin is None and "moe" in parts:
        w1v = b.w1e[:]
        b.xin = [w1v[:, 0:c.D], w1v[:, c.D:2 * c.D]]
    if b.xin is None:
        b.xin = [sb(f"xin{i}", [128, c.D], BF16)[:] for i in range(2)]
    return b




class Carver:
    def __init__(self, base_ap, limit_bytes):
        self.base = base_ap
        self.off = 0
        self.limit = limit_bytes

    def take(self, dt, cols, parts=128):
        sz = 4 if dt == F32 else 2
        off = (self.off + 31) // 32 * 32
        self.off = off + cols * sz
        assert self.off <= self.limit, f"arena overflow: {self.off} > {self.limit}"
        return self.base.bitcast(dt)[0:parts, off // sz: off // sz + cols]


def alloc_bufs_fused(nc, ctx, cfg):
    b = Bufs()
    c = cfg
    sb = lambda name, shape, dt: ctx.enter_context(nc.sbuf_tensor(name, shape, dt))
    ps = lambda name, shape, dt: ctx.enter_context(nc.psum_tensor(name, shape, dt))

    b.x = sb("x_sb", [128, c.NC * c.NT], BF16)
    b.acc = sb("acc_fm", [128, c.NC * c.TT], F32)
    b.sq = sb("sq_bf", [128, 2 * c.TT], BF16)
    b.strow = sb("strow", [1, 3 * c.TT], F32)
    b.ident = sb("ident_bf", [128, 128], BF16)
    b.ident32 = sb("ident_f32", [128, 128], F32)
    b.ones_bf = sb("ones_bf", [128, 128], BF16)
    b.ones32 = None
    b.onesc32 = sb("onesc32", [128, 1], F32)
    b.onescbf = sb("onescbf", [128, 1], BF16)
    b.onesr32 = sb("onesr32", [1, 128], F32)
    b.xin = [sb(f"xin{i}", [128, c.D], BF16)[:] for i in range(2)]

    b.pA = [ps(f"pA{i}", [128, 512], F32) for i in range(4)]
    b.pCtx = ps("pCtx", [128, 1024], F32)
    b.pT = [ps("pTa", [128, 256], BF16), ps("pTb", [128, 256], BF16)]

    rem = nc.sbuf_bytes_remaining - 64
    AR = sb("arena", [128, rem // 2 // 32 * 32], BF16)[:]
    arena_bytes = AR.shape[1] * 2
    b.ph = {}

    cv = Carver(AR, arena_bytes)
    d = {}
    d["wT"] = [cv.take(BF16, c.NC * c.D) for _ in range(4)]
    d["qb"] = cv.take(BF16, c.NC * c.TT)
    d["kb"] = cv.take(BF16, c.NC * c.TT)
    d["vb"] = cv.take(BF16, c.NC * c.TT)
    d["ctxb"] = cv.take(BF16, c.NC * c.TT)
    d["wload"] = d["ctxb"][:, 0:c.NC // 4 * c.D]
    d["kg"] = [cv.take(BF16, c.NC * c.G) for _ in range(c.NSEQ)]
    d["vgT"] = [cv.take(BF16, c.D, parts=c.G) for _ in range(c.NSEQ)]
    d["vwT"] = cv.take(BF16, c.D)
    d["expb"] = [cv.take(F32, c.G + c.WIN) for _ in range(2)]
    d["probs"] = [cv.take(BF16, c.G + c.WIN) for _ in range(2)]
    d["probsT"] = [cv.take(BF16, 256) for _ in range(2)]
    d["nm"] = cv.take(F32, 8)
    d["x32st"] = cv.take(F32, c.D)
    b.ph["l0a"] = d

    cv = Carver(AR, arena_bytes)
    d = {}
    qsets = []
    for _ in range(2):
        qsets.append((cv.take(F32, c.LW), cv.take(F32, c.LW), cv.take(F32, c.LW),
                      cv.take(F32, c.LW), cv.take(F32, 4 * (c.LW // 16))))
    d["qsets"] = qsets
    d["qs_h2"] = [cv.take(F32, c.LW) for _ in range(2)]
    d["xtm32_2"] = [cv.take(F32, c.D) for _ in range(2)]
    d["xqtm2"] = [cv.take(BF16, c.D) for _ in range(2)]
    d["xq_fm"] = cv.take(BF16, c.NC * c.TT)
    hb = cv.take(BF16, 4 * c.I)
    d["hb"] = hb
    d["wq_in2"] = [hb.bitcast(F32)[:, 0:c.I], hb.bitcast(F32)[:, c.I:2 * c.I]]
    d["hq_tm"] = cv.take(BF16, c.I)
    hq_fm = cv.take(BF16, c.NI * c.TT)
    d["hq_fm"] = hq_fm
    d["wq_out2"] = [d["hq_tm"][:] if hasattr(d["hq_tm"], "ap") else d["hq_tm"],
                    hq_fm[:, 2 * c.NI * 128:2 * c.NI * 128 + c.I]]
    d["wstg"] = [hq_fm[:, 0:c.NI * 128], hq_fm[:, c.NI * 128:2 * c.NI * 128]]
    d["wstg_free"] = [None, None]
    wg0 = cv.take(BF16, c.NC * 512)
    wu0 = cv.take(BF16, c.NC * 512)
    wd0 = cv.take(BF16, c.NI * 128)
    d["wg_s"] = [wg0, wg0]
    d["wup_s"] = [wu0, wu0]
    d["wd_s"] = [wd0, wd0]
    d["silu"] = cv.take(BF16, 512)
    b.ph["l0f"] = d

    cv = Carver(AR, arena_bytes)
    d = {}
    d["kS"] = cv.take(BF16, c.NC * c.SEQ)
    d["vS"] = cv.take(BF16, (c.SEQ // 128) * c.D)
    d["qg"] = cv.take(BF16, c.NC * 512)
    d["vtmp"] = cv.take(BF16, 512)
    d["cmask"] = cv.take(BF16, 4 * 512)
    d["probs1"] = cv.take(BF16, 4 * 512)
    d["probsT1"] = cv.take(BF16, 16 * 128)
    d["nm1"] = cv.take(F32, 16)
    d["wstr"] = [cv.take(BF16, c.NC * 128) for _ in range(2)]
    d["wstg"] = [cv.take(BF16, c.NC * 128) for _ in range(2)]
    d["wstg_free"] = [None, None]
    d["wload2"] = [cv.take(BF16, c.NC // 4 * c.D) for _ in range(2)]
    d["wload_free"] = [None, None]
    d["wload_par"] = 0
    d["pT_free"] = [None, None]
    b.ph["l1a"] = d

    cv = Carver(AR, arena_bytes)
    d = {}
    d["w1e"] = cv.take(BF16, c.NC * c.F)
    d["w2e"] = cv.take(BF16, c.NC * c.F)
    d["w3s"] = [cv.take(BF16, c.NF * 128) for _ in range(2)]
    hid = cv.take(BF16, c.NF * 512)
    d["hid"] = hid
    hv = hid.bitcast(F32)
    d["sqf"] = hv[:, 0:c.D]
    d["wg_sb"] = hv[0:8, c.D:2 * c.D]
    d["otm"] = hv[:, 2 * c.D:3 * c.D]
    R = cv.take(BF16, 3 * c.NC // 4 * c.D)
    d["scores"] = R[0:8, 0:c.NT]
    d["g_fm"] = d["scores"]
    d["wload2"] = [R[:, 0:c.NC // 4 * c.D], cv.take(BF16, c.NC // 4 * c.D)]
    d["wload_free"] = [None, None]
    d["wload_par"] = 0
    d["pT_free"] = [None, None]
    d["wstg"] = [R[:, c.NC // 4 * c.D:2 * (c.NC // 4) * c.D],
                 R[:, 2 * (c.NC // 4) * c.D:3 * (c.NC // 4) * c.D]]
    d["wstg_free"] = [None, None]
    d["gld"] = cv.take(BF16, 512, parts=8)
    d["sel"] = cv.take(BF16, 8 * 128, parts=8)
    gB = cv.take(F32, 512)
    d["gB"] = gB
    d["msil"] = cv.take(BF16, 512)
    d["mt1"] = cv.take(BF16, 512)
    d["mst"] = cv.take(F32, 12)
    d["sctm"] = cv.take(BF16, 8)
    d["top8"] = cv.take(BF16, 8)
    d["rep"] = cv.take(BF16, 8)
    d["ms2"] = cv.take(BF16, 8)
    d["ms3"] = cv.take(BF16, 8)
    d["mk1"] = cv.take(F32, 8)
    d["mk2"] = cv.take(F32, 8)
    d["gtm"] = cv.take(F32, 8)
    d["gbf"] = cv.take(BF16, 8)
    d["wgT"] = cv.take(BF16, c.NC * 8)
    b.ph["moe"] = d
    return b


def phase_bufs(b, name):
    nb = Bufs()
    nb.__dict__.update({k: v for k, v in b.__dict__.items() if k != "ph"})
    nb.__dict__.update(b.ph[name])
    return nb




def emit_consts(em, b, cfg):
    e1 = em.gp(lambda e: e.memset(b.ones_bf[:], 1.0), inc=True)
    em.gp(lambda e: e.affine_select(
        out=b.ident[:], in_=b.ones_bf[:], pattern=[[-1, 128]],
        compare_op=ALU.is_equal, fill=0.0, base=0, channel_multiplier=1), waits=[e1], inc=True)
    if b.ident32 is not None:
        e2 = em.v(lambda e: e.memset(b.acc[:, 0:128], 1.0), inc=True)
        em.gp(lambda e: e.affine_select(
            out=b.ident32[:], in_=b.acc[:, 0:128], pattern=[[-1, 128]],
            compare_op=ALU.is_equal, fill=0.0, base=0, channel_multiplier=1), waits=[e2], inc=True)
    if getattr(b, "onesc32", None) is not None:
        em.v(lambda e: e.memset(b.onesc32[:], 1.0), inc=True)
        em.v(lambda e: e.memset(b.onescbf[:], 1.0), inc=True)
        em.v(lambda e: e.memset(b.onesr32[:], 1.0), inc=True)
    return em.gp(lambda e: e.engine_nop(), waits=[em.ev("sv"), em.ev("sgp")], inc=True)


def emit_ingest(em, b, cfg, x_in):
    c = cfg
    xflat = x_in.rearrange("s t d -> (s t) d")
    tile_ev = []
    prev_use = [None, None]
    for ti in range(c.NT // 128):
        buf = ti % 2
        w = [prev_use[buf]] if prev_use[buf] else []
        ld = em.dma_gp(f"dxi{buf}", b.xin[buf], xflat[ti * 128:(ti + 1) * 128, :], waits=w)
        evs = None
        for ch in range(c.NC):
            slot = ch % 2
            te = em.pe(lambda e, ch=ch, buf=buf, slot=slot: e.transpose(
                b.pT[slot][:, 0:128],
                b.xin[buf][:, ch * 128:(ch + 1) * 128], b.ident[:]),
                waits=[ld, em.ev("ssc")], inc=True)
            evs = em.sc(lambda e, ch=ch, ti=ti, slot=slot: e.copy(
                b.x[:, ch * c.NT + ti * 128: ch * c.NT + (ti + 1) * 128],
                b.pT[slot][:, 0:128]),
                waits=[te], inc=True)
        prev_use[buf] = evs
        if (ti + 1) % (c.TT // 128) == 0:
            tile_ev.append(evs)
    return tile_ev


def _consts_wait(em, ce):
    return [ce]


def emit_weight_load_T(em, b, cfg, w_dram, dst, ce, ready_after=()):
    c = cfg
    O = w_dram.shape[0]
    D = w_dram.shape[1]
    noc = O // 128
    ndc = D // 128
    hoc = max(1, (c.NC // 4 * c.D) // D)
    wre = w_dram.rearrange("(oc p) d -> p oc d", p=128)
    last = None
    prev_reads = list(ready_after)
    for og in range(0, noc, hoc):
        og_n = min(hoc, noc - og)
        ld = em.dma_gp("dwl", b.wload[:, 0:og_n * D].rearrange("p (oc d) -> p oc d", d=D),
                       wre[:, og:og + og_n, :], waits=prev_reads)
        te = None
        for oc in range(og, og + og_n):
            for dc in range(ndc):
                slot = (oc * ndc + dc) % 2
                te = em.pe(lambda e, oc=oc, dc=dc, slot=slot, og=og: e.transpose(
                    b.pT[slot][:, 0:128],
                    b.wload[:, (oc - og) * D + dc * 128: (oc - og) * D + (dc + 1) * 128],
                    b.ident[:]),
                    waits=[ld, ce, em.ev("ssc")], inc=True)
                last = em.sc(lambda e, oc=oc, dc=dc, slot=slot: e.copy(
                    dst[:, dc * O + oc * 128: dc * O + (oc + 1) * 128],
                    b.pT[slot][:, 0:128]),
                    waits=[te], inc=True)
        prev_reads = [te]
    return last




def emit_ln_fm(em, b, cfg, ti, acc_ev, prev_ln_ev=None, extra_waits=(), keep_acc=False):
    c = cfg
    T = c.TT
    s1 = None
    for ch in range(c.NC):
        s1 = em.pe(lambda e, ch=ch: e.matmul(
            b.pA[2][0:1, 0:T], b.onesc32[:], b.acc[:, ch * T:(ch + 1) * T],
            start=(ch == 0), stop=(ch == c.NC - 1)),
            waits=[acc_ev] + ([prev_ln_ev] if prev_ln_ev else [])
                  + (list(extra_waits) if ch == 0 else []),
            inc=(ch == c.NC - 1))
    s2 = None
    sq_evs = [None, None]
    for ch in range(c.NC):
        sl = ch % 2
        sq_evs[sl] = em.sc(lambda e, ch=ch, sl=sl: e.activation(
            b.sq[:, sl * T:(sl + 1) * T], b.acc[:, ch * T:(ch + 1) * T],
            AF.Square, bias=0.0, scale=1.0),
            waits=[acc_ev, s2], inc=True)
        s2 = em.pe(lambda e, ch=ch, sl=sl: e.matmul(
            b.pA[3][0:1, 0:T], b.onescbf[:], b.sq[:, sl * T:(sl + 1) * T],
            start=(ch == 0), stop=(ch == c.NC - 1)),
            waits=[sq_evs[sl]] + ([prev_ln_ev] if prev_ln_ev and ch == 0 else []),
            inc=True)
    st = b.strow
    t1 = st[0:1, 0:T]
    m_ = st[0:1, T:2 * T]
    t2 = st[0:1, 2 * T:3 * T]
    mr = t2
    em.v(lambda e: e.tensor_scalar(m_, b.pA[2][0:1, 0:T], 1.0 / c.D, None, ALU.mult),
         waits=[s1], inc=True)
    em.v(lambda e: e.tensor_scalar(t1, b.pA[3][0:1, 0:T], 1.0 / c.D, None, ALU.mult),
         waits=[s2], inc=True)
    em.v(lambda e: e.tensor_tensor(t2, m_, m_, ALU.mult), inc=True)
    em.v(lambda e: e.tensor_sub(t1, t1, t2), inc=True)
    ve = em.v(lambda e: e.tensor_scalar(t1, t1, 1e-5, None, ALU.add), inc=True)
    se = em.sc(lambda e: e.activation(t2, t1, AF.Sqrt, bias=0.0, scale=1.0),
               waits=[ve], inc=True)
    re_ = em.v(lambda e: e.reciprocal(t1, t2), waits=[se], inc=True)
    mre = em.v(lambda e: e.tensor_tensor(mr, m_, t1, ALU.mult), waits=[re_], inc=True)
    rstd = t1
    bc1 = em.pe(lambda e: e.matmul(b.pA[2][:, 0:T], b.onesr32[:], rstd,
                                   start=True, stop=True), waits=[re_, mre], inc=True)
    bc2 = em.pe(lambda e: e.matmul(b.pA[3][:, 0:T], b.onesr32[:], mr,
                                   start=True, stop=True), waits=[mre], inc=True)
    out_ev = None
    for ch in range(c.NC):
        em.v(lambda e, ch=ch: e.tensor_tensor(
            b.acc[:, ch * T:(ch + 1) * T],
            b.acc[:, ch * T:(ch + 1) * T], b.pA[2][:, 0:T], ALU.mult),
            waits=[bc1], inc=True)
        if keep_acc:
            em.v(lambda e, ch=ch: e.tensor_tensor(
                b.acc[:, ch * T:(ch + 1) * T],
                b.acc[:, ch * T:(ch + 1) * T],
                b.pA[3][:, 0:T], ALU.subtract),
                waits=[bc2], inc=True)
            out_ev = em.v(lambda e, ch=ch, ti=ti: e.tensor_copy(
                b.x[:, ch * c.NT + ti * T: ch * c.NT + (ti + 1) * T],
                b.acc[:, ch * T:(ch + 1) * T]), inc=True)
        else:
            out_ev = em.v(lambda e, ch=ch, ti=ti: e.tensor_tensor(
                b.x[:, ch * c.NT + ti * T: ch * c.NT + (ti + 1) * T],
                b.acc[:, ch * T:(ch + 1) * T],
                b.pA[3][:, 0:T], ALU.subtract),
                waits=[bc2], inc=True)
    return out_ev




def emit_egress(em, b, cfg, x_out, waits):
    c = cfg
    eg = [b.acc[:, 0:c.D], b.acc[:, c.D:2 * c.D]]
    xflat = x_out.rearrange("s t d -> (s t) d")
    dmas = []
    prev_dma = [None, None]
    for ti in range(c.NT // 128):
        sl = ti % 2
        ev = None
        for ch in range(c.NC):
            slot = ch % 2
            te = em.pe(lambda e, ch=ch, ti=ti, slot=slot: e.transpose(
                b.pT[slot][:, 0:128],
                b.x[:, ch * c.NT + ti * 128: ch * c.NT + (ti + 1) * 128],
                b.ident[:]),
                waits=list(waits) + [em.ev("ssc")], inc=True)
            ev = em.sc(lambda e, ch=ch, slot=slot, sl=sl: e.copy(
                eg[sl][:, ch * 128:(ch + 1) * 128],
                b.pT[slot][:, 0:128]),
                waits=[te, prev_dma[sl]], inc=True)
        dma = em.dma_sy(f"dout{sl}", xflat[ti * 128:(ti + 1) * 128, :],
                        eg[sl], waits=[ev])
        prev_dma[sl] = dma
        dmas.append(dma)
    return dmas




def emit_l0_attn(em, b, cfg, ce, x_tile_evs, prev_ln_ev=None, x32_dram=None):
    c = cfg
    T = c.TT
    wq_d, wk_d, wv_d, wo_d = b.l0a_w
    wq_ev = emit_weight_load_T(em, b, cfg, wq_d, b.wT[0][:], ce)
    wk_ev = emit_weight_load_T(em, b, cfg, wk_d, b.wT[1][:], ce, ready_after=[wq_ev])
    wv_ev = emit_weight_load_T(em, b, cfg, wv_d, b.wT[2][:], ce, ready_after=[wk_ev])
    wo_ev = emit_weight_load_T(em, b, cfg, wo_d, b.wT[3][:], ce, ready_after=[wv_ev])

    tiles_per_seq = c.SEQ // T
    ln_evs = []
    kg_ev = [None] * c.NSEQ
    vg_ev = [None] * c.NSEQ
    pSc_free = [None, None]
    probs_free = [None, None]
    expb_free = [None, None]
    probsT_free = [None, None]
    pT_free = [None, None]
    pCtx_free = None
    acc_free = None
    x32d = None
    prev_qkv_evac = None

    for ti in range(c.NTILE):
        s = ti // tiles_per_seq
        xev = x_tile_evs[ti]
        evac_evs = []
        for dst_i, (dst, wti, wev) in enumerate(
                ((b.qb, 0, wq_ev), (b.kb, 1, wk_ev), (b.vb, 2, wv_ev))):
            for oc in range(c.NC):
                bank = (dst_i * c.NC + oc) % 2
                mm = None
                for dc in range(c.NC):
                    w = [xev, wev]
                    if dc == 0:
                        w.append(pSc_free[bank] or em.ev("sv"))
                    mm = em.pe(lambda e, wti=wti, oc=oc, dc=dc, bank=bank, ti=ti: e.matmul(
                        b.pA[bank][:, 0:T],
                        b.wT[wti][:, dc * c.D + oc * 128: dc * c.D + (oc + 1) * 128],
                        b.x[:, dc * c.NT + ti * T: dc * c.NT + (ti + 1) * T],
                        start=(dc == 0), stop=(dc == c.NC - 1)),
                        waits=w, inc=(dc == c.NC - 1))
                if dst_i == 0:
                    ev = em.v(lambda e, oc=oc, bank=bank: e.tensor_scalar(
                        b.qb[:, oc * T:(oc + 1) * T], b.pA[bank][:, 0:T],
                        0.125, None, ALU.mult), waits=[mm], inc=True)
                else:
                    ev = em.sc(lambda e, dst=dst, oc=oc, bank=bank: e.copy(
                        dst[:, oc * T:(oc + 1) * T], b.pA[bank][:, 0:T]),
                        waits=[mm], inc=True)
                pSc_free[bank] = ev
                evac_evs.append(ev)
        qkv_ev = evac_evs[-1]
        qkv_all = [em.ev("sv"), em.ev("ssc")]

        for w4 in range(c.WPT):
            widx = ti * c.WPT + w4
            wi = widx % (c.SEQ // c.WIN)
            t0 = w4 * c.WIN
            vw_evs = []
            for chk in range(c.NC):
                slot = chk % 2
                te = em.pe(lambda e, chk=chk, slot=slot, t0=t0: e.transpose(
                    b.pT[slot][:, 0:128],
                    b.vb[:, chk * T + t0: chk * T + t0 + c.WIN],
                    b.ident[:]),
                    waits=qkv_all + [pT_free[slot] or em.ev("ssc")], inc=True)
                ev = em.sc(lambda e, chk=chk, slot=slot: e.copy(
                    b.vwT[:, chk * 128:(chk + 1) * 128],
                    b.pT[slot][:, 0:128]),
                    waits=[te], inc=True)
                pT_free[slot] = ev
                vw_evs.append(ev)
            vw_ev = vw_evs[-1]

            if wi == 0:
                kg_ev[s] = em.v(lambda e, s=s, t0=t0: e.tensor_copy(
                    b.kg[s][:].rearrange("p (c g) -> p c g", g=c.G),
                    b.kb[:].rearrange("p (c t) -> p c t", t=T)[:, :, t0:t0 + c.G]),
                    waits=qkv_all, inc=True)
                gevs = []
                for chk in range(c.NC):
                    slot = chk % 2
                    te = em.pe(lambda e, chk=chk, slot=slot, t0=t0: e.transpose(
                        b.pT[slot][0:c.G, 0:128],
                        b.vb[:, chk * T + t0: chk * T + t0 + c.G],
                        b.ident[:]),
                        waits=qkv_all + [pT_free[slot] or em.ev("ssc")], inc=True)
                    ev = em.sc(lambda e, chk=chk, slot=slot, s=s: e.copy(
                        b.vgT[s][0:c.G, chk * 128:(chk + 1) * 128],
                        b.pT[slot][0:c.G, 0:128]),
                        waits=[te], inc=True)
                    pT_free[slot] = ev
                    gevs.append(ev)
                vg_ev[s] = gevs[-1]

            ctx_mm_evs = []
            for h in range(c.HEADS):
                dc = h // 2
                r0 = (h % 2) * 64
                bank = h % 2
                hb = h % 2
                w0 = [qkv_ev, kg_ev[s], pSc_free[bank] or em.ev("sv")]
                em.pe(lambda e, dc=dc, r0=r0, bank=bank, s=s, t0=t0: e.matmul(
                    b.pA[bank][:, 0:c.G],
                    b.qb[r0:r0 + 64, dc * T + t0: dc * T + t0 + c.WIN],
                    b.kg[s][r0:r0 + 64, dc * c.G: (dc + 1) * c.G],
                    start=True, stop=True), waits=w0, inc=False)
                sc_mm = em.pe(lambda e, dc=dc, r0=r0, bank=bank, t0=t0: e.matmul(
                    b.pA[bank][:, c.G:c.G + c.WIN],
                    b.qb[r0:r0 + 64, dc * T + t0: dc * T + t0 + c.WIN],
                    b.kb[r0:r0 + 64, dc * T + t0: dc * T + t0 + c.WIN],
                    start=True, stop=True), inc=True)
                nmx = em.v(lambda e, bank=bank, hb=hb: e.reduce_max(
                    b.nm[:, hb:hb + 1], b.pA[bank][:, 0:c.G + c.WIN],
                    axis=AX.X, negate=True), waits=[sc_mm], inc=True)
                ex = em.sc(lambda e, bank=bank, hb=hb: e.activation(
                    b.expb[hb][:], b.pA[bank][:, 0:c.G + c.WIN], AF.Exp,
                    bias=b.nm[:, hb:hb + 1], scale=1.0,
                    accum_out=b.nm[:, 2 + hb:3 + hb]),
                    waits=[nmx, expb_free[hb] or em.ev("sv")], inc=True)
                pSc_free[bank] = ex
                rc = em.v(lambda e, hb=hb: e.reciprocal(
                    b.nm[:, 4 + hb:5 + hb], b.nm[:, 2 + hb:3 + hb]), waits=[ex], inc=True)
                pn = em.v(lambda e, hb=hb: e.tensor_scalar(
                    b.probs[hb][:], b.expb[hb][:], b.nm[:, 4 + hb:5 + hb], None,
                    ALU.mult), waits=[rc, probs_free[hb] or em.ev("spe")], inc=True)
                expb_free[hb] = pn
                tl = em.pe(lambda e, hb=hb: e.transpose(
                    b.pT[0][:, 0:128], b.probs[hb][:, c.G:c.G + c.WIN], b.ident[:]),
                    waits=[pn, pT_free[0] or em.ev("ssc")], inc=True)
                tg = em.pe(lambda e, hb=hb: e.transpose(
                    b.pT[1][0:c.G, 0:128], b.probs[hb][:, 0:c.G], b.ident[:]),
                    waits=[pn, pT_free[1] or em.ev("ssc")], inc=True)
                pe1 = em.sc(lambda e, hb=hb: e.copy(
                    b.probsT[hb][:, 0:128], b.pT[0][:, 0:128]),
                    waits=[tl, probsT_free[hb] or em.ev("spe")], inc=True)
                pe2 = em.sc(lambda e, hb=hb: e.copy(
                    b.probsT[hb][0:c.G, 128:128 + c.WIN], b.pT[1][0:c.G, 0:128]),
                    waits=[tg], inc=True)
                pT_free[0] = pe1
                pT_free[1] = pe2
                probs_free[hb] = pe2
                cw = [pe1, pe2, vw_ev, vg_ev[s]]
                if pCtx_free is not None and h == 0:
                    cw.append(pCtx_free)
                em.pe(lambda e, h=h, dc=dc, r0=r0, hb=hb, s=s: e.matmul(
                    b.pCtx[r0:r0 + 64, dc * 128:(dc + 1) * 128],
                    b.vgT[s][:, h * 64:(h + 1) * 64],
                    b.probsT[hb][0:c.G, 128:128 + c.WIN],
                    start=True, stop=False), waits=cw, inc=False)
                cme = em.pe(lambda e, h=h, dc=dc, r0=r0, hb=hb: e.matmul(
                    b.pCtx[r0:r0 + 64, dc * 128:(dc + 1) * 128],
                    b.vwT[:, h * 64:(h + 1) * 64],
                    b.probsT[hb][:, 0:128],
                    start=False, stop=True), inc=True)
                probsT_free[hb] = cme
                ctx_mm_evs.append(cme)
            ctx_ev = None
            for chk in range(c.NC):
                ctx_ev = em.v(lambda e, chk=chk, t0=t0: e.tensor_copy(
                    b.ctxb[:, chk * T + t0: chk * T + t0 + c.WIN],
                    b.pCtx[:, chk * 128:(chk + 1) * 128]),
                    waits=[ctx_mm_evs[-1]], inc=True)
            pCtx_free = ctx_ev

        acc_ev = None
        for oc in range(c.NC):
            bank = oc % 2
            mm = None
            for dc in range(c.NC):
                w = [pCtx_free, wo_ev]
                if dc == 0:
                    w.append(pSc_free[bank] or em.ev("sv"))
                mm = em.pe(lambda e, oc=oc, dc=dc, bank=bank: e.matmul(
                    b.pA[bank][:, 0:T],
                    b.wT[3][:, dc * c.D + oc * 128: dc * c.D + (oc + 1) * 128],
                    b.ctxb[:, dc * T:(dc + 1) * T],
                    start=(dc == 0), stop=(dc == c.NC - 1)),
                    waits=w, inc=(dc == c.NC - 1))
            acc_ev = em.v(lambda e, oc=oc, bank=bank, ti=ti: e.tensor_tensor(
                b.acc[:, oc * T:(oc + 1) * T], b.pA[bank][:, 0:T],
                b.x[:, oc * c.NT + ti * T: oc * c.NT + (ti + 1) * T], ALU.add),
                waits=[mm] + ([acc_free] if oc == 0 else []), inc=True)
            pSc_free[bank] = acc_ev
        ln_ev = emit_ln_fm(em, b, cfg, ti, acc_ev, prev_ln_ev,
                           keep_acc=(x32_dram is not None))
        prev_ln_ev = ln_ev
        if x32_dram is not None:
            for sub in range(4):
                ev = None
                for ch in range(c.NC):
                    slot = ch % 2
                    ptv = b.pT[slot][:].bitcast(F32)
                    te = em.pe(lambda e, ch=ch, sub=sub, slot=slot, ptv=ptv: e.transpose(
                        ptv[:, 0:128],
                        b.acc[:, ch * T + sub * 128: ch * T + sub * 128 + 128],
                        b.ident32[:]),
                        waits=[ln_ev, em.ev("ssc"),
                               pT_free[slot] or em.ev("ssc")], inc=True)
                    ev = em.sc(lambda e, ch=ch, slot=slot, ptv=ptv: e.copy(
                        b.x32st[:, ch * 128:(ch + 1) * 128], ptv[:, 0:128]),
                        waits=[te, x32d], inc=True)
                    pT_free[slot] = ev
                r0 = ti * T + sub * 128
                x32d = em.dma_sy("dx32", x32_dram[r0:r0 + 128, :], b.x32st[:],
                                 waits=[ev])
            acc_free = em.ev("spe")
            ln_ev = x32d
        else:
            acc_free = None
        ln_evs.append(ln_ev)
    return ln_evs




def build_program(cfg, parts):
    c = cfg
    nc = bass.Bass("TRN2", target_bir_lowering=False)
    x_in = nc.dram_tensor("x_in", [c.NSEQ, c.SEQ, c.D], F32, kind="ExternalInput").ap()
    x_out = nc.dram_tensor("x_out", [c.NSEQ, c.SEQ, c.D], F32, kind="ExternalOutput").ap()

    fused = len(parts) > 1 or parts == ["moe"]
    with ExitStack() as ctx:
        if fused:
            b = alloc_bufs_fused(nc, ctx, cfg)
        else:
            b = alloc_bufs(nc, ctx, cfg, parts)
        if "l0a" in parts:
            b.l0a_w = tuple(
                nc.dram_tensor(n, [c.D, c.D], F32, kind="ExternalInput").ap()
                for n in ("l0_wq", "l0_wk", "l0_wv", "l0_wo"))
        if "l1a" in parts:
            b.l1a_w = tuple(
                nc.dram_tensor(n, [c.D, c.D], F32, kind="ExternalInput").ap()
                for n in ("l1_wq", "l1_wk", "l1_wv"))
            b.l1a_wT = tuple(
                nc.dram_tensor(n, [128, c.NC, c.D], BF16).ap()
                for n in ("l1_wqT", "l1_wkT", "l1_wvT"))
        if "moe" in parts:
            b.moe_w = (
                nc.dram_tensor("l1_wg", [c.E, c.D], F32, kind="ExternalInput").ap(),
                nc.dram_tensor("l1_w1", [c.E, c.F, c.D], F32, kind="ExternalInput").ap(),
                nc.dram_tensor("l1_w2", [c.E, c.F, c.D], F32, kind="ExternalInput").ap(),
                nc.dram_tensor("l1_w3", [c.E, c.D, c.F], F32, kind="ExternalInput").ap(),
            )
            b.moe_w1T = nc.dram_tensor("m_w1T", [c.E, 128, c.NC, c.F], BF16).ap()
            b.moe_w2T = nc.dram_tensor("m_w2T", [c.E, 128, c.NC, c.F], BF16).ap()
            b.moe_w3T = nc.dram_tensor("m_w3T", [c.E, 128, c.NF, c.D], BF16).ap()
            b.accm = nc.dram_tensor("m_acc", [128, c.NC, c.NT], F32).ap()
        if "l0f" in parts:
            b.l0f_w = (
                nc.dram_tensor("l0_wgate", [c.I, c.D], F32, kind="ExternalInput").ap(),
                nc.dram_tensor("l0_wup", [c.I, c.D], F32, kind="ExternalInput").ap(),
                nc.dram_tensor("l0_wdown", [c.D, c.I], F32, kind="ExternalInput").ap(),
            )
            b.l0f_wgT = nc.dram_tensor("l0f_wgT", [128, c.NC, c.I], BF16).ap()
            b.l0f_wupT = nc.dram_tensor("l0f_wupT", [128, c.NC, c.I], BF16).ap()
            b.l0f_wdT = nc.dram_tensor("l0f_wdT", [128, c.NI, c.D], BF16).ap()
        if fused:
            b.x32s = nc.dram_tensor("x32s", [c.NT, c.D], F32).ap()
        if "moe" in parts:
            b.g_dram = nc.dram_tensor("g_dram", [8, c.NT], BF16).ap()
        em = Em(nc)

        bp = phase_bufs(b, "l0a") if fused else b
        ce = emit_consts(em, bp, cfg)
        tile_evs = emit_ingest(em, bp, cfg, x_in)
        final_evs = tile_evs
        if "l0a" in parts:
            final_evs = emit_l0_attn(em, bp, cfg, ce, tile_evs,
                                     x32_dram=b.x32s if fused else None)
        if "l0f" in parts:
            if fused:
                em.phase_barrier()
                bp = phase_bufs(b, "l0f")
                x32 = b.x32s
            else:
                x32 = x_in if parts[0] == "l0f" else None
            final_evs = emit_l0_ffn(em, bp, cfg, ce, final_evs, x32_dram=x32)
        if "l1a" in parts:
            if fused:
                em.phase_barrier()
                bp = phase_bufs(b, "l1a")
            final_evs = emit_l1_attn(em, bp, cfg, ce, final_evs)
        if "moe" in parts:
            if fused:
                em.phase_barrier()
                bp = phase_bufs(b, "moe")
            out_dmas = emit_l1_moe(em, bp, cfg, ce, final_evs, x_out)
        else:
            out_dmas = emit_egress(em, bp, cfg, x_out, [final_evs[-1]] if final_evs else [])
        finals = {}
        for s, v in out_dmas:
            finals[s] = max(finals.get(s, 0), v)
        em.emit("sy", lambda e: e.nop(), waits=sorted(finals.items()))

        for s in sorted(em.cnt.keys()):
            em.sems[s] = ctx.enter_context(nc.semaphore(s))
        block = ctx.enter_context(nc.Block())
        em.play(block)
    return nc




def emit_quant(em, b, cfg, src, dst, W, waits, eng="v", qs=None, dst_war=()):
    c = cfg
    E = em.v if eng == "v" else em.gp
    q_xn, q_g, q_a, q_c, q_sc = qs if qs is not None else b.qsets[0]
    ev = None
    for lo in range(0, W, c.LW):
        wsub = min(c.LW, W - lo)
        nb = wsub // 16
        s3 = src[:, lo:lo + wsub].rearrange("p (a b) -> p a b", b=16)
        sc16 = q_sc[:, 0:nb]
        scl = q_sc[:, nb:2 * nb]
        rscl = q_sc[:, 2 * nb:3 * nb]
        xn = q_xn[:, 0:wsub]
        xn3 = xn.rearrange("p (a b) -> p a b", b=16)
        g = q_g[:, 0:wsub]
        a = q_a[:, 0:wsub]
        cc = q_c[:, 0:wsub]
        rev = em.v(lambda e, s3=s3, sc16=sc16: e.reduce_max(
            sc16, s3, axis=AX.X, apply_absolute_value=True),
            waits=list(waits) + ([ev] if ev is not None else []))
        sev2 = em.v(lambda e, scl=scl, sc16=sc16: e.tensor_scalar(
            scl, sc16, 1.0 / 6.0, 1e-12, ALU.mult, ALU.max), waits=[rev])
        rcev = em.v(lambda e, rscl=rscl, scl=scl: e.reciprocal(rscl, scl))
        vev = E(lambda e, xn3=xn3, s3=s3, rscl=rscl, nb=nb: e.tensor_tensor(
            xn3, s3, rscl.unsqueeze(2).broadcast_to((128, nb, 16)), ALU.mult),
            waits=[rcev])
        em.sc(lambda e, g=g, xn=xn: e.sign(g, xn), waits=[vev])
        sev = em.sc(lambda e, a=a, xn=xn: e.activation(a, xn, AF.Abs, bias=0.0, scale=1.0))
        if eng == "v":
            E(lambda e, cc=cc, a=a: e.tensor_scalar(cc, a, 0.25, None, ALU.is_gt),
              waits=[sev])
            for mid in (0.75, 1.25, 1.75):
                E(lambda e, cc=cc, a=a, mid=mid: e.scalar_tensor_tensor(
                    cc, a, mid, cc, ALU.is_gt, ALU.add))
            E(lambda e, xn=xn, a=a: e.tensor_scalar(xn, a, 2.5, None, ALU.is_gt))
            E(lambda e, xn=xn, a=a: e.scalar_tensor_tensor(
                xn, a, 3.5, xn, ALU.is_gt, ALU.add))
            E(lambda e, cc=cc, xn=xn: e.scalar_tensor_tensor(
                cc, cc, 0.5, xn, ALU.mult, ALU.add))
            E(lambda e, a=a: e.tensor_scalar(a, a, 5.0, 2.0, ALU.is_gt, ALU.mult))
            E(lambda e, cc=cc, a=a: e.tensor_tensor(cc, cc, a, ALU.add))
            gev = E(lambda e, cc=cc, g=g: e.tensor_tensor(cc, cc, g, ALU.mult))
            sfac = 1.0
        else:
            E(lambda e, cc=cc, a=a: e.tensor_scalar(cc, a, 0.25, None, ALU.is_gt),
              waits=[sev])
            for mid, wgt in ((0.75, None), (1.25, None), (1.75, None),
                             (2.5, 2.0), (3.5, 2.0), (5.0, 4.0)):
                if wgt is None:
                    E(lambda e, xn=xn, a=a, mid=mid: e.tensor_scalar(
                        xn, a, mid, None, ALU.is_gt))
                else:
                    E(lambda e, xn=xn, a=a, mid=mid, wgt=wgt: e.tensor_scalar(
                        xn, a, mid, wgt, ALU.is_gt, ALU.mult))
                E(lambda e, cc=cc, xn=xn: e.tensor_tensor(cc, cc, xn, ALU.add))
            gev = E(lambda e, cc=cc, g=g: e.tensor_tensor(cc, cc, g, ALU.mult))
            sfac = 0.5
        hscl = q_sc[:, 3 * nb:4 * nb]
        hev = None
        if sfac != 1.0:
            hev = em.v(lambda e, hscl=hscl, scl=scl: e.tensor_scalar(
                hscl, scl, 0.5, None, ALU.mult), waits=[sev2])
            sclx = hscl
        else:
            sclx = scl
        cc3 = cc.rearrange("p (a b) -> p a b", b=16)
        d3 = dst[:, lo:lo + wsub].rearrange("p (a b) -> p a b", b=16)
        ev = E(lambda e, d3=d3, cc3=cc3, sclx=sclx, nb=nb: e.tensor_tensor(
            d3, cc3, sclx.unsqueeze(2).broadcast_to((128, nb, 16)), ALU.mult),
            waits=[gev, hev] + list(dst_war))
    return ev




def emit_l0f_weight_prep(em, b, cfg, ce, after=()):
    c = cfg
    jobs = (
        (b.l0f_w[0], b.l0f_wgT, c.I, c.D),
        (b.l0f_w[1], b.l0f_wupT, c.I, c.D),
        (b.l0f_w[2], b.l0f_wdT, c.D, c.I),
    )
    in_free = [list(after), list(after)]
    out_free = [None, None]
    pT_free = [em.ev("ssc"), em.ev("ssc")]
    rc_idx = 0
    for (w_d, wT_d, R, W) in jobs:
        nrc = R // 128
        nwc = W // 128
        wre = w_d.rearrange("(rc p) d -> p rc d", p=128)
        for rc in range(nrc):
            eng = "gp" if rc_idx % 3 == 2 else "v"
            sl = 0 if eng == "v" else 1
            ld = em.dma_gp(f"dwqin{sl}", b.wq_in2[sl][:, 0:W], wre[:, rc, :],
                           waits=in_free[sl] + [ce])
            qev = emit_quant(em, b, cfg, b.wq_in2[sl][:, 0:W],
                             b.wq_out2[sl][:, 0:W], W, [ld],
                             eng=eng, qs=b.qsets[sl], dst_war=[out_free[sl]])
            in_free[sl] = [qev]
            ssl = rc_idx % 2
            sev = None
            te = None
            for wc in range(nwc):
                slot = wc % 2
                te = em.pe(lambda e, wc=wc, slot=slot, sl=sl: e.transpose(
                    b.pT[slot][:, 0:128],
                    b.wq_out2[sl][:, wc * 128:(wc + 1) * 128], b.ident[:]),
                    waits=[qev, ce, pT_free[slot]], inc=True)
                sev = em.sc(lambda e, wc=wc, slot=slot, ssl=ssl: e.copy(
                    b.wstg[ssl][:, wc * 128:(wc + 1) * 128],
                    b.pT[slot][:, 0:128]),
                    waits=[te, b.wstg_free[ssl]], inc=True)
                pT_free[slot] = sev
            out_free[sl] = te
            dma = em.dma_gp(f"dstg{ssl}",
                            wT_d[:, :, rc * 128:(rc + 1) * 128],
                            b.wstg[ssl][:, 0:nwc * 128].rearrange(
                                "p (wc m) -> p wc m", m=128),
                            waits=[sev])
            b.wstg_free[ssl] = dma
            rc_idx += 1
    return [em.ev("dstg0"), em.ev("dstg1")]

def emit_l0_ffn(em, b, cfg, ce, x_tile_evs, prev_ln_ev=None, x32_dram=None):
    c = cfg
    T = c.TT
    if x32_dram is None:
        xflat32 = None
    elif len(x32_dram.shape) == 3:
        xflat32 = x32_dram.rearrange("s t d -> (s t) d")
    else:
        xflat32 = x32_dram
    prep_evs = emit_l0f_weight_prep(em, b, cfg, ce, after=[x_tile_evs[-1]])
    NFB = (c.I + 511) // 512
    ln_evs = []
    pa_free = [None, None, None, None]
    pT_free = [em.ev("ssc"), em.ev("ssc")]
    qi = [0]
    xq_rd = None
    hb_rd = [None] * 4
    hqfm_rd = None
    wg_free = [None, None]
    wd_free = [None, None]
    silu_free = None
    xtm_free = [None, None]
    xqtm_free = [None, None]
    qset_free = [None, None]
    qsh_free = [None, None]
    hqtm_free = None

    for ti in range(c.NTILE):
        xev = x_tile_evs[ti]
        xq_ev = None
        for sub in range(4):
            eng = "gp" if qi[0] % 3 == 2 else "v"
            qi[0] += 1
            sl = 0 if eng == "v" else 1
            r0 = ti * T + sub * 128
            ev32 = em.dma_gp(f"dxq32_{sl}", b.xtm32_2[sl], xflat32[r0:r0 + 128, :],
                             waits=[xev, xtm_free[sl]] + prep_evs)
            qev = emit_quant(em, b, cfg, b.xtm32_2[sl], b.xqtm2[sl], c.D,
                             [ev32, qset_free[sl]], eng=eng, qs=b.qsets[sl],
                             dst_war=[xqtm_free[sl]])
            xtm_free[sl] = qev
            qset_free[sl] = qev
            te = None
            for ch in range(c.NC):
                slot = ch % 2
                te = em.pe(lambda e, ch=ch, slot=slot, sl=sl: e.transpose(
                    b.pT[slot][:, 0:128],
                    b.xqtm2[sl][:, ch * 128:(ch + 1) * 128], b.ident[:]),
                    waits=[qev, pT_free[slot]], inc=True)
                xq_ev = em.sc(lambda e, ch=ch, sub=sub, slot=slot: e.copy(
                    b.xq_fm[:, ch * T + sub * 128: ch * T + sub * 128 + 128],
                    b.pT[slot][:, 0:128]),
                    waits=[te, xq_rd], inc=True)
                pT_free[slot] = xq_ev
            xqtm_free[sl] = te
        h_ev = None
        h_sub_ev = [None] * 4
        for fb in range(NFB):
            fw = min(512, c.I - fb * 512)
            gsl = 0
            wgl = em.dma_gp(f"dwg{gsl}",
                            b.wg_s[gsl][:, 0:c.NC * fw].rearrange("p (dc f) -> p dc f", f=fw),
                            b.l0f_wgT[:, :, fb * 512: fb * 512 + fw],
                            waits=[wg_free[gsl]] + prep_evs)
            wul = em.dma_gp(f"dwu{gsl}",
                            b.wup_s[gsl][:, 0:c.NC * fw].rearrange("p (dc f) -> p dc f", f=fw),
                            b.l0f_wupT[:, :, fb * 512: fb * 512 + fw],
                            waits=[wg_free[gsl]] + prep_evs)
            last_mm = [None, None]
            for sub in range(4):
                bsel = (fb * 4 + sub) % 2
                bg, bu = (0, 1) if bsel == 0 else (2, 3)
                mmg = None
                mmu = None
                for dc in range(c.NC):
                    w = [xq_ev, wgl]
                    if dc == 0:
                        w.append(pa_free[bg])
                    mmg = em.pe(lambda e, dc=dc, sub=sub, fw=fw, gsl=gsl, bg=bg: e.matmul(
                        b.pA[bg][:, 0:fw],
                        b.xq_fm[:, dc * T + sub * 128: dc * T + sub * 128 + 128],
                        b.wg_s[gsl][:, dc * fw: (dc + 1) * fw],
                        start=(dc == 0), stop=(dc == c.NC - 1)),
                        waits=w, inc=(dc == c.NC - 1))
                for dc in range(c.NC):
                    w = [xq_ev, wul]
                    if dc == 0:
                        w.append(pa_free[bu])
                    mmu = em.pe(lambda e, dc=dc, sub=sub, fw=fw, gsl=gsl, bu=bu: e.matmul(
                        b.pA[bu][:, 0:fw],
                        b.xq_fm[:, dc * T + sub * 128: dc * T + sub * 128 + 128],
                        b.wup_s[gsl][:, dc * fw: (dc + 1) * fw],
                        start=(dc == 0), stop=(dc == c.NC - 1)),
                        waits=w, inc=(dc == c.NC - 1))
                sev = em.sc(lambda e, fw=fw, bg=bg: e.activation(
                    b.silu[:, 0:fw], b.pA[bg][:, 0:fw], AF.Silu, bias=0.0, scale=1.0),
                    waits=[mmg, silu_free], inc=True)
                pa_free[bg] = sev
                h_ev = em.v(lambda e, sub=sub, fb=fb, fw=fw, bu=bu: e.tensor_tensor(
                    b.hb[:, sub * c.I + fb * 512: sub * c.I + fb * 512 + fw],
                    b.silu[:, 0:fw], b.pA[bu][:, 0:fw], ALU.mult),
                    waits=[mmu, sev, hb_rd[sub]], inc=True)
                if fb == NFB - 1:
                    h_sub_ev[sub] = h_ev
                pa_free[bu] = h_ev
                silu_free = h_ev
                last_mm[0] = mmg
                last_mm[1] = mmu
            wg_free[gsl] = last_mm[1]
        hq_ev = None
        for sub in range(4):
            qevs = []
            for lo in range(0, c.I, c.LW):
                wsub = min(c.LW, c.I - lo)
                eng = "gp" if qi[0] % 3 == 2 else "v"
                qi[0] += 1
                sl = 0 if eng == "v" else 1
                PC = em.v if eng == "v" else em.gp
                pc = PC(lambda e, sub=sub, lo=lo, wsub=wsub, sl=sl: e.tensor_copy(
                    b.qs_h2[sl][:, 0:wsub],
                    b.hb[:, sub * c.I + lo: sub * c.I + lo + wsub]),
                    waits=[h_sub_ev[sub], qsh_free[sl]])
                qv = emit_quant(em, b, cfg, b.qs_h2[sl][:, 0:wsub],
                                b.hq_tm[:, lo:lo + wsub], wsub, [pc, qset_free[sl]],
                                eng=eng, qs=b.qsets[sl],
                                dst_war=[hqtm_free])
                qsh_free[sl] = qv
                qset_free[sl] = qv
                qevs.append(qv)
            hb_rd[sub] = qevs[-1]
            te = None
            for ic in range(c.NI):
                slot = ic % 2
                qev = qevs[(ic * 128) // c.LW if (ic * 128) // c.LW < len(qevs) else -1]
                te = em.pe(lambda e, ic=ic, slot=slot: e.transpose(
                    b.pT[slot][:, 0:128],
                    b.hq_tm[:, ic * 128:(ic + 1) * 128], b.ident[:]),
                    waits=qevs + [pT_free[slot]], inc=True)
                hq_ev = em.sc(lambda e, ic=ic, sub=sub, slot=slot: e.copy(
                    b.hq_fm[:, ic * T + sub * 128: ic * T + sub * 128 + 128],
                    b.pT[slot][:, 0:128]),
                    waits=[te, hqfm_rd], inc=True)
                pT_free[slot] = hq_ev
            hqtm_free = te
        acc_ev = None
        down_mm = None
        for oc in range(c.NC):
            dsl = 0
            wdl = em.dma_gp(f"dwd{dsl}",
                            b.wd_s[dsl][:].rearrange("p (ic o) -> p ic o", o=128),
                            b.l0f_wdT[:, :, oc * 128:(oc + 1) * 128],
                            waits=[wd_free[dsl]] + prep_evs)
            bank = oc % 2
            for ic in range(c.NI):
                w = [hq_ev, wdl]
                if ic == 0:
                    w.append(pa_free[bank])
                down_mm = em.pe(lambda e, oc=oc, ic=ic, dsl=dsl, bank=bank: e.matmul(
                    b.pA[bank][:, 0:T],
                    b.wd_s[dsl][:, ic * 128:(ic + 1) * 128],
                    b.hq_fm[:, ic * T:(ic + 1) * T],
                    start=(ic == 0), stop=(ic == c.NI - 1)),
                    waits=w, inc=(ic == c.NI - 1))
            wd_free[dsl] = down_mm
            acc_ev = em.v(lambda e, oc=oc, bank=bank, ti=ti: e.tensor_tensor(
                b.acc[:, oc * T:(oc + 1) * T], b.pA[bank][:, 0:T],
                b.x[:, oc * c.NT + ti * T: oc * c.NT + (ti + 1) * T], ALU.add),
                waits=[down_mm], inc=True)
            pa_free[bank] = acc_ev
        hqfm_rd = down_mm
        xq_rd = last_mm[1]
        ln_ev = emit_ln_fm(em, b, cfg, ti, acc_ev, prev_ln_ev,
                           extra_waits=[pa_free[2], pa_free[3]])
        pa_free[2] = ln_ev
        pa_free[3] = ln_ev
        prev_ln_ev = ln_ev
        ln_evs.append(ln_ev)
    return ln_evs


class PrepJob:

    def __init__(self, em, b, cfg, w_dram, wT_dram, ce, after=()):
        self.em, self.b, self.cfg = em, b, cfg
        self.wT = wT_dram
        self.ce = ce
        self.O = w_dram.shape[0]
        self.D = w_dram.shape[1]
        self.noc = self.O // 128
        self.ndc = self.D // 128
        self.hoc = max(1, (cfg.NC // 4 * cfg.D) // self.D)
        self.wre = w_dram.rearrange("(oc p) d -> p oc d", p=128)
        self.extra = list(after)
        self.og = 0

    def done(self):
        return self.og >= self.noc

    def load_chunk(self):
        em, b = self.em, self.b
        og = self.og
        og_n = min(self.hoc, self.noc - og)
        self.og += og_n
        D = self.D
        par = b.wload_par
        b.wload_par ^= 1
        ld = em.dma_gp(f"dwl{par}",
                       b.wload2[par][:, 0:og_n * D].rearrange("p (oc d) -> p oc d", d=D),
                       self.wre[:, og:og + og_n, :],
                       waits=self.extra + [b.wload_free[par], self.ce])
        self.extra = []
        return (og, og_n, ld, par)

    def tail_chunk(self, tok):
        em, b = self.em, self.b
        og, og_n, ld, par = tok
        D = self.D
        te = None
        for oc in range(og, og + og_n):
            ssl = oc % 2
            evs = None
            for dc in range(self.ndc):
                slot = dc % 2
                te = em.pe(lambda e, oc=oc, dc=dc, slot=slot, og=og, D=D, par=par: e.transpose(
                    b.pT[slot][:, 0:128],
                    b.wload2[par][:, (oc - og) * D + dc * 128: (oc - og) * D + (dc + 1) * 128],
                    b.ident[:]),
                    waits=[ld, self.ce, b.pT_free[slot]], inc=True)
                evs = em.sc(lambda e, dc=dc, slot=slot, ssl=ssl: e.copy(
                    b.wstg[ssl][:, dc * 128:(dc + 1) * 128],
                    b.pT[slot][:, 0:128]),
                    waits=[te, b.wstg_free[ssl]], inc=True)
                b.pT_free[slot] = evs
            dma = em.dma_gp(f"dstg{ssl}",
                            self.wT[:, :, oc * 128:(oc + 1) * 128],
                            b.wstg[ssl][:, 0:self.ndc * 128].rearrange(
                                "p (dc m) -> p dc m", m=128),
                            waits=[evs])
            b.wstg_free[ssl] = dma
        b.wload_free[par] = te


class PrepQueue:

    def __init__(self, jobs):
        self.jobs = list(jobs)
        self.ji = 0
        self.pending = None

    def done(self):
        return self.ji >= len(self.jobs) and self.pending is None

    def step(self):
        tok = None
        while self.ji < len(self.jobs):
            if not self.jobs[self.ji].done():
                tok = (self.jobs[self.ji], self.jobs[self.ji].load_chunk())
                break
            self.ji += 1
        if self.pending is not None:
            j, t = self.pending
            j.tail_chunk(t)
        self.pending = tok

    def run(self, k=-1):
        n = 0
        while not self.done() and (k < 0 or n < k):
            self.step()
            n += 1


def emit_weight_prep_dram(em, b, cfg, w_dram, wT_dram, ce, after=()):
    q = PrepQueue([PrepJob(em, b, cfg, w_dram, wT_dram, ce, after)])
    q.run()
    return [em.ev("dstg0"), em.ev("dstg1")]


def emit_l1_attn(em, b, cfg, ce, x_tile_evs, prev_ln_ev=None):
    c = cfg
    T = c.TT
    pq = emit_weight_prep_dram(em, b, cfg, b.l1a_w[0], b.l1a_wT[0], ce)
    pk = emit_weight_prep_dram(em, b, cfg, b.l1a_w[1], b.l1a_wT[1], ce)
    pv = emit_weight_prep_dram(em, b, cfg, b.l1a_w[2], b.l1a_wT[2], ce)
    prep_evs = pv

    mz = em.gp(lambda e: e.memset(b.cmask[:], 0.0), waits=[ce], inc=True)
    mask_ev = mz
    for o in range(4):
        mask_ev = em.gp(lambda e, o=o: e.affine_select(
            out=b.cmask[:, o * 512:(o + 1) * 512], in_=b.cmask[:, o * 512:(o + 1) * 512],
            pattern=[[-1, 512]], compare_op=ALU.is_ge, fill=NEG,
            base=128 * o, channel_multiplier=1), waits=[mz], inc=True)

    ngrp = c.SEQ // T
    ntpg = T // 128
    ln_evs = []
    pa_free = [None, None, None, None]
    pT_free = [em.ev("ssc"), em.ev("ssc")]
    pCtx_free = None
    probsT_free = [None, None]
    probs_free = None
    kv_free = None
    qg_free = None
    vtmp_free = None
    wstr_free = [None, None]

    def proj_stream(wT_dram, oc, wsl, extra):
        return em.dma_gp(f"dws{wsl}",
                         b.wstr[wsl][:].rearrange("p (dc m) -> p dc m", m=128),
                         wT_dram[:, :, oc * 128:(oc + 1) * 128],
                         waits=[wstr_free[wsl]] + prep_evs + list(extra))

    for s in range(c.NSEQ):
        base = s * c.SEQ
        kv_ev = None
        for tg in range(ngrp):
            ti_glob = (base + tg * T) // T
            xev = x_tile_evs[ti_glob]
            for oc in range(c.NC):
                bank = oc % 2
                wsl = oc % 2
                wl = proj_stream(b.l1a_wT[1], oc, wsl, [])
                mm = None
                for dc in range(c.NC):
                    w = [xev, wl]
                    if dc == 0:
                        w += [pa_free[bank], kv_free]
                    mm = em.pe(lambda e, oc=oc, dc=dc, bank=bank, tg=tg, base=base, wsl=wsl: e.matmul(
                        b.pA[bank][:, 0:T],
                        b.wstr[wsl][:, dc * 128:(dc + 1) * 128],
                        b.x[:, dc * c.NT + base + tg * T: dc * c.NT + base + (tg + 1) * T],
                        start=(dc == 0), stop=(dc == c.NC - 1)),
                        waits=w, inc=(dc == c.NC - 1))
                wstr_free[wsl] = mm
                kv_ev = em.sc(lambda e, oc=oc, bank=bank, tg=tg, base=base: e.copy(
                    b.kS[:, oc * c.SEQ + tg * T: oc * c.SEQ + (tg + 1) * T],
                    b.pA[bank][:, 0:T]), waits=[mm], inc=True)
                pa_free[bank] = kv_ev
            for oc in range(c.NC):
                bank = oc % 2
                wsl = oc % 2
                wl = proj_stream(b.l1a_wT[2], oc, wsl, [])
                mm = None
                for dc in range(c.NC):
                    w = [xev, wl]
                    if dc == 0:
                        w += [pa_free[bank], kv_free]
                    mm = em.pe(lambda e, oc=oc, dc=dc, bank=bank, tg=tg, base=base, wsl=wsl: e.matmul(
                        b.pA[bank][:, 0:T],
                        b.wstr[wsl][:, dc * 128:(dc + 1) * 128],
                        b.x[:, dc * c.NT + base + tg * T: dc * c.NT + base + (tg + 1) * T],
                        start=(dc == 0), stop=(dc == c.NC - 1)),
                        waits=w, inc=(dc == c.NC - 1))
                wstr_free[wsl] = mm
                vt = em.v(lambda e, bank=bank: e.tensor_copy(
                    b.vtmp[:], b.pA[bank][:, 0:T]), waits=[mm, vtmp_free], inc=True)
                pa_free[bank] = vt
                for sj in range(4):
                    slot = sj % 2
                    te = em.pe(lambda e, sj=sj, slot=slot: e.transpose(
                        b.pT[slot][:, 0:128], b.vtmp[:, sj * 128:(sj + 1) * 128],
                        b.ident[:]), waits=[vt, pT_free[slot] or em.ev("ssc")], inc=True)
                    ev = em.sc(lambda e, oc=oc, tg=tg, sj=sj, slot=slot: e.copy(
                        b.vS[:, (tg * 4 + sj) * c.D + oc * 128: (tg * 4 + sj) * c.D + (oc + 1) * 128],
                        b.pT[slot][:, 0:128]), waits=[te], inc=True)
                    pT_free[slot] = ev
                    vtmp_free = ev
                kv_ev = vtmp_free

        for tg in range(ngrp):
            ti_glob = (base + tg * T) // T
            xev = x_tile_evs[ti_glob]
            q_ev = None
            for oc in range(c.NC):
                bank = oc % 2
                wsl = oc % 2
                wl = proj_stream(b.l1a_wT[0], oc, wsl, [])
                mm = None
                for dc in range(c.NC):
                    w = [xev, wl]
                    if dc == 0:
                        w += [pa_free[bank], qg_free]
                    mm = em.pe(lambda e, oc=oc, dc=dc, bank=bank, tg=tg, base=base, wsl=wsl: e.matmul(
                        b.pA[bank][:, 0:T],
                        b.wstr[wsl][:, dc * 128:(dc + 1) * 128],
                        b.x[:, dc * c.NT + base + tg * T: dc * c.NT + base + (tg + 1) * T],
                        start=(dc == 0), stop=(dc == c.NC - 1)),
                        waits=w, inc=(dc == c.NC - 1))
                wstr_free[wsl] = mm
                q_ev = em.v(lambda e, oc=oc, bank=bank: e.tensor_scalar(
                    b.qg[:, oc * T:(oc + 1) * T], b.pA[bank][:, 0:T],
                    1.0 / 32.0, None, ALU.mult), waits=[mm], inc=True)
                pa_free[bank] = q_ev

            for qt in range(ntpg):
                i = tg * ntpg + qt
                nblk = i // 4 + 1
                exp_evs = []
                for j in range(nblk):
                    diag = (j == nblk - 1)
                    mmS = None
                    for dc in range(c.NC):
                        w = [q_ev, kv_ev]
                        if dc == 0:
                            w.append(pa_free[j])
                        mmS = em.pe(lambda e, dc=dc, j=j, qt=qt: e.matmul(
                            b.pA[j][:, 0:T],
                            b.qg[:, dc * T + qt * 128: dc * T + qt * 128 + 128],
                            b.kS[:, dc * c.SEQ + j * T: dc * c.SEQ + (j + 1) * T],
                            start=(dc == 0), stop=(dc == c.NC - 1)),
                            waits=w, inc=(dc == c.NC - 1))
                    if diag:
                        mmS = em.v(lambda e, j=j, i=i: e.tensor_tensor(
                            b.pA[j][:, 0:T], b.pA[j][:, 0:T],
                            b.cmask[:, (i % 4) * 512:(i % 4) * 512 + 512], ALU.add),
                            waits=[mmS, mask_ev], inc=True)
                    em.v(lambda e, j=j: e.reduce_max(
                        b.nm1[:, j:j + 1], b.pA[j][:, 0:T], axis=AX.X, negate=True),
                        waits=[mmS], inc=True)
                cm = em.v(lambda e: e.tensor_copy(b.nm1[:, 4:5], b.nm1[:, 0:1]), inc=True)
                for j in range(1, nblk):
                    cm = em.v(lambda e, j=j: e.tensor_tensor(
                        b.nm1[:, 4:5], b.nm1[:, 4:5], b.nm1[:, j:j + 1], ALU.min),
                        inc=True)
                for j in range(nblk):
                    ex = em.sc(lambda e, j=j: e.activation(
                        b.probs1[:, j * T:(j + 1) * T], b.pA[j][:, 0:T], AF.Exp,
                        bias=b.nm1[:, 4:5], scale=1.0,
                        accum_out=b.nm1[:, 5 + j:6 + j]),
                        waits=[cm, probs_free], inc=True)
                    pa_free[j] = ex
                    exp_evs.append(ex)
                sm = em.v(lambda e: e.tensor_copy(b.nm1[:, 9:10], b.nm1[:, 5:6]),
                          waits=[exp_evs[0]], inc=True)
                for j in range(1, nblk):
                    sm = em.v(lambda e, j=j: e.tensor_tensor(
                        b.nm1[:, 9:10], b.nm1[:, 9:10], b.nm1[:, 5 + j:6 + j], ALU.add),
                        waits=[exp_evs[j]], inc=True)
                rc = em.v(lambda e: e.reciprocal(b.nm1[:, 10:11], b.nm1[:, 9:10]),
                          waits=[sm], inc=True)
                pt_ev = None
                for j in range(nblk):
                    pn = em.v(lambda e, j=j: e.tensor_scalar(
                        b.probs1[:, j * T:(j + 1) * T], b.probs1[:, j * T:(j + 1) * T],
                        b.nm1[:, 10:11], None, ALU.mult),
                        waits=[rc, exp_evs[j]], inc=True)
                    for sj in range(4):
                        slot = sj % 2
                        te = em.pe(lambda e, j=j, sj=sj, slot=slot: e.transpose(
                            b.pT[slot][:, 0:128],
                            b.probs1[:, j * T + sj * 128: j * T + sj * 128 + 128],
                            b.ident[:]),
                            waits=[pn, pT_free[slot] or em.ev("ssc"), probsT_free[0]], inc=True)
                        pT_free[slot] = em.sc(lambda e, j=j, sj=sj, slot=slot: e.copy(
                            b.probsT1[:, (j * 4 + sj) * 128:(j * 4 + sj) * 128 + 128],
                            b.pT[slot][:, 0:128]),
                            waits=[te], inc=True)
                pt_ev = pT_free[(nblk * 4 - 1) % 2]
                ctx_mm = None
                for ch in range(c.NC):
                    for j in range(nblk):
                        for sj in range(4):
                            w = [pt_ev, kv_ev]
                            if ch == 0 and j == 0 and sj == 0:
                                w.append(pCtx_free)
                            ctx_mm = em.pe(lambda e, ch=ch, sj=sj, j=j, nblk=nblk: e.matmul(
                                b.pCtx[:, ch * 128:(ch + 1) * 128],
                                b.vS[:, (j * 4 + sj) * c.D + ch * 128: (j * 4 + sj) * c.D + (ch + 1) * 128],
                                b.probsT1[:, (j * 4 + sj) * 128:(j * 4 + sj) * 128 + 128],
                                start=(j == 0 and sj == 0), stop=(j == nblk - 1 and sj == 3)),
                                waits=w, inc=(ch == c.NC - 1 and j == nblk - 1 and sj == 3))
                probsT_free = [ctx_mm, ctx_mm]
                probs_free = pt_ev
                acc_ev = None
                for ch in range(c.NC):
                    acc_ev = em.v(lambda e, ch=ch, qt=qt, i=i, base=base: e.tensor_tensor(
                        b.acc[:, ch * T + qt * 128: ch * T + qt * 128 + 128],
                        b.pCtx[:, ch * 128:(ch + 1) * 128],
                        b.x[:, ch * c.NT + base + i * 128: ch * c.NT + base + i * 128 + 128],
                        ALU.add), waits=[ctx_mm], inc=True)
                pCtx_free = acc_ev
            ln_ev = emit_ln_fm(em, b, cfg, ti_glob, acc_ev, prev_ln_ev,
                               extra_waits=[pa_free[2], pa_free[3]])
            pa_free[2] = ln_ev
            pa_free[3] = ln_ev
            prev_ln_ev = ln_ev
            ln_evs.append(ln_ev)
            qg_free = em.ev("spe")
        kv_free = em.ev("spe")
    return ln_evs




def emit_l1_moe(em, b, cfg, ce, x_tile_evs, x_out):
    c = cfg
    T = c.TT
    F = c.F
    xoflat = x_out.rearrange("s t d -> (s t) d")

    wgl = em.dma_gp("dwg_in", b.wg_sb, b.moe_w[0], waits=[ce])
    wgc = em.v(lambda e: e.tensor_copy(b.scores[0:8, 0:c.D], b.wg_sb),
               waits=[wgl], inc=True)
    wgt_ev = None
    for dc in range(c.NC):
        slot = dc % 2
        te = em.pe(lambda e, dc=dc, slot=slot: e.transpose(
            b.pT[slot][:, 0:8], b.scores[0:8, dc * 128:(dc + 1) * 128],
            b.ident[0:8, 0:8]),
            waits=[wgc, ce, em.ev("ssc"), b.pT_free[slot]], inc=True)
        wgt_ev = em.sc(lambda e, dc=dc, slot=slot: e.copy(
            b.wgT[:, dc * 8:(dc + 1) * 8], b.pT[slot][0:128, 0:8]),
            waits=[te], inc=True)
        b.pT_free[slot] = wgt_ev

    pa_free = [None, None, None, None]
    sc_ev = None
    for ti in range(c.NTILE):
        bank = ti % 2
        mm = None
        for dc in range(c.NC):
            w = [x_tile_evs[ti], wgt_ev]
            if dc == 0:
                w.append(pa_free[bank])
            mm = em.pe(lambda e, ti=ti, dc=dc, bank=bank: e.matmul(
                b.pA[bank][0:8, 0:T],
                b.wgT[:, dc * 8:(dc + 1) * 8],
                b.x[:, dc * c.NT + ti * T: dc * c.NT + (ti + 1) * T],
                start=(dc == 0), stop=(dc == c.NC - 1)),
                waits=w, inc=(dc == c.NC - 1))
        sc_ev = em.sc(lambda e, ti=ti, bank=bank: e.copy(
            b.scores[0:8, ti * T:(ti + 1) * T], b.pA[bank][0:8, 0:T]),
            waits=[mm], inc=True)
        pa_free[bank] = sc_ev

    g_ev = None
    for t in range(c.NT // 128):
        slot = t % 2
        te = em.pe(lambda e, t=t, slot=slot: e.transpose(
            b.pT[slot][:, 0:8], b.scores[0:8, t * 128:(t + 1) * 128],
            b.ident[0:8, 0:8]), waits=[sc_ev, b.pT_free[slot]], inc=True)
        ev = em.sc(lambda e, slot=slot: e.copy(
            b.sctm[:], b.pT[slot][0:128, 0:8]), waits=[te, g_ev], inc=True)
        b.pT_free[slot] = ev
        em.v(lambda e: e.max(out=b.top8[:], in_=b.sctm[:]), waits=[ev])
        em.v(lambda e: e.memset(b.rep[:], NEG))
        em.v(lambda e: e.tensor_copy(b.rep[:, 0:1], b.top8[:, 0:1]))
        em.v(lambda e: e.match_replace(b.ms2[:], b.rep[:], b.sctm[:], -2e30))
        em.v(lambda e: e.tensor_tensor(b.mk1[:], b.sctm[:], b.ms2[:], ALU.not_equal))
        em.v(lambda e: e.tensor_copy(b.rep[:, 0:1], b.top8[:, 1:2]))
        em.v(lambda e: e.match_replace(b.ms3[:], b.rep[:], b.ms2[:], -2e30))
        em.v(lambda e: e.tensor_tensor(b.mk2[:], b.ms2[:], b.ms3[:], ALU.not_equal))
        dv = em.v(lambda e: e.tensor_sub(b.mst[:, 0:1], b.top8[:, 0:1], b.top8[:, 1:2]))
        p1 = em.sc(lambda e: e.activation(b.mst[:, 1:2], b.mst[:, 0:1], AF.Sigmoid,
                                          bias=0.0, scale=1.0), waits=[dv], inc=True)
        p2 = em.sc(lambda e: e.activation(b.mst[:, 2:3], b.mst[:, 0:1], AF.Sigmoid,
                                          bias=0.0, scale=-1.0), inc=True)
        em.v(lambda e: e.tensor_scalar(b.gtm[:], b.mk1[:], b.mst[:, 1:2], None, ALU.mult),
             waits=[p1, p2])
        em.v(lambda e: e.scalar_tensor_tensor(b.gtm[:], b.mk2[:], b.mst[:, 2:3],
                                              b.gtm[:], ALU.mult, ALU.add))
        gc = em.v(lambda e: e.tensor_copy(b.gbf[:], b.gtm[:]))
        tb = em.pe(lambda e, t=t, slot=slot: e.transpose(
            b.pT[slot][0:8, 128:256], b.gbf[:], b.ident[:]),
            waits=[gc, b.pT_free[slot]], inc=True)
        g_ev = em.sc(lambda e, t=t, slot=slot: e.copy(
            b.g_fm[0:8, t * 128:(t + 1) * 128], b.pT[slot][0:8, 128:256]),
            waits=[tb], inc=True)
        b.pT_free[slot] = g_ev
    g_dma = em.dma_gp("dgout", b.g_dram, b.g_fm[0:8, :], waits=[g_ev])

    sel_ev = None
    for e_ in range(c.E):
        sel_ev = em.gp(lambda e, e_=e_: e.affine_select(
            out=b.sel[0:8, e_ * 128:(e_ + 1) * 128], in_=b.ones_bf[0:8, 0:128],
            pattern=[[0, 128]], compare_op=ALU.is_equal, fill=0.0,
            base=-e_, channel_multiplier=1), waits=[ce], inc=True)

    queues = []
    for e in range(c.E):
        queues.append(PrepQueue([
            PrepJob(em, b, cfg, b.moe_w[1][e], b.moe_w1T[e], ce,
                    after=[g_dma] if e == 0 else ()),
            PrepJob(em, b, cfg, b.moe_w[2][e], b.moe_w2T[e], ce),
            PrepJob(em, b, cfg, b.moe_w[3][e], b.moe_w3T[e], ce),
        ]))
    prep_done = {}

    def emit_prep(e, n_chunks):
        queues[e].run(n_chunks)
        if queues[e].done() and e not in prep_done:
            prep_done[e] = [em.ev("dstg0"), em.ev("dstg1")]

    emit_prep(0, -1)

    w_rd = [None, None]
    w3_free = [None, None]
    pctx_free = None
    hid_rd = None
    stage_dma = None
    msil_free = None
    mt1_free = None
    gB_free = None
    gld_free = None
    acc_dmas = [None] * c.NTILE
    for e in range(c.E):
        pe_evs = prep_done[e]
        NSL = 4
        slw = F // NSL
        l1s, l2s = [], []
        for sl_ in range(NSL):
            l1s.append(em.dma_gp(f"dw1e{sl_}",
                b.w1e[:].rearrange("p (dc f) -> p dc f", f=F)[:, :, sl_ * slw:(sl_ + 1) * slw],
                b.moe_w1T[e][:, :, sl_ * slw:(sl_ + 1) * slw],
                waits=[w_rd[0], x_tile_evs[-1]] + pe_evs))
            l2s.append(em.dma_gp(f"dw2e{sl_}",
                b.w2e[:].rearrange("p (dc f) -> p dc f", f=F)[:, :, sl_ * slw:(sl_ + 1) * slw],
                b.moe_w2T[e][:, :, sl_ * slw:(sl_ + 1) * slw],
                waits=[w_rd[1]] + pe_evs))
        for ti in range(c.NTILE):
            gl = em.dma_gp("dgld", b.gld, b.g_dram[0:8, ti * T:(ti + 1) * T],
                           waits=[gld_free, g_dma])
            gmm = em.pe(lambda e_, e=e, ti=ti: e_.matmul(
                b.pCtx[:, 0:T], b.sel[0:8, e * 128:(e + 1) * 128],
                b.gld, start=True, stop=True),
                waits=[gl, sel_ev, pctx_free], inc=True)
            gld_free = gmm
            gbe = em.v(lambda e_: e_.tensor_copy(b.gB[:], b.pCtx[:, 0:T]),
                       waits=[gmm, gB_free], inc=True)
            pctx_free = gbe
            h_ev = None
            h_evs = [None] * c.NF
            mmh1_last = None
            for fc in range(c.NF):
                bs = fc % 2
                bh1, bh2 = (0, 1) if bs == 0 else (2, 3)
                mmh1 = None
                mmh2 = None
                for dc in range(c.NC):
                    w = [l1s[fc * 128 // slw], x_tile_evs[ti]]
                    if dc == 0:
                        w.append(pa_free[bh1])
                    mmh1 = em.pe(lambda e_, fc=fc, dc=dc, ti=ti, bh1=bh1: e_.matmul(
                        b.pA[bh1][:, 0:T],
                        b.w1e[:, dc * F + fc * 128: dc * F + fc * 128 + 128],
                        b.x[:, dc * c.NT + ti * T: dc * c.NT + (ti + 1) * T],
                        start=(dc == 0), stop=(dc == c.NC - 1)),
                        waits=w, inc=(dc == c.NC - 1))
                for dc in range(c.NC):
                    w = [l2s[fc * 128 // slw]]
                    if dc == 0:
                        w.append(pa_free[bh2])
                    mmh2 = em.pe(lambda e_, fc=fc, dc=dc, ti=ti, bh2=bh2: e_.matmul(
                        b.pA[bh2][:, 0:T],
                        b.w2e[:, dc * F + fc * 128: dc * F + fc * 128 + 128],
                        b.x[:, dc * c.NT + ti * T: dc * c.NT + (ti + 1) * T],
                        start=(dc == 0), stop=(dc == c.NC - 1)),
                        waits=w, inc=(dc == c.NC - 1))
                sev = em.sc(lambda e_, bh1=bh1: e_.activation(
                    b.msil[:], b.pA[bh1][:, 0:T], AF.Silu, bias=0.0, scale=1.0),
                    waits=[mmh1, msil_free], inc=True)
                pa_free[bh1] = sev
                gpe = em.gp(lambda e_: e_.tensor_tensor(
                    b.mt1[:], b.msil[:], b.gB[:], ALU.mult),
                    waits=[sev, gbe, mt1_free], inc=True)
                msil_free = gpe
                h_ev = em.v(lambda e_, fc=fc, bh2=bh2: e_.tensor_tensor(
                    b.hid[:, fc * T:(fc + 1) * T], b.mt1[:], b.pA[bh2][:, 0:T],
                    ALU.mult), waits=[gpe, mmh2, hid_rd], inc=True)
                h_evs[fc] = h_ev
                pa_free[bh2] = h_ev
                mt1_free = h_ev
                mmh1_last = mmh1
            gB_free = gpe
            down_last = None
            stg_ev = None
            for oc in range(c.NC):
                bank = oc % 4
                w3sl = oc % 2
                l3 = em.dma_gp(f"dw3s{w3sl}",
                               b.w3s[w3sl].rearrange("p (fc m) -> p fc m", m=128),
                               b.moe_w3T[e][:, :, oc * 128:(oc + 1) * 128],
                               waits=[w3_free[w3sl]] + pe_evs)
                mm = None
                for fc in range(c.NF):
                    w = [h_evs[fc], l3]
                    if fc == 0:
                        w.append(pa_free[bank])
                    mm = em.pe(lambda e_, oc=oc, fc=fc, bank=bank, w3sl=w3sl: e_.matmul(
                        b.pA[bank][:, 0:T],
                        b.w3s[w3sl][:, fc * 128:(fc + 1) * 128],
                        b.hid[:, fc * T:(fc + 1) * T],
                        start=(fc == 0), stop=(fc == c.NF - 1)),
                        waits=w, inc=(fc == c.NF - 1))
                w3_free[w3sl] = mm
                down_last = mm
                if e == 0:
                    stg_ev = em.v(lambda e_, oc=oc, ti=ti, bank=bank: e_.tensor_tensor(
                        b.acc[:, oc * T:(oc + 1) * T], b.pA[bank][:, 0:T],
                        b.x[:, oc * c.NT + ti * T: oc * c.NT + (ti + 1) * T], ALU.add),
                        waits=[mm, stage_dma], inc=True)
                else:
                    stg_ev = em.v(lambda e_, oc=oc, bank=bank: e_.tensor_copy(
                        b.acc[:, oc * T:(oc + 1) * T], b.pA[bank][:, 0:T]),
                        waits=[mm, stage_dma], inc=True)
                pa_free[bank] = stg_ev
            hid_rd = down_last
            dma = em.dma_gp(f"dacc{ti}",
                            b.accm[:, :, ti * T:(ti + 1) * T],
                            b.acc[:, 0:c.NC * T].rearrange("p (ch t) -> p ch t", t=T),
                            waits=[stg_ev, acc_dmas[ti]],
                            accum_op=ALU.add if e > 0 else ALU.bypass)
            acc_dmas[ti] = dma
            stage_dma = dma
            if e + 1 < c.E:
                emit_prep(e + 1, 3)
        if e + 1 < c.E:
            emit_prep(e + 1, -1)
        w_rd = [mmh1_last, h_ev]

    out_dmas = []
    rb_free = None
    otm_dma = None
    for ti in range(c.NTILE):
        rb = em.dma_gp("drb", b.acc[:, 0:c.NC * T].rearrange("p (ch t) -> p ch t", t=T),
                       b.accm[:, :, ti * T:(ti + 1) * T],
                       waits=list(acc_dmas) + [rb_free])
        for sub in range(4):
            ev = None
            for ch in range(c.NC):
                slot = ch % 2
                ptv = b.pT[slot][:].bitcast(F32)
                te = em.pe(lambda e_, ch=ch, sub=sub, slot=slot, ptv=ptv: e_.transpose(
                    ptv[:, 0:128],
                    b.acc[:, ch * T + sub * 128: ch * T + sub * 128 + 128],
                    b.ident32[:]),
                    waits=[rb, em.ev("ssc"), ce, b.pT_free[slot]], inc=True)
                ev = em.sc(lambda e_, ch=ch, slot=slot, ptv=ptv: e_.copy(
                    b.otm[:, ch * 128:(ch + 1) * 128], ptv[:, 0:128]),
                    waits=[te, otm_dma], inc=True)
                b.pT_free[slot] = ev
            rb_free = ev
            s1 = em.v(lambda e_: e_.reduce_sum(b.mst[:, 3:4], b.otm, axis=AX.X),
                      waits=[ev], inc=True)
            s2 = em.sc(lambda e_: e_.activation(b.sqf, b.otm, AF.Square,
                                                bias=0.0, scale=1.0,
                                                accum_out=b.mst[:, 4:5]),
                       waits=[ev], inc=True)
            em.v(lambda e_: e_.tensor_scalar(b.mst[:, 5:6], b.mst[:, 3:4],
                                             1.0 / c.D, None, ALU.mult), waits=[s1])
            em.v(lambda e_: e_.tensor_scalar(b.mst[:, 6:7], b.mst[:, 4:5],
                                             1.0 / c.D, None, ALU.mult), waits=[s2])
            em.v(lambda e_: e_.tensor_tensor(b.mst[:, 7:8], b.mst[:, 5:6],
                                             b.mst[:, 5:6], ALU.mult))
            em.v(lambda e_: e_.tensor_sub(b.mst[:, 6:7], b.mst[:, 6:7], b.mst[:, 7:8]))
            vv = em.v(lambda e_: e_.tensor_scalar(b.mst[:, 6:7], b.mst[:, 6:7],
                                                  1e-5, None, ALU.add))
            sq = em.sc(lambda e_: e_.activation(b.mst[:, 7:8], b.mst[:, 6:7], AF.Sqrt,
                                                bias=0.0, scale=1.0), waits=[vv], inc=True)
            rr = em.v(lambda e_: e_.reciprocal(b.mst[:, 8:9], b.mst[:, 7:8]),
                      waits=[sq])
            ap = em.v(lambda e_: e_.tensor_scalar(
                b.otm, b.otm, b.mst[:, 5:6], b.mst[:, 8:9],
                ALU.subtract, ALU.mult), waits=[rr])
            r0 = ti * T + sub * 128
            otm_dma = em.dma_sy("dout0", xoflat[r0:r0 + 128, :], b.otm, waits=[ap])
            out_dmas.append(otm_dma)
    return out_dmas



from concourse.bass_utils import run_bass_kernel_spmd

_PROG_CACHE = {}


def _get_prog():
    if "fused" not in _PROG_CACHE:
        cfg = Cfg(SEQ=2048, NSEQ=2)
        _PROG_CACHE["fused"] = build_program(cfg, parts=["l0a", "l0f", "l1a", "moe"])
    return _PROG_CACHE["fused"]


def kernel(L0_attn_x, L0_attn_Wq, L0_attn_Wk, L0_attn_Wv, L0_attn_Wo,
           L0_ffn_w_gate, L0_ffn_w_up, L0_ffn_w_down,
           L1_attn_Wq, L1_attn_Wk, L1_attn_Wv,
           L1_ffn_Wg, L1_ffn_W1, L1_ffn_W2, L1_ffn_W3):
    x = np.ascontiguousarray(np.asarray(L0_attn_x, dtype=np.float32))
    f32 = lambda a: np.ascontiguousarray(np.asarray(a, dtype=np.float32))
    wmap = dict(
        l0_wq=f32(L0_attn_Wq), l0_wk=f32(L0_attn_Wk),
        l0_wv=f32(L0_attn_Wv), l0_wo=f32(L0_attn_Wo),
        l0_wgate=f32(L0_ffn_w_gate), l0_wup=f32(L0_ffn_w_up),
        l0_wdown=f32(L0_ffn_w_down),
        l1_wq=f32(L1_attn_Wq), l1_wk=f32(L1_attn_Wk), l1_wv=f32(L1_attn_Wv),
        l1_wg=f32(L1_ffn_Wg), l1_w1=f32(L1_ffn_W1),
        l1_w2=f32(L1_ffn_W2), l1_w3=f32(L1_ffn_W3))
    nc = _get_prog()
    in_maps = []
    for c in range(8):
        m = dict(x_in=x[2 * c:2 * c + 2])
        m.update(wmap)
        in_maps.append(m)
    res = run_bass_kernel_spmd(nc, in_maps, core_ids=list(range(8)))
    return np.concatenate([np.asarray(r["x_out"]) for r in res.results], axis=0)


# version 05
# speedup vs baseline: 1.3434x; 1.3106x over previous
import numpy as np
from contextlib import ExitStack

import concourse.bass as bass
import concourse.mybir as mybir

F32 = mybir.dt.float32
BF16 = mybir.dt.bfloat16
AF = mybir.ActivationFunctionType
ALU = mybir.AluOpType
AX = mybir.AxisListType

NEG = -1e30


def _ap(x):
    try:
        return x[:]
    except Exception:
        return x



ENGINES = ("pe", "v", "sc", "gp", "sy")


class Em:

    def __init__(self, nc):
        self.nc = nc
        self.streams = {e: [] for e in ENGINES}
        self.cnt = {}
        self.waited = {e: {} for e in ENGINES}
        self.sems = {}

    def ev(self, sem):
        return (sem, self.cnt.get(sem, 0))

    AUTOCHAIN = {"v": "sv", "sc": "ssc", "gp": "sgp"}

    def emit(self, eng, fn, waits=(), inc=None):
        waits = [w for w in waits if w is not None]
        ac = self.AUTOCHAIN.get(eng)
        if ac is not None:
            waits = list(waits) + [(ac, self.cnt.get(ac, 0))]
            if inc is None:
                inc = (ac, 1)
        real = []
        for (s, v) in waits:
            if v > self.waited[eng].get(s, 0):
                self.waited[eng][s] = v
                real.append((s, v))
        evt = None
        if inc is not None:
            s, a = inc
            self.cnt[s] = self.cnt.get(s, 0) + a
            evt = (s, self.cnt[s])
        self.streams[eng].append((real, fn, inc))
        return evt

    def pe(self, fn, waits=(), inc=False):
        return self.emit("pe", fn, waits, ("spe", 1) if inc else None)

    def v(self, fn, waits=(), inc=False):
        return self.emit("v", fn, waits, ("sv", 1) if inc else None)

    def sc(self, fn, waits=(), inc=False):
        return self.emit("sc", fn, waits, ("ssc", 1) if inc else None)

    def gp(self, fn, waits=(), inc=False):
        return self.emit("gp", fn, waits, ("sgp", 1) if inc else None)

    def phase_barrier(self):
        waits = [(s, v) for s, v in self.cnt.items() if v > 0]
        for eng in ENGINES:
            fn = (lambda e: e.nop())
            self.emit(eng, fn, waits=list(waits))

    def dma_sy(self, sem, out, in_, waits=()):
        return self.emit("sy", lambda e: e.dma_start(out=out, in_=in_), waits, (sem, 16))

    def dma_gp(self, sem, out, in_, waits=(), **kw):
        return self.emit("gp", lambda e: e.dma_start(out=out, in_=in_, **kw), waits, (sem, 16))

    def play(self, block):
        eng_attr = {"pe": "tensor", "v": "vector", "sc": "scalar", "gp": "gpsimd", "sy": "sync"}
        for name in ENGINES:
            stream = self.streams[name]

            def body(eng, stream=stream):
                for waits, fn, inc in stream:
                    for s, v in waits:
                        eng.wait_ge(self.sems[s], v)
                    ins = fn(eng)
                    if inc is not None:
                        ins.then_inc(self.sems[inc[0]], inc[1])

            getattr(block, eng_attr[name])(body)


class Cfg:
    def __init__(self, SEQ=2048, NSEQ=2):
        self.SEQ = SEQ
        self.NSEQ = NSEQ
        self.NT = SEQ * NSEQ
        self.D = 1024
        self.NC = 8
        self.HEADS = 16
        self.HD = 64
        self.WIN = 128
        self.G = 64
        self.I = 2816
        self.NI = 22
        self.E = 8
        self.F = 2048
        self.NF = 16
        self.TT = 512
        self.NTILE = self.NT // self.TT
        self.WPT = self.TT // self.WIN
        self.NWIN = SEQ // self.WIN
        self.LW = 512




class Bufs:
    pass


def alloc_bufs(nc, ctx, cfg, parts):
    b = Bufs()
    sb = lambda name, shape, dt: ctx.enter_context(nc.sbuf_tensor(name, shape, dt))
    ps = lambda name, shape, dt: ctx.enter_context(nc.psum_tensor(name, shape, dt))
    c = cfg

    b.x = sb("x_sb", [128, c.NC * c.NT], BF16)

    b.ident = sb("ident_bf", [128, 128], BF16)
    b.ones_bf = sb("ones_bf", [128, 128], BF16)
    b.ident32 = sb("ident_f32", [128, 128], F32) if "moe" in parts else None
    b.ones32 = None
    if any(p in parts for p in ("l0a", "l0f", "l1a")):
        b.onesc32 = sb("onesc32", [128, 1], F32)
        b.onescbf = sb("onescbf", [128, 1], BF16)
        b.onesr32 = sb("onesr32", [1, 128], F32)
    else:
        b.onesc32 = None

    b.pA = [ps(f"pA{i}", [128, 512], F32) for i in range(4)]
    b.pCtx = ps("pCtx", [128, 1024], F32)
    b.pT = [ps("pTa", [128, 256], BF16), ps("pTb", [128, 256], BF16)]

    b.xin = None

    b.acc = sb("acc_fm", [128, c.NC * c.TT], F32)
    if any(p in parts for p in ("l0a", "l0f", "l1a")):
        b.sq = sb("sq_bf", [128, 2 * c.TT], BF16)
        b.strow = sb("strow", [1, 3 * c.TT], F32)

    if "moe" in parts:
        b.w1e = sb("w1e", [128, c.NC * c.F], BF16)
        b.w2e = sb("w2e", [128, c.NC * c.F], BF16)
        b.w3s = [sb(f"w3s{i}", [128, c.NF * 128], BF16)[:] for i in range(2)]
        b.scores = sb("scores_m", [8, c.NT], BF16)
        b.g_fm = b.scores
        b.sel = sb("sel_m", [8, 8 * 128], BF16)
        b.hid = sb("hid_m", [128, c.NF * 512], BF16)
        b.gB = sb("gB_m", [128, 512], F32)
        b.msil = sb("msil", [128, 512], BF16)
        b.mt1 = sb("mt1", [128, 512], BF16)

        b.mst = sb("mst", [128, 12], F32)
        b.sctm = sb("sctm", [128, 8], BF16)
        b.top8 = sb("top8", [128, 8], BF16)
        b.rep = sb("rep", [128, 8], BF16)
        b.ms2 = sb("ms2", [128, 8], BF16)
        b.ms3 = sb("ms3", [128, 8], BF16)
        b.mk1 = sb("mk1", [128, 8], F32)
        b.mk2 = sb("mk2", [128, 8], F32)
        b.gtm = sb("gtm", [128, 8], F32)
        b.gbf = sb("gbf", [128, 8], BF16)
        b.wgT = sb("wgT", [128, c.NC * 8], BF16)
        hv = b.hid[:].bitcast(F32)
        b.sqf = hv[:, 0:c.D]
        b.wg_sb = hv[0:8, c.D:2 * c.D]
        b.otm = hv[:, 2 * c.D:3 * c.D]
        b.wload = b.hid[:].bitcast(BF16)[:, 4 * c.D:6 * c.D]
        gv = b.gB[:].bitcast(BF16)
        b.wtb = [gv[:, 0:128], gv[:, 128:256]]


    if "l0a" in parts or "l1a" in parts or "moe" in parts:
        nw = 4 if "l0a" in parts else 0
        b.wT = [sb(f"wT{i}", [128, c.NC * c.D], BF16) for i in range(nw)]
        if "moe" not in parts:
            b.wload = sb("wload", [128, c.NC // 4 * c.D], BF16)
    if "l0a" in parts:
        b.qb = sb("qb", [128, c.NC * c.TT], BF16)
        b.kb = sb("kb", [128, c.NC * c.TT], BF16)
        b.vb = sb("vb", [128, c.NC * c.TT], BF16)

    if "l1a" in parts:
        b.kS = sb("kS", [128, c.NC * c.SEQ], BF16)
        b.vS = sb("vS", [128, (c.SEQ // 128) * c.D], BF16)
        b.qg = sb("qg", [128, c.NC * 512], BF16)
        b.vtmp = sb("vtmp", [128, 512], BF16)
        b.cmask = sb("cmask", [128, 4 * 512], BF16)
        b.probs1 = sb("probs1", [128, 4 * 512], BF16)
        b.probsT1 = sb("probsT1", [128, 16 * 128], BF16)
        b.nm1 = sb("nm1", [128, 16], F32)
        b.wstg = [sb(f"wstg{i}", [128, c.NC * 128], BF16)[:] for i in range(2)]
        b.wstg_free = [None, None]
        b.wload2 = [sb(f"wload2_{i}", [128, c.NC // 4 * c.D], BF16)[:] for i in range(2)]
        b.wload_free = [None, None]
        b.wload_par = 0
        b.pT_free = [None, None]
        b.wstr = [sb(f"wstr{i}", [128, c.NC * 128], BF16) for i in range(2)]

    if "l0f" in parts:
        b.wq_in = sb("wq_in", [128, c.I], F32)
        b.wq_in_b = sb("wq_in_b", [128, c.I], F32)
        b.wtb = [sb(f"wtb{i}", [128, 128], BF16) for i in range(2)]
        b.qsets = []
        for i in range(2):
            b.qsets.append((sb(f"qs_xn{i}", [128, c.LW], F32)[:],
                            sb(f"qs_g{i}", [128, c.LW], F32)[:],
                            sb(f"qs_a{i}", [128, c.LW], F32)[:],
                            sb(f"qs_c{i}", [128, c.LW], F32)[:],
                            sb(f"qs_sc{i}", [128, 6 * (c.LW // 16)], F32)[:]))
        b.qs_h2 = [sb(f"qs_h{i}", [128, c.LW], F32)[:] for i in range(2)]
        b.xtm32 = None
        b.xqtm = None
        b.xq_fm = sb("xq_fm", [128, c.NC * c.TT], BF16)
        b.hb = sb("hb", [128, 4 * c.I], BF16)
        b.hq_tm = sb("hq_tm", [128, c.I], BF16)
        b.hq_fm = sb("hq_fm", [128, c.NI * c.TT], BF16)
        b.wg_s = [sb("wg_s0", [128, c.NC * 512], BF16)] * 2
        b.wup_s = [sb("wup_s0", [128, c.NC * 512], BF16)] * 2
        b.wd_s = [sb("wd_s0", [128, c.NI * 128], BF16)] * 2
        b.silu = sb("silu_b", [128, 512], BF16)
        b.wq_in2 = [b.wq_in[:], b.wq_in_b[:]]
        wiv = b.wq_in[:].bitcast(BF16)
        wiv2 = b.wq_in_b[:].bitcast(BF16)
        b.xin = [wiv[:, 0:c.D], wiv[:, c.D:2 * c.D]]
        b.xtm32_2 = [b.wq_in[:][:, 0:c.D], b.wq_in_b[:][:, 0:c.D]]
        b.xqtm2 = [wiv[:, 2 * c.D:3 * c.D], wiv2[:, 2 * c.D:3 * c.D]]
        hqv = b.hq_fm[:]
        b.wstg = [hqv[:, 0:c.NI * 128], hqv[:, c.NI * 128:2 * c.NI * 128]]
        b.wstg_free = [None, None]
        b.wq_out2 = [b.hq_tm[:], hqv[:, 2 * c.NI * 128:2 * c.NI * 128 + c.I]]

    if "l0a" in parts:
        b.kg = [sb(f"kg{s}", [128, c.NC * c.G], BF16) for s in range(c.NSEQ)]
        b.vgT = [sb(f"vgT{s}", [c.G, c.D], BF16) for s in range(c.NSEQ)]
        b.vwT = sb("vwT", [128, c.D], BF16)
        b.expb = [sb(f"expb{i}", [128, c.G + c.WIN], F32) for i in range(2)]
        b.nm = sb("nm_sums", [128, 8], F32)
        b.probs = [sb(f"probs{i}", [128, c.G + c.WIN], BF16) for i in range(2)]
        b.probsT = [sb(f"probsT{i}", [128, 256], BF16) for i in range(2)]
        b.ctxb = sb("ctxb", [128, c.NC * c.TT], BF16)

    if b.xin is None and "moe" in parts:
        w1v = b.w1e[:]
        b.xin = [w1v[:, 0:c.D], w1v[:, c.D:2 * c.D]]
    if b.xin is None:
        b.xin = [sb(f"xin{i}", [128, c.D], BF16)[:] for i in range(2)]
    return b




class Carver:
    def __init__(self, base_ap, limit_bytes):
        self.base = base_ap
        self.off = 0
        self.limit = limit_bytes

    def take(self, dt, cols, parts=128):
        sz = 4 if dt == F32 else 2
        off = (self.off + 31) // 32 * 32
        self.off = off + cols * sz
        assert self.off <= self.limit, f"arena overflow: {self.off} > {self.limit}"
        return self.base.bitcast(dt)[0:parts, off // sz: off // sz + cols]


def alloc_bufs_fused(nc, ctx, cfg):
    b = Bufs()
    c = cfg
    sb = lambda name, shape, dt: ctx.enter_context(nc.sbuf_tensor(name, shape, dt))
    ps = lambda name, shape, dt: ctx.enter_context(nc.psum_tensor(name, shape, dt))

    b.x = sb("x_sb", [128, c.NC * c.NT], BF16)
    b.acc = sb("acc_fm", [128, c.NC * c.TT], F32)
    b.sq = sb("sq_bf", [128, 2 * c.TT], BF16)
    b.strow = sb("strow", [1, 3 * c.TT], F32)
    b.ident = sb("ident_bf", [128, 128], BF16)
    b.ident32 = sb("ident_f32", [128, 128], F32)
    b.ones_bf = sb("ones_bf", [128, 128], BF16)
    b.ones32 = None
    b.onesc32 = sb("onesc32", [128, 1], F32)
    b.onescbf = sb("onescbf", [128, 1], BF16)
    b.onesr32 = sb("onesr32", [1, 128], F32)
    b.xin = [sb(f"xin{i}", [128, c.D], BF16)[:] for i in range(2)]

    b.pA = [ps(f"pA{i}", [128, 512], F32) for i in range(4)]
    b.pCtx = ps("pCtx", [128, 1024], F32)
    b.pT = [ps("pTa", [128, 256], BF16), ps("pTb", [128, 256], BF16)]

    rem = nc.sbuf_bytes_remaining - 64
    AR = sb("arena", [128, rem // 2 // 32 * 32], BF16)[:]
    arena_bytes = AR.shape[1] * 2
    b.ph = {}

    cv = Carver(AR, arena_bytes)
    d = {}
    d["wT"] = [cv.take(BF16, c.NC * c.D) for _ in range(4)]
    d["qb"] = cv.take(BF16, c.NC * c.TT)
    d["kb"] = cv.take(BF16, c.NC * c.TT)
    d["vb"] = cv.take(BF16, c.NC * c.TT)
    d["ctxb"] = cv.take(BF16, c.NC * c.TT)
    d["wload"] = d["ctxb"][:, 0:c.NC // 4 * c.D]
    d["kg"] = [cv.take(BF16, c.NC * c.G) for _ in range(c.NSEQ)]
    d["vgT"] = [cv.take(BF16, c.D, parts=c.G) for _ in range(c.NSEQ)]
    d["vwT"] = cv.take(BF16, c.D)
    d["expb"] = [cv.take(F32, c.G + c.WIN) for _ in range(2)]
    d["probs"] = [cv.take(BF16, c.G + c.WIN) for _ in range(2)]
    d["probsT"] = [cv.take(BF16, 256) for _ in range(2)]
    d["nm"] = cv.take(F32, 8)
    d["x32st"] = cv.take(F32, c.D)
    b.ph["l0a"] = d

    cv = Carver(AR, arena_bytes)
    d = {}
    qsets = []
    for _ in range(2):
        qsets.append((cv.take(F32, c.LW), cv.take(F32, c.LW), cv.take(F32, c.LW),
                      cv.take(F32, c.LW), cv.take(F32, 6 * (c.LW // 16))))
    d["qsets"] = qsets
    d["qs_h2"] = [cv.take(F32, c.LW) for _ in range(2)]
    d["xtm32_2"] = [cv.take(F32, c.D) for _ in range(2)]
    d["xqtm2"] = [cv.take(BF16, c.D) for _ in range(2)]
    d["xq_fm"] = cv.take(BF16, c.NC * c.TT)
    hb = cv.take(BF16, 4 * c.I)
    d["hb"] = hb
    d["wq_in2"] = [hb.bitcast(F32)[:, 0:c.I], hb.bitcast(F32)[:, c.I:2 * c.I]]
    d["hq_tm"] = cv.take(BF16, c.I)
    hq_fm = cv.take(BF16, c.NI * c.TT)
    d["hq_fm"] = hq_fm
    d["wq_out2"] = [d["hq_tm"][:] if hasattr(d["hq_tm"], "ap") else d["hq_tm"],
                    hq_fm[:, 2 * c.NI * 128:2 * c.NI * 128 + c.I]]
    d["wstg"] = [hq_fm[:, 0:c.NI * 128], hq_fm[:, c.NI * 128:2 * c.NI * 128]]
    d["wstg_free"] = [None, None]
    wg0 = cv.take(BF16, c.NC * 512)
    wu0 = cv.take(BF16, c.NC * 512)
    wd0 = cv.take(BF16, c.NI * 128)
    d["wg_s"] = [wg0, wg0]
    d["wup_s"] = [wu0, wu0]
    d["wd_s"] = [wd0, wd0]
    d["silu"] = cv.take(BF16, 512)
    b.ph["l0f"] = d

    cv = Carver(AR, arena_bytes)
    d = {}
    d["kS"] = cv.take(BF16, c.NC * c.SEQ)
    d["vS"] = cv.take(BF16, (c.SEQ // 128) * c.D)
    d["qg"] = cv.take(BF16, c.NC * 512)
    d["vtmp"] = cv.take(BF16, 512)
    d["cmask"] = cv.take(BF16, 4 * 512)
    d["probs1"] = cv.take(BF16, 4 * 512)
    d["probsT1"] = cv.take(BF16, 16 * 128)
    d["nm1"] = cv.take(F32, 16)
    d["wstr"] = [cv.take(BF16, c.NC * 128) for _ in range(2)]
    d["wstg"] = [cv.take(BF16, c.NC * 128) for _ in range(2)]
    d["wstg_free"] = [None, None]
    d["wload2"] = [cv.take(BF16, c.NC // 4 * c.D) for _ in range(2)]
    d["wload_free"] = [None, None]
    d["wload_par"] = 0
    d["pT_free"] = [None, None]
    b.ph["l1a"] = d

    cv = Carver(AR, arena_bytes)
    d = {}
    d["w1e"] = cv.take(BF16, c.NC * c.F)
    d["w2e"] = cv.take(BF16, c.NC * c.F)
    d["w3s"] = [cv.take(BF16, c.NF * 128) for _ in range(2)]
    hid = cv.take(BF16, c.NF * 512)
    d["hid"] = hid
    hv = hid.bitcast(F32)
    d["sqf"] = hv[:, 0:c.D]
    d["wg_sb"] = hv[0:8, c.D:2 * c.D]
    d["otm"] = hv[:, 2 * c.D:3 * c.D]
    R = cv.take(BF16, 3 * c.NC // 4 * c.D)
    d["scores"] = R[0:8, 0:c.NT]
    d["g_fm"] = d["scores"]
    d["wload2"] = [R[:, 0:c.NC // 4 * c.D], cv.take(BF16, c.NC // 4 * c.D)]
    d["wload_free"] = [None, None]
    d["wload_par"] = 0
    d["pT_free"] = [None, None]
    d["wstg"] = [R[:, c.NC // 4 * c.D:2 * (c.NC // 4) * c.D],
                 R[:, 2 * (c.NC // 4) * c.D:3 * (c.NC // 4) * c.D]]
    d["wstg_free"] = [None, None]
    d["gld"] = cv.take(BF16, 512, parts=8)
    d["sel"] = cv.take(BF16, 8 * 128, parts=8)
    gB = cv.take(F32, 512)
    d["gB"] = gB
    d["msil"] = cv.take(BF16, 512)
    d["mt1"] = cv.take(BF16, 512)
    d["mst"] = cv.take(F32, 12)
    d["sctm"] = cv.take(BF16, 8)
    d["top8"] = cv.take(BF16, 8)
    d["rep"] = cv.take(BF16, 8)
    d["ms2"] = cv.take(BF16, 8)
    d["ms3"] = cv.take(BF16, 8)
    d["mk1"] = cv.take(F32, 8)
    d["mk2"] = cv.take(F32, 8)
    d["gtm"] = cv.take(F32, 8)
    d["gbf"] = cv.take(BF16, 8)
    d["wgT"] = cv.take(BF16, c.NC * 8)
    b.ph["moe"] = d
    return b


def phase_bufs(b, name):
    nb = Bufs()
    nb.__dict__.update({k: v for k, v in b.__dict__.items() if k != "ph"})
    nb.__dict__.update(b.ph[name])
    return nb




def emit_consts(em, b, cfg):
    e1 = em.gp(lambda e: e.memset(b.ones_bf[:], 1.0), inc=True)
    em.gp(lambda e: e.affine_select(
        out=b.ident[:], in_=b.ones_bf[:], pattern=[[-1, 128]],
        compare_op=ALU.is_equal, fill=0.0, base=0, channel_multiplier=1), waits=[e1], inc=True)
    if b.ident32 is not None:
        e2 = em.v(lambda e: e.memset(b.acc[:, 0:128], 1.0), inc=True)
        em.gp(lambda e: e.affine_select(
            out=b.ident32[:], in_=b.acc[:, 0:128], pattern=[[-1, 128]],
            compare_op=ALU.is_equal, fill=0.0, base=0, channel_multiplier=1), waits=[e2], inc=True)
    if getattr(b, "onesc32", None) is not None:
        em.v(lambda e: e.memset(b.onesc32[:], 1.0), inc=True)
        em.v(lambda e: e.memset(b.onescbf[:], 1.0), inc=True)
        em.v(lambda e: e.memset(b.onesr32[:], 1.0), inc=True)
    return em.gp(lambda e: e.engine_nop(), waits=[em.ev("sv"), em.ev("sgp")], inc=True)


def emit_ingest(em, b, cfg, x_in):
    c = cfg
    xflat = x_in.rearrange("s t d -> (s t) d")
    tile_ev = []
    prev_use = [None, None]
    for ti in range(c.NT // 128):
        buf = ti % 2
        w = [prev_use[buf]] if prev_use[buf] else []
        ld = em.dma_gp(f"dxi{buf}", b.xin[buf], xflat[ti * 128:(ti + 1) * 128, :], waits=w)
        evs = None
        for ch in range(c.NC):
            slot = ch % 2
            te = em.pe(lambda e, ch=ch, buf=buf, slot=slot: e.transpose(
                b.pT[slot][:, 0:128],
                b.xin[buf][:, ch * 128:(ch + 1) * 128], b.ident[:]),
                waits=[ld, em.ev("ssc")], inc=True)
            evs = em.sc(lambda e, ch=ch, ti=ti, slot=slot: e.copy(
                b.x[:, ch * c.NT + ti * 128: ch * c.NT + (ti + 1) * 128],
                b.pT[slot][:, 0:128]),
                waits=[te], inc=True)
        prev_use[buf] = evs
        if (ti + 1) % (c.TT // 128) == 0:
            tile_ev.append(evs)
    return tile_ev


def _consts_wait(em, ce):
    return [ce]


def emit_weight_load_T(em, b, cfg, w_dram, dst, ce, ready_after=()):
    c = cfg
    O = w_dram.shape[0]
    D = w_dram.shape[1]
    noc = O // 128
    ndc = D // 128
    hoc = max(1, (c.NC // 4 * c.D) // D)
    wre = w_dram.rearrange("(oc p) d -> p oc d", p=128)
    last = None
    prev_reads = list(ready_after)
    for og in range(0, noc, hoc):
        og_n = min(hoc, noc - og)
        ld = em.dma_gp("dwl", b.wload[:, 0:og_n * D].rearrange("p (oc d) -> p oc d", d=D),
                       wre[:, og:og + og_n, :], waits=prev_reads)
        te = None
        for oc in range(og, og + og_n):
            for dc in range(ndc):
                slot = (oc * ndc + dc) % 2
                te = em.pe(lambda e, oc=oc, dc=dc, slot=slot, og=og: e.transpose(
                    b.pT[slot][:, 0:128],
                    b.wload[:, (oc - og) * D + dc * 128: (oc - og) * D + (dc + 1) * 128],
                    b.ident[:]),
                    waits=[ld, ce, em.ev("ssc")], inc=True)
                last = em.sc(lambda e, oc=oc, dc=dc, slot=slot: e.copy(
                    dst[:, dc * O + oc * 128: dc * O + (oc + 1) * 128],
                    b.pT[slot][:, 0:128]),
                    waits=[te], inc=True)
        prev_reads = [te]
    return last




def emit_ln_fm(em, b, cfg, ti, acc_ev, prev_ln_ev=None, extra_waits=(), keep_acc=False):
    c = cfg
    T = c.TT
    s1 = None
    for ch in range(c.NC):
        s1 = em.pe(lambda e, ch=ch: e.matmul(
            b.pA[2][0:1, 0:T], b.onesc32[:], b.acc[:, ch * T:(ch + 1) * T],
            start=(ch == 0), stop=(ch == c.NC - 1)),
            waits=[acc_ev] + ([prev_ln_ev] if prev_ln_ev else [])
                  + (list(extra_waits) if ch == 0 else []),
            inc=(ch == c.NC - 1))
    s2 = None
    sq_evs = [None, None]
    for ch in range(c.NC):
        sl = ch % 2
        sq_evs[sl] = em.sc(lambda e, ch=ch, sl=sl: e.activation(
            b.sq[:, sl * T:(sl + 1) * T], b.acc[:, ch * T:(ch + 1) * T],
            AF.Square, bias=0.0, scale=1.0),
            waits=[acc_ev, s2], inc=True)
        s2 = em.pe(lambda e, ch=ch, sl=sl: e.matmul(
            b.pA[3][0:1, 0:T], b.onescbf[:], b.sq[:, sl * T:(sl + 1) * T],
            start=(ch == 0), stop=(ch == c.NC - 1)),
            waits=[sq_evs[sl]] + ([prev_ln_ev] if prev_ln_ev and ch == 0 else []),
            inc=True)
    st = b.strow
    t1 = st[0:1, 0:T]
    m_ = st[0:1, T:2 * T]
    t2 = st[0:1, 2 * T:3 * T]
    mr = t2
    em.v(lambda e: e.tensor_scalar(m_, b.pA[2][0:1, 0:T], 1.0 / c.D, None, ALU.mult),
         waits=[s1], inc=True)
    em.v(lambda e: e.tensor_scalar(t1, b.pA[3][0:1, 0:T], 1.0 / c.D, None, ALU.mult),
         waits=[s2], inc=True)
    em.v(lambda e: e.tensor_tensor(t2, m_, m_, ALU.mult), inc=True)
    em.v(lambda e: e.tensor_sub(t1, t1, t2), inc=True)
    ve = em.v(lambda e: e.tensor_scalar(t1, t1, 1e-5, None, ALU.add), inc=True)
    se = em.sc(lambda e: e.activation(t2, t1, AF.Sqrt, bias=0.0, scale=1.0),
               waits=[ve], inc=True)
    re_ = em.v(lambda e: e.reciprocal(t1, t2), waits=[se], inc=True)
    mre = em.v(lambda e: e.tensor_tensor(mr, m_, t1, ALU.mult), waits=[re_], inc=True)
    rstd = t1
    bc1 = em.pe(lambda e: e.matmul(b.pA[2][:, 0:T], b.onesr32[:], rstd,
                                   start=True, stop=True), waits=[re_, mre], inc=True)
    bc2 = em.pe(lambda e: e.matmul(b.pA[3][:, 0:T], b.onesr32[:], mr,
                                   start=True, stop=True), waits=[mre], inc=True)
    out_ev = None
    for ch in range(c.NC):
        em.v(lambda e, ch=ch: e.tensor_tensor(
            b.acc[:, ch * T:(ch + 1) * T],
            b.acc[:, ch * T:(ch + 1) * T], b.pA[2][:, 0:T], ALU.mult),
            waits=[bc1], inc=True)
        if keep_acc:
            em.v(lambda e, ch=ch: e.tensor_tensor(
                b.acc[:, ch * T:(ch + 1) * T],
                b.acc[:, ch * T:(ch + 1) * T],
                b.pA[3][:, 0:T], ALU.subtract),
                waits=[bc2], inc=True)
            out_ev = em.v(lambda e, ch=ch, ti=ti: e.tensor_copy(
                b.x[:, ch * c.NT + ti * T: ch * c.NT + (ti + 1) * T],
                b.acc[:, ch * T:(ch + 1) * T]), inc=True)
        else:
            out_ev = em.v(lambda e, ch=ch, ti=ti: e.tensor_tensor(
                b.x[:, ch * c.NT + ti * T: ch * c.NT + (ti + 1) * T],
                b.acc[:, ch * T:(ch + 1) * T],
                b.pA[3][:, 0:T], ALU.subtract),
                waits=[bc2], inc=True)
    return out_ev




def emit_egress(em, b, cfg, x_out, waits):
    c = cfg
    eg = [b.acc[:, 0:c.D], b.acc[:, c.D:2 * c.D]]
    xflat = x_out.rearrange("s t d -> (s t) d")
    dmas = []
    prev_dma = [None, None]
    for ti in range(c.NT // 128):
        sl = ti % 2
        ev = None
        for ch in range(c.NC):
            slot = ch % 2
            te = em.pe(lambda e, ch=ch, ti=ti, slot=slot: e.transpose(
                b.pT[slot][:, 0:128],
                b.x[:, ch * c.NT + ti * 128: ch * c.NT + (ti + 1) * 128],
                b.ident[:]),
                waits=list(waits) + [em.ev("ssc")], inc=True)
            ev = em.sc(lambda e, ch=ch, slot=slot, sl=sl: e.copy(
                eg[sl][:, ch * 128:(ch + 1) * 128],
                b.pT[slot][:, 0:128]),
                waits=[te, prev_dma[sl]], inc=True)
        dma = em.dma_sy(f"dout{sl}", xflat[ti * 128:(ti + 1) * 128, :],
                        eg[sl], waits=[ev])
        prev_dma[sl] = dma
        dmas.append(dma)
    return dmas




def emit_l0_attn(em, b, cfg, ce, x_tile_evs, prev_ln_ev=None, x32_dram=None):
    c = cfg
    T = c.TT
    wq_d, wk_d, wv_d, wo_d = b.l0a_w
    wq_ev = emit_weight_load_T(em, b, cfg, wq_d, b.wT[0][:], ce)
    wk_ev = emit_weight_load_T(em, b, cfg, wk_d, b.wT[1][:], ce, ready_after=[wq_ev])
    wv_ev = emit_weight_load_T(em, b, cfg, wv_d, b.wT[2][:], ce, ready_after=[wk_ev])
    wo_ev = emit_weight_load_T(em, b, cfg, wo_d, b.wT[3][:], ce, ready_after=[wv_ev])

    tiles_per_seq = c.SEQ // T
    ln_evs = []
    kg_ev = [None] * c.NSEQ
    vg_ev = [None] * c.NSEQ
    pSc_free = [None, None]
    probs_free = [None, None]
    expb_free = [None, None]
    probsT_free = [None, None]
    pT_free = [None, None]
    pCtx_free = None
    acc_free = None
    x32d = None
    prev_qkv_evac = None

    for ti in range(c.NTILE):
        s = ti // tiles_per_seq
        xev = x_tile_evs[ti]
        evac_evs = []
        for dst_i, (dst, wti, wev) in enumerate(
                ((b.qb, 0, wq_ev), (b.kb, 1, wk_ev), (b.vb, 2, wv_ev))):
            for oc in range(c.NC):
                bank = (dst_i * c.NC + oc) % 2
                mm = None
                for dc in range(c.NC):
                    w = [xev, wev]
                    if dc == 0:
                        w.append(pSc_free[bank] or em.ev("sv"))
                    mm = em.pe(lambda e, wti=wti, oc=oc, dc=dc, bank=bank, ti=ti: e.matmul(
                        b.pA[bank][:, 0:T],
                        b.wT[wti][:, dc * c.D + oc * 128: dc * c.D + (oc + 1) * 128],
                        b.x[:, dc * c.NT + ti * T: dc * c.NT + (ti + 1) * T],
                        start=(dc == 0), stop=(dc == c.NC - 1)),
                        waits=w, inc=(dc == c.NC - 1))
                if dst_i == 0:
                    ev = em.v(lambda e, oc=oc, bank=bank: e.tensor_scalar(
                        b.qb[:, oc * T:(oc + 1) * T], b.pA[bank][:, 0:T],
                        0.125, None, ALU.mult), waits=[mm], inc=True)
                else:
                    ev = em.sc(lambda e, dst=dst, oc=oc, bank=bank: e.copy(
                        dst[:, oc * T:(oc + 1) * T], b.pA[bank][:, 0:T]),
                        waits=[mm], inc=True)
                pSc_free[bank] = ev
                evac_evs.append(ev)
        qkv_ev = evac_evs[-1]
        qkv_all = [em.ev("sv"), em.ev("ssc")]

        for w4 in range(c.WPT):
            widx = ti * c.WPT + w4
            wi = widx % (c.SEQ // c.WIN)
            t0 = w4 * c.WIN
            vw_evs = []
            for chk in range(c.NC):
                slot = chk % 2
                te = em.pe(lambda e, chk=chk, slot=slot, t0=t0: e.transpose(
                    b.pT[slot][:, 0:128],
                    b.vb[:, chk * T + t0: chk * T + t0 + c.WIN],
                    b.ident[:]),
                    waits=qkv_all + [pT_free[slot] or em.ev("ssc")], inc=True)
                ev = em.sc(lambda e, chk=chk, slot=slot: e.copy(
                    b.vwT[:, chk * 128:(chk + 1) * 128],
                    b.pT[slot][:, 0:128]),
                    waits=[te], inc=True)
                pT_free[slot] = ev
                vw_evs.append(ev)
            vw_ev = vw_evs[-1]

            if wi == 0:
                kg_ev[s] = em.v(lambda e, s=s, t0=t0: e.tensor_copy(
                    b.kg[s][:].rearrange("p (c g) -> p c g", g=c.G),
                    b.kb[:].rearrange("p (c t) -> p c t", t=T)[:, :, t0:t0 + c.G]),
                    waits=qkv_all, inc=True)
                gevs = []
                for chk in range(c.NC):
                    slot = chk % 2
                    te = em.pe(lambda e, chk=chk, slot=slot, t0=t0: e.transpose(
                        b.pT[slot][0:c.G, 0:128],
                        b.vb[:, chk * T + t0: chk * T + t0 + c.G],
                        b.ident[:]),
                        waits=qkv_all + [pT_free[slot] or em.ev("ssc")], inc=True)
                    ev = em.sc(lambda e, chk=chk, slot=slot, s=s: e.copy(
                        b.vgT[s][0:c.G, chk * 128:(chk + 1) * 128],
                        b.pT[slot][0:c.G, 0:128]),
                        waits=[te], inc=True)
                    pT_free[slot] = ev
                    gevs.append(ev)
                vg_ev[s] = gevs[-1]

            ctx_mm_evs = []
            for h in range(c.HEADS):
                dc = h // 2
                r0 = (h % 2) * 64
                bank = h % 2
                hb = h % 2
                w0 = [qkv_ev, kg_ev[s], pSc_free[bank] or em.ev("sv")]
                em.pe(lambda e, dc=dc, r0=r0, bank=bank, s=s, t0=t0: e.matmul(
                    b.pA[bank][:, 0:c.G],
                    b.qb[r0:r0 + 64, dc * T + t0: dc * T + t0 + c.WIN],
                    b.kg[s][r0:r0 + 64, dc * c.G: (dc + 1) * c.G],
                    start=True, stop=True), waits=w0, inc=False)
                sc_mm = em.pe(lambda e, dc=dc, r0=r0, bank=bank, t0=t0: e.matmul(
                    b.pA[bank][:, c.G:c.G + c.WIN],
                    b.qb[r0:r0 + 64, dc * T + t0: dc * T + t0 + c.WIN],
                    b.kb[r0:r0 + 64, dc * T + t0: dc * T + t0 + c.WIN],
                    start=True, stop=True), inc=True)
                nmx = em.v(lambda e, bank=bank, hb=hb: e.reduce_max(
                    b.nm[:, hb:hb + 1], b.pA[bank][:, 0:c.G + c.WIN],
                    axis=AX.X, negate=True), waits=[sc_mm], inc=True)
                ex = em.sc(lambda e, bank=bank, hb=hb: e.activation(
                    b.expb[hb][:], b.pA[bank][:, 0:c.G + c.WIN], AF.Exp,
                    bias=b.nm[:, hb:hb + 1], scale=1.0,
                    accum_out=b.nm[:, 2 + hb:3 + hb]),
                    waits=[nmx, expb_free[hb] or em.ev("sv")], inc=True)
                pSc_free[bank] = ex
                rc = em.v(lambda e, hb=hb: e.reciprocal(
                    b.nm[:, 4 + hb:5 + hb], b.nm[:, 2 + hb:3 + hb]), waits=[ex], inc=True)
                pn = em.v(lambda e, hb=hb: e.tensor_scalar(
                    b.probs[hb][:], b.expb[hb][:], b.nm[:, 4 + hb:5 + hb], None,
                    ALU.mult), waits=[rc, probs_free[hb] or em.ev("spe")], inc=True)
                expb_free[hb] = pn
                tl = em.pe(lambda e, hb=hb: e.transpose(
                    b.pT[0][:, 0:128], b.probs[hb][:, c.G:c.G + c.WIN], b.ident[:]),
                    waits=[pn, pT_free[0] or em.ev("ssc")], inc=True)
                tg = em.pe(lambda e, hb=hb: e.transpose(
                    b.pT[1][0:c.G, 0:128], b.probs[hb][:, 0:c.G], b.ident[:]),
                    waits=[pn, pT_free[1] or em.ev("ssc")], inc=True)
                pe1 = em.sc(lambda e, hb=hb: e.copy(
                    b.probsT[hb][:, 0:128], b.pT[0][:, 0:128]),
                    waits=[tl, probsT_free[hb] or em.ev("spe")], inc=True)
                pe2 = em.sc(lambda e, hb=hb: e.copy(
                    b.probsT[hb][0:c.G, 128:128 + c.WIN], b.pT[1][0:c.G, 0:128]),
                    waits=[tg], inc=True)
                pT_free[0] = pe1
                pT_free[1] = pe2
                probs_free[hb] = pe2
                cw = [pe1, pe2, vw_ev, vg_ev[s]]
                if pCtx_free is not None and h == 0:
                    cw.append(pCtx_free)
                em.pe(lambda e, h=h, dc=dc, r0=r0, hb=hb, s=s: e.matmul(
                    b.pCtx[r0:r0 + 64, dc * 128:(dc + 1) * 128],
                    b.vgT[s][:, h * 64:(h + 1) * 64],
                    b.probsT[hb][0:c.G, 128:128 + c.WIN],
                    start=True, stop=False), waits=cw, inc=False)
                cme = em.pe(lambda e, h=h, dc=dc, r0=r0, hb=hb: e.matmul(
                    b.pCtx[r0:r0 + 64, dc * 128:(dc + 1) * 128],
                    b.vwT[:, h * 64:(h + 1) * 64],
                    b.probsT[hb][:, 0:128],
                    start=False, stop=True), inc=True)
                probsT_free[hb] = cme
                ctx_mm_evs.append(cme)
            ctx_ev = None
            for chk in range(c.NC):
                ctx_ev = em.v(lambda e, chk=chk, t0=t0: e.tensor_copy(
                    b.ctxb[:, chk * T + t0: chk * T + t0 + c.WIN],
                    b.pCtx[:, chk * 128:(chk + 1) * 128]),
                    waits=[ctx_mm_evs[-1]], inc=True)
            pCtx_free = ctx_ev

        acc_ev = None
        for oc in range(c.NC):
            bank = oc % 2
            mm = None
            for dc in range(c.NC):
                w = [pCtx_free, wo_ev]
                if dc == 0:
                    w.append(pSc_free[bank] or em.ev("sv"))
                mm = em.pe(lambda e, oc=oc, dc=dc, bank=bank: e.matmul(
                    b.pA[bank][:, 0:T],
                    b.wT[3][:, dc * c.D + oc * 128: dc * c.D + (oc + 1) * 128],
                    b.ctxb[:, dc * T:(dc + 1) * T],
                    start=(dc == 0), stop=(dc == c.NC - 1)),
                    waits=w, inc=(dc == c.NC - 1))
            acc_ev = em.v(lambda e, oc=oc, bank=bank, ti=ti: e.tensor_tensor(
                b.acc[:, oc * T:(oc + 1) * T], b.pA[bank][:, 0:T],
                b.x[:, oc * c.NT + ti * T: oc * c.NT + (ti + 1) * T], ALU.add),
                waits=[mm] + ([acc_free] if oc == 0 else []), inc=True)
            pSc_free[bank] = acc_ev
        ln_ev = emit_ln_fm(em, b, cfg, ti, acc_ev, prev_ln_ev,
                           keep_acc=(x32_dram is not None))
        prev_ln_ev = ln_ev
        if x32_dram is not None:
            for sub in range(4):
                ev = None
                for ch in range(c.NC):
                    slot = ch % 2
                    ptv = b.pT[slot][:].bitcast(F32)
                    te = em.pe(lambda e, ch=ch, sub=sub, slot=slot, ptv=ptv: e.transpose(
                        ptv[:, 0:128],
                        b.acc[:, ch * T + sub * 128: ch * T + sub * 128 + 128],
                        b.ident32[:]),
                        waits=[ln_ev, em.ev("ssc"),
                               pT_free[slot] or em.ev("ssc")], inc=True)
                    ev = em.sc(lambda e, ch=ch, slot=slot, ptv=ptv: e.copy(
                        b.x32st[:, ch * 128:(ch + 1) * 128], ptv[:, 0:128]),
                        waits=[te, x32d], inc=True)
                    pT_free[slot] = ev
                r0 = ti * T + sub * 128
                x32d = em.dma_sy("dx32", x32_dram[r0:r0 + 128, :], b.x32st[:],
                                 waits=[ev])
            acc_free = em.ev("spe")
            ln_ev = x32d
        else:
            acc_free = None
        ln_evs.append(ln_ev)
    return ln_evs




def build_program(cfg, parts):
    c = cfg
    nc = bass.Bass("TRN2", target_bir_lowering=False)
    x_in = nc.dram_tensor("x_in", [c.NSEQ, c.SEQ, c.D], F32, kind="ExternalInput").ap()
    x_out = nc.dram_tensor("x_out", [c.NSEQ, c.SEQ, c.D], F32, kind="ExternalOutput").ap()

    fused = len(parts) > 1 or parts == ["moe"]
    with ExitStack() as ctx:
        if fused:
            b = alloc_bufs_fused(nc, ctx, cfg)
        else:
            b = alloc_bufs(nc, ctx, cfg, parts)
        if "l0a" in parts:
            b.l0a_w = tuple(
                nc.dram_tensor(n, [c.D, c.D], F32, kind="ExternalInput").ap()
                for n in ("l0_wq", "l0_wk", "l0_wv", "l0_wo"))
        if "l1a" in parts:
            b.l1a_w = tuple(
                nc.dram_tensor(n, [c.D, c.D], F32, kind="ExternalInput").ap()
                for n in ("l1_wq", "l1_wk", "l1_wv"))
            b.l1a_wT = tuple(
                nc.dram_tensor(n, [128, c.NC, c.D], BF16).ap()
                for n in ("l1_wqT", "l1_wkT", "l1_wvT"))
        if "moe" in parts:
            b.moe_w = (
                nc.dram_tensor("l1_wg", [c.E, c.D], F32, kind="ExternalInput").ap(),
                nc.dram_tensor("l1_w1", [c.E, c.F, c.D], F32, kind="ExternalInput").ap(),
                nc.dram_tensor("l1_w2", [c.E, c.F, c.D], F32, kind="ExternalInput").ap(),
                nc.dram_tensor("l1_w3", [c.E, c.D, c.F], F32, kind="ExternalInput").ap(),
            )
            b.moe_w1T = nc.dram_tensor("m_w1T", [c.E, 128, c.NC, c.F], BF16).ap()
            b.moe_w2T = nc.dram_tensor("m_w2T", [c.E, 128, c.NC, c.F], BF16).ap()
            b.moe_w3T = nc.dram_tensor("m_w3T", [c.E, 128, c.NF, c.D], BF16).ap()
            b.accm = nc.dram_tensor("m_acc", [128, c.NC, c.NT], F32).ap()
        if "l0f" in parts:
            b.l0f_w = (
                nc.dram_tensor("l0_wgate", [c.I, c.D], F32, kind="ExternalInput").ap(),
                nc.dram_tensor("l0_wup", [c.I, c.D], F32, kind="ExternalInput").ap(),
                nc.dram_tensor("l0_wdown", [c.D, c.I], F32, kind="ExternalInput").ap(),
            )
            b.l0f_wgT = nc.dram_tensor("l0f_wgT", [128, c.NC, c.I], BF16).ap()
            b.l0f_wupT = nc.dram_tensor("l0f_wupT", [128, c.NC, c.I], BF16).ap()
            b.l0f_wdT = nc.dram_tensor("l0f_wdT", [128, c.NI, c.D], BF16).ap()
        if fused:
            b.x32s = nc.dram_tensor("x32s", [c.NT, c.D], F32).ap()
        if "moe" in parts:
            b.g_dram = nc.dram_tensor("g_dram", [8, c.NT], BF16).ap()
        em = Em(nc)

        bp = phase_bufs(b, "l0a") if fused else b
        ce = emit_consts(em, bp, cfg)
        tile_evs = emit_ingest(em, bp, cfg, x_in)
        final_evs = tile_evs
        if "l0a" in parts:
            final_evs = emit_l0_attn(em, bp, cfg, ce, tile_evs,
                                     x32_dram=b.x32s if fused else None)
        if "l0f" in parts:
            if fused:
                em.phase_barrier()
                bp = phase_bufs(b, "l0f")
                x32 = b.x32s
            else:
                x32 = x_in if parts[0] == "l0f" else None
            final_evs = emit_l0_ffn(em, bp, cfg, ce, final_evs, x32_dram=x32)
        if "l1a" in parts:
            if fused:
                em.phase_barrier()
                bp = phase_bufs(b, "l1a")
            final_evs = emit_l1_attn(em, bp, cfg, ce, final_evs)
        if "moe" in parts:
            if fused:
                em.phase_barrier()
                bp = phase_bufs(b, "moe")
            out_dmas = emit_l1_moe(em, bp, cfg, ce, final_evs, x_out)
        else:
            out_dmas = emit_egress(em, bp, cfg, x_out, [final_evs[-1]] if final_evs else [])
        finals = {}
        for s, v in out_dmas:
            finals[s] = max(finals.get(s, 0), v)
        em.emit("sy", lambda e: e.nop(), waits=sorted(finals.items()))

        for s in sorted(em.cnt.keys()):
            em.sems[s] = ctx.enter_context(nc.semaphore(s))
        block = ctx.enter_context(nc.Block())
        em.play(block)
    return nc




def emit_quant(em, b, cfg, src, dst, W, waits, qs=None, dt16=False, dst_war=()):
    c = cfg
    q_xn, q_g, q_a, q_c, q_sc = qs if qs is not None else b.qsets[0]
    LWE = 2 * c.LW if dt16 else c.LW
    ev = None
    for lo in range(0, W, LWE):
        wsub = min(LWE, W - lo)
        nb = wsub // 16
        s3 = src[:, lo:lo + wsub].rearrange("p (a b) -> p a b", b=16)
        sc16 = q_sc[:, 0:nb]
        scl = q_sc[:, nb:2 * nb]
        rscl = q_sc[:, 2 * nb:3 * nb]
        if dt16:
            xn = q_xn.bitcast(BF16)[:, 0:wsub]
            a = q_a.bitcast(BF16)[:, 0:wsub]
        else:
            xn = q_xn[:, 0:wsub]
            a = q_a[:, 0:wsub]
        g = q_g.bitcast(BF16)[:, 0:wsub]
        cc = q_c.bitcast(BF16)[:, 0:wsub]
        if dt16:
            u = b.qsets[1][0].bitcast(BF16)[:, 0:wsub]
        else:
            u = q_c.bitcast(BF16)[:, c.LW:c.LW + wsub]
        xn3 = xn.rearrange("p (a b) -> p a b", b=16)
        rev = em.v(lambda e, s3=s3, sc16=sc16: e.reduce_max(
            sc16, s3, axis=AX.X, apply_absolute_value=True),
            waits=list(waits) + ([ev] if ev is not None else []))
        sev2 = em.v(lambda e, scl=scl, sc16=sc16: e.tensor_scalar(
            scl, sc16, 1.0 / 6.0, 1e-12, ALU.mult, ALU.max), waits=[rev])
        rcev = em.v(lambda e, rscl=rscl, scl=scl: e.reciprocal(rscl, scl))
        vev = em.v(lambda e, xn3=xn3, s3=s3, rscl=rscl, nb=nb: e.tensor_tensor(
            xn3, s3, rscl.unsqueeze(2).broadcast_to((128, nb, 16)), ALU.mult),
            waits=[rcev])
        em.sc(lambda e, g=g, xn=xn: e.sign(g, xn), waits=[vev])
        sev = em.sc(lambda e, a=a, xn=xn: e.activation(a, xn, AF.Abs, bias=0.0, scale=1.0))
        em.v(lambda e, cc=cc, a=a: e.tensor_scalar(cc, a, 0.25, 0.5, ALU.is_gt, ALU.mult),
             waits=[sev])
        for mid, inc in ((0.75, 0.5), (1.25, 0.5), (1.75, 0.5),
                         (2.5, 1.0), (3.5, 1.0), (5.0, 2.0)):
            em.v(lambda e, uu=u, a=a, mid=mid, inc=inc: e.tensor_scalar(
                uu, a, mid, inc, ALU.is_gt, ALU.mult))
            em.v(lambda e, cc=cc, uu=u: e.tensor_tensor(cc, cc, uu, ALU.add))
        gev = em.v(lambda e, cc=cc, g=g: e.tensor_tensor(cc, cc, g, ALU.mult))
        cc3 = cc.rearrange("p (a b) -> p a b", b=16)
        d3 = dst[:, lo:lo + wsub].rearrange("p (a b) -> p a b", b=16)
        ev = em.v(lambda e, d3=d3, cc3=cc3, scl=scl, nb=nb: e.tensor_tensor(
            d3, cc3, scl.unsqueeze(2).broadcast_to((128, nb, 16)), ALU.mult),
            waits=[gev] + list(dst_war))
    return ev

def emit_l0f_weight_prep(em, b, cfg, ce, after=()):
    c = cfg
    jobs = (
        (b.l0f_w[0], b.l0f_wgT, c.I, c.D),
        (b.l0f_w[1], b.l0f_wupT, c.I, c.D),
        (b.l0f_w[2], b.l0f_wdT, c.D, c.I),
    )
    in_free = [list(after), list(after)]
    out_free = [None, None]
    pT_free = [em.ev("ssc"), em.ev("ssc")]
    rc_idx = 0
    for (w_d, wT_d, R, W) in jobs:
        nrc = R // 128
        nwc = W // 128
        wre = w_d.rearrange("(rc p) d -> p rc d", p=128)
        for rc in range(nrc):
            sl = rc_idx % 2
            ld = em.dma_gp(f"dwqin{sl}", b.wq_in2[sl][:, 0:W], wre[:, rc, :],
                           waits=in_free[sl] + [ce])
            qev = emit_quant(em, b, cfg, b.wq_in2[sl][:, 0:W],
                             b.wq_out2[sl][:, 0:W], W, [ld],
                             dst_war=[out_free[sl]])
            in_free[sl] = [qev]
            ssl = rc_idx % 2
            sev = None
            te = None
            for wc in range(nwc):
                slot = wc % 2
                te = em.pe(lambda e, wc=wc, slot=slot, sl=sl: e.transpose(
                    b.pT[slot][:, 0:128],
                    b.wq_out2[sl][:, wc * 128:(wc + 1) * 128], b.ident[:]),
                    waits=[qev, ce, pT_free[slot]], inc=True)
                sev = em.sc(lambda e, wc=wc, slot=slot, ssl=ssl: e.copy(
                    b.wstg[ssl][:, wc * 128:(wc + 1) * 128],
                    b.pT[slot][:, 0:128]),
                    waits=[te, b.wstg_free[ssl]], inc=True)
                pT_free[slot] = sev
            out_free[sl] = te
            dma = em.dma_gp(f"dstg{ssl}",
                            wT_d[:, :, rc * 128:(rc + 1) * 128],
                            b.wstg[ssl][:, 0:nwc * 128].rearrange(
                                "p (wc m) -> p wc m", m=128),
                            waits=[sev])
            b.wstg_free[ssl] = dma
            rc_idx += 1
    return [em.ev("dstg0"), em.ev("dstg1")]

def emit_l0_ffn(em, b, cfg, ce, x_tile_evs, prev_ln_ev=None, x32_dram=None):
    c = cfg
    T = c.TT
    if x32_dram is None:
        xflat32 = None
    elif len(x32_dram.shape) == 3:
        xflat32 = x32_dram.rearrange("s t d -> (s t) d")
    else:
        xflat32 = x32_dram
    prep_evs = emit_l0f_weight_prep(em, b, cfg, ce, after=[x_tile_evs[-1]])
    NFB = (c.I + 511) // 512
    ln_evs = []
    pa_free = [None, None, None, None]
    pT_free = [em.ev("ssc"), em.ev("ssc")]
    qi = [0]
    xq_rd = None
    hb_rd = [None] * 4
    hqfm_rd = None
    wg_free = [None, None]
    wd_free = [None, None]
    silu_free = None
    xtm_free = [None, None]
    xqtm_free = [None, None]
    qset_free = [None, None]
    qsh_free = [None, None]
    hqtm_free = None

    for ti in range(c.NTILE):
        xev = x_tile_evs[ti]
        xq_ev = None
        for sub in range(4):
            sl = sub % 2
            r0 = ti * T + sub * 128
            ev32 = em.dma_gp(f"dxq32_{sl}", b.xtm32_2[sl], xflat32[r0:r0 + 128, :],
                             waits=[xev, xtm_free[sl]] + prep_evs)
            qev = emit_quant(em, b, cfg, b.xtm32_2[sl], b.xqtm2[sl], c.D,
                             [ev32], dst_war=[xqtm_free[sl]])
            xtm_free[sl] = qev
            te = None
            for ch in range(c.NC):
                slot = ch % 2
                te = em.pe(lambda e, ch=ch, slot=slot, sl=sl: e.transpose(
                    b.pT[slot][:, 0:128],
                    b.xqtm2[sl][:, ch * 128:(ch + 1) * 128], b.ident[:]),
                    waits=[qev, pT_free[slot]], inc=True)
                xq_ev = em.sc(lambda e, ch=ch, sub=sub, slot=slot: e.copy(
                    b.xq_fm[:, ch * T + sub * 128: ch * T + sub * 128 + 128],
                    b.pT[slot][:, 0:128]),
                    waits=[te, xq_rd], inc=True)
                pT_free[slot] = xq_ev
            xqtm_free[sl] = te
        h_ev = None
        h_sub_ev = [None] * 4
        for fb in range(NFB):
            fw = min(512, c.I - fb * 512)
            gsl = 0
            wgl = em.dma_gp(f"dwg{gsl}",
                            b.wg_s[gsl][:, 0:c.NC * fw].rearrange("p (dc f) -> p dc f", f=fw),
                            b.l0f_wgT[:, :, fb * 512: fb * 512 + fw],
                            waits=[wg_free[gsl]] + prep_evs)
            wul = em.dma_gp(f"dwu{gsl}",
                            b.wup_s[gsl][:, 0:c.NC * fw].rearrange("p (dc f) -> p dc f", f=fw),
                            b.l0f_wupT[:, :, fb * 512: fb * 512 + fw],
                            waits=[wg_free[gsl]] + prep_evs)
            last_mm = [None, None]
            for sub in range(4):
                bsel = (fb * 4 + sub) % 2
                bg, bu = (0, 1) if bsel == 0 else (2, 3)
                mmg = None
                mmu = None
                for dc in range(c.NC):
                    w = [xq_ev, wgl]
                    if dc == 0:
                        w.append(pa_free[bg])
                    mmg = em.pe(lambda e, dc=dc, sub=sub, fw=fw, gsl=gsl, bg=bg: e.matmul(
                        b.pA[bg][:, 0:fw],
                        b.xq_fm[:, dc * T + sub * 128: dc * T + sub * 128 + 128],
                        b.wg_s[gsl][:, dc * fw: (dc + 1) * fw],
                        start=(dc == 0), stop=(dc == c.NC - 1)),
                        waits=w, inc=(dc == c.NC - 1))
                for dc in range(c.NC):
                    w = [xq_ev, wul]
                    if dc == 0:
                        w.append(pa_free[bu])
                    mmu = em.pe(lambda e, dc=dc, sub=sub, fw=fw, gsl=gsl, bu=bu: e.matmul(
                        b.pA[bu][:, 0:fw],
                        b.xq_fm[:, dc * T + sub * 128: dc * T + sub * 128 + 128],
                        b.wup_s[gsl][:, dc * fw: (dc + 1) * fw],
                        start=(dc == 0), stop=(dc == c.NC - 1)),
                        waits=w, inc=(dc == c.NC - 1))
                sev = em.sc(lambda e, fw=fw, bg=bg: e.activation(
                    b.silu[:, 0:fw], b.pA[bg][:, 0:fw], AF.Silu, bias=0.0, scale=1.0),
                    waits=[mmg, silu_free], inc=True)
                pa_free[bg] = sev
                h_ev = em.v(lambda e, sub=sub, fb=fb, fw=fw, bu=bu: e.tensor_tensor(
                    b.hb[:, sub * c.I + fb * 512: sub * c.I + fb * 512 + fw],
                    b.silu[:, 0:fw], b.pA[bu][:, 0:fw], ALU.mult),
                    waits=[mmu, sev, hb_rd[sub]], inc=True)
                if fb == NFB - 1:
                    h_sub_ev[sub] = h_ev
                pa_free[bu] = h_ev
                silu_free = h_ev
                last_mm[0] = mmg
                last_mm[1] = mmu
            wg_free[gsl] = last_mm[1]
        hq_ev = None
        for sub in range(4):
            qv = emit_quant(em, b, cfg, b.hb[:, sub * c.I:(sub + 1) * c.I],
                            b.hq_tm[:], c.I, [h_sub_ev[sub]],
                            dt16=True, dst_war=[hqtm_free])
            qevs = [qv]
            hb_rd[sub] = qv
            te = None
            for ic in range(c.NI):
                slot = ic % 2
                qev = qevs[(ic * 128) // c.LW if (ic * 128) // c.LW < len(qevs) else -1]
                te = em.pe(lambda e, ic=ic, slot=slot: e.transpose(
                    b.pT[slot][:, 0:128],
                    b.hq_tm[:, ic * 128:(ic + 1) * 128], b.ident[:]),
                    waits=qevs + [pT_free[slot]], inc=True)
                hq_ev = em.sc(lambda e, ic=ic, sub=sub, slot=slot: e.copy(
                    b.hq_fm[:, ic * T + sub * 128: ic * T + sub * 128 + 128],
                    b.pT[slot][:, 0:128]),
                    waits=[te, hqfm_rd], inc=True)
                pT_free[slot] = hq_ev
            hqtm_free = te
        acc_ev = None
        down_mm = None
        for oc in range(c.NC):
            dsl = 0
            wdl = em.dma_gp(f"dwd{dsl}",
                            b.wd_s[dsl][:].rearrange("p (ic o) -> p ic o", o=128),
                            b.l0f_wdT[:, :, oc * 128:(oc + 1) * 128],
                            waits=[wd_free[dsl]] + prep_evs)
            bank = oc % 2
            for ic in range(c.NI):
                w = [hq_ev, wdl]
                if ic == 0:
                    w.append(pa_free[bank])
                down_mm = em.pe(lambda e, oc=oc, ic=ic, dsl=dsl, bank=bank: e.matmul(
                    b.pA[bank][:, 0:T],
                    b.wd_s[dsl][:, ic * 128:(ic + 1) * 128],
                    b.hq_fm[:, ic * T:(ic + 1) * T],
                    start=(ic == 0), stop=(ic == c.NI - 1)),
                    waits=w, inc=(ic == c.NI - 1))
            wd_free[dsl] = down_mm
            acc_ev = em.v(lambda e, oc=oc, bank=bank, ti=ti: e.tensor_tensor(
                b.acc[:, oc * T:(oc + 1) * T], b.pA[bank][:, 0:T],
                b.x[:, oc * c.NT + ti * T: oc * c.NT + (ti + 1) * T], ALU.add),
                waits=[down_mm], inc=True)
            pa_free[bank] = acc_ev
        hqfm_rd = down_mm
        xq_rd = last_mm[1]
        ln_ev = emit_ln_fm(em, b, cfg, ti, acc_ev, prev_ln_ev,
                           extra_waits=[pa_free[2], pa_free[3]])
        pa_free[2] = ln_ev
        pa_free[3] = ln_ev
        prev_ln_ev = ln_ev
        ln_evs.append(ln_ev)
    return ln_evs


class PrepJob:

    def __init__(self, em, b, cfg, w_dram, wT_dram, ce, after=()):
        self.em, self.b, self.cfg = em, b, cfg
        self.wT = wT_dram
        self.ce = ce
        self.O = w_dram.shape[0]
        self.D = w_dram.shape[1]
        self.noc = self.O // 128
        self.ndc = self.D // 128
        self.hoc = max(1, (cfg.NC // 4 * cfg.D) // self.D)
        self.wre = w_dram.rearrange("(oc p) d -> p oc d", p=128)
        self.extra = list(after)
        self.og = 0

    def done(self):
        return self.og >= self.noc

    def load_chunk(self):
        em, b = self.em, self.b
        og = self.og
        og_n = min(self.hoc, self.noc - og)
        self.og += og_n
        D = self.D
        par = b.wload_par
        b.wload_par ^= 1
        ld = em.dma_gp(f"dwl{par}",
                       b.wload2[par][:, 0:og_n * D].rearrange("p (oc d) -> p oc d", d=D),
                       self.wre[:, og:og + og_n, :],
                       waits=self.extra + [b.wload_free[par], self.ce])
        self.extra = []
        return (og, og_n, ld, par)

    def tail_chunk(self, tok):
        em, b = self.em, self.b
        og, og_n, ld, par = tok
        D = self.D
        te = None
        for oc in range(og, og + og_n):
            ssl = oc % 2
            evs = None
            for dc in range(self.ndc):
                slot = dc % 2
                te = em.pe(lambda e, oc=oc, dc=dc, slot=slot, og=og, D=D, par=par: e.transpose(
                    b.pT[slot][:, 0:128],
                    b.wload2[par][:, (oc - og) * D + dc * 128: (oc - og) * D + (dc + 1) * 128],
                    b.ident[:]),
                    waits=[ld, self.ce, b.pT_free[slot]], inc=True)
                evs = em.sc(lambda e, dc=dc, slot=slot, ssl=ssl: e.copy(
                    b.wstg[ssl][:, dc * 128:(dc + 1) * 128],
                    b.pT[slot][:, 0:128]),
                    waits=[te, b.wstg_free[ssl]], inc=True)
                b.pT_free[slot] = evs
            dma = em.dma_gp(f"dstg{ssl}",
                            self.wT[:, :, oc * 128:(oc + 1) * 128],
                            b.wstg[ssl][:, 0:self.ndc * 128].rearrange(
                                "p (dc m) -> p dc m", m=128),
                            waits=[evs])
            b.wstg_free[ssl] = dma
        b.wload_free[par] = te


class PrepQueue:

    def __init__(self, jobs):
        self.jobs = list(jobs)
        self.ji = 0
        self.pending = None

    def done(self):
        return self.ji >= len(self.jobs) and self.pending is None

    def step(self):
        tok = None
        while self.ji < len(self.jobs):
            if not self.jobs[self.ji].done():
                tok = (self.jobs[self.ji], self.jobs[self.ji].load_chunk())
                break
            self.ji += 1
        if self.pending is not None:
            j, t = self.pending
            j.tail_chunk(t)
        self.pending = tok

    def run(self, k=-1):
        n = 0
        while not self.done() and (k < 0 or n < k):
            self.step()
            n += 1


def emit_weight_prep_dram(em, b, cfg, w_dram, wT_dram, ce, after=()):
    q = PrepQueue([PrepJob(em, b, cfg, w_dram, wT_dram, ce, after)])
    q.run()
    return [em.ev("dstg0"), em.ev("dstg1")]


def emit_l1_attn(em, b, cfg, ce, x_tile_evs, prev_ln_ev=None):
    c = cfg
    T = c.TT
    pq = emit_weight_prep_dram(em, b, cfg, b.l1a_w[0], b.l1a_wT[0], ce)
    pk = emit_weight_prep_dram(em, b, cfg, b.l1a_w[1], b.l1a_wT[1], ce)
    pv = emit_weight_prep_dram(em, b, cfg, b.l1a_w[2], b.l1a_wT[2], ce)
    prep_evs = pv

    mz = em.gp(lambda e: e.memset(b.cmask[:], 0.0), waits=[ce], inc=True)
    mask_ev = mz
    for o in range(4):
        mask_ev = em.gp(lambda e, o=o: e.affine_select(
            out=b.cmask[:, o * 512:(o + 1) * 512], in_=b.cmask[:, o * 512:(o + 1) * 512],
            pattern=[[-1, 512]], compare_op=ALU.is_ge, fill=NEG,
            base=128 * o, channel_multiplier=1), waits=[mz], inc=True)

    ngrp = c.SEQ // T
    ntpg = T // 128
    ln_evs = []
    pa_free = [None, None, None, None]
    pT_free = [em.ev("ssc"), em.ev("ssc")]
    pCtx_free = None
    probsT_free = [None, None]
    probs_free = None
    kv_free = None
    qg_free = None
    vtmp_free = None
    wstr_free = [None, None]

    def proj_stream(wT_dram, oc, wsl, extra):
        return em.dma_gp(f"dws{wsl}",
                         b.wstr[wsl][:].rearrange("p (dc m) -> p dc m", m=128),
                         wT_dram[:, :, oc * 128:(oc + 1) * 128],
                         waits=[wstr_free[wsl]] + prep_evs + list(extra))

    for s in range(c.NSEQ):
        base = s * c.SEQ
        kv_ev = None
        for tg in range(ngrp):
            ti_glob = (base + tg * T) // T
            xev = x_tile_evs[ti_glob]
            for oc in range(c.NC):
                bank = oc % 2
                wsl = oc % 2
                wl = proj_stream(b.l1a_wT[1], oc, wsl, [])
                mm = None
                for dc in range(c.NC):
                    w = [xev, wl]
                    if dc == 0:
                        w += [pa_free[bank], kv_free]
                    mm = em.pe(lambda e, oc=oc, dc=dc, bank=bank, tg=tg, base=base, wsl=wsl: e.matmul(
                        b.pA[bank][:, 0:T],
                        b.wstr[wsl][:, dc * 128:(dc + 1) * 128],
                        b.x[:, dc * c.NT + base + tg * T: dc * c.NT + base + (tg + 1) * T],
                        start=(dc == 0), stop=(dc == c.NC - 1)),
                        waits=w, inc=(dc == c.NC - 1))
                wstr_free[wsl] = mm
                kv_ev = em.sc(lambda e, oc=oc, bank=bank, tg=tg, base=base: e.copy(
                    b.kS[:, oc * c.SEQ + tg * T: oc * c.SEQ + (tg + 1) * T],
                    b.pA[bank][:, 0:T]), waits=[mm], inc=True)
                pa_free[bank] = kv_ev
            for oc in range(c.NC):
                bank = oc % 2
                wsl = oc % 2
                wl = proj_stream(b.l1a_wT[2], oc, wsl, [])
                mm = None
                for dc in range(c.NC):
                    w = [xev, wl]
                    if dc == 0:
                        w += [pa_free[bank], kv_free]
                    mm = em.pe(lambda e, oc=oc, dc=dc, bank=bank, tg=tg, base=base, wsl=wsl: e.matmul(
                        b.pA[bank][:, 0:T],
                        b.wstr[wsl][:, dc * 128:(dc + 1) * 128],
                        b.x[:, dc * c.NT + base + tg * T: dc * c.NT + base + (tg + 1) * T],
                        start=(dc == 0), stop=(dc == c.NC - 1)),
                        waits=w, inc=(dc == c.NC - 1))
                wstr_free[wsl] = mm
                vt = em.v(lambda e, bank=bank: e.tensor_copy(
                    b.vtmp[:], b.pA[bank][:, 0:T]), waits=[mm, vtmp_free], inc=True)
                pa_free[bank] = vt
                for sj in range(4):
                    slot = sj % 2
                    te = em.pe(lambda e, sj=sj, slot=slot: e.transpose(
                        b.pT[slot][:, 0:128], b.vtmp[:, sj * 128:(sj + 1) * 128],
                        b.ident[:]), waits=[vt, pT_free[slot] or em.ev("ssc")], inc=True)
                    ev = em.sc(lambda e, oc=oc, tg=tg, sj=sj, slot=slot: e.copy(
                        b.vS[:, (tg * 4 + sj) * c.D + oc * 128: (tg * 4 + sj) * c.D + (oc + 1) * 128],
                        b.pT[slot][:, 0:128]), waits=[te], inc=True)
                    pT_free[slot] = ev
                    vtmp_free = ev
                kv_ev = vtmp_free

        for tg in range(ngrp):
            ti_glob = (base + tg * T) // T
            xev = x_tile_evs[ti_glob]
            q_ev = None
            for oc in range(c.NC):
                bank = oc % 2
                wsl = oc % 2
                wl = proj_stream(b.l1a_wT[0], oc, wsl, [])
                mm = None
                for dc in range(c.NC):
                    w = [xev, wl]
                    if dc == 0:
                        w += [pa_free[bank], qg_free]
                    mm = em.pe(lambda e, oc=oc, dc=dc, bank=bank, tg=tg, base=base, wsl=wsl: e.matmul(
                        b.pA[bank][:, 0:T],
                        b.wstr[wsl][:, dc * 128:(dc + 1) * 128],
                        b.x[:, dc * c.NT + base + tg * T: dc * c.NT + base + (tg + 1) * T],
                        start=(dc == 0), stop=(dc == c.NC - 1)),
                        waits=w, inc=(dc == c.NC - 1))
                wstr_free[wsl] = mm
                q_ev = em.v(lambda e, oc=oc, bank=bank: e.tensor_scalar(
                    b.qg[:, oc * T:(oc + 1) * T], b.pA[bank][:, 0:T],
                    1.0 / 32.0, None, ALU.mult), waits=[mm], inc=True)
                pa_free[bank] = q_ev

            for qt in range(ntpg):
                i = tg * ntpg + qt
                nblk = i // 4 + 1
                exp_evs = []
                for j in range(nblk):
                    diag = (j == nblk - 1)
                    mmS = None
                    for dc in range(c.NC):
                        w = [q_ev, kv_ev]
                        if dc == 0:
                            w.append(pa_free[j])
                        mmS = em.pe(lambda e, dc=dc, j=j, qt=qt: e.matmul(
                            b.pA[j][:, 0:T],
                            b.qg[:, dc * T + qt * 128: dc * T + qt * 128 + 128],
                            b.kS[:, dc * c.SEQ + j * T: dc * c.SEQ + (j + 1) * T],
                            start=(dc == 0), stop=(dc == c.NC - 1)),
                            waits=w, inc=(dc == c.NC - 1))
                    if diag:
                        mmS = em.v(lambda e, j=j, i=i: e.tensor_tensor(
                            b.pA[j][:, 0:T], b.pA[j][:, 0:T],
                            b.cmask[:, (i % 4) * 512:(i % 4) * 512 + 512], ALU.add),
                            waits=[mmS, mask_ev], inc=True)
                    em.v(lambda e, j=j: e.reduce_max(
                        b.nm1[:, j:j + 1], b.pA[j][:, 0:T], axis=AX.X, negate=True),
                        waits=[mmS], inc=True)
                cm = em.v(lambda e: e.tensor_copy(b.nm1[:, 4:5], b.nm1[:, 0:1]), inc=True)
                for j in range(1, nblk):
                    cm = em.v(lambda e, j=j: e.tensor_tensor(
                        b.nm1[:, 4:5], b.nm1[:, 4:5], b.nm1[:, j:j + 1], ALU.min),
                        inc=True)
                for j in range(nblk):
                    ex = em.sc(lambda e, j=j: e.activation(
                        b.probs1[:, j * T:(j + 1) * T], b.pA[j][:, 0:T], AF.Exp,
                        bias=b.nm1[:, 4:5], scale=1.0,
                        accum_out=b.nm1[:, 5 + j:6 + j]),
                        waits=[cm, probs_free], inc=True)
                    pa_free[j] = ex
                    exp_evs.append(ex)
                sm = em.v(lambda e: e.tensor_copy(b.nm1[:, 9:10], b.nm1[:, 5:6]),
                          waits=[exp_evs[0]], inc=True)
                for j in range(1, nblk):
                    sm = em.v(lambda e, j=j: e.tensor_tensor(
                        b.nm1[:, 9:10], b.nm1[:, 9:10], b.nm1[:, 5 + j:6 + j], ALU.add),
                        waits=[exp_evs[j]], inc=True)
                rc = em.v(lambda e: e.reciprocal(b.nm1[:, 10:11], b.nm1[:, 9:10]),
                          waits=[sm], inc=True)
                pt_ev = None
                for j in range(nblk):
                    pn = em.v(lambda e, j=j: e.tensor_scalar(
                        b.probs1[:, j * T:(j + 1) * T], b.probs1[:, j * T:(j + 1) * T],
                        b.nm1[:, 10:11], None, ALU.mult),
                        waits=[rc, exp_evs[j]], inc=True)
                    for sj in range(4):
                        slot = sj % 2
                        te = em.pe(lambda e, j=j, sj=sj, slot=slot: e.transpose(
                            b.pT[slot][:, 0:128],
                            b.probs1[:, j * T + sj * 128: j * T + sj * 128 + 128],
                            b.ident[:]),
                            waits=[pn, pT_free[slot] or em.ev("ssc"), probsT_free[0]], inc=True)
                        pT_free[slot] = em.sc(lambda e, j=j, sj=sj, slot=slot: e.copy(
                            b.probsT1[:, (j * 4 + sj) * 128:(j * 4 + sj) * 128 + 128],
                            b.pT[slot][:, 0:128]),
                            waits=[te], inc=True)
                pt_ev = pT_free[(nblk * 4 - 1) % 2]
                ctx_mm = None
                for ch in range(c.NC):
                    for j in range(nblk):
                        for sj in range(4):
                            w = [pt_ev, kv_ev]
                            if ch == 0 and j == 0 and sj == 0:
                                w.append(pCtx_free)
                            ctx_mm = em.pe(lambda e, ch=ch, sj=sj, j=j, nblk=nblk: e.matmul(
                                b.pCtx[:, ch * 128:(ch + 1) * 128],
                                b.vS[:, (j * 4 + sj) * c.D + ch * 128: (j * 4 + sj) * c.D + (ch + 1) * 128],
                                b.probsT1[:, (j * 4 + sj) * 128:(j * 4 + sj) * 128 + 128],
                                start=(j == 0 and sj == 0), stop=(j == nblk - 1 and sj == 3)),
                                waits=w, inc=(ch == c.NC - 1 and j == nblk - 1 and sj == 3))
                probsT_free = [ctx_mm, ctx_mm]
                probs_free = pt_ev
                acc_ev = None
                for ch in range(c.NC):
                    acc_ev = em.v(lambda e, ch=ch, qt=qt, i=i, base=base: e.tensor_tensor(
                        b.acc[:, ch * T + qt * 128: ch * T + qt * 128 + 128],
                        b.pCtx[:, ch * 128:(ch + 1) * 128],
                        b.x[:, ch * c.NT + base + i * 128: ch * c.NT + base + i * 128 + 128],
                        ALU.add), waits=[ctx_mm], inc=True)
                pCtx_free = acc_ev
            ln_ev = emit_ln_fm(em, b, cfg, ti_glob, acc_ev, prev_ln_ev,
                               extra_waits=[pa_free[2], pa_free[3]])
            pa_free[2] = ln_ev
            pa_free[3] = ln_ev
            prev_ln_ev = ln_ev
            ln_evs.append(ln_ev)
            qg_free = em.ev("spe")
        kv_free = em.ev("spe")
    return ln_evs




def emit_l1_moe(em, b, cfg, ce, x_tile_evs, x_out):
    c = cfg
    T = c.TT
    F = c.F
    xoflat = x_out.rearrange("s t d -> (s t) d")

    wgl = em.dma_gp("dwg_in", b.wg_sb, b.moe_w[0], waits=[ce])
    wgc = em.v(lambda e: e.tensor_copy(b.scores[0:8, 0:c.D], b.wg_sb),
               waits=[wgl], inc=True)
    wgt_ev = None
    for dc in range(c.NC):
        slot = dc % 2
        te = em.pe(lambda e, dc=dc, slot=slot: e.transpose(
            b.pT[slot][:, 0:8], b.scores[0:8, dc * 128:(dc + 1) * 128],
            b.ident[0:8, 0:8]),
            waits=[wgc, ce, em.ev("ssc"), b.pT_free[slot]], inc=True)
        wgt_ev = em.sc(lambda e, dc=dc, slot=slot: e.copy(
            b.wgT[:, dc * 8:(dc + 1) * 8], b.pT[slot][0:128, 0:8]),
            waits=[te], inc=True)
        b.pT_free[slot] = wgt_ev

    pa_free = [None, None, None, None]
    sc_ev = None
    for ti in range(c.NTILE):
        bank = ti % 2
        mm = None
        for dc in range(c.NC):
            w = [x_tile_evs[ti], wgt_ev]
            if dc == 0:
                w.append(pa_free[bank])
            mm = em.pe(lambda e, ti=ti, dc=dc, bank=bank: e.matmul(
                b.pA[bank][0:8, 0:T],
                b.wgT[:, dc * 8:(dc + 1) * 8],
                b.x[:, dc * c.NT + ti * T: dc * c.NT + (ti + 1) * T],
                start=(dc == 0), stop=(dc == c.NC - 1)),
                waits=w, inc=(dc == c.NC - 1))
        sc_ev = em.sc(lambda e, ti=ti, bank=bank: e.copy(
            b.scores[0:8, ti * T:(ti + 1) * T], b.pA[bank][0:8, 0:T]),
            waits=[mm], inc=True)
        pa_free[bank] = sc_ev

    g_ev = None
    for t in range(c.NT // 128):
        slot = t % 2
        te = em.pe(lambda e, t=t, slot=slot: e.transpose(
            b.pT[slot][:, 0:8], b.scores[0:8, t * 128:(t + 1) * 128],
            b.ident[0:8, 0:8]), waits=[sc_ev, b.pT_free[slot]], inc=True)
        ev = em.sc(lambda e, slot=slot: e.copy(
            b.sctm[:], b.pT[slot][0:128, 0:8]), waits=[te, g_ev], inc=True)
        b.pT_free[slot] = ev
        em.v(lambda e: e.max(out=b.top8[:], in_=b.sctm[:]), waits=[ev])
        em.v(lambda e: e.memset(b.rep[:], NEG))
        em.v(lambda e: e.tensor_copy(b.rep[:, 0:1], b.top8[:, 0:1]))
        em.v(lambda e: e.match_replace(b.ms2[:], b.rep[:], b.sctm[:], -2e30))
        em.v(lambda e: e.tensor_tensor(b.mk1[:], b.sctm[:], b.ms2[:], ALU.not_equal))
        em.v(lambda e: e.tensor_copy(b.rep[:, 0:1], b.top8[:, 1:2]))
        em.v(lambda e: e.match_replace(b.ms3[:], b.rep[:], b.ms2[:], -2e30))
        em.v(lambda e: e.tensor_tensor(b.mk2[:], b.ms2[:], b.ms3[:], ALU.not_equal))
        dv = em.v(lambda e: e.tensor_sub(b.mst[:, 0:1], b.top8[:, 0:1], b.top8[:, 1:2]))
        p1 = em.sc(lambda e: e.activation(b.mst[:, 1:2], b.mst[:, 0:1], AF.Sigmoid,
                                          bias=0.0, scale=1.0), waits=[dv], inc=True)
        p2 = em.sc(lambda e: e.activation(b.mst[:, 2:3], b.mst[:, 0:1], AF.Sigmoid,
                                          bias=0.0, scale=-1.0), inc=True)
        em.v(lambda e: e.tensor_scalar(b.gtm[:], b.mk1[:], b.mst[:, 1:2], None, ALU.mult),
             waits=[p1, p2])
        em.v(lambda e: e.scalar_tensor_tensor(b.gtm[:], b.mk2[:], b.mst[:, 2:3],
                                              b.gtm[:], ALU.mult, ALU.add))
        gc = em.v(lambda e: e.tensor_copy(b.gbf[:], b.gtm[:]))
        tb = em.pe(lambda e, t=t, slot=slot: e.transpose(
            b.pT[slot][0:8, 128:256], b.gbf[:], b.ident[:]),
            waits=[gc, b.pT_free[slot]], inc=True)
        g_ev = em.sc(lambda e, t=t, slot=slot: e.copy(
            b.g_fm[0:8, t * 128:(t + 1) * 128], b.pT[slot][0:8, 128:256]),
            waits=[tb], inc=True)
        b.pT_free[slot] = g_ev
    g_dma = em.dma_gp("dgout", b.g_dram, b.g_fm[0:8, :], waits=[g_ev])

    sel_ev = None
    for e_ in range(c.E):
        sel_ev = em.gp(lambda e, e_=e_: e.affine_select(
            out=b.sel[0:8, e_ * 128:(e_ + 1) * 128], in_=b.ones_bf[0:8, 0:128],
            pattern=[[0, 128]], compare_op=ALU.is_equal, fill=0.0,
            base=-e_, channel_multiplier=1), waits=[ce], inc=True)

    queues = []
    for e in range(c.E):
        queues.append(PrepQueue([
            PrepJob(em, b, cfg, b.moe_w[1][e], b.moe_w1T[e], ce,
                    after=[g_dma] if e == 0 else ()),
            PrepJob(em, b, cfg, b.moe_w[2][e], b.moe_w2T[e], ce),
            PrepJob(em, b, cfg, b.moe_w[3][e], b.moe_w3T[e], ce),
        ]))
    prep_done = {}

    def emit_prep(e, n_chunks):
        queues[e].run(n_chunks)
        if queues[e].done() and e not in prep_done:
            prep_done[e] = [em.ev("dstg0"), em.ev("dstg1")]

    emit_prep(0, -1)

    w_rd = [None, None]
    w3_free = [None, None]
    pctx_free = None
    hid_rd = None
    stage_dma = None
    msil_free = None
    mt1_free = None
    gB_free = None
    gld_free = None
    acc_dmas = [None] * c.NTILE
    for e in range(c.E):
        pe_evs = prep_done[e]
        NSL = 4
        slw = F // NSL
        l1s, l2s = [], []
        for sl_ in range(NSL):
            l1s.append(em.dma_gp(f"dw1e{sl_}",
                b.w1e[:].rearrange("p (dc f) -> p dc f", f=F)[:, :, sl_ * slw:(sl_ + 1) * slw],
                b.moe_w1T[e][:, :, sl_ * slw:(sl_ + 1) * slw],
                waits=[w_rd[0], x_tile_evs[-1]] + pe_evs))
            l2s.append(em.dma_gp(f"dw2e{sl_}",
                b.w2e[:].rearrange("p (dc f) -> p dc f", f=F)[:, :, sl_ * slw:(sl_ + 1) * slw],
                b.moe_w2T[e][:, :, sl_ * slw:(sl_ + 1) * slw],
                waits=[w_rd[1]] + pe_evs))
        for ti in range(c.NTILE):
            gl = em.dma_gp("dgld", b.gld, b.g_dram[0:8, ti * T:(ti + 1) * T],
                           waits=[gld_free, g_dma])
            gmm = em.pe(lambda e_, e=e, ti=ti: e_.matmul(
                b.pCtx[:, 0:T], b.sel[0:8, e * 128:(e + 1) * 128],
                b.gld, start=True, stop=True),
                waits=[gl, sel_ev, pctx_free], inc=True)
            gld_free = gmm
            gbe = em.v(lambda e_: e_.tensor_copy(b.gB[:], b.pCtx[:, 0:T]),
                       waits=[gmm, gB_free], inc=True)
            pctx_free = gbe
            h_ev = None
            h_evs = [None] * c.NF
            mmh1_last = None
            for fc in range(c.NF):
                bs = fc % 2
                bh1, bh2 = (0, 1) if bs == 0 else (2, 3)
                mmh1 = None
                mmh2 = None
                for dc in range(c.NC):
                    w = [l1s[fc * 128 // slw], x_tile_evs[ti]]
                    if dc == 0:
                        w.append(pa_free[bh1])
                    mmh1 = em.pe(lambda e_, fc=fc, dc=dc, ti=ti, bh1=bh1: e_.matmul(
                        b.pA[bh1][:, 0:T],
                        b.w1e[:, dc * F + fc * 128: dc * F + fc * 128 + 128],
                        b.x[:, dc * c.NT + ti * T: dc * c.NT + (ti + 1) * T],
                        start=(dc == 0), stop=(dc == c.NC - 1)),
                        waits=w, inc=(dc == c.NC - 1))
                for dc in range(c.NC):
                    w = [l2s[fc * 128 // slw]]
                    if dc == 0:
                        w.append(pa_free[bh2])
                    mmh2 = em.pe(lambda e_, fc=fc, dc=dc, ti=ti, bh2=bh2: e_.matmul(
                        b.pA[bh2][:, 0:T],
                        b.w2e[:, dc * F + fc * 128: dc * F + fc * 128 + 128],
                        b.x[:, dc * c.NT + ti * T: dc * c.NT + (ti + 1) * T],
                        start=(dc == 0), stop=(dc == c.NC - 1)),
                        waits=w, inc=(dc == c.NC - 1))
                sev = em.sc(lambda e_, bh1=bh1: e_.activation(
                    b.msil[:], b.pA[bh1][:, 0:T], AF.Silu, bias=0.0, scale=1.0),
                    waits=[mmh1, msil_free], inc=True)
                pa_free[bh1] = sev
                gpe = em.gp(lambda e_: e_.tensor_tensor(
                    b.mt1[:], b.msil[:], b.gB[:], ALU.mult),
                    waits=[sev, gbe, mt1_free], inc=True)
                msil_free = gpe
                h_ev = em.v(lambda e_, fc=fc, bh2=bh2: e_.tensor_tensor(
                    b.hid[:, fc * T:(fc + 1) * T], b.mt1[:], b.pA[bh2][:, 0:T],
                    ALU.mult), waits=[gpe, mmh2, hid_rd], inc=True)
                h_evs[fc] = h_ev
                pa_free[bh2] = h_ev
                mt1_free = h_ev
                mmh1_last = mmh1
            gB_free = gpe
            down_last = None
            stg_ev = None
            for oc in range(c.NC):
                bank = oc % 4
                w3sl = oc % 2
                l3 = em.dma_gp(f"dw3s{w3sl}",
                               b.w3s[w3sl].rearrange("p (fc m) -> p fc m", m=128),
                               b.moe_w3T[e][:, :, oc * 128:(oc + 1) * 128],
                               waits=[w3_free[w3sl]] + pe_evs)
                mm = None
                for fc in range(c.NF):
                    w = [h_evs[fc], l3]
                    if fc == 0:
                        w.append(pa_free[bank])
                    mm = em.pe(lambda e_, oc=oc, fc=fc, bank=bank, w3sl=w3sl: e_.matmul(
                        b.pA[bank][:, 0:T],
                        b.w3s[w3sl][:, fc * 128:(fc + 1) * 128],
                        b.hid[:, fc * T:(fc + 1) * T],
                        start=(fc == 0), stop=(fc == c.NF - 1)),
                        waits=w, inc=(fc == c.NF - 1))
                w3_free[w3sl] = mm
                down_last = mm
                if e == 0:
                    stg_ev = em.v(lambda e_, oc=oc, ti=ti, bank=bank: e_.tensor_tensor(
                        b.acc[:, oc * T:(oc + 1) * T], b.pA[bank][:, 0:T],
                        b.x[:, oc * c.NT + ti * T: oc * c.NT + (ti + 1) * T], ALU.add),
                        waits=[mm, stage_dma], inc=True)
                else:
                    stg_ev = em.v(lambda e_, oc=oc, bank=bank: e_.tensor_copy(
                        b.acc[:, oc * T:(oc + 1) * T], b.pA[bank][:, 0:T]),
                        waits=[mm, stage_dma], inc=True)
                pa_free[bank] = stg_ev
            hid_rd = down_last
            dma = em.dma_gp(f"dacc{ti}",
                            b.accm[:, :, ti * T:(ti + 1) * T],
                            b.acc[:, 0:c.NC * T].rearrange("p (ch t) -> p ch t", t=T),
                            waits=[stg_ev, acc_dmas[ti]],
                            accum_op=ALU.add if e > 0 else ALU.bypass)
            acc_dmas[ti] = dma
            stage_dma = dma
            if e + 1 < c.E:
                emit_prep(e + 1, 3)
        if e + 1 < c.E:
            emit_prep(e + 1, -1)
        w_rd = [mmh1_last, h_ev]

    out_dmas = []
    rb_free = None
    otm_dma = None
    for ti in range(c.NTILE):
        rb = em.dma_gp("drb", b.acc[:, 0:c.NC * T].rearrange("p (ch t) -> p ch t", t=T),
                       b.accm[:, :, ti * T:(ti + 1) * T],
                       waits=list(acc_dmas) + [rb_free])
        for sub in range(4):
            ev = None
            for ch in range(c.NC):
                slot = ch % 2
                ptv = b.pT[slot][:].bitcast(F32)
                te = em.pe(lambda e_, ch=ch, sub=sub, slot=slot, ptv=ptv: e_.transpose(
                    ptv[:, 0:128],
                    b.acc[:, ch * T + sub * 128: ch * T + sub * 128 + 128],
                    b.ident32[:]),
                    waits=[rb, em.ev("ssc"), ce, b.pT_free[slot]], inc=True)
                ev = em.sc(lambda e_, ch=ch, slot=slot, ptv=ptv: e_.copy(
                    b.otm[:, ch * 128:(ch + 1) * 128], ptv[:, 0:128]),
                    waits=[te, otm_dma], inc=True)
                b.pT_free[slot] = ev
            rb_free = ev
            s1 = em.v(lambda e_: e_.reduce_sum(b.mst[:, 3:4], b.otm, axis=AX.X),
                      waits=[ev], inc=True)
            s2 = em.sc(lambda e_: e_.activation(b.sqf, b.otm, AF.Square,
                                                bias=0.0, scale=1.0,
                                                accum_out=b.mst[:, 4:5]),
                       waits=[ev], inc=True)
            em.v(lambda e_: e_.tensor_scalar(b.mst[:, 5:6], b.mst[:, 3:4],
                                             1.0 / c.D, None, ALU.mult), waits=[s1])
            em.v(lambda e_: e_.tensor_scalar(b.mst[:, 6:7], b.mst[:, 4:5],
                                             1.0 / c.D, None, ALU.mult), waits=[s2])
            em.v(lambda e_: e_.tensor_tensor(b.mst[:, 7:8], b.mst[:, 5:6],
                                             b.mst[:, 5:6], ALU.mult))
            em.v(lambda e_: e_.tensor_sub(b.mst[:, 6:7], b.mst[:, 6:7], b.mst[:, 7:8]))
            vv = em.v(lambda e_: e_.tensor_scalar(b.mst[:, 6:7], b.mst[:, 6:7],
                                                  1e-5, None, ALU.add))
            sq = em.sc(lambda e_: e_.activation(b.mst[:, 7:8], b.mst[:, 6:7], AF.Sqrt,
                                                bias=0.0, scale=1.0), waits=[vv], inc=True)
            rr = em.v(lambda e_: e_.reciprocal(b.mst[:, 8:9], b.mst[:, 7:8]),
                      waits=[sq])
            ap = em.v(lambda e_: e_.tensor_scalar(
                b.otm, b.otm, b.mst[:, 5:6], b.mst[:, 8:9],
                ALU.subtract, ALU.mult), waits=[rr])
            r0 = ti * T + sub * 128
            otm_dma = em.dma_sy("dout0", xoflat[r0:r0 + 128, :], b.otm, waits=[ap])
            out_dmas.append(otm_dma)
    return out_dmas



from concourse.bass_utils import run_bass_kernel_spmd

_PROG_CACHE = {}


def _get_prog():
    if "fused" not in _PROG_CACHE:
        cfg = Cfg(SEQ=2048, NSEQ=2)
        _PROG_CACHE["fused"] = build_program(cfg, parts=["l0a", "l0f", "l1a", "moe"])
    return _PROG_CACHE["fused"]


def kernel(L0_attn_x, L0_attn_Wq, L0_attn_Wk, L0_attn_Wv, L0_attn_Wo,
           L0_ffn_w_gate, L0_ffn_w_up, L0_ffn_w_down,
           L1_attn_Wq, L1_attn_Wk, L1_attn_Wv,
           L1_ffn_Wg, L1_ffn_W1, L1_ffn_W2, L1_ffn_W3):
    x = np.ascontiguousarray(np.asarray(L0_attn_x, dtype=np.float32))
    f32 = lambda a: np.ascontiguousarray(np.asarray(a, dtype=np.float32))
    wmap = dict(
        l0_wq=f32(L0_attn_Wq), l0_wk=f32(L0_attn_Wk),
        l0_wv=f32(L0_attn_Wv), l0_wo=f32(L0_attn_Wo),
        l0_wgate=f32(L0_ffn_w_gate), l0_wup=f32(L0_ffn_w_up),
        l0_wdown=f32(L0_ffn_w_down),
        l1_wq=f32(L1_attn_Wq), l1_wk=f32(L1_attn_Wk), l1_wv=f32(L1_attn_Wv),
        l1_wg=f32(L1_ffn_Wg), l1_w1=f32(L1_ffn_W1),
        l1_w2=f32(L1_ffn_W2), l1_w3=f32(L1_ffn_W3))
    nc = _get_prog()
    in_maps = []
    for c in range(8):
        m = dict(x_in=x[2 * c:2 * c + 2])
        m.update(wmap)
        in_maps.append(m)
    res = run_bass_kernel_spmd(nc, in_maps, core_ids=list(range(8)))
    return np.concatenate([np.asarray(r["x_out"]) for r in res.results], axis=0)


# version 06
# speedup vs baseline: 1.3620x; 1.0138x over previous
import numpy as np
from contextlib import ExitStack

import concourse.bass as bass
import concourse.mybir as mybir

F32 = mybir.dt.float32
BF16 = mybir.dt.bfloat16
AF = mybir.ActivationFunctionType
ALU = mybir.AluOpType
AX = mybir.AxisListType

NEG = -1e30


def _ap(x):
    try:
        return x[:]
    except Exception:
        return x



ENGINES = ("pe", "v", "sc", "gp", "sy")


class Em:

    def __init__(self, nc):
        self.nc = nc
        self.streams = {e: [] for e in ENGINES}
        self.cnt = {}
        self.waited = {e: {} for e in ENGINES}
        self.sems = {}

    def ev(self, sem):
        return (sem, self.cnt.get(sem, 0))

    AUTOCHAIN = {"v": "sv", "sc": "ssc", "gp": "sgp"}

    def emit(self, eng, fn, waits=(), inc=None):
        waits = [w for w in waits if w is not None]
        ac = self.AUTOCHAIN.get(eng)
        if ac is not None:
            waits = list(waits) + [(ac, self.cnt.get(ac, 0))]
            if inc is None:
                inc = (ac, 1)
        real = []
        for (s, v) in waits:
            if v > self.waited[eng].get(s, 0):
                self.waited[eng][s] = v
                real.append((s, v))
        evt = None
        if inc is not None:
            s, a = inc
            self.cnt[s] = self.cnt.get(s, 0) + a
            evt = (s, self.cnt[s])
        self.streams[eng].append((real, fn, inc))
        return evt

    def pe(self, fn, waits=(), inc=False):
        return self.emit("pe", fn, waits, ("spe", 1) if inc else None)

    def v(self, fn, waits=(), inc=False):
        return self.emit("v", fn, waits, ("sv", 1) if inc else None)

    def sc(self, fn, waits=(), inc=False):
        return self.emit("sc", fn, waits, ("ssc", 1) if inc else None)

    def gp(self, fn, waits=(), inc=False):
        return self.emit("gp", fn, waits, ("sgp", 1) if inc else None)

    def phase_barrier(self):
        waits = [(s, v) for s, v in self.cnt.items() if v > 0]
        for eng in ENGINES:
            fn = (lambda e: e.nop())
            self.emit(eng, fn, waits=list(waits))

    def dma_sy(self, sem, out, in_, waits=()):
        return self.emit("sy", lambda e: e.dma_start(out=out, in_=in_), waits, (sem, 16))

    def dma_gp(self, sem, out, in_, waits=(), **kw):
        return self.emit("gp", lambda e: e.dma_start(out=out, in_=in_, **kw), waits, (sem, 16))

    def play(self, block):
        eng_attr = {"pe": "tensor", "v": "vector", "sc": "scalar", "gp": "gpsimd", "sy": "sync"}
        for name in ENGINES:
            stream = self.streams[name]

            def body(eng, stream=stream):
                for waits, fn, inc in stream:
                    for s, v in waits:
                        eng.wait_ge(self.sems[s], v)
                    ins = fn(eng)
                    if inc is not None:
                        ins.then_inc(self.sems[inc[0]], inc[1])

            getattr(block, eng_attr[name])(body)


class Cfg:
    def __init__(self, SEQ=2048, NSEQ=2):
        self.SEQ = SEQ
        self.NSEQ = NSEQ
        self.NT = SEQ * NSEQ
        self.D = 1024
        self.NC = 8
        self.HEADS = 16
        self.HD = 64
        self.WIN = 128
        self.G = 64
        self.I = 2816
        self.NI = 22
        self.E = 8
        self.F = 2048
        self.NF = 16
        self.TT = 512
        self.NTILE = self.NT // self.TT
        self.WPT = self.TT // self.WIN
        self.NWIN = SEQ // self.WIN
        self.LW = 512




class Bufs:
    pass


def alloc_bufs(nc, ctx, cfg, parts):
    b = Bufs()
    sb = lambda name, shape, dt: ctx.enter_context(nc.sbuf_tensor(name, shape, dt))
    ps = lambda name, shape, dt: ctx.enter_context(nc.psum_tensor(name, shape, dt))
    c = cfg

    b.x = sb("x_sb", [128, c.NC * c.NT], BF16)

    b.ident = sb("ident_bf", [128, 128], BF16)
    b.ones_bf = sb("ones_bf", [128, 128], BF16)
    b.ident32 = sb("ident_f32", [128, 128], F32) if "moe" in parts else None
    b.ones32 = None
    if any(p in parts for p in ("l0a", "l0f", "l1a")):
        b.onesc32 = sb("onesc32", [128, 1], F32)
        b.onescbf = sb("onescbf", [128, 1], BF16)
        b.onesr32 = sb("onesr32", [1, 128], F32)
    else:
        b.onesc32 = None

    b.pA = [ps(f"pA{i}", [128, 512], F32) for i in range(4)]
    b.pCtx = ps("pCtx", [128, 1024], F32)
    b.pT = [ps("pTa", [128, 256], BF16), ps("pTb", [128, 256], BF16)]

    b.xin = None

    b.acc = sb("acc_fm", [128, c.NC * c.TT], F32)
    if any(p in parts for p in ("l0a", "l0f", "l1a")):
        b.sq = sb("sq_bf", [128, 2 * c.TT], BF16)
        b.strow = sb("strow", [1, 3 * c.TT], F32)

    if "moe" in parts:
        b.w1e = sb("w1e", [128, c.NC * c.F], BF16)
        b.w2e = sb("w2e", [128, c.NC * c.F], BF16)
        b.w3s = [sb(f"w3s{i}", [128, c.NF * 128], BF16)[:] for i in range(2)]
        b.scores = sb("scores_m", [8, c.NT], BF16)
        b.g_fm = b.scores
        b.sel = sb("sel_m", [8, 8 * 128], BF16)
        b.hid = sb("hid_m", [128, c.NF * 512], BF16)
        b.gB = sb("gB_m", [128, 512], F32)
        b.msil = sb("msil", [128, 512], BF16)
        b.mt1 = sb("mt1", [128, 512], BF16)

        b.mst = sb("mst", [128, 12], F32)
        b.sctm = sb("sctm", [128, 8], BF16)
        b.top8 = sb("top8", [128, 8], BF16)
        b.rep = sb("rep", [128, 8], BF16)
        b.ms2 = sb("ms2", [128, 8], BF16)
        b.ms3 = sb("ms3", [128, 8], BF16)
        b.mk1 = sb("mk1", [128, 8], F32)
        b.mk2 = sb("mk2", [128, 8], F32)
        b.gtm = sb("gtm", [128, 8], F32)
        b.gbf = sb("gbf", [128, 8], BF16)
        b.wgT = sb("wgT", [128, c.NC * 8], BF16)
        hv = b.hid[:].bitcast(F32)
        b.sqf = hv[:, 0:c.D]
        b.wg_sb = hv[0:8, c.D:2 * c.D]
        b.otm = hv[:, 2 * c.D:3 * c.D]
        b.wload = b.hid[:].bitcast(BF16)[:, 4 * c.D:6 * c.D]
        gv = b.gB[:].bitcast(BF16)
        b.wtb = [gv[:, 0:128], gv[:, 128:256]]


    if "l0a" in parts or "l1a" in parts or "moe" in parts:
        nw = 4 if "l0a" in parts else 0
        b.wT = [sb(f"wT{i}", [128, c.NC * c.D], BF16) for i in range(nw)]
        if "moe" not in parts:
            b.wload = sb("wload", [128, c.NC // 4 * c.D], BF16)
    if "l0a" in parts:
        b.qb = sb("qb", [128, c.NC * c.TT], BF16)
        b.kb = sb("kb", [128, c.NC * c.TT], BF16)
        b.vb = sb("vb", [128, c.NC * c.TT], BF16)

    if "l1a" in parts:
        b.kS = sb("kS", [128, c.NC * c.SEQ], BF16)
        b.vS = sb("vS", [128, (c.SEQ // 128) * c.D], BF16)
        b.qg = sb("qg", [128, c.NC * 512], BF16)
        b.vtmp = sb("vtmp", [128, 512], BF16)
        b.cmask = sb("cmask", [128, 4 * 512], BF16)
        b.probs1 = sb("probs1", [128, 4 * 512], BF16)
        b.probsT1 = sb("probsT1", [128, 16 * 128], BF16)
        b.nm1 = sb("nm1", [128, 16], F32)
        b.wstg = [sb(f"wstg{i}", [128, c.NC * 128], BF16)[:] for i in range(2)]
        b.wstg_free = [None, None]
        b.wload2 = [sb(f"wload2_{i}", [128, c.NC // 4 * c.D], BF16)[:] for i in range(2)]
        b.wload_free = [None, None]
        b.wload_par = 0
        b.pT_free = [None, None]
        b.wstr = [sb(f"wstr{i}", [128, c.NC * 128], BF16) for i in range(2)]

    if "l0f" in parts:
        b.wq_in = sb("wq_in", [128, c.I], F32)
        b.wq_in_b = sb("wq_in_b", [128, c.I], F32)
        b.wtb = [sb(f"wtb{i}", [128, 128], BF16) for i in range(2)]
        b.qsets = []
        for i in range(2):
            b.qsets.append((sb(f"qs_xn{i}", [128, c.LW], F32)[:],
                            sb(f"qs_g{i}", [128, c.LW], F32)[:],
                            sb(f"qs_a{i}", [128, c.LW], F32)[:],
                            sb(f"qs_c{i}", [128, c.LW], F32)[:],
                            sb(f"qs_sc{i}", [128, 6 * (c.LW // 16)], F32)[:]))
        b.qs_h2 = [sb(f"qs_h{i}", [128, c.LW], F32)[:] for i in range(2)]
        b.xtm32 = None
        b.xqtm = None
        b.xq_fm = sb("xq_fm", [128, c.NC * c.TT], BF16)
        b.hb = sb("hb", [128, 4 * c.I], BF16)
        b.hq_tm = sb("hq_tm", [128, c.I], BF16)
        b.hq_fm = sb("hq_fm", [128, c.NI * c.TT], BF16)
        b.wg_s = [sb("wg_s0", [128, c.NC * 512], BF16)] * 2
        b.wup_s = [sb("wup_s0", [128, c.NC * 512], BF16)] * 2
        b.wd_s = [sb("wd_s0", [128, c.NI * 128], BF16)] * 2
        b.silu = sb("silu_b", [128, 512], BF16)
        b.wq_in2 = [b.wq_in[:], b.wq_in_b[:]]
        wiv = b.wq_in[:].bitcast(BF16)
        wiv2 = b.wq_in_b[:].bitcast(BF16)
        b.xin = [wiv[:, 0:c.D], wiv[:, c.D:2 * c.D]]
        b.xtm32_2 = [b.wq_in[:][:, 0:c.D], b.wq_in_b[:][:, 0:c.D]]
        b.xqtm2 = [wiv[:, 2 * c.D:3 * c.D], wiv2[:, 2 * c.D:3 * c.D]]
        hqv = b.hq_fm[:]
        b.wstg = [hqv[:, 0:c.NI * 128], hqv[:, c.NI * 128:2 * c.NI * 128]]
        b.wstg_free = [None, None]
        b.wq_out2 = [b.hq_tm[:], hqv[:, 2 * c.NI * 128:2 * c.NI * 128 + c.I]]

    if "l0a" in parts:
        b.kg = [sb(f"kg{s}", [128, c.NC * c.G], BF16) for s in range(c.NSEQ)]
        b.vgT = [sb(f"vgT{s}", [c.G, c.D], BF16) for s in range(c.NSEQ)]
        b.vwT = sb("vwT", [128, c.D], BF16)
        b.expb = [sb(f"expb{i}", [128, c.G + c.WIN], F32) for i in range(2)]
        b.nm = sb("nm_sums", [128, 8], F32)
        b.probs = [sb(f"probs{i}", [128, c.G + c.WIN], BF16) for i in range(2)]
        b.probsT = [sb(f"probsT{i}", [128, 256], BF16) for i in range(2)]
        b.ctxb = sb("ctxb", [128, c.NC * c.TT], BF16)

    if b.xin is None and "moe" in parts:
        w1v = b.w1e[:]
        b.xin = [w1v[:, 0:c.D], w1v[:, c.D:2 * c.D]]
    if b.xin is None:
        b.xin = [sb(f"xin{i}", [128, c.D], BF16)[:] for i in range(2)]
    return b




class Carver:
    def __init__(self, base_ap, limit_bytes):
        self.base = base_ap
        self.off = 0
        self.limit = limit_bytes

    def take(self, dt, cols, parts=128):
        sz = 4 if dt == F32 else 2
        off = (self.off + 31) // 32 * 32
        self.off = off + cols * sz
        assert self.off <= self.limit, f"arena overflow: {self.off} > {self.limit}"
        return self.base.bitcast(dt)[0:parts, off // sz: off // sz + cols]


def alloc_bufs_fused(nc, ctx, cfg):
    b = Bufs()
    c = cfg
    sb = lambda name, shape, dt: ctx.enter_context(nc.sbuf_tensor(name, shape, dt))
    ps = lambda name, shape, dt: ctx.enter_context(nc.psum_tensor(name, shape, dt))

    b.x = sb("x_sb", [128, c.NC * c.NT], BF16)
    b.acc = sb("acc_fm", [128, c.NC * c.TT], F32)
    b.sq = sb("sq_bf", [128, 2 * c.TT], BF16)
    b.strow = sb("strow", [1, 3 * c.TT], F32)
    b.ident = sb("ident_bf", [128, 128], BF16)
    b.ident32 = sb("ident_f32", [128, 128], F32)
    b.ones_bf = sb("ones_bf", [128, 128], BF16)
    b.ones32 = None
    b.onesc32 = sb("onesc32", [128, 1], F32)
    b.onescbf = sb("onescbf", [128, 1], BF16)
    b.onesr32 = sb("onesr32", [1, 128], F32)
    b.xin = [sb(f"xin{i}", [128, c.D], BF16)[:] for i in range(2)]

    b.pA = [ps(f"pA{i}", [128, 512], F32) for i in range(4)]
    b.pCtx = ps("pCtx", [128, 1024], F32)
    b.pT = [ps("pTa", [128, 256], BF16), ps("pTb", [128, 256], BF16)]

    rem = nc.sbuf_bytes_remaining - 64
    AR = sb("arena", [128, rem // 2 // 32 * 32], BF16)[:]
    arena_bytes = AR.shape[1] * 2
    b.ph = {}

    cv = Carver(AR, arena_bytes)
    d = {}
    d["wT"] = [cv.take(BF16, c.NC * c.D) for _ in range(4)]
    d["qb"] = cv.take(BF16, c.NC * c.TT)
    d["kb"] = cv.take(BF16, c.NC * c.TT)
    d["vb"] = cv.take(BF16, c.NC * c.TT)
    d["ctxb"] = cv.take(BF16, c.NC * c.TT)
    d["wload"] = d["ctxb"][:, 0:c.NC // 4 * c.D]
    d["kg"] = [cv.take(BF16, c.NC * c.G) for _ in range(c.NSEQ)]
    d["vgT"] = [cv.take(BF16, c.D, parts=c.G) for _ in range(c.NSEQ)]
    d["vwT"] = cv.take(BF16, c.D)
    d["expb"] = [cv.take(F32, c.G + c.WIN) for _ in range(2)]
    d["probs"] = [cv.take(BF16, c.G + c.WIN) for _ in range(2)]
    d["probsT"] = [cv.take(BF16, 256) for _ in range(2)]
    d["nm"] = cv.take(F32, 8)
    d["x32st"] = cv.take(F32, c.D)
    b.ph["l0a"] = d

    cv = Carver(AR, arena_bytes)
    d = {}
    qsets = []
    for _ in range(2):
        qsets.append((cv.take(F32, c.LW), cv.take(F32, c.LW), cv.take(F32, c.LW),
                      cv.take(F32, c.LW), cv.take(F32, 6 * (c.LW // 16))))
    d["qsets"] = qsets
    d["qs_h2"] = [cv.take(F32, c.LW) for _ in range(2)]
    d["xtm32_2"] = [cv.take(F32, c.D) for _ in range(2)]
    d["xqtm2"] = [cv.take(BF16, c.D) for _ in range(2)]
    d["xq_fm"] = cv.take(BF16, c.NC * c.TT)
    hb = cv.take(BF16, 4 * c.I)
    d["hb"] = hb
    d["wq_in2"] = [hb.bitcast(F32)[:, 0:c.I], hb.bitcast(F32)[:, c.I:2 * c.I]]
    d["hq_tm"] = cv.take(BF16, c.I)
    hq_fm = cv.take(BF16, c.NI * c.TT)
    d["hq_fm"] = hq_fm
    d["wq_out2"] = [d["hq_tm"][:] if hasattr(d["hq_tm"], "ap") else d["hq_tm"],
                    hq_fm[:, 2 * c.NI * 128:2 * c.NI * 128 + c.I]]
    d["wstg"] = [hq_fm[:, 0:c.NI * 128], hq_fm[:, c.NI * 128:2 * c.NI * 128]]
    d["wstg_free"] = [None, None]
    wg0 = cv.take(BF16, c.NC * 512)
    wu0 = cv.take(BF16, c.NC * 512)
    wd0 = cv.take(BF16, c.NI * 128)
    d["wg_s"] = [wg0, wg0]
    d["wup_s"] = [wu0, wu0]
    d["wd_s"] = [wd0, wd0]
    d["silu"] = cv.take(BF16, 512)
    b.ph["l0f"] = d

    cv = Carver(AR, arena_bytes)
    d = {}
    d["kS"] = cv.take(BF16, c.NC * c.SEQ)
    d["vS"] = cv.take(BF16, (c.SEQ // 128) * c.D)
    d["qg"] = cv.take(BF16, c.NC * 512)
    d["vtmp"] = cv.take(BF16, 512)
    d["cmask"] = cv.take(BF16, 4 * 512)
    d["probs1"] = cv.take(BF16, 4 * 512)
    d["probsT1"] = cv.take(BF16, 16 * 128)
    d["nm1"] = cv.take(F32, 16)
    d["wstr"] = [cv.take(BF16, c.NC * 128) for _ in range(2)]
    d["wstg"] = [cv.take(BF16, c.NC * 128) for _ in range(2)]
    d["wstg_free"] = [None, None]
    d["wload2"] = [cv.take(BF16, c.NC // 4 * c.D) for _ in range(2)]
    d["wload_free"] = [None, None]
    d["wload_par"] = 0
    d["pT_free"] = [None, None]
    b.ph["l1a"] = d

    cv = Carver(AR, arena_bytes)
    d = {}
    d["w1e"] = cv.take(BF16, c.NC * c.F)
    d["w2e"] = cv.take(BF16, c.NC * c.F)
    d["w3s"] = [cv.take(BF16, c.NF * 128) for _ in range(2)]
    hid = cv.take(BF16, c.NF * 512)
    d["hid"] = hid
    hv = hid.bitcast(F32)
    d["sqf"] = hv[:, 0:c.D]
    d["wg_sb"] = hv[0:8, c.D:2 * c.D]
    d["otm"] = hv[:, 2 * c.D:3 * c.D]
    R = cv.take(BF16, 3 * c.NC // 4 * c.D)
    d["scores"] = R[0:8, 0:c.NT]
    d["g_fm"] = d["scores"]
    d["wload2"] = [R[:, 0:c.NC // 4 * c.D], cv.take(BF16, c.NC // 4 * c.D)]
    d["wload_free"] = [None, None]
    d["wload_par"] = 0
    d["pT_free"] = [None, None]
    d["wstg"] = [R[:, c.NC // 4 * c.D:2 * (c.NC // 4) * c.D],
                 R[:, 2 * (c.NC // 4) * c.D:3 * (c.NC // 4) * c.D]]
    d["wstg_free"] = [None, None]
    d["gld"] = cv.take(BF16, 512, parts=8)
    d["sel"] = cv.take(BF16, 8 * 128, parts=8)
    gB = cv.take(F32, 512)
    d["gB"] = gB
    d["msil"] = cv.take(BF16, 512)
    d["mt1"] = cv.take(BF16, 512)
    d["mst"] = cv.take(F32, 12)
    d["sctm"] = cv.take(BF16, 8)
    d["top8"] = cv.take(BF16, 8)
    d["rep"] = cv.take(BF16, 8)
    d["ms2"] = cv.take(BF16, 8)
    d["ms3"] = cv.take(BF16, 8)
    d["mk1"] = cv.take(F32, 8)
    d["mk2"] = cv.take(F32, 8)
    d["gtm"] = cv.take(F32, 8)
    d["gbf"] = cv.take(BF16, 8)
    d["wgT"] = cv.take(BF16, c.NC * 8)
    b.ph["moe"] = d
    return b


def phase_bufs(b, name):
    nb = Bufs()
    nb.__dict__.update({k: v for k, v in b.__dict__.items() if k != "ph"})
    nb.__dict__.update(b.ph[name])
    return nb




def emit_consts(em, b, cfg):
    e1 = em.gp(lambda e: e.memset(b.ones_bf[:], 1.0), inc=True)
    em.gp(lambda e: e.affine_select(
        out=b.ident[:], in_=b.ones_bf[:], pattern=[[-1, 128]],
        compare_op=ALU.is_equal, fill=0.0, base=0, channel_multiplier=1), waits=[e1], inc=True)
    if b.ident32 is not None:
        e2 = em.v(lambda e: e.memset(b.acc[:, 0:128], 1.0), inc=True)
        em.gp(lambda e: e.affine_select(
            out=b.ident32[:], in_=b.acc[:, 0:128], pattern=[[-1, 128]],
            compare_op=ALU.is_equal, fill=0.0, base=0, channel_multiplier=1), waits=[e2], inc=True)
    if getattr(b, "onesc32", None) is not None:
        em.v(lambda e: e.memset(b.onesc32[:], 1.0), inc=True)
        em.v(lambda e: e.memset(b.onescbf[:], 1.0), inc=True)
        em.v(lambda e: e.memset(b.onesr32[:], 1.0), inc=True)
    return em.gp(lambda e: e.engine_nop(), waits=[em.ev("sv"), em.ev("sgp")], inc=True)


def emit_ingest(em, b, cfg, x_in):
    c = cfg
    xflat = x_in.rearrange("s t d -> (s t) d")
    tile_ev = []
    prev_use = [None, None]
    for ti in range(c.NT // 128):
        buf = ti % 2
        w = [prev_use[buf]] if prev_use[buf] else []
        ld = em.dma_gp(f"dxi{buf}", b.xin[buf], xflat[ti * 128:(ti + 1) * 128, :], waits=w)
        evs = None
        for ch in range(c.NC):
            slot = ch % 2
            te = em.pe(lambda e, ch=ch, buf=buf, slot=slot: e.transpose(
                b.pT[slot][:, 0:128],
                b.xin[buf][:, ch * 128:(ch + 1) * 128], b.ident[:]),
                waits=[ld, em.ev("ssc")], inc=True)
            evs = em.sc(lambda e, ch=ch, ti=ti, slot=slot: e.copy(
                b.x[:, ch * c.NT + ti * 128: ch * c.NT + (ti + 1) * 128],
                b.pT[slot][:, 0:128]),
                waits=[te], inc=True)
        prev_use[buf] = evs
        if (ti + 1) % (c.TT // 128) == 0:
            tile_ev.append(evs)
    return tile_ev


def _consts_wait(em, ce):
    return [ce]


def emit_weight_load_T(em, b, cfg, w_dram, dst, ce, ready_after=()):
    c = cfg
    O = w_dram.shape[0]
    D = w_dram.shape[1]
    noc = O // 128
    ndc = D // 128
    hoc = max(1, (c.NC // 4 * c.D) // D)
    wre = w_dram.rearrange("(oc p) d -> p oc d", p=128)
    last = None
    prev_reads = list(ready_after)
    for og in range(0, noc, hoc):
        og_n = min(hoc, noc - og)
        ld = em.dma_gp("dwl", b.wload[:, 0:og_n * D].rearrange("p (oc d) -> p oc d", d=D),
                       wre[:, og:og + og_n, :], waits=prev_reads)
        te = None
        for oc in range(og, og + og_n):
            for dc in range(ndc):
                slot = (oc * ndc + dc) % 2
                te = em.pe(lambda e, oc=oc, dc=dc, slot=slot, og=og: e.transpose(
                    b.pT[slot][:, 0:128],
                    b.wload[:, (oc - og) * D + dc * 128: (oc - og) * D + (dc + 1) * 128],
                    b.ident[:]),
                    waits=[ld, ce, em.ev("ssc")], inc=True)
                last = em.sc(lambda e, oc=oc, dc=dc, slot=slot: e.copy(
                    dst[:, dc * O + oc * 128: dc * O + (oc + 1) * 128],
                    b.pT[slot][:, 0:128]),
                    waits=[te], inc=True)
        prev_reads = [te]
    return last




def emit_ln_fm(em, b, cfg, ti, acc_ev, prev_ln_ev=None, extra_waits=(), keep_acc=False):
    c = cfg
    T = c.TT
    s1 = None
    for ch in range(c.NC):
        s1 = em.pe(lambda e, ch=ch: e.matmul(
            b.pA[2][0:1, 0:T], b.onesc32[:], b.acc[:, ch * T:(ch + 1) * T],
            start=(ch == 0), stop=(ch == c.NC - 1)),
            waits=[acc_ev] + ([prev_ln_ev] if prev_ln_ev else [])
                  + (list(extra_waits) if ch == 0 else []),
            inc=(ch == c.NC - 1))
    s2 = None
    sq_evs = [None, None]
    for ch in range(c.NC):
        sl = ch % 2
        sq_evs[sl] = em.sc(lambda e, ch=ch, sl=sl: e.activation(
            b.sq[:, sl * T:(sl + 1) * T], b.acc[:, ch * T:(ch + 1) * T],
            AF.Square, bias=0.0, scale=1.0),
            waits=[acc_ev, s2], inc=True)
        s2 = em.pe(lambda e, ch=ch, sl=sl: e.matmul(
            b.pA[3][0:1, 0:T], b.onescbf[:], b.sq[:, sl * T:(sl + 1) * T],
            start=(ch == 0), stop=(ch == c.NC - 1)),
            waits=[sq_evs[sl]] + ([prev_ln_ev] if prev_ln_ev and ch == 0 else []),
            inc=True)
    st = b.strow
    t1 = st[0:1, 0:T]
    m_ = st[0:1, T:2 * T]
    t2 = st[0:1, 2 * T:3 * T]
    mr = t2
    em.v(lambda e: e.tensor_scalar(m_, b.pA[2][0:1, 0:T], 1.0 / c.D, None, ALU.mult),
         waits=[s1], inc=True)
    em.v(lambda e: e.tensor_scalar(t1, b.pA[3][0:1, 0:T], 1.0 / c.D, None, ALU.mult),
         waits=[s2], inc=True)
    em.v(lambda e: e.tensor_tensor(t2, m_, m_, ALU.mult), inc=True)
    em.v(lambda e: e.tensor_sub(t1, t1, t2), inc=True)
    ve = em.v(lambda e: e.tensor_scalar(t1, t1, 1e-5, None, ALU.add), inc=True)
    se = em.sc(lambda e: e.activation(t2, t1, AF.Sqrt, bias=0.0, scale=1.0),
               waits=[ve], inc=True)
    re_ = em.v(lambda e: e.reciprocal(t1, t2), waits=[se], inc=True)
    mre = em.v(lambda e: e.tensor_tensor(mr, m_, t1, ALU.mult), waits=[re_], inc=True)
    rstd = t1
    bc1 = em.pe(lambda e: e.matmul(b.pA[2][:, 0:T], b.onesr32[:], rstd,
                                   start=True, stop=True), waits=[re_, mre], inc=True)
    bc2 = em.pe(lambda e: e.matmul(b.pA[3][:, 0:T], b.onesr32[:], mr,
                                   start=True, stop=True), waits=[mre], inc=True)
    out_ev = None
    for ch in range(c.NC):
        em.v(lambda e, ch=ch: e.tensor_tensor(
            b.acc[:, ch * T:(ch + 1) * T],
            b.acc[:, ch * T:(ch + 1) * T], b.pA[2][:, 0:T], ALU.mult),
            waits=[bc1], inc=True)
        if keep_acc:
            em.v(lambda e, ch=ch: e.tensor_tensor(
                b.acc[:, ch * T:(ch + 1) * T],
                b.acc[:, ch * T:(ch + 1) * T],
                b.pA[3][:, 0:T], ALU.subtract),
                waits=[bc2], inc=True)
            out_ev = em.v(lambda e, ch=ch, ti=ti: e.tensor_copy(
                b.x[:, ch * c.NT + ti * T: ch * c.NT + (ti + 1) * T],
                b.acc[:, ch * T:(ch + 1) * T]), inc=True)
        else:
            out_ev = em.v(lambda e, ch=ch, ti=ti: e.tensor_tensor(
                b.x[:, ch * c.NT + ti * T: ch * c.NT + (ti + 1) * T],
                b.acc[:, ch * T:(ch + 1) * T],
                b.pA[3][:, 0:T], ALU.subtract),
                waits=[bc2], inc=True)
    return out_ev




def emit_egress(em, b, cfg, x_out, waits):
    c = cfg
    eg = [b.acc[:, 0:c.D], b.acc[:, c.D:2 * c.D]]
    xflat = x_out.rearrange("s t d -> (s t) d")
    dmas = []
    prev_dma = [None, None]
    for ti in range(c.NT // 128):
        sl = ti % 2
        ev = None
        for ch in range(c.NC):
            slot = ch % 2
            te = em.pe(lambda e, ch=ch, ti=ti, slot=slot: e.transpose(
                b.pT[slot][:, 0:128],
                b.x[:, ch * c.NT + ti * 128: ch * c.NT + (ti + 1) * 128],
                b.ident[:]),
                waits=list(waits) + [em.ev("ssc")], inc=True)
            ev = em.sc(lambda e, ch=ch, slot=slot, sl=sl: e.copy(
                eg[sl][:, ch * 128:(ch + 1) * 128],
                b.pT[slot][:, 0:128]),
                waits=[te, prev_dma[sl]], inc=True)
        dma = em.dma_sy(f"dout{sl}", xflat[ti * 128:(ti + 1) * 128, :],
                        eg[sl], waits=[ev])
        prev_dma[sl] = dma
        dmas.append(dma)
    return dmas




def emit_l0_attn(em, b, cfg, ce, x_tile_evs, prev_ln_ev=None, x32_dram=None):
    c = cfg
    T = c.TT
    wq_d, wk_d, wv_d, wo_d = b.l0a_w
    wq_ev = emit_weight_load_T(em, b, cfg, wq_d, b.wT[0][:], ce)
    wk_ev = emit_weight_load_T(em, b, cfg, wk_d, b.wT[1][:], ce, ready_after=[wq_ev])
    wv_ev = emit_weight_load_T(em, b, cfg, wv_d, b.wT[2][:], ce, ready_after=[wk_ev])
    wo_ev = emit_weight_load_T(em, b, cfg, wo_d, b.wT[3][:], ce, ready_after=[wv_ev])

    tiles_per_seq = c.SEQ // T
    ln_evs = []
    kg_ev = [None] * c.NSEQ
    vg_ev = [None] * c.NSEQ
    pSc_free = [None, None]
    probs_free = [None, None]
    expb_free = [None, None]
    probsT_free = [None, None]
    pT_free = [None, None]
    pCtx_free = None
    acc_free = None
    x32d = None
    prev_qkv_evac = None

    for ti in range(c.NTILE):
        s = ti // tiles_per_seq
        xev = x_tile_evs[ti]
        evac_evs = []
        for dst_i, (dst, wti, wev) in enumerate(
                ((b.qb, 0, wq_ev), (b.kb, 1, wk_ev), (b.vb, 2, wv_ev))):
            for oc in range(c.NC):
                bank = (dst_i * c.NC + oc) % 2
                mm = None
                for dc in range(c.NC):
                    w = [xev, wev]
                    if dc == 0:
                        w.append(pSc_free[bank] or em.ev("sv"))
                    mm = em.pe(lambda e, wti=wti, oc=oc, dc=dc, bank=bank, ti=ti: e.matmul(
                        b.pA[bank][:, 0:T],
                        b.wT[wti][:, dc * c.D + oc * 128: dc * c.D + (oc + 1) * 128],
                        b.x[:, dc * c.NT + ti * T: dc * c.NT + (ti + 1) * T],
                        start=(dc == 0), stop=(dc == c.NC - 1)),
                        waits=w, inc=(dc == c.NC - 1))
                if dst_i == 0:
                    ev = em.v(lambda e, oc=oc, bank=bank: e.tensor_scalar(
                        b.qb[:, oc * T:(oc + 1) * T], b.pA[bank][:, 0:T],
                        0.125, None, ALU.mult), waits=[mm], inc=True)
                else:
                    ev = em.sc(lambda e, dst=dst, oc=oc, bank=bank: e.copy(
                        dst[:, oc * T:(oc + 1) * T], b.pA[bank][:, 0:T]),
                        waits=[mm], inc=True)
                pSc_free[bank] = ev
                evac_evs.append(ev)
        qkv_ev = evac_evs[-1]
        qkv_all = [em.ev("sv"), em.ev("ssc")]

        for w4 in range(c.WPT):
            widx = ti * c.WPT + w4
            wi = widx % (c.SEQ // c.WIN)
            t0 = w4 * c.WIN
            vw_evs = []
            for chk in range(c.NC):
                slot = chk % 2
                te = em.pe(lambda e, chk=chk, slot=slot, t0=t0: e.transpose(
                    b.pT[slot][:, 0:128],
                    b.vb[:, chk * T + t0: chk * T + t0 + c.WIN],
                    b.ident[:]),
                    waits=qkv_all + [pT_free[slot] or em.ev("ssc")], inc=True)
                ev = em.sc(lambda e, chk=chk, slot=slot: e.copy(
                    b.vwT[:, chk * 128:(chk + 1) * 128],
                    b.pT[slot][:, 0:128]),
                    waits=[te], inc=True)
                pT_free[slot] = ev
                vw_evs.append(ev)
            vw_ev = vw_evs[-1]

            if wi == 0:
                kg_ev[s] = em.v(lambda e, s=s, t0=t0: e.tensor_copy(
                    b.kg[s][:].rearrange("p (c g) -> p c g", g=c.G),
                    b.kb[:].rearrange("p (c t) -> p c t", t=T)[:, :, t0:t0 + c.G]),
                    waits=qkv_all, inc=True)
                gevs = []
                for chk in range(c.NC):
                    slot = chk % 2
                    te = em.pe(lambda e, chk=chk, slot=slot, t0=t0: e.transpose(
                        b.pT[slot][0:c.G, 0:128],
                        b.vb[:, chk * T + t0: chk * T + t0 + c.G],
                        b.ident[:]),
                        waits=qkv_all + [pT_free[slot] or em.ev("ssc")], inc=True)
                    ev = em.sc(lambda e, chk=chk, slot=slot, s=s: e.copy(
                        b.vgT[s][0:c.G, chk * 128:(chk + 1) * 128],
                        b.pT[slot][0:c.G, 0:128]),
                        waits=[te], inc=True)
                    pT_free[slot] = ev
                    gevs.append(ev)
                vg_ev[s] = gevs[-1]

            ctx_mm_evs = []
            for h in range(c.HEADS):
                dc = h // 2
                r0 = (h % 2) * 64
                bank = h % 2
                hb = h % 2
                w0 = [qkv_ev, kg_ev[s], pSc_free[bank] or em.ev("sv")]
                em.pe(lambda e, dc=dc, r0=r0, bank=bank, s=s, t0=t0: e.matmul(
                    b.pA[bank][:, 0:c.G],
                    b.qb[r0:r0 + 64, dc * T + t0: dc * T + t0 + c.WIN],
                    b.kg[s][r0:r0 + 64, dc * c.G: (dc + 1) * c.G],
                    start=True, stop=True), waits=w0, inc=False)
                sc_mm = em.pe(lambda e, dc=dc, r0=r0, bank=bank, t0=t0: e.matmul(
                    b.pA[bank][:, c.G:c.G + c.WIN],
                    b.qb[r0:r0 + 64, dc * T + t0: dc * T + t0 + c.WIN],
                    b.kb[r0:r0 + 64, dc * T + t0: dc * T + t0 + c.WIN],
                    start=True, stop=True), inc=True)
                nmx = em.v(lambda e, bank=bank, hb=hb: e.reduce_max(
                    b.nm[:, hb:hb + 1], b.pA[bank][:, 0:c.G + c.WIN],
                    axis=AX.X, negate=True), waits=[sc_mm], inc=True)
                ex = em.sc(lambda e, bank=bank, hb=hb: e.activation(
                    b.expb[hb][:], b.pA[bank][:, 0:c.G + c.WIN], AF.Exp,
                    bias=b.nm[:, hb:hb + 1], scale=1.0,
                    accum_out=b.nm[:, 2 + hb:3 + hb]),
                    waits=[nmx, expb_free[hb] or em.ev("sv")], inc=True)
                pSc_free[bank] = ex
                rc = em.v(lambda e, hb=hb: e.reciprocal(
                    b.nm[:, 4 + hb:5 + hb], b.nm[:, 2 + hb:3 + hb]), waits=[ex], inc=True)
                pn = em.v(lambda e, hb=hb: e.tensor_scalar(
                    b.probs[hb][:], b.expb[hb][:], b.nm[:, 4 + hb:5 + hb], None,
                    ALU.mult), waits=[rc, probs_free[hb] or em.ev("spe")], inc=True)
                expb_free[hb] = pn
                tl = em.pe(lambda e, hb=hb: e.transpose(
                    b.pT[0][:, 0:128], b.probs[hb][:, c.G:c.G + c.WIN], b.ident[:]),
                    waits=[pn, pT_free[0] or em.ev("ssc")], inc=True)
                tg = em.pe(lambda e, hb=hb: e.transpose(
                    b.pT[1][0:c.G, 0:128], b.probs[hb][:, 0:c.G], b.ident[:]),
                    waits=[pn, pT_free[1] or em.ev("ssc")], inc=True)
                pe1 = em.sc(lambda e, hb=hb: e.copy(
                    b.probsT[hb][:, 0:128], b.pT[0][:, 0:128]),
                    waits=[tl, probsT_free[hb] or em.ev("spe")], inc=True)
                pe2 = em.sc(lambda e, hb=hb: e.copy(
                    b.probsT[hb][0:c.G, 128:128 + c.WIN], b.pT[1][0:c.G, 0:128]),
                    waits=[tg], inc=True)
                pT_free[0] = pe1
                pT_free[1] = pe2
                probs_free[hb] = pe2
                cw = [pe1, pe2, vw_ev, vg_ev[s]]
                if pCtx_free is not None and h == 0:
                    cw.append(pCtx_free)
                em.pe(lambda e, h=h, dc=dc, r0=r0, hb=hb, s=s: e.matmul(
                    b.pCtx[r0:r0 + 64, dc * 128:(dc + 1) * 128],
                    b.vgT[s][:, h * 64:(h + 1) * 64],
                    b.probsT[hb][0:c.G, 128:128 + c.WIN],
                    start=True, stop=False), waits=cw, inc=False)
                cme = em.pe(lambda e, h=h, dc=dc, r0=r0, hb=hb: e.matmul(
                    b.pCtx[r0:r0 + 64, dc * 128:(dc + 1) * 128],
                    b.vwT[:, h * 64:(h + 1) * 64],
                    b.probsT[hb][:, 0:128],
                    start=False, stop=True), inc=True)
                probsT_free[hb] = cme
                ctx_mm_evs.append(cme)
            ctx_ev = None
            for chk in range(c.NC):
                ctx_ev = em.v(lambda e, chk=chk, t0=t0: e.tensor_copy(
                    b.ctxb[:, chk * T + t0: chk * T + t0 + c.WIN],
                    b.pCtx[:, chk * 128:(chk + 1) * 128]),
                    waits=[ctx_mm_evs[-1]], inc=True)
            pCtx_free = ctx_ev

        acc_ev = None
        for oc in range(c.NC):
            bank = oc % 2
            mm = None
            for dc in range(c.NC):
                w = [pCtx_free, wo_ev]
                if dc == 0:
                    w.append(pSc_free[bank] or em.ev("sv"))
                mm = em.pe(lambda e, oc=oc, dc=dc, bank=bank: e.matmul(
                    b.pA[bank][:, 0:T],
                    b.wT[3][:, dc * c.D + oc * 128: dc * c.D + (oc + 1) * 128],
                    b.ctxb[:, dc * T:(dc + 1) * T],
                    start=(dc == 0), stop=(dc == c.NC - 1)),
                    waits=w, inc=(dc == c.NC - 1))
            acc_ev = em.v(lambda e, oc=oc, bank=bank, ti=ti: e.tensor_tensor(
                b.acc[:, oc * T:(oc + 1) * T], b.pA[bank][:, 0:T],
                b.x[:, oc * c.NT + ti * T: oc * c.NT + (ti + 1) * T], ALU.add),
                waits=[mm] + ([acc_free] if oc == 0 else []), inc=True)
            pSc_free[bank] = acc_ev
        ln_ev = emit_ln_fm(em, b, cfg, ti, acc_ev, prev_ln_ev,
                           keep_acc=(x32_dram is not None))
        prev_ln_ev = ln_ev
        if x32_dram is not None:
            for sub in range(4):
                ev = None
                for ch in range(c.NC):
                    slot = ch % 2
                    ptv = b.pT[slot][:].bitcast(F32)
                    te = em.pe(lambda e, ch=ch, sub=sub, slot=slot, ptv=ptv: e.transpose(
                        ptv[:, 0:128],
                        b.acc[:, ch * T + sub * 128: ch * T + sub * 128 + 128],
                        b.ident32[:]),
                        waits=[ln_ev, em.ev("ssc"),
                               pT_free[slot] or em.ev("ssc")], inc=True)
                    ev = em.sc(lambda e, ch=ch, slot=slot, ptv=ptv: e.copy(
                        b.x32st[:, ch * 128:(ch + 1) * 128], ptv[:, 0:128]),
                        waits=[te, x32d], inc=True)
                    pT_free[slot] = ev
                r0 = ti * T + sub * 128
                x32d = em.dma_sy("dx32", x32_dram[r0:r0 + 128, :], b.x32st[:],
                                 waits=[ev])
            acc_free = em.ev("spe")
            ln_ev = x32d
        else:
            acc_free = None
        ln_evs.append(ln_ev)
    return ln_evs




def build_program(cfg, parts):
    c = cfg
    nc = bass.Bass("TRN2", target_bir_lowering=False)
    x_in = nc.dram_tensor("x_in", [c.NSEQ, c.SEQ, c.D], F32, kind="ExternalInput").ap()
    x_out = nc.dram_tensor("x_out", [c.NSEQ, c.SEQ, c.D], F32, kind="ExternalOutput").ap()

    fused = len(parts) > 1 or parts == ["moe"]
    with ExitStack() as ctx:
        if fused:
            b = alloc_bufs_fused(nc, ctx, cfg)
        else:
            b = alloc_bufs(nc, ctx, cfg, parts)
        if "l0a" in parts:
            b.l0a_w = tuple(
                nc.dram_tensor(n, [c.D, c.D], F32, kind="ExternalInput").ap()
                for n in ("l0_wq", "l0_wk", "l0_wv", "l0_wo"))
        if "l1a" in parts:
            b.l1a_w = tuple(
                nc.dram_tensor(n, [c.D, c.D], F32, kind="ExternalInput").ap()
                for n in ("l1_wq", "l1_wk", "l1_wv"))
            b.l1a_wT = tuple(
                nc.dram_tensor(n, [128, c.NC, c.D], BF16).ap()
                for n in ("l1_wqT", "l1_wkT", "l1_wvT"))
        if "moe" in parts:
            b.moe_w = (
                nc.dram_tensor("l1_wg", [c.E, c.D], F32, kind="ExternalInput").ap(),
                nc.dram_tensor("l1_w1", [c.E, c.F, c.D], F32, kind="ExternalInput").ap(),
                nc.dram_tensor("l1_w2", [c.E, c.F, c.D], F32, kind="ExternalInput").ap(),
                nc.dram_tensor("l1_w3", [c.E, c.D, c.F], F32, kind="ExternalInput").ap(),
            )
            b.moe_w1T = nc.dram_tensor("m_w1T", [c.E, 128, c.NC, c.F], BF16).ap()
            b.moe_w2T = nc.dram_tensor("m_w2T", [c.E, 128, c.NC, c.F], BF16).ap()
            b.moe_w3T = nc.dram_tensor("m_w3T", [c.E, 128, c.NF, c.D], BF16).ap()
            b.accm = nc.dram_tensor("m_acc", [128, c.NC, c.NT], F32).ap()
        if "l0f" in parts:
            b.l0f_w = (
                nc.dram_tensor("l0_wgate", [c.I, c.D], F32, kind="ExternalInput").ap(),
                nc.dram_tensor("l0_wup", [c.I, c.D], F32, kind="ExternalInput").ap(),
                nc.dram_tensor("l0_wdown", [c.D, c.I], F32, kind="ExternalInput").ap(),
            )
            b.l0f_wgT = nc.dram_tensor("l0f_wgT", [128, c.NC, c.I], BF16).ap()
            b.l0f_wupT = nc.dram_tensor("l0f_wupT", [128, c.NC, c.I], BF16).ap()
            b.l0f_wdT = nc.dram_tensor("l0f_wdT", [128, c.NI, c.D], BF16).ap()
        if fused:
            b.x32s = nc.dram_tensor("x32s", [c.NT, c.D], F32).ap()
        if "moe" in parts:
            b.g_dram = nc.dram_tensor("g_dram", [8, c.NT], BF16).ap()
        em = Em(nc)

        bp = phase_bufs(b, "l0a") if fused else b
        ce = emit_consts(em, bp, cfg)
        tile_evs = emit_ingest(em, bp, cfg, x_in)
        final_evs = tile_evs
        if "l0a" in parts:
            final_evs = emit_l0_attn(em, bp, cfg, ce, tile_evs,
                                     x32_dram=b.x32s if fused else None)
        if "l0f" in parts:
            if fused:
                em.phase_barrier()
                bp = phase_bufs(b, "l0f")
                x32 = b.x32s
            else:
                x32 = x_in if parts[0] == "l0f" else None
            final_evs = emit_l0_ffn(em, bp, cfg, ce, final_evs, x32_dram=x32)
        if "l1a" in parts:
            if fused:
                em.phase_barrier()
                bp = phase_bufs(b, "l1a")
            final_evs = emit_l1_attn(em, bp, cfg, ce, final_evs)
        if "moe" in parts:
            if fused:
                em.phase_barrier()
                bp = phase_bufs(b, "moe")
            out_dmas = emit_l1_moe(em, bp, cfg, ce, final_evs, x_out)
        else:
            out_dmas = emit_egress(em, bp, cfg, x_out, [final_evs[-1]] if final_evs else [])
        finals = {}
        for s, v in out_dmas:
            finals[s] = max(finals.get(s, 0), v)
        em.emit("sy", lambda e: e.nop(), waits=sorted(finals.items()))

        for s in sorted(em.cnt.keys()):
            em.sems[s] = ctx.enter_context(nc.semaphore(s))
        block = ctx.enter_context(nc.Block())
        em.play(block)
    return nc




def emit_quant(em, b, cfg, src, dst, W, waits, qs=None, dt16=False, dst_war=()):
    c = cfg
    q_xn, q_g, q_a, q_c, q_sc = qs if qs is not None else b.qsets[0]
    LWE = 2 * c.LW if dt16 else c.LW
    ev = None
    for lo in range(0, W, LWE):
        wsub = min(LWE, W - lo)
        nb = wsub // 16
        s3 = src[:, lo:lo + wsub].rearrange("p (a b) -> p a b", b=16)
        sc16 = q_sc[:, 0:nb]
        scl = q_sc[:, nb:2 * nb]
        rscl = q_sc[:, 2 * nb:3 * nb]
        if dt16:
            xn = q_xn.bitcast(BF16)[:, 0:wsub]
            a = q_a.bitcast(BF16)[:, 0:wsub]
        else:
            xn = q_xn[:, 0:wsub]
            a = q_a[:, 0:wsub]
        g = q_g.bitcast(BF16)[:, 0:wsub]
        cc = q_c.bitcast(BF16)[:, 0:wsub]
        if dt16:
            u = b.qsets[1][0].bitcast(BF16)[:, 0:wsub]
        else:
            u = q_c.bitcast(BF16)[:, c.LW:c.LW + wsub]
        xn3 = xn.rearrange("p (a b) -> p a b", b=16)
        rev = em.v(lambda e, s3=s3, sc16=sc16: e.reduce_max(
            sc16, s3, axis=AX.X, apply_absolute_value=True),
            waits=list(waits) + ([ev] if ev is not None else []))
        sev2 = em.v(lambda e, scl=scl, sc16=sc16: e.tensor_scalar(
            scl, sc16, 1.0 / 6.0, 1e-12, ALU.mult, ALU.max), waits=[rev])
        rcev = em.v(lambda e, rscl=rscl, scl=scl: e.reciprocal(rscl, scl))
        vev = em.v(lambda e, xn3=xn3, s3=s3, rscl=rscl, nb=nb: e.tensor_tensor(
            xn3, s3, rscl.unsqueeze(2).broadcast_to((128, nb, 16)), ALU.mult),
            waits=[rcev])
        em.sc(lambda e, g=g, xn=xn: e.sign(g, xn), waits=[vev])
        sev = em.sc(lambda e, a=a, xn=xn: e.activation(a, xn, AF.Abs, bias=0.0, scale=1.0))
        em.v(lambda e, cc=cc, a=a: e.tensor_scalar(cc, a, 0.25, 0.5, ALU.is_gt, ALU.mult),
             waits=[sev])
        for mid, inc in ((0.75, 0.5), (1.25, 0.5), (1.75, 0.5),
                         (2.5, 1.0), (3.5, 1.0), (5.0, 2.0)):
            em.v(lambda e, uu=u, a=a, mid=mid, inc=inc: e.tensor_scalar(
                uu, a, mid, inc, ALU.is_gt, ALU.mult))
            em.v(lambda e, cc=cc, uu=u: e.tensor_tensor(cc, cc, uu, ALU.add))
        gev = em.v(lambda e, cc=cc, g=g: e.tensor_tensor(cc, cc, g, ALU.mult))
        cc3 = cc.rearrange("p (a b) -> p a b", b=16)
        d3 = dst[:, lo:lo + wsub].rearrange("p (a b) -> p a b", b=16)
        ev = em.v(lambda e, d3=d3, cc3=cc3, scl=scl, nb=nb: e.tensor_tensor(
            d3, cc3, scl.unsqueeze(2).broadcast_to((128, nb, 16)), ALU.mult),
            waits=[gev] + list(dst_war))
    return ev

def emit_l0f_weight_prep(em, b, cfg, ce, after=()):
    c = cfg
    jobs = (
        (b.l0f_w[0], b.l0f_wgT, c.I, c.D),
        (b.l0f_w[1], b.l0f_wupT, c.I, c.D),
        (b.l0f_w[2], b.l0f_wdT, c.D, c.I),
    )
    in_free = [list(after), list(after)]
    out_free = [None, None]
    pT_free = [em.ev("ssc"), em.ev("ssc")]
    rc_idx = 0
    for (w_d, wT_d, R, W) in jobs:
        nrc = R // 128
        nwc = W // 128
        wre = w_d.rearrange("(rc p) d -> p rc d", p=128)
        for rc in range(nrc):
            sl = rc_idx % 2
            ld = em.dma_gp(f"dwqin{sl}", b.wq_in2[sl][:, 0:W], wre[:, rc, :],
                           waits=in_free[sl] + [ce])
            qev = emit_quant(em, b, cfg, b.wq_in2[sl][:, 0:W],
                             b.wq_out2[sl][:, 0:W], W, [ld],
                             dst_war=[out_free[sl]])
            in_free[sl] = [qev]
            ssl = rc_idx % 2
            sev = None
            te = None
            for wc in range(nwc):
                slot = wc % 2
                te = em.pe(lambda e, wc=wc, slot=slot, sl=sl: e.transpose(
                    b.pT[slot][:, 0:128],
                    b.wq_out2[sl][:, wc * 128:(wc + 1) * 128], b.ident[:]),
                    waits=[qev, ce, pT_free[slot]], inc=True)
                sev = em.sc(lambda e, wc=wc, slot=slot, ssl=ssl: e.copy(
                    b.wstg[ssl][:, wc * 128:(wc + 1) * 128],
                    b.pT[slot][:, 0:128]),
                    waits=[te, b.wstg_free[ssl]], inc=True)
                pT_free[slot] = sev
            out_free[sl] = te
            dma = em.dma_gp(f"dstg{ssl}",
                            wT_d[:, :, rc * 128:(rc + 1) * 128],
                            b.wstg[ssl][:, 0:nwc * 128].rearrange(
                                "p (wc m) -> p wc m", m=128),
                            waits=[sev])
            b.wstg_free[ssl] = dma
            rc_idx += 1
    return [em.ev("dstg0"), em.ev("dstg1")]

def emit_l0_ffn(em, b, cfg, ce, x_tile_evs, prev_ln_ev=None, x32_dram=None):
    c = cfg
    T = c.TT
    if x32_dram is None:
        xflat32 = None
    elif len(x32_dram.shape) == 3:
        xflat32 = x32_dram.rearrange("s t d -> (s t) d")
    else:
        xflat32 = x32_dram
    prep_evs = emit_l0f_weight_prep(em, b, cfg, ce, after=[x_tile_evs[-1]])
    NFB = (c.I + 511) // 512
    ln_evs = []
    pa_free = [None, None, None, None]
    pT_free = [em.ev("ssc"), em.ev("ssc")]
    qi = [0]
    xq_rd = None
    hb_rd = [None] * 4
    hqfm_rd = None
    wg_free = [None, None]
    wd_free = [None, None]
    silu_free = None
    xtm_free = [None, None]
    xqtm_free = [None, None]
    qset_free = [None, None]
    qsh_free = [None, None]
    hqtm_free = None

    for ti in range(c.NTILE):
        xev = x_tile_evs[ti]
        xq_ev = None
        xq_sub_ev = [None] * 4
        for sub in range(4):
            sl = sub % 2
            r0 = ti * T + sub * 128
            ev32 = em.dma_gp(f"dxq32_{sl}", b.xtm32_2[sl], xflat32[r0:r0 + 128, :],
                             waits=[xev, xtm_free[sl]] + prep_evs)
            qev = emit_quant(em, b, cfg, b.xtm32_2[sl], b.xqtm2[sl], c.D,
                             [ev32], dst_war=[xqtm_free[sl]])
            xtm_free[sl] = qev
            te = None
            for ch in range(c.NC):
                slot = ch % 2
                te = em.pe(lambda e, ch=ch, slot=slot, sl=sl: e.transpose(
                    b.pT[slot][:, 0:128],
                    b.xqtm2[sl][:, ch * 128:(ch + 1) * 128], b.ident[:]),
                    waits=[qev, pT_free[slot]], inc=True)
                xq_ev = em.sc(lambda e, ch=ch, sub=sub, slot=slot: e.copy(
                    b.xq_fm[:, ch * T + sub * 128: ch * T + sub * 128 + 128],
                    b.pT[slot][:, 0:128]),
                    waits=[te, xq_rd], inc=True)
                pT_free[slot] = xq_ev
            xq_sub_ev[sub] = xq_ev
            xqtm_free[sl] = te
        h_ev = None
        h_sub_ev = [None] * 4
        for fb in range(NFB):
            fw = min(512, c.I - fb * 512)
            gsl = 0
            wgl = em.dma_gp(f"dwg{gsl}",
                            b.wg_s[gsl][:, 0:c.NC * fw].rearrange("p (dc f) -> p dc f", f=fw),
                            b.l0f_wgT[:, :, fb * 512: fb * 512 + fw],
                            waits=[wg_free[gsl]] + prep_evs)
            wul = em.dma_gp(f"dwu{gsl}",
                            b.wup_s[gsl][:, 0:c.NC * fw].rearrange("p (dc f) -> p dc f", f=fw),
                            b.l0f_wupT[:, :, fb * 512: fb * 512 + fw],
                            waits=[wg_free[gsl]] + prep_evs)
            last_mm = [None, None]
            for sub in range(4):
                bsel = (fb * 4 + sub) % 2
                bg, bu = (0, 1) if bsel == 0 else (2, 3)
                mmg = None
                mmu = None
                for dc in range(c.NC):
                    w = [xq_sub_ev[sub], wgl]
                    if dc == 0:
                        w.append(pa_free[bg])
                    mmg = em.pe(lambda e, dc=dc, sub=sub, fw=fw, gsl=gsl, bg=bg: e.matmul(
                        b.pA[bg][:, 0:fw],
                        b.xq_fm[:, dc * T + sub * 128: dc * T + sub * 128 + 128],
                        b.wg_s[gsl][:, dc * fw: (dc + 1) * fw],
                        start=(dc == 0), stop=(dc == c.NC - 1)),
                        waits=w, inc=(dc == c.NC - 1))
                for dc in range(c.NC):
                    w = [xq_sub_ev[sub], wul]
                    if dc == 0:
                        w.append(pa_free[bu])
                    mmu = em.pe(lambda e, dc=dc, sub=sub, fw=fw, gsl=gsl, bu=bu: e.matmul(
                        b.pA[bu][:, 0:fw],
                        b.xq_fm[:, dc * T + sub * 128: dc * T + sub * 128 + 128],
                        b.wup_s[gsl][:, dc * fw: (dc + 1) * fw],
                        start=(dc == 0), stop=(dc == c.NC - 1)),
                        waits=w, inc=(dc == c.NC - 1))
                sev = em.sc(lambda e, fw=fw, bg=bg: e.activation(
                    b.silu[:, 0:fw], b.pA[bg][:, 0:fw], AF.Silu, bias=0.0, scale=1.0),
                    waits=[mmg, silu_free], inc=True)
                pa_free[bg] = sev
                h_ev = em.v(lambda e, sub=sub, fb=fb, fw=fw, bu=bu: e.tensor_tensor(
                    b.hb[:, sub * c.I + fb * 512: sub * c.I + fb * 512 + fw],
                    b.silu[:, 0:fw], b.pA[bu][:, 0:fw], ALU.mult),
                    waits=[mmu, sev, hb_rd[sub]], inc=True)
                if fb == NFB - 1:
                    h_sub_ev[sub] = h_ev
                pa_free[bu] = h_ev
                silu_free = h_ev
                last_mm[0] = mmg
                last_mm[1] = mmu
            wg_free[gsl] = last_mm[1]
        hq_ev = None
        for sub in range(4):
            qv = emit_quant(em, b, cfg, b.hb[:, sub * c.I:(sub + 1) * c.I],
                            b.hq_tm[:], c.I, [h_sub_ev[sub]],
                            dt16=True, dst_war=[hqtm_free])
            qevs = [qv]
            hb_rd[sub] = qv
            te = None
            for ic in range(c.NI):
                slot = ic % 2
                qev = qevs[(ic * 128) // c.LW if (ic * 128) // c.LW < len(qevs) else -1]
                te = em.pe(lambda e, ic=ic, slot=slot: e.transpose(
                    b.pT[slot][:, 0:128],
                    b.hq_tm[:, ic * 128:(ic + 1) * 128], b.ident[:]),
                    waits=qevs + [pT_free[slot]], inc=True)
                hq_ev = em.sc(lambda e, ic=ic, sub=sub, slot=slot: e.copy(
                    b.hq_fm[:, ic * T + sub * 128: ic * T + sub * 128 + 128],
                    b.pT[slot][:, 0:128]),
                    waits=[te, hqfm_rd], inc=True)
                pT_free[slot] = hq_ev
            hqtm_free = te
        acc_ev = None
        down_mm = None
        for oc in range(c.NC):
            dsl = 0
            wdl = em.dma_gp(f"dwd{dsl}",
                            b.wd_s[dsl][:].rearrange("p (ic o) -> p ic o", o=128),
                            b.l0f_wdT[:, :, oc * 128:(oc + 1) * 128],
                            waits=[wd_free[dsl]] + prep_evs)
            bank = oc % 2
            for ic in range(c.NI):
                w = [hq_ev, wdl]
                if ic == 0:
                    w.append(pa_free[bank])
                down_mm = em.pe(lambda e, oc=oc, ic=ic, dsl=dsl, bank=bank: e.matmul(
                    b.pA[bank][:, 0:T],
                    b.wd_s[dsl][:, ic * 128:(ic + 1) * 128],
                    b.hq_fm[:, ic * T:(ic + 1) * T],
                    start=(ic == 0), stop=(ic == c.NI - 1)),
                    waits=w, inc=(ic == c.NI - 1))
            wd_free[dsl] = down_mm
            acc_ev = em.v(lambda e, oc=oc, bank=bank, ti=ti: e.tensor_tensor(
                b.acc[:, oc * T:(oc + 1) * T], b.pA[bank][:, 0:T],
                b.x[:, oc * c.NT + ti * T: oc * c.NT + (ti + 1) * T], ALU.add),
                waits=[down_mm], inc=True)
            pa_free[bank] = acc_ev
        hqfm_rd = down_mm
        xq_rd = last_mm[1]
        ln_ev = emit_ln_fm(em, b, cfg, ti, acc_ev, prev_ln_ev,
                           extra_waits=[pa_free[2], pa_free[3]])
        pa_free[2] = ln_ev
        pa_free[3] = ln_ev
        prev_ln_ev = ln_ev
        ln_evs.append(ln_ev)
    return ln_evs


class PrepJob:

    def __init__(self, em, b, cfg, w_dram, wT_dram, ce, after=()):
        self.em, self.b, self.cfg = em, b, cfg
        self.wT = wT_dram
        self.ce = ce
        self.O = w_dram.shape[0]
        self.D = w_dram.shape[1]
        self.noc = self.O // 128
        self.ndc = self.D // 128
        self.hoc = max(1, (cfg.NC // 4 * cfg.D) // self.D)
        self.wre = w_dram.rearrange("(oc p) d -> p oc d", p=128)
        self.extra = list(after)
        self.og = 0

    def done(self):
        return self.og >= self.noc

    def load_chunk(self):
        em, b = self.em, self.b
        og = self.og
        og_n = min(self.hoc, self.noc - og)
        self.og += og_n
        D = self.D
        par = b.wload_par
        b.wload_par ^= 1
        ld = em.dma_gp(f"dwl{par}",
                       b.wload2[par][:, 0:og_n * D].rearrange("p (oc d) -> p oc d", d=D),
                       self.wre[:, og:og + og_n, :],
                       waits=self.extra + [b.wload_free[par], self.ce])
        self.extra = []
        return (og, og_n, ld, par)

    def tail_chunk(self, tok):
        em, b = self.em, self.b
        og, og_n, ld, par = tok
        D = self.D
        te = None
        for oc in range(og, og + og_n):
            ssl = oc % 2
            evs = None
            for dc in range(self.ndc):
                slot = dc % 2
                te = em.pe(lambda e, oc=oc, dc=dc, slot=slot, og=og, D=D, par=par: e.transpose(
                    b.pT[slot][:, 0:128],
                    b.wload2[par][:, (oc - og) * D + dc * 128: (oc - og) * D + (dc + 1) * 128],
                    b.ident[:]),
                    waits=[ld, self.ce, b.pT_free[slot]], inc=True)
                evs = em.sc(lambda e, dc=dc, slot=slot, ssl=ssl: e.copy(
                    b.wstg[ssl][:, dc * 128:(dc + 1) * 128],
                    b.pT[slot][:, 0:128]),
                    waits=[te, b.wstg_free[ssl]], inc=True)
                b.pT_free[slot] = evs
            dma = em.dma_gp(f"dstg{ssl}",
                            self.wT[:, :, oc * 128:(oc + 1) * 128],
                            b.wstg[ssl][:, 0:self.ndc * 128].rearrange(
                                "p (dc m) -> p dc m", m=128),
                            waits=[evs])
            b.wstg_free[ssl] = dma
        b.wload_free[par] = te


class PrepQueue:

    def __init__(self, jobs):
        self.jobs = list(jobs)
        self.ji = 0
        self.pending = None

    def done(self):
        return self.ji >= len(self.jobs) and self.pending is None

    def step(self):
        tok = None
        while self.ji < len(self.jobs):
            if not self.jobs[self.ji].done():
                tok = (self.jobs[self.ji], self.jobs[self.ji].load_chunk())
                break
            self.ji += 1
        if self.pending is not None:
            j, t = self.pending
            j.tail_chunk(t)
        self.pending = tok

    def run(self, k=-1):
        n = 0
        while not self.done() and (k < 0 or n < k):
            self.step()
            n += 1


def emit_weight_prep_dram(em, b, cfg, w_dram, wT_dram, ce, after=()):
    q = PrepQueue([PrepJob(em, b, cfg, w_dram, wT_dram, ce, after)])
    q.run()
    return [em.ev("dstg0"), em.ev("dstg1")]


def emit_l1_attn(em, b, cfg, ce, x_tile_evs, prev_ln_ev=None):
    c = cfg
    T = c.TT
    pq = emit_weight_prep_dram(em, b, cfg, b.l1a_w[0], b.l1a_wT[0], ce)
    pk = emit_weight_prep_dram(em, b, cfg, b.l1a_w[1], b.l1a_wT[1], ce)
    pv = emit_weight_prep_dram(em, b, cfg, b.l1a_w[2], b.l1a_wT[2], ce)
    prep_evs = pv

    mz = em.gp(lambda e: e.memset(b.cmask[:], 0.0), waits=[ce], inc=True)
    mask_ev = mz
    for o in range(4):
        mask_ev = em.gp(lambda e, o=o: e.affine_select(
            out=b.cmask[:, o * 512:(o + 1) * 512], in_=b.cmask[:, o * 512:(o + 1) * 512],
            pattern=[[-1, 512]], compare_op=ALU.is_ge, fill=NEG,
            base=128 * o, channel_multiplier=1), waits=[mz], inc=True)

    ngrp = c.SEQ // T
    ntpg = T // 128
    ln_evs = []
    pa_free = [None, None, None, None]
    pT_free = [em.ev("ssc"), em.ev("ssc")]
    pCtx_free = None
    probsT_free = [None, None]
    probs_free = None
    kv_free = None
    qg_free = None
    vtmp_free = None
    wstr_free = [None, None]

    def proj_stream(wT_dram, oc, wsl, extra):
        return em.dma_gp(f"dws{wsl}",
                         b.wstr[wsl][:].rearrange("p (dc m) -> p dc m", m=128),
                         wT_dram[:, :, oc * 128:(oc + 1) * 128],
                         waits=[wstr_free[wsl]] + prep_evs + list(extra))

    for s in range(c.NSEQ):
        base = s * c.SEQ
        kv_ev = None
        for tg in range(ngrp):
            ti_glob = (base + tg * T) // T
            xev = x_tile_evs[ti_glob]
            for oc in range(c.NC):
                bank = oc % 2
                wsl = oc % 2
                wl = proj_stream(b.l1a_wT[1], oc, wsl, [])
                mm = None
                for dc in range(c.NC):
                    w = [xev, wl]
                    if dc == 0:
                        w += [pa_free[bank], kv_free]
                    mm = em.pe(lambda e, oc=oc, dc=dc, bank=bank, tg=tg, base=base, wsl=wsl: e.matmul(
                        b.pA[bank][:, 0:T],
                        b.wstr[wsl][:, dc * 128:(dc + 1) * 128],
                        b.x[:, dc * c.NT + base + tg * T: dc * c.NT + base + (tg + 1) * T],
                        start=(dc == 0), stop=(dc == c.NC - 1)),
                        waits=w, inc=(dc == c.NC - 1))
                wstr_free[wsl] = mm
                kv_ev = em.sc(lambda e, oc=oc, bank=bank, tg=tg, base=base: e.copy(
                    b.kS[:, oc * c.SEQ + tg * T: oc * c.SEQ + (tg + 1) * T],
                    b.pA[bank][:, 0:T]), waits=[mm], inc=True)
                pa_free[bank] = kv_ev
            for oc in range(c.NC):
                bank = oc % 2
                wsl = oc % 2
                wl = proj_stream(b.l1a_wT[2], oc, wsl, [])
                mm = None
                for dc in range(c.NC):
                    w = [xev, wl]
                    if dc == 0:
                        w += [pa_free[bank], kv_free]
                    mm = em.pe(lambda e, oc=oc, dc=dc, bank=bank, tg=tg, base=base, wsl=wsl: e.matmul(
                        b.pA[bank][:, 0:T],
                        b.wstr[wsl][:, dc * 128:(dc + 1) * 128],
                        b.x[:, dc * c.NT + base + tg * T: dc * c.NT + base + (tg + 1) * T],
                        start=(dc == 0), stop=(dc == c.NC - 1)),
                        waits=w, inc=(dc == c.NC - 1))
                wstr_free[wsl] = mm
                vt = em.v(lambda e, bank=bank: e.tensor_copy(
                    b.vtmp[:], b.pA[bank][:, 0:T]), waits=[mm, vtmp_free], inc=True)
                pa_free[bank] = vt
                for sj in range(4):
                    slot = sj % 2
                    te = em.pe(lambda e, sj=sj, slot=slot: e.transpose(
                        b.pT[slot][:, 0:128], b.vtmp[:, sj * 128:(sj + 1) * 128],
                        b.ident[:]), waits=[vt, pT_free[slot] or em.ev("ssc")], inc=True)
                    ev = em.sc(lambda e, oc=oc, tg=tg, sj=sj, slot=slot: e.copy(
                        b.vS[:, (tg * 4 + sj) * c.D + oc * 128: (tg * 4 + sj) * c.D + (oc + 1) * 128],
                        b.pT[slot][:, 0:128]), waits=[te], inc=True)
                    pT_free[slot] = ev
                    vtmp_free = ev
                kv_ev = vtmp_free

        for tg in range(ngrp):
            ti_glob = (base + tg * T) // T
            xev = x_tile_evs[ti_glob]
            q_ev = None
            for oc in range(c.NC):
                bank = oc % 2
                wsl = oc % 2
                wl = proj_stream(b.l1a_wT[0], oc, wsl, [])
                mm = None
                for dc in range(c.NC):
                    w = [xev, wl]
                    if dc == 0:
                        w += [pa_free[bank], qg_free]
                    mm = em.pe(lambda e, oc=oc, dc=dc, bank=bank, tg=tg, base=base, wsl=wsl: e.matmul(
                        b.pA[bank][:, 0:T],
                        b.wstr[wsl][:, dc * 128:(dc + 1) * 128],
                        b.x[:, dc * c.NT + base + tg * T: dc * c.NT + base + (tg + 1) * T],
                        start=(dc == 0), stop=(dc == c.NC - 1)),
                        waits=w, inc=(dc == c.NC - 1))
                wstr_free[wsl] = mm
                q_ev = em.v(lambda e, oc=oc, bank=bank: e.tensor_scalar(
                    b.qg[:, oc * T:(oc + 1) * T], b.pA[bank][:, 0:T],
                    1.0 / 32.0, None, ALU.mult), waits=[mm], inc=True)
                pa_free[bank] = q_ev

            for qt in range(ntpg):
                i = tg * ntpg + qt
                nblk = i // 4 + 1
                exp_evs = []
                for j in range(nblk):
                    diag = (j == nblk - 1)
                    mmS = None
                    for dc in range(c.NC):
                        w = [q_ev, kv_ev]
                        if dc == 0:
                            w.append(pa_free[j])
                        mmS = em.pe(lambda e, dc=dc, j=j, qt=qt: e.matmul(
                            b.pA[j][:, 0:T],
                            b.qg[:, dc * T + qt * 128: dc * T + qt * 128 + 128],
                            b.kS[:, dc * c.SEQ + j * T: dc * c.SEQ + (j + 1) * T],
                            start=(dc == 0), stop=(dc == c.NC - 1)),
                            waits=w, inc=(dc == c.NC - 1))
                    if diag:
                        mmS = em.v(lambda e, j=j, i=i: e.tensor_tensor(
                            b.pA[j][:, 0:T], b.pA[j][:, 0:T],
                            b.cmask[:, (i % 4) * 512:(i % 4) * 512 + 512], ALU.add),
                            waits=[mmS, mask_ev], inc=True)
                    em.v(lambda e, j=j: e.reduce_max(
                        b.nm1[:, j:j + 1], b.pA[j][:, 0:T], axis=AX.X, negate=True),
                        waits=[mmS], inc=True)
                cm = em.v(lambda e: e.tensor_copy(b.nm1[:, 4:5], b.nm1[:, 0:1]), inc=True)
                for j in range(1, nblk):
                    cm = em.v(lambda e, j=j: e.tensor_tensor(
                        b.nm1[:, 4:5], b.nm1[:, 4:5], b.nm1[:, j:j + 1], ALU.min),
                        inc=True)
                for j in range(nblk):
                    ex = em.sc(lambda e, j=j: e.activation(
                        b.probs1[:, j * T:(j + 1) * T], b.pA[j][:, 0:T], AF.Exp,
                        bias=b.nm1[:, 4:5], scale=1.0,
                        accum_out=b.nm1[:, 5 + j:6 + j]),
                        waits=[cm, probs_free], inc=True)
                    pa_free[j] = ex
                    exp_evs.append(ex)
                sm = em.v(lambda e: e.tensor_copy(b.nm1[:, 9:10], b.nm1[:, 5:6]),
                          waits=[exp_evs[0]], inc=True)
                for j in range(1, nblk):
                    sm = em.v(lambda e, j=j: e.tensor_tensor(
                        b.nm1[:, 9:10], b.nm1[:, 9:10], b.nm1[:, 5 + j:6 + j], ALU.add),
                        waits=[exp_evs[j]], inc=True)
                rc = em.v(lambda e: e.reciprocal(b.nm1[:, 10:11], b.nm1[:, 9:10]),
                          waits=[sm], inc=True)
                pt_ev = None
                for j in range(nblk):
                    pn = em.v(lambda e, j=j: e.tensor_scalar(
                        b.probs1[:, j * T:(j + 1) * T], b.probs1[:, j * T:(j + 1) * T],
                        b.nm1[:, 10:11], None, ALU.mult),
                        waits=[rc, exp_evs[j]], inc=True)
                    for sj in range(4):
                        slot = sj % 2
                        te = em.pe(lambda e, j=j, sj=sj, slot=slot: e.transpose(
                            b.pT[slot][:, 0:128],
                            b.probs1[:, j * T + sj * 128: j * T + sj * 128 + 128],
                            b.ident[:]),
                            waits=[pn, pT_free[slot] or em.ev("ssc"), probsT_free[0]], inc=True)
                        pT_free[slot] = em.sc(lambda e, j=j, sj=sj, slot=slot: e.copy(
                            b.probsT1[:, (j * 4 + sj) * 128:(j * 4 + sj) * 128 + 128],
                            b.pT[slot][:, 0:128]),
                            waits=[te], inc=True)
                pt_ev = pT_free[(nblk * 4 - 1) % 2]
                ctx_mm = None
                for ch in range(c.NC):
                    for j in range(nblk):
                        for sj in range(4):
                            w = [pt_ev, kv_ev]
                            if ch == 0 and j == 0 and sj == 0:
                                w.append(pCtx_free)
                            ctx_mm = em.pe(lambda e, ch=ch, sj=sj, j=j, nblk=nblk: e.matmul(
                                b.pCtx[:, ch * 128:(ch + 1) * 128],
                                b.vS[:, (j * 4 + sj) * c.D + ch * 128: (j * 4 + sj) * c.D + (ch + 1) * 128],
                                b.probsT1[:, (j * 4 + sj) * 128:(j * 4 + sj) * 128 + 128],
                                start=(j == 0 and sj == 0), stop=(j == nblk - 1 and sj == 3)),
                                waits=w, inc=(ch == c.NC - 1 and j == nblk - 1 and sj == 3))
                probsT_free = [ctx_mm, ctx_mm]
                probs_free = pt_ev
                acc_ev = None
                for ch in range(c.NC):
                    acc_ev = em.v(lambda e, ch=ch, qt=qt, i=i, base=base: e.tensor_tensor(
                        b.acc[:, ch * T + qt * 128: ch * T + qt * 128 + 128],
                        b.pCtx[:, ch * 128:(ch + 1) * 128],
                        b.x[:, ch * c.NT + base + i * 128: ch * c.NT + base + i * 128 + 128],
                        ALU.add), waits=[ctx_mm], inc=True)
                pCtx_free = acc_ev
            ln_ev = emit_ln_fm(em, b, cfg, ti_glob, acc_ev, prev_ln_ev,
                               extra_waits=[pa_free[2], pa_free[3]])
            pa_free[2] = ln_ev
            pa_free[3] = ln_ev
            prev_ln_ev = ln_ev
            ln_evs.append(ln_ev)
            qg_free = em.ev("spe")
        kv_free = em.ev("spe")
    return ln_evs




def emit_l1_moe(em, b, cfg, ce, x_tile_evs, x_out):
    c = cfg
    T = c.TT
    F = c.F
    xoflat = x_out.rearrange("s t d -> (s t) d")

    wgl = em.dma_gp("dwg_in", b.wg_sb, b.moe_w[0], waits=[ce])
    wgc = em.v(lambda e: e.tensor_copy(b.scores[0:8, 0:c.D], b.wg_sb),
               waits=[wgl], inc=True)
    wgt_ev = None
    for dc in range(c.NC):
        slot = dc % 2
        te = em.pe(lambda e, dc=dc, slot=slot: e.transpose(
            b.pT[slot][:, 0:8], b.scores[0:8, dc * 128:(dc + 1) * 128],
            b.ident[0:8, 0:8]),
            waits=[wgc, ce, em.ev("ssc"), b.pT_free[slot]], inc=True)
        wgt_ev = em.sc(lambda e, dc=dc, slot=slot: e.copy(
            b.wgT[:, dc * 8:(dc + 1) * 8], b.pT[slot][0:128, 0:8]),
            waits=[te], inc=True)
        b.pT_free[slot] = wgt_ev

    pa_free = [None, None, None, None]
    sc_ev = None
    for ti in range(c.NTILE):
        bank = ti % 2
        mm = None
        for dc in range(c.NC):
            w = [x_tile_evs[ti], wgt_ev]
            if dc == 0:
                w.append(pa_free[bank])
            mm = em.pe(lambda e, ti=ti, dc=dc, bank=bank: e.matmul(
                b.pA[bank][0:8, 0:T],
                b.wgT[:, dc * 8:(dc + 1) * 8],
                b.x[:, dc * c.NT + ti * T: dc * c.NT + (ti + 1) * T],
                start=(dc == 0), stop=(dc == c.NC - 1)),
                waits=w, inc=(dc == c.NC - 1))
        sc_ev = em.sc(lambda e, ti=ti, bank=bank: e.copy(
            b.scores[0:8, ti * T:(ti + 1) * T], b.pA[bank][0:8, 0:T]),
            waits=[mm], inc=True)
        pa_free[bank] = sc_ev

    g_ev = None
    for t in range(c.NT // 128):
        slot = t % 2
        te = em.pe(lambda e, t=t, slot=slot: e.transpose(
            b.pT[slot][:, 0:8], b.scores[0:8, t * 128:(t + 1) * 128],
            b.ident[0:8, 0:8]), waits=[sc_ev, b.pT_free[slot]], inc=True)
        ev = em.sc(lambda e, slot=slot: e.copy(
            b.sctm[:], b.pT[slot][0:128, 0:8]), waits=[te, g_ev], inc=True)
        b.pT_free[slot] = ev
        em.v(lambda e: e.max(out=b.top8[:], in_=b.sctm[:]), waits=[ev])
        em.v(lambda e: e.memset(b.rep[:], NEG))
        em.v(lambda e: e.tensor_copy(b.rep[:, 0:1], b.top8[:, 0:1]))
        em.v(lambda e: e.match_replace(b.ms2[:], b.rep[:], b.sctm[:], -2e30))
        em.v(lambda e: e.tensor_tensor(b.mk1[:], b.sctm[:], b.ms2[:], ALU.not_equal))
        em.v(lambda e: e.tensor_copy(b.rep[:, 0:1], b.top8[:, 1:2]))
        em.v(lambda e: e.match_replace(b.ms3[:], b.rep[:], b.ms2[:], -2e30))
        em.v(lambda e: e.tensor_tensor(b.mk2[:], b.ms2[:], b.ms3[:], ALU.not_equal))
        dv = em.v(lambda e: e.tensor_sub(b.mst[:, 0:1], b.top8[:, 0:1], b.top8[:, 1:2]))
        p1 = em.sc(lambda e: e.activation(b.mst[:, 1:2], b.mst[:, 0:1], AF.Sigmoid,
                                          bias=0.0, scale=1.0), waits=[dv], inc=True)
        p2 = em.sc(lambda e: e.activation(b.mst[:, 2:3], b.mst[:, 0:1], AF.Sigmoid,
                                          bias=0.0, scale=-1.0), inc=True)
        em.v(lambda e: e.tensor_scalar(b.gtm[:], b.mk1[:], b.mst[:, 1:2], None, ALU.mult),
             waits=[p1, p2])
        em.v(lambda e: e.scalar_tensor_tensor(b.gtm[:], b.mk2[:], b.mst[:, 2:3],
                                              b.gtm[:], ALU.mult, ALU.add))
        gc = em.v(lambda e: e.tensor_copy(b.gbf[:], b.gtm[:]))
        tb = em.pe(lambda e, t=t, slot=slot: e.transpose(
            b.pT[slot][0:8, 128:256], b.gbf[:], b.ident[:]),
            waits=[gc, b.pT_free[slot]], inc=True)
        g_ev = em.sc(lambda e, t=t, slot=slot: e.copy(
            b.g_fm[0:8, t * 128:(t + 1) * 128], b.pT[slot][0:8, 128:256]),
            waits=[tb], inc=True)
        b.pT_free[slot] = g_ev
    g_dma = em.dma_gp("dgout", b.g_dram, b.g_fm[0:8, :], waits=[g_ev])

    sel_ev = None
    for e_ in range(c.E):
        sel_ev = em.gp(lambda e, e_=e_: e.affine_select(
            out=b.sel[0:8, e_ * 128:(e_ + 1) * 128], in_=b.ones_bf[0:8, 0:128],
            pattern=[[0, 128]], compare_op=ALU.is_equal, fill=0.0,
            base=-e_, channel_multiplier=1), waits=[ce], inc=True)

    queues = []
    for e in range(c.E):
        queues.append(PrepQueue([
            PrepJob(em, b, cfg, b.moe_w[1][e], b.moe_w1T[e], ce,
                    after=[g_dma] if e == 0 else ()),
            PrepJob(em, b, cfg, b.moe_w[2][e], b.moe_w2T[e], ce),
            PrepJob(em, b, cfg, b.moe_w[3][e], b.moe_w3T[e], ce),
        ]))
    prep_done = {}

    def emit_prep(e, n_chunks):
        queues[e].run(n_chunks)
        if queues[e].done() and e not in prep_done:
            prep_done[e] = [em.ev("dstg0"), em.ev("dstg1")]

    emit_prep(0, -1)

    w_rd = [None, None]
    w3_free = [None, None]
    pctx_free = None
    hid_rd = None
    stage_dma = None
    msil_free = None
    mt1_free = None
    gB_free = None
    gld_free = None
    acc_dmas = [None] * c.NTILE
    for e in range(c.E):
        pe_evs = prep_done[e]
        NSL = 4
        slw = F // NSL
        l1s, l2s = [], []
        for sl_ in range(NSL):
            l1s.append(em.dma_gp(f"dw1e{sl_}",
                b.w1e[:].rearrange("p (dc f) -> p dc f", f=F)[:, :, sl_ * slw:(sl_ + 1) * slw],
                b.moe_w1T[e][:, :, sl_ * slw:(sl_ + 1) * slw],
                waits=[w_rd[0], x_tile_evs[-1]] + pe_evs))
            l2s.append(em.dma_gp(f"dw2e{sl_}",
                b.w2e[:].rearrange("p (dc f) -> p dc f", f=F)[:, :, sl_ * slw:(sl_ + 1) * slw],
                b.moe_w2T[e][:, :, sl_ * slw:(sl_ + 1) * slw],
                waits=[w_rd[1]] + pe_evs))
        for ti in range(c.NTILE):
            gl = em.dma_gp("dgld", b.gld, b.g_dram[0:8, ti * T:(ti + 1) * T],
                           waits=[gld_free, g_dma])
            gmm = em.pe(lambda e_, e=e, ti=ti: e_.matmul(
                b.pCtx[:, 0:T], b.sel[0:8, e * 128:(e + 1) * 128],
                b.gld, start=True, stop=True),
                waits=[gl, sel_ev, pctx_free], inc=True)
            gld_free = gmm
            gbe = em.v(lambda e_: e_.tensor_copy(b.gB[:], b.pCtx[:, 0:T]),
                       waits=[gmm, gB_free], inc=True)
            pctx_free = gbe
            h_ev = None
            h_evs = [None] * c.NF
            mmh1_last = None
            for fc in range(c.NF):
                bs = fc % 2
                bh1, bh2 = (0, 1) if bs == 0 else (2, 3)
                mmh1 = None
                mmh2 = None
                for dc in range(c.NC):
                    w = [l1s[fc * 128 // slw], x_tile_evs[ti]]
                    if dc == 0:
                        w.append(pa_free[bh1])
                    mmh1 = em.pe(lambda e_, fc=fc, dc=dc, ti=ti, bh1=bh1: e_.matmul(
                        b.pA[bh1][:, 0:T],
                        b.w1e[:, dc * F + fc * 128: dc * F + fc * 128 + 128],
                        b.x[:, dc * c.NT + ti * T: dc * c.NT + (ti + 1) * T],
                        start=(dc == 0), stop=(dc == c.NC - 1)),
                        waits=w, inc=(dc == c.NC - 1))
                for dc in range(c.NC):
                    w = [l2s[fc * 128 // slw]]
                    if dc == 0:
                        w.append(pa_free[bh2])
                    mmh2 = em.pe(lambda e_, fc=fc, dc=dc, ti=ti, bh2=bh2: e_.matmul(
                        b.pA[bh2][:, 0:T],
                        b.w2e[:, dc * F + fc * 128: dc * F + fc * 128 + 128],
                        b.x[:, dc * c.NT + ti * T: dc * c.NT + (ti + 1) * T],
                        start=(dc == 0), stop=(dc == c.NC - 1)),
                        waits=w, inc=(dc == c.NC - 1))
                sev = em.sc(lambda e_, bh1=bh1: e_.activation(
                    b.msil[:], b.pA[bh1][:, 0:T], AF.Silu, bias=0.0, scale=1.0),
                    waits=[mmh1, msil_free], inc=True)
                pa_free[bh1] = sev
                h_ev = em.v(lambda e_, fc=fc, bh2=bh2: e_.tensor_tensor(
                    b.hid[:, fc * T:(fc + 1) * T], b.msil[:], b.pA[bh2][:, 0:T],
                    ALU.mult), waits=[sev, mmh2, hid_rd], inc=True)
                h_evs[fc] = h_ev
                pa_free[bh2] = h_ev
                msil_free = h_ev
                mmh1_last = mmh1
            down_last = None
            stg_ev = None
            for oc in range(c.NC):
                bank = oc % 4
                w3sl = oc % 2
                l3 = em.dma_gp(f"dw3s{w3sl}",
                               b.w3s[w3sl].rearrange("p (fc m) -> p fc m", m=128),
                               b.moe_w3T[e][:, :, oc * 128:(oc + 1) * 128],
                               waits=[w3_free[w3sl]] + pe_evs)
                mm = None
                for fc in range(c.NF):
                    w = [h_evs[fc], l3]
                    if fc == 0:
                        w.append(pa_free[bank])
                    mm = em.pe(lambda e_, oc=oc, fc=fc, bank=bank, w3sl=w3sl: e_.matmul(
                        b.pA[bank][:, 0:T],
                        b.w3s[w3sl][:, fc * 128:(fc + 1) * 128],
                        b.hid[:, fc * T:(fc + 1) * T],
                        start=(fc == 0), stop=(fc == c.NF - 1)),
                        waits=w, inc=(fc == c.NF - 1))
                w3_free[w3sl] = mm
                down_last = mm
                ge = em.v(lambda e_, oc=oc, bank=bank: e_.tensor_tensor(
                    b.mt1[:], b.pA[bank][:, 0:T], b.gB[:], ALU.mult),
                    waits=[mm, gbe, stage_dma, mt1_free], inc=True)
                pa_free[bank] = ge
                if e == 0:
                    stg_ev = em.v(lambda e_, oc=oc, ti=ti: e_.tensor_tensor(
                        b.acc[:, oc * T:(oc + 1) * T], b.mt1[:],
                        b.x[:, oc * c.NT + ti * T: oc * c.NT + (ti + 1) * T], ALU.add),
                        waits=[ge], inc=True)
                else:
                    stg_ev = em.v(lambda e_, oc=oc: e_.tensor_copy(
                        b.acc[:, oc * T:(oc + 1) * T], b.mt1[:]),
                        waits=[ge], inc=True)
                mt1_free = stg_ev
            hid_rd = down_last
            gB_free = stg_ev
            dma = em.dma_gp(f"dacc{ti}",
                            b.accm[:, :, ti * T:(ti + 1) * T],
                            b.acc[:, 0:c.NC * T].rearrange("p (ch t) -> p ch t", t=T),
                            waits=[stg_ev, acc_dmas[ti]],
                            accum_op=ALU.add if e > 0 else ALU.bypass)
            acc_dmas[ti] = dma
            stage_dma = dma
            if e + 1 < c.E:
                emit_prep(e + 1, 3)
        if e + 1 < c.E:
            emit_prep(e + 1, -1)
        w_rd = [mmh1_last, h_ev]

    out_dmas = []
    rb_free = None
    otm_dma = None
    for ti in range(c.NTILE):
        rb = em.dma_gp("drb", b.acc[:, 0:c.NC * T].rearrange("p (ch t) -> p ch t", t=T),
                       b.accm[:, :, ti * T:(ti + 1) * T],
                       waits=list(acc_dmas) + [rb_free])
        for sub in range(4):
            ev = None
            for ch in range(c.NC):
                slot = ch % 2
                ptv = b.pT[slot][:].bitcast(F32)
                te = em.pe(lambda e_, ch=ch, sub=sub, slot=slot, ptv=ptv: e_.transpose(
                    ptv[:, 0:128],
                    b.acc[:, ch * T + sub * 128: ch * T + sub * 128 + 128],
                    b.ident32[:]),
                    waits=[rb, em.ev("ssc"), ce, b.pT_free[slot]], inc=True)
                ev = em.sc(lambda e_, ch=ch, slot=slot, ptv=ptv: e_.copy(
                    b.otm[:, ch * 128:(ch + 1) * 128], ptv[:, 0:128]),
                    waits=[te, otm_dma], inc=True)
                b.pT_free[slot] = ev
            rb_free = ev
            s1 = em.v(lambda e_: e_.reduce_sum(b.mst[:, 3:4], b.otm, axis=AX.X),
                      waits=[ev], inc=True)
            s2 = em.sc(lambda e_: e_.activation(b.sqf, b.otm, AF.Square,
                                                bias=0.0, scale=1.0,
                                                accum_out=b.mst[:, 4:5]),
                       waits=[ev], inc=True)
            em.v(lambda e_: e_.tensor_scalar(b.mst[:, 5:6], b.mst[:, 3:4],
                                             1.0 / c.D, None, ALU.mult), waits=[s1])
            em.v(lambda e_: e_.tensor_scalar(b.mst[:, 6:7], b.mst[:, 4:5],
                                             1.0 / c.D, None, ALU.mult), waits=[s2])
            em.v(lambda e_: e_.tensor_tensor(b.mst[:, 7:8], b.mst[:, 5:6],
                                             b.mst[:, 5:6], ALU.mult))
            em.v(lambda e_: e_.tensor_sub(b.mst[:, 6:7], b.mst[:, 6:7], b.mst[:, 7:8]))
            vv = em.v(lambda e_: e_.tensor_scalar(b.mst[:, 6:7], b.mst[:, 6:7],
                                                  1e-5, None, ALU.add))
            sq = em.sc(lambda e_: e_.activation(b.mst[:, 7:8], b.mst[:, 6:7], AF.Sqrt,
                                                bias=0.0, scale=1.0), waits=[vv], inc=True)
            rr = em.v(lambda e_: e_.reciprocal(b.mst[:, 8:9], b.mst[:, 7:8]),
                      waits=[sq])
            ap = em.v(lambda e_: e_.tensor_scalar(
                b.otm, b.otm, b.mst[:, 5:6], b.mst[:, 8:9],
                ALU.subtract, ALU.mult), waits=[rr])
            r0 = ti * T + sub * 128
            otm_dma = em.dma_sy("dout0", xoflat[r0:r0 + 128, :], b.otm, waits=[ap])
            out_dmas.append(otm_dma)
    return out_dmas



from concourse.bass_utils import run_bass_kernel_spmd

_PROG_CACHE = {}


def _get_prog():
    if "fused" not in _PROG_CACHE:
        cfg = Cfg(SEQ=2048, NSEQ=2)
        _PROG_CACHE["fused"] = build_program(cfg, parts=["l0a", "l0f", "l1a", "moe"])
    return _PROG_CACHE["fused"]


def kernel(L0_attn_x, L0_attn_Wq, L0_attn_Wk, L0_attn_Wv, L0_attn_Wo,
           L0_ffn_w_gate, L0_ffn_w_up, L0_ffn_w_down,
           L1_attn_Wq, L1_attn_Wk, L1_attn_Wv,
           L1_ffn_Wg, L1_ffn_W1, L1_ffn_W2, L1_ffn_W3):
    x = np.ascontiguousarray(np.asarray(L0_attn_x, dtype=np.float32))
    f32 = lambda a: np.ascontiguousarray(np.asarray(a, dtype=np.float32))
    wmap = dict(
        l0_wq=f32(L0_attn_Wq), l0_wk=f32(L0_attn_Wk),
        l0_wv=f32(L0_attn_Wv), l0_wo=f32(L0_attn_Wo),
        l0_wgate=f32(L0_ffn_w_gate), l0_wup=f32(L0_ffn_w_up),
        l0_wdown=f32(L0_ffn_w_down),
        l1_wq=f32(L1_attn_Wq), l1_wk=f32(L1_attn_Wk), l1_wv=f32(L1_attn_Wv),
        l1_wg=f32(L1_ffn_Wg), l1_w1=f32(L1_ffn_W1),
        l1_w2=f32(L1_ffn_W2), l1_w3=f32(L1_ffn_W3))
    nc = _get_prog()
    in_maps = []
    for c in range(8):
        m = dict(x_in=x[2 * c:2 * c + 2])
        m.update(wmap)
        in_maps.append(m)
    res = run_bass_kernel_spmd(nc, in_maps, core_ids=list(range(8)))
    return np.concatenate([np.asarray(r["x_out"]) for r in res.results], axis=0)


# version 07
# speedup vs baseline: 1.3963x; 1.0252x over previous
import numpy as np
from contextlib import ExitStack

import concourse.bass as bass
import concourse.mybir as mybir

F32 = mybir.dt.float32
BF16 = mybir.dt.bfloat16
AF = mybir.ActivationFunctionType
ALU = mybir.AluOpType
AX = mybir.AxisListType

NEG = -1e30


def _ap(x):
    try:
        return x[:]
    except Exception:
        return x



ENGINES = ("pe", "v", "sc", "gp", "sy")


class Em:

    def __init__(self, nc):
        self.nc = nc
        self.streams = {e: [] for e in ENGINES}
        self.cnt = {}
        self.waited = {e: {} for e in ENGINES}
        self.sems = {}

    def ev(self, sem):
        return (sem, self.cnt.get(sem, 0))

    AUTOCHAIN = {"v": "sv", "sc": "ssc", "gp": "sgp"}

    def emit(self, eng, fn, waits=(), inc=None):
        waits = [w for w in waits if w is not None]
        ac = self.AUTOCHAIN.get(eng)
        if ac is not None:
            waits = list(waits) + [(ac, self.cnt.get(ac, 0))]
            if inc is None:
                inc = (ac, 1)
        real = []
        for (s, v) in waits:
            if v > self.waited[eng].get(s, 0):
                self.waited[eng][s] = v
                real.append((s, v))
        evt = None
        if inc is not None:
            s, a = inc
            self.cnt[s] = self.cnt.get(s, 0) + a
            evt = (s, self.cnt[s])
        self.streams[eng].append((real, fn, inc))
        return evt

    def pe(self, fn, waits=(), inc=False):
        return self.emit("pe", fn, waits, ("spe", 1) if inc else None)

    def v(self, fn, waits=(), inc=False):
        return self.emit("v", fn, waits, ("sv", 1) if inc else None)

    def sc(self, fn, waits=(), inc=False):
        return self.emit("sc", fn, waits, ("ssc", 1) if inc else None)

    def gp(self, fn, waits=(), inc=False):
        return self.emit("gp", fn, waits, ("sgp", 1) if inc else None)

    def phase_barrier(self):
        waits = [(s, v) for s, v in self.cnt.items() if v > 0]
        for eng in ENGINES:
            fn = (lambda e: e.nop())
            self.emit(eng, fn, waits=list(waits))

    def dma_sy(self, sem, out, in_, waits=()):
        return self.emit("sy", lambda e: e.dma_start(out=out, in_=in_), waits, (sem, 16))

    def dma_gp(self, sem, out, in_, waits=(), **kw):
        return self.emit("gp", lambda e: e.dma_start(out=out, in_=in_, **kw), waits, (sem, 16))

    def play(self, block):
        eng_attr = {"pe": "tensor", "v": "vector", "sc": "scalar", "gp": "gpsimd", "sy": "sync"}
        for name in ENGINES:
            stream = self.streams[name]

            def body(eng, stream=stream):
                for waits, fn, inc in stream:
                    for s, v in waits:
                        eng.wait_ge(self.sems[s], v)
                    ins = fn(eng)
                    if inc is not None:
                        ins.then_inc(self.sems[inc[0]], inc[1])

            getattr(block, eng_attr[name])(body)


class Cfg:
    def __init__(self, SEQ=2048, NSEQ=2):
        self.SEQ = SEQ
        self.NSEQ = NSEQ
        self.NT = SEQ * NSEQ
        self.D = 1024
        self.NC = 8
        self.HEADS = 16
        self.HD = 64
        self.WIN = 128
        self.G = 64
        self.I = 2816
        self.NI = 22
        self.E = 8
        self.F = 2048
        self.NF = 16
        self.TT = 512
        self.NTILE = self.NT // self.TT
        self.WPT = self.TT // self.WIN
        self.NWIN = SEQ // self.WIN
        self.LW = 512




class Bufs:
    pass


def alloc_bufs(nc, ctx, cfg, parts):
    b = Bufs()
    sb = lambda name, shape, dt: ctx.enter_context(nc.sbuf_tensor(name, shape, dt))
    ps = lambda name, shape, dt: ctx.enter_context(nc.psum_tensor(name, shape, dt))
    c = cfg

    b.x = sb("x_sb", [128, c.NC * c.NT], BF16)

    b.ident = sb("ident_bf", [128, 128], BF16)
    b.ones_bf = sb("ones_bf", [128, 128], BF16)
    b.ident32 = sb("ident_f32", [128, 128], F32) if "moe" in parts else None
    b.ones32 = None
    if any(p in parts for p in ("l0a", "l0f", "l1a")):
        b.onesc32 = sb("onesc32", [128, 1], F32)
        b.onescbf = sb("onescbf", [128, 1], BF16)
        b.onesr32 = sb("onesr32", [1, 128], F32)
    else:
        b.onesc32 = None

    b.pA = [ps(f"pA{i}", [128, 512], F32) for i in range(4)]
    b.pCtx = ps("pCtx", [128, 1024], F32)
    b.pT = [ps("pTa", [128, 256], BF16), ps("pTb", [128, 256], BF16)]

    b.xin = None

    b.acc = sb("acc_fm", [128, c.NC * c.TT], F32)
    if any(p in parts for p in ("l0a", "l0f", "l1a")):
        b.sq = sb("sq_bf", [128, 2 * c.TT], BF16)
        b.strow = sb("strow", [1, 3 * c.TT], F32)

    if "moe" in parts:
        b.w1e = sb("w1e", [128, c.NC * c.F], BF16)
        b.w2e = sb("w2e", [128, c.NC * c.F], BF16)
        b.w3s = [sb(f"w3s{i}", [128, c.NF * 128], BF16)[:] for i in range(2)]
        b.scores = sb("scores_m", [8, c.NT], BF16)
        b.g_fm = b.scores
        b.sel = sb("sel_m", [8, 8 * 128], BF16)
        b.hid = sb("hid_m", [128, c.NF * 512], BF16)
        b.gB = sb("gB_m", [128, 512], F32)
        b.msil = sb("msil", [128, 512], BF16)
        b.mt1 = sb("mt1", [128, 512], BF16)

        b.mst = sb("mst", [128, 12], F32)
        b.sctm = sb("sctm", [128, 8], BF16)
        b.top8 = sb("top8", [128, 8], BF16)
        b.rep = sb("rep", [128, 8], BF16)
        b.ms2 = sb("ms2", [128, 8], BF16)
        b.ms3 = sb("ms3", [128, 8], BF16)
        b.mk1 = sb("mk1", [128, 8], F32)
        b.mk2 = sb("mk2", [128, 8], F32)
        b.gtm = sb("gtm", [128, 8], F32)
        b.gbf = sb("gbf", [128, 8], BF16)
        b.wgT = sb("wgT", [128, c.NC * 8], BF16)
        hv = b.hid[:].bitcast(F32)
        b.sqf = hv[:, 0:c.D]
        b.wg_sb = hv[0:8, c.D:2 * c.D]
        b.otm = hv[:, 2 * c.D:3 * c.D]
        b.wload = b.hid[:].bitcast(BF16)[:, 4 * c.D:6 * c.D]
        gv = b.gB[:].bitcast(BF16)
        b.wtb = [gv[:, 0:128], gv[:, 128:256]]


    if "l0a" in parts or "l1a" in parts or "moe" in parts:
        nw = 4 if "l0a" in parts else 0
        b.wT = [sb(f"wT{i}", [128, c.NC * c.D], BF16) for i in range(nw)]
        if "moe" not in parts:
            b.wload = sb("wload", [128, c.NC // 4 * c.D], BF16)
    if "l0a" in parts:
        b.qb = sb("qb", [128, c.NC * c.TT], BF16)
        b.kb = sb("kb", [128, c.NC * c.TT], BF16)
        b.vb = sb("vb", [128, c.NC * c.TT], BF16)

    if "l1a" in parts:
        b.kS = sb("kS", [128, c.NC * c.SEQ], BF16)
        b.vS = sb("vS", [128, (c.SEQ // 128) * c.D], BF16)
        b.qg = sb("qg", [128, c.NC * 512], BF16)
        b.vtmp = sb("vtmp", [128, 512], BF16)
        b.cmask = sb("cmask", [128, 4 * 512], BF16)
        b.probs1 = sb("probs1", [128, 4 * 512], BF16)
        b.probsT1 = sb("probsT1", [128, 16 * 128], BF16)
        b.nm1 = sb("nm1", [128, 16], F32)
        b.wstg = [sb(f"wstg{i}", [128, c.NC * 128], BF16)[:] for i in range(2)]
        b.wstg_free = [None, None]
        b.wload2 = [sb(f"wload2_{i}", [128, c.NC // 4 * c.D], BF16)[:] for i in range(2)]
        b.wload_free = [None, None]
        b.wload_par = 0
        b.pT_free = [None, None]
        b.wstr = [sb(f"wstr{i}", [128, c.NC * 128], BF16) for i in range(2)]

    if "l0f" in parts:
        b.wq_in = sb("wq_in", [128, c.I], F32)
        b.wq_in_b = sb("wq_in_b", [128, c.I], F32)
        b.wtb = [sb(f"wtb{i}", [128, 128], BF16) for i in range(2)]
        b.qsets = [(sb("qs_xn0", [128, c.LW], F32)[:],
                    sb("qs_g0", [128, c.LW], F32)[:],
                    sb("qs_a0", [128, c.LW], F32)[:],
                    sb("qs_c0", [128, c.LW], F32)[:],
                    sb("qs_sc0", [128, 6 * (c.LW // 16)], F32)[:])]
        b.qu = sb("qu", [128, 2 * c.LW], BF16)[:]
        b.xtm32 = None
        b.xqtm = None
        b.xq_fm = sb("xq_fm", [128, c.NC * c.TT], BF16)
        b.hb = sb("hb", [128, 4 * c.I], BF16)
        b.hq_tm = sb("hq_tm", [128, c.I], BF16)
        b.hq_fm = sb("hq_fm", [128, c.NI * c.TT], BF16)
        b.wg_s = [sb("wg_s0", [128, c.NC * 512], BF16)] * 2
        b.wup_s = [sb("wup_s0", [128, c.NC * 512], BF16)] * 2
        b.wd_s = [sb(f"wd_s{i}", [128, c.NI * 128], BF16) for i in range(2)]
        b.silu = sb("silu_b", [128, 512], BF16)
        b.wq_in2 = [b.wq_in[:], b.wq_in_b[:]]
        wiv = b.wq_in[:].bitcast(BF16)
        wiv2 = b.wq_in_b[:].bitcast(BF16)
        b.xin = [wiv[:, 0:c.D], wiv[:, c.D:2 * c.D]]
        b.xtm32_2 = [b.wq_in[:][:, 0:c.D], b.wq_in_b[:][:, 0:c.D]]
        b.xqtm2 = [wiv[:, 2 * c.D:3 * c.D], wiv2[:, 2 * c.D:3 * c.D]]
        hqv = b.hq_fm[:]
        b.wstg = [hqv[:, 0:c.NI * 128], hqv[:, c.NI * 128:2 * c.NI * 128]]
        b.wstg_free = [None, None]
        b.wq_out2 = [b.hq_tm[:], hqv[:, 2 * c.NI * 128:2 * c.NI * 128 + c.I]]

    if "l0a" in parts:
        b.kg = [sb(f"kg{s}", [128, c.NC * c.G], BF16) for s in range(c.NSEQ)]
        b.vgT = [sb(f"vgT{s}", [c.G, c.D], BF16) for s in range(c.NSEQ)]
        b.vwT = sb("vwT", [128, c.D], BF16)
        b.expb = [sb(f"expb{i}", [128, c.G + c.WIN], F32) for i in range(2)]
        b.nm = sb("nm_sums", [128, 8], F32)
        b.probs = [sb(f"probs{i}", [128, c.G + c.WIN], BF16) for i in range(2)]
        b.probsT = [sb(f"probsT{i}", [128, 256], BF16) for i in range(2)]
        b.ctxb = sb("ctxb", [128, c.NC * c.TT], BF16)

    if b.xin is None and "moe" in parts:
        w1v = b.w1e[:]
        b.xin = [w1v[:, 0:c.D], w1v[:, c.D:2 * c.D]]
    if b.xin is None:
        b.xin = [sb(f"xin{i}", [128, c.D], BF16)[:] for i in range(2)]
    return b




class Carver:
    def __init__(self, base_ap, limit_bytes):
        self.base = base_ap
        self.off = 0
        self.limit = limit_bytes

    def take(self, dt, cols, parts=128):
        sz = 4 if dt == F32 else 2
        off = (self.off + 31) // 32 * 32
        self.off = off + cols * sz
        assert self.off <= self.limit, f"arena overflow: {self.off} > {self.limit}"
        return self.base.bitcast(dt)[0:parts, off // sz: off // sz + cols]


def alloc_bufs_fused(nc, ctx, cfg):
    b = Bufs()
    c = cfg
    sb = lambda name, shape, dt: ctx.enter_context(nc.sbuf_tensor(name, shape, dt))
    ps = lambda name, shape, dt: ctx.enter_context(nc.psum_tensor(name, shape, dt))

    b.x = sb("x_sb", [128, c.NC * c.NT], BF16)
    b.acc = sb("acc_fm", [128, c.NC * c.TT], F32)
    b.sq = sb("sq_bf", [128, 2 * c.TT], BF16)
    b.strow = sb("strow", [1, 3 * c.TT], F32)
    b.ident = sb("ident_bf", [128, 128], BF16)
    b.ident32 = sb("ident_f32", [128, 128], F32)
    b.ones_bf = sb("ones_bf", [128, 128], BF16)
    b.ones32 = None
    b.onesc32 = sb("onesc32", [128, 1], F32)
    b.onescbf = sb("onescbf", [128, 1], BF16)
    b.onesr32 = sb("onesr32", [1, 128], F32)
    b.xin = [sb(f"xin{i}", [128, c.D], BF16)[:] for i in range(2)]

    b.pA = [ps(f"pA{i}", [128, 512], F32) for i in range(4)]
    b.pCtx = ps("pCtx", [128, 1024], F32)
    b.pT = [ps("pTa", [128, 256], BF16), ps("pTb", [128, 256], BF16)]

    rem = nc.sbuf_bytes_remaining - 64
    AR = sb("arena", [128, rem // 2 // 32 * 32], BF16)[:]
    arena_bytes = AR.shape[1] * 2
    b.ph = {}

    cv = Carver(AR, arena_bytes)
    d = {}
    d["wT"] = [cv.take(BF16, c.NC * c.D) for _ in range(4)]
    d["qb"] = cv.take(BF16, c.NC * c.TT)
    d["kb"] = cv.take(BF16, c.NC * c.TT)
    d["vb"] = cv.take(BF16, c.NC * c.TT)
    d["ctxb"] = cv.take(BF16, c.NC * c.TT)
    d["wload"] = d["ctxb"][:, 0:c.NC // 4 * c.D]
    d["kg"] = [cv.take(BF16, c.NC * c.G) for _ in range(c.NSEQ)]
    d["vgT"] = [cv.take(BF16, c.D, parts=c.G) for _ in range(c.NSEQ)]
    d["vwT"] = cv.take(BF16, c.D)
    d["expb"] = [cv.take(F32, c.G + c.WIN) for _ in range(2)]
    d["probs"] = [cv.take(BF16, c.G + c.WIN) for _ in range(2)]
    d["probsT"] = [cv.take(BF16, 256) for _ in range(2)]
    d["nm"] = cv.take(F32, 8)
    d["x32st"] = cv.take(F32, c.D)
    b.ph["l0a"] = d

    cv = Carver(AR, arena_bytes)
    d = {}
    d["qsets"] = [(cv.take(F32, c.LW), cv.take(F32, c.LW), cv.take(F32, c.LW),
                   cv.take(F32, c.LW), cv.take(F32, 6 * (c.LW // 16)))]
    d["qu"] = cv.take(BF16, 2 * c.LW)
    d["xtm32_2"] = [cv.take(F32, c.D) for _ in range(2)]
    d["xqtm2"] = [cv.take(BF16, c.D) for _ in range(2)]
    d["xq_fm"] = cv.take(BF16, c.NC * c.TT)
    hb = cv.take(BF16, 4 * c.I)
    d["hb"] = hb
    d["wq_in2"] = [hb.bitcast(F32)[:, 0:c.I], hb.bitcast(F32)[:, c.I:2 * c.I]]
    d["hq_tm"] = cv.take(BF16, c.I)
    hq_fm = cv.take(BF16, c.NI * c.TT)
    d["hq_fm"] = hq_fm
    d["wq_out2"] = [d["hq_tm"][:] if hasattr(d["hq_tm"], "ap") else d["hq_tm"],
                    hq_fm[:, 2 * c.NI * 128:2 * c.NI * 128 + c.I]]
    d["wstg"] = [hq_fm[:, 0:c.NI * 128], hq_fm[:, c.NI * 128:2 * c.NI * 128]]
    d["wstg_free"] = [None, None]
    wg0 = cv.take(BF16, c.NC * 512)
    wu0 = cv.take(BF16, c.NC * 512)
    d["wg_s"] = [wg0, wg0]
    d["wup_s"] = [wu0, wu0]
    d["wd_s"] = [cv.take(BF16, c.NI * 128) for _ in range(2)]
    d["silu"] = cv.take(BF16, 512)
    b.ph["l0f"] = d

    cv = Carver(AR, arena_bytes)
    d = {}
    d["kS"] = cv.take(BF16, c.NC * c.SEQ)
    d["vS"] = cv.take(BF16, (c.SEQ // 128) * c.D)
    d["qg"] = cv.take(BF16, c.NC * 512)
    d["vtmp"] = cv.take(BF16, 512)
    d["cmask"] = cv.take(BF16, 4 * 512)
    d["probs1"] = cv.take(BF16, 4 * 512)
    d["probsT1"] = cv.take(BF16, 16 * 128)
    d["nm1"] = cv.take(F32, 16)
    d["wstr"] = [cv.take(BF16, c.NC * 128) for _ in range(2)]
    d["wstg"] = [cv.take(BF16, c.NC * 128) for _ in range(2)]
    d["wstg_free"] = [None, None]
    d["wload2"] = [cv.take(BF16, c.NC // 4 * c.D) for _ in range(2)]
    d["wload_free"] = [None, None]
    d["wload_par"] = 0
    d["pT_free"] = [None, None]
    b.ph["l1a"] = d

    cv = Carver(AR, arena_bytes)
    d = {}
    d["w1e"] = cv.take(BF16, c.NC * c.F)
    d["w2e"] = cv.take(BF16, c.NC * c.F)
    d["w3s"] = [cv.take(BF16, c.NF * 128) for _ in range(2)]
    hid = cv.take(BF16, c.NF * 512)
    d["hid"] = hid
    hv = hid.bitcast(F32)
    d["sqf"] = hv[:, 0:c.D]
    d["wg_sb"] = hv[0:8, c.D:2 * c.D]
    d["otm"] = hv[:, 2 * c.D:3 * c.D]
    R = cv.take(BF16, 3 * c.NC // 4 * c.D)
    d["scores"] = R[0:8, 0:c.NT]
    d["g_fm"] = d["scores"]
    d["wload2"] = [R[:, 0:c.NC // 4 * c.D], cv.take(BF16, c.NC // 4 * c.D)]
    d["wload_free"] = [None, None]
    d["wload_par"] = 0
    d["pT_free"] = [None, None]
    d["wstg"] = [R[:, c.NC // 4 * c.D:2 * (c.NC // 4) * c.D],
                 R[:, 2 * (c.NC // 4) * c.D:3 * (c.NC // 4) * c.D]]
    d["wstg_free"] = [None, None]
    d["gld"] = cv.take(BF16, 512, parts=8)
    d["sel"] = cv.take(BF16, 8 * 128, parts=8)
    gB = cv.take(F32, 512)
    d["gB"] = gB
    d["msil"] = cv.take(BF16, 512)
    d["mt1"] = cv.take(BF16, 512)
    d["mst"] = cv.take(F32, 12)
    d["sctm"] = cv.take(BF16, 8)
    d["top8"] = cv.take(BF16, 8)
    d["rep"] = cv.take(BF16, 8)
    d["ms2"] = cv.take(BF16, 8)
    d["ms3"] = cv.take(BF16, 8)
    d["mk1"] = cv.take(F32, 8)
    d["mk2"] = cv.take(F32, 8)
    d["gtm"] = cv.take(F32, 8)
    d["gbf"] = cv.take(BF16, 8)
    d["wgT"] = cv.take(BF16, c.NC * 8)
    b.ph["moe"] = d
    return b


def phase_bufs(b, name):
    nb = Bufs()
    nb.__dict__.update({k: v for k, v in b.__dict__.items() if k != "ph"})
    nb.__dict__.update(b.ph[name])
    return nb




def emit_consts(em, b, cfg):
    e1 = em.gp(lambda e: e.memset(b.ones_bf[:], 1.0), inc=True)
    em.gp(lambda e: e.affine_select(
        out=b.ident[:], in_=b.ones_bf[:], pattern=[[-1, 128]],
        compare_op=ALU.is_equal, fill=0.0, base=0, channel_multiplier=1), waits=[e1], inc=True)
    if b.ident32 is not None:
        e2 = em.v(lambda e: e.memset(b.acc[:, 0:128], 1.0), inc=True)
        em.gp(lambda e: e.affine_select(
            out=b.ident32[:], in_=b.acc[:, 0:128], pattern=[[-1, 128]],
            compare_op=ALU.is_equal, fill=0.0, base=0, channel_multiplier=1), waits=[e2], inc=True)
    if getattr(b, "onesc32", None) is not None:
        em.v(lambda e: e.memset(b.onesc32[:], 1.0), inc=True)
        em.v(lambda e: e.memset(b.onescbf[:], 1.0), inc=True)
        em.v(lambda e: e.memset(b.onesr32[:], 1.0), inc=True)
    return em.gp(lambda e: e.engine_nop(), waits=[em.ev("sv"), em.ev("sgp")], inc=True)


def emit_ingest(em, b, cfg, x_in):
    c = cfg
    xflat = x_in.rearrange("s t d -> (s t) d")
    tile_ev = []
    prev_use = [None, None]
    for ti in range(c.NT // 128):
        buf = ti % 2
        w = [prev_use[buf]] if prev_use[buf] else []
        ld = em.dma_gp(f"dxi{buf}", b.xin[buf], xflat[ti * 128:(ti + 1) * 128, :], waits=w)
        evs = None
        for ch in range(c.NC):
            slot = ch % 2
            te = em.pe(lambda e, ch=ch, buf=buf, slot=slot: e.transpose(
                b.pT[slot][:, 0:128],
                b.xin[buf][:, ch * 128:(ch + 1) * 128], b.ident[:]),
                waits=[ld, em.ev("ssc")], inc=True)
            evs = em.sc(lambda e, ch=ch, ti=ti, slot=slot: e.copy(
                b.x[:, ch * c.NT + ti * 128: ch * c.NT + (ti + 1) * 128],
                b.pT[slot][:, 0:128]),
                waits=[te], inc=True)
        prev_use[buf] = evs
        if (ti + 1) % (c.TT // 128) == 0:
            tile_ev.append(evs)
    return tile_ev


def _consts_wait(em, ce):
    return [ce]


def emit_weight_load_T(em, b, cfg, w_dram, dst, ce, ready_after=()):
    c = cfg
    O = w_dram.shape[0]
    D = w_dram.shape[1]
    noc = O // 128
    ndc = D // 128
    hoc = max(1, (c.NC // 4 * c.D) // D)
    wre = w_dram.rearrange("(oc p) d -> p oc d", p=128)
    last = None
    prev_reads = list(ready_after)
    for og in range(0, noc, hoc):
        og_n = min(hoc, noc - og)
        ld = em.dma_gp("dwl", b.wload[:, 0:og_n * D].rearrange("p (oc d) -> p oc d", d=D),
                       wre[:, og:og + og_n, :], waits=prev_reads)
        te = None
        for oc in range(og, og + og_n):
            for dc in range(ndc):
                slot = (oc * ndc + dc) % 2
                te = em.pe(lambda e, oc=oc, dc=dc, slot=slot, og=og: e.transpose(
                    b.pT[slot][:, 0:128],
                    b.wload[:, (oc - og) * D + dc * 128: (oc - og) * D + (dc + 1) * 128],
                    b.ident[:]),
                    waits=[ld, ce, em.ev("ssc")], inc=True)
                last = em.sc(lambda e, oc=oc, dc=dc, slot=slot: e.copy(
                    dst[:, dc * O + oc * 128: dc * O + (oc + 1) * 128],
                    b.pT[slot][:, 0:128]),
                    waits=[te], inc=True)
        prev_reads = [te]
    return last




def emit_ln_fm(em, b, cfg, ti, acc_ev, prev_ln_ev=None, extra_waits=(), keep_acc=False):
    c = cfg
    T = c.TT
    s1 = None
    for ch in range(c.NC):
        s1 = em.pe(lambda e, ch=ch: e.matmul(
            b.pA[2][0:1, 0:T], b.onesc32[:], b.acc[:, ch * T:(ch + 1) * T],
            start=(ch == 0), stop=(ch == c.NC - 1)),
            waits=[acc_ev] + ([prev_ln_ev] if prev_ln_ev else [])
                  + (list(extra_waits) if ch == 0 else []),
            inc=(ch == c.NC - 1))
    s2 = None
    sq_evs = [None, None]
    for ch in range(c.NC):
        sl = ch % 2
        sq_evs[sl] = em.sc(lambda e, ch=ch, sl=sl: e.activation(
            b.sq[:, sl * T:(sl + 1) * T], b.acc[:, ch * T:(ch + 1) * T],
            AF.Square, bias=0.0, scale=1.0),
            waits=[acc_ev, s2], inc=True)
        s2 = em.pe(lambda e, ch=ch, sl=sl: e.matmul(
            b.pA[3][0:1, 0:T], b.onescbf[:], b.sq[:, sl * T:(sl + 1) * T],
            start=(ch == 0), stop=(ch == c.NC - 1)),
            waits=[sq_evs[sl]] + ([prev_ln_ev] if prev_ln_ev and ch == 0 else []),
            inc=True)
    st = b.strow
    t1 = st[0:1, 0:T]
    m_ = st[0:1, T:2 * T]
    t2 = st[0:1, 2 * T:3 * T]
    mr = t2
    em.v(lambda e: e.tensor_scalar(m_, b.pA[2][0:1, 0:T], 1.0 / c.D, None, ALU.mult),
         waits=[s1], inc=True)
    em.v(lambda e: e.tensor_scalar(t1, b.pA[3][0:1, 0:T], 1.0 / c.D, None, ALU.mult),
         waits=[s2], inc=True)
    em.v(lambda e: e.tensor_tensor(t2, m_, m_, ALU.mult), inc=True)
    em.v(lambda e: e.tensor_sub(t1, t1, t2), inc=True)
    ve = em.v(lambda e: e.tensor_scalar(t1, t1, 1e-5, None, ALU.add), inc=True)
    se = em.sc(lambda e: e.activation(t2, t1, AF.Sqrt, bias=0.0, scale=1.0),
               waits=[ve], inc=True)
    re_ = em.v(lambda e: e.reciprocal(t1, t2), waits=[se], inc=True)
    mre = em.v(lambda e: e.tensor_tensor(mr, m_, t1, ALU.mult), waits=[re_], inc=True)
    rstd = t1
    bc1 = em.pe(lambda e: e.matmul(b.pA[2][:, 0:T], b.onesr32[:], rstd,
                                   start=True, stop=True), waits=[re_, mre], inc=True)
    bc2 = em.pe(lambda e: e.matmul(b.pA[3][:, 0:T], b.onesr32[:], mr,
                                   start=True, stop=True), waits=[mre], inc=True)
    out_ev = None
    for ch in range(c.NC):
        em.v(lambda e, ch=ch: e.tensor_tensor(
            b.acc[:, ch * T:(ch + 1) * T],
            b.acc[:, ch * T:(ch + 1) * T], b.pA[2][:, 0:T], ALU.mult),
            waits=[bc1], inc=True)
        if keep_acc:
            em.v(lambda e, ch=ch: e.tensor_tensor(
                b.acc[:, ch * T:(ch + 1) * T],
                b.acc[:, ch * T:(ch + 1) * T],
                b.pA[3][:, 0:T], ALU.subtract),
                waits=[bc2], inc=True)
            out_ev = em.v(lambda e, ch=ch, ti=ti: e.tensor_copy(
                b.x[:, ch * c.NT + ti * T: ch * c.NT + (ti + 1) * T],
                b.acc[:, ch * T:(ch + 1) * T]), inc=True)
        else:
            out_ev = em.v(lambda e, ch=ch, ti=ti: e.tensor_tensor(
                b.x[:, ch * c.NT + ti * T: ch * c.NT + (ti + 1) * T],
                b.acc[:, ch * T:(ch + 1) * T],
                b.pA[3][:, 0:T], ALU.subtract),
                waits=[bc2], inc=True)
    return out_ev




def emit_egress(em, b, cfg, x_out, waits):
    c = cfg
    eg = [b.acc[:, 0:c.D], b.acc[:, c.D:2 * c.D]]
    xflat = x_out.rearrange("s t d -> (s t) d")
    dmas = []
    prev_dma = [None, None]
    for ti in range(c.NT // 128):
        sl = ti % 2
        ev = None
        for ch in range(c.NC):
            slot = ch % 2
            te = em.pe(lambda e, ch=ch, ti=ti, slot=slot: e.transpose(
                b.pT[slot][:, 0:128],
                b.x[:, ch * c.NT + ti * 128: ch * c.NT + (ti + 1) * 128],
                b.ident[:]),
                waits=list(waits) + [em.ev("ssc")], inc=True)
            ev = em.sc(lambda e, ch=ch, slot=slot, sl=sl: e.copy(
                eg[sl][:, ch * 128:(ch + 1) * 128],
                b.pT[slot][:, 0:128]),
                waits=[te, prev_dma[sl]], inc=True)
        dma = em.dma_sy(f"dout{sl}", xflat[ti * 128:(ti + 1) * 128, :],
                        eg[sl], waits=[ev])
        prev_dma[sl] = dma
        dmas.append(dma)
    return dmas




def emit_l0_attn(em, b, cfg, ce, x_tile_evs, prev_ln_ev=None, x32_dram=None):
    c = cfg
    T = c.TT
    wq_d, wk_d, wv_d, wo_d = b.l0a_w
    wq_ev = emit_weight_load_T(em, b, cfg, wq_d, b.wT[0][:], ce)
    wk_ev = emit_weight_load_T(em, b, cfg, wk_d, b.wT[1][:], ce, ready_after=[wq_ev])
    wv_ev = emit_weight_load_T(em, b, cfg, wv_d, b.wT[2][:], ce, ready_after=[wk_ev])
    wo_ev = emit_weight_load_T(em, b, cfg, wo_d, b.wT[3][:], ce, ready_after=[wv_ev])

    tiles_per_seq = c.SEQ // T
    ln_evs = []
    kg_ev = [None] * c.NSEQ
    vg_ev = [None] * c.NSEQ
    pSc_free = [None, None]
    probs_free = [None, None]
    expb_free = [None, None]
    probsT_free = [None, None]
    pT_free = [None, None]
    pCtx_free = None
    acc_free = None
    x32d = None
    prev_qkv_evac = None

    for ti in range(c.NTILE):
        s = ti // tiles_per_seq
        xev = x_tile_evs[ti]
        evac_evs = []
        for dst_i, (dst, wti, wev) in enumerate(
                ((b.qb, 0, wq_ev), (b.kb, 1, wk_ev), (b.vb, 2, wv_ev))):
            for oc in range(c.NC):
                bank = (dst_i * c.NC + oc) % 2
                mm = None
                for dc in range(c.NC):
                    w = [xev, wev]
                    if dc == 0:
                        w.append(pSc_free[bank] or em.ev("sv"))
                    mm = em.pe(lambda e, wti=wti, oc=oc, dc=dc, bank=bank, ti=ti: e.matmul(
                        b.pA[bank][:, 0:T],
                        b.wT[wti][:, dc * c.D + oc * 128: dc * c.D + (oc + 1) * 128],
                        b.x[:, dc * c.NT + ti * T: dc * c.NT + (ti + 1) * T],
                        start=(dc == 0), stop=(dc == c.NC - 1)),
                        waits=w, inc=(dc == c.NC - 1))
                if dst_i == 0:
                    ev = em.v(lambda e, oc=oc, bank=bank: e.tensor_scalar(
                        b.qb[:, oc * T:(oc + 1) * T], b.pA[bank][:, 0:T],
                        0.125, None, ALU.mult), waits=[mm], inc=True)
                else:
                    ev = em.sc(lambda e, dst=dst, oc=oc, bank=bank: e.copy(
                        dst[:, oc * T:(oc + 1) * T], b.pA[bank][:, 0:T]),
                        waits=[mm], inc=True)
                pSc_free[bank] = ev
                evac_evs.append(ev)
        qkv_ev = evac_evs[-1]
        qkv_all = [em.ev("sv"), em.ev("ssc")]

        for w4 in range(c.WPT):
            widx = ti * c.WPT + w4
            wi = widx % (c.SEQ // c.WIN)
            t0 = w4 * c.WIN
            vw_evs = []
            for chk in range(c.NC):
                slot = chk % 2
                te = em.pe(lambda e, chk=chk, slot=slot, t0=t0: e.transpose(
                    b.pT[slot][:, 0:128],
                    b.vb[:, chk * T + t0: chk * T + t0 + c.WIN],
                    b.ident[:]),
                    waits=qkv_all + [pT_free[slot] or em.ev("ssc")], inc=True)
                ev = em.sc(lambda e, chk=chk, slot=slot: e.copy(
                    b.vwT[:, chk * 128:(chk + 1) * 128],
                    b.pT[slot][:, 0:128]),
                    waits=[te], inc=True)
                pT_free[slot] = ev
                vw_evs.append(ev)
            vw_ev = vw_evs[-1]

            if wi == 0:
                kg_ev[s] = em.v(lambda e, s=s, t0=t0: e.tensor_copy(
                    b.kg[s][:].rearrange("p (c g) -> p c g", g=c.G),
                    b.kb[:].rearrange("p (c t) -> p c t", t=T)[:, :, t0:t0 + c.G]),
                    waits=qkv_all, inc=True)
                gevs = []
                for chk in range(c.NC):
                    slot = chk % 2
                    te = em.pe(lambda e, chk=chk, slot=slot, t0=t0: e.transpose(
                        b.pT[slot][0:c.G, 0:128],
                        b.vb[:, chk * T + t0: chk * T + t0 + c.G],
                        b.ident[:]),
                        waits=qkv_all + [pT_free[slot] or em.ev("ssc")], inc=True)
                    ev = em.sc(lambda e, chk=chk, slot=slot, s=s: e.copy(
                        b.vgT[s][0:c.G, chk * 128:(chk + 1) * 128],
                        b.pT[slot][0:c.G, 0:128]),
                        waits=[te], inc=True)
                    pT_free[slot] = ev
                    gevs.append(ev)
                vg_ev[s] = gevs[-1]

            ctx_mm_evs = []
            for h in range(c.HEADS):
                dc = h // 2
                r0 = (h % 2) * 64
                bank = h % 2
                hb = h % 2
                w0 = [qkv_ev, kg_ev[s], pSc_free[bank] or em.ev("sv")]
                em.pe(lambda e, dc=dc, r0=r0, bank=bank, s=s, t0=t0: e.matmul(
                    b.pA[bank][:, 0:c.G],
                    b.qb[r0:r0 + 64, dc * T + t0: dc * T + t0 + c.WIN],
                    b.kg[s][r0:r0 + 64, dc * c.G: (dc + 1) * c.G],
                    start=True, stop=True), waits=w0, inc=False)
                sc_mm = em.pe(lambda e, dc=dc, r0=r0, bank=bank, t0=t0: e.matmul(
                    b.pA[bank][:, c.G:c.G + c.WIN],
                    b.qb[r0:r0 + 64, dc * T + t0: dc * T + t0 + c.WIN],
                    b.kb[r0:r0 + 64, dc * T + t0: dc * T + t0 + c.WIN],
                    start=True, stop=True), inc=True)
                nmx = em.v(lambda e, bank=bank, hb=hb: e.reduce_max(
                    b.nm[:, hb:hb + 1], b.pA[bank][:, 0:c.G + c.WIN],
                    axis=AX.X, negate=True), waits=[sc_mm], inc=True)
                ex = em.sc(lambda e, bank=bank, hb=hb: e.activation(
                    b.expb[hb][:], b.pA[bank][:, 0:c.G + c.WIN], AF.Exp,
                    bias=b.nm[:, hb:hb + 1], scale=1.0,
                    accum_out=b.nm[:, 2 + hb:3 + hb]),
                    waits=[nmx, expb_free[hb] or em.ev("sv")], inc=True)
                pSc_free[bank] = ex
                rc = em.v(lambda e, hb=hb: e.reciprocal(
                    b.nm[:, 4 + hb:5 + hb], b.nm[:, 2 + hb:3 + hb]), waits=[ex], inc=True)
                pn = em.v(lambda e, hb=hb: e.tensor_scalar(
                    b.probs[hb][:], b.expb[hb][:], b.nm[:, 4 + hb:5 + hb], None,
                    ALU.mult), waits=[rc, probs_free[hb] or em.ev("spe")], inc=True)
                expb_free[hb] = pn
                tl = em.pe(lambda e, hb=hb: e.transpose(
                    b.pT[0][:, 0:128], b.probs[hb][:, c.G:c.G + c.WIN], b.ident[:]),
                    waits=[pn, pT_free[0] or em.ev("ssc")], inc=True)
                tg = em.pe(lambda e, hb=hb: e.transpose(
                    b.pT[1][0:c.G, 0:128], b.probs[hb][:, 0:c.G], b.ident[:]),
                    waits=[pn, pT_free[1] or em.ev("ssc")], inc=True)
                pe1 = em.sc(lambda e, hb=hb: e.copy(
                    b.probsT[hb][:, 0:128], b.pT[0][:, 0:128]),
                    waits=[tl, probsT_free[hb] or em.ev("spe")], inc=True)
                pe2 = em.sc(lambda e, hb=hb: e.copy(
                    b.probsT[hb][0:c.G, 128:128 + c.WIN], b.pT[1][0:c.G, 0:128]),
                    waits=[tg], inc=True)
                pT_free[0] = pe1
                pT_free[1] = pe2
                probs_free[hb] = pe2
                cw = [pe1, pe2, vw_ev, vg_ev[s]]
                if pCtx_free is not None and h == 0:
                    cw.append(pCtx_free)
                em.pe(lambda e, h=h, dc=dc, r0=r0, hb=hb, s=s: e.matmul(
                    b.pCtx[r0:r0 + 64, dc * 128:(dc + 1) * 128],
                    b.vgT[s][:, h * 64:(h + 1) * 64],
                    b.probsT[hb][0:c.G, 128:128 + c.WIN],
                    start=True, stop=False), waits=cw, inc=False)
                cme = em.pe(lambda e, h=h, dc=dc, r0=r0, hb=hb: e.matmul(
                    b.pCtx[r0:r0 + 64, dc * 128:(dc + 1) * 128],
                    b.vwT[:, h * 64:(h + 1) * 64],
                    b.probsT[hb][:, 0:128],
                    start=False, stop=True), inc=True)
                probsT_free[hb] = cme
                ctx_mm_evs.append(cme)
            ctx_ev = None
            for chk in range(c.NC):
                ctx_ev = em.v(lambda e, chk=chk, t0=t0: e.tensor_copy(
                    b.ctxb[:, chk * T + t0: chk * T + t0 + c.WIN],
                    b.pCtx[:, chk * 128:(chk + 1) * 128]),
                    waits=[ctx_mm_evs[-1]], inc=True)
            pCtx_free = ctx_ev

        acc_ev = None
        for oc in range(c.NC):
            bank = oc % 2
            mm = None
            for dc in range(c.NC):
                w = [pCtx_free, wo_ev]
                if dc == 0:
                    w.append(pSc_free[bank] or em.ev("sv"))
                mm = em.pe(lambda e, oc=oc, dc=dc, bank=bank: e.matmul(
                    b.pA[bank][:, 0:T],
                    b.wT[3][:, dc * c.D + oc * 128: dc * c.D + (oc + 1) * 128],
                    b.ctxb[:, dc * T:(dc + 1) * T],
                    start=(dc == 0), stop=(dc == c.NC - 1)),
                    waits=w, inc=(dc == c.NC - 1))
            acc_ev = em.v(lambda e, oc=oc, bank=bank, ti=ti: e.tensor_tensor(
                b.acc[:, oc * T:(oc + 1) * T], b.pA[bank][:, 0:T],
                b.x[:, oc * c.NT + ti * T: oc * c.NT + (ti + 1) * T], ALU.add),
                waits=[mm] + ([acc_free] if oc == 0 else []), inc=True)
            pSc_free[bank] = acc_ev
        ln_ev = emit_ln_fm(em, b, cfg, ti, acc_ev, prev_ln_ev,
                           keep_acc=(x32_dram is not None))
        prev_ln_ev = ln_ev
        if x32_dram is not None:
            for sub in range(4):
                ev = None
                for ch in range(c.NC):
                    slot = ch % 2
                    ptv = b.pT[slot][:].bitcast(F32)
                    te = em.pe(lambda e, ch=ch, sub=sub, slot=slot, ptv=ptv: e.transpose(
                        ptv[:, 0:128],
                        b.acc[:, ch * T + sub * 128: ch * T + sub * 128 + 128],
                        b.ident32[:]),
                        waits=[ln_ev, em.ev("ssc"),
                               pT_free[slot] or em.ev("ssc")], inc=True)
                    ev = em.sc(lambda e, ch=ch, slot=slot, ptv=ptv: e.copy(
                        b.x32st[:, ch * 128:(ch + 1) * 128], ptv[:, 0:128]),
                        waits=[te, x32d], inc=True)
                    pT_free[slot] = ev
                r0 = ti * T + sub * 128
                x32d = em.dma_sy("dx32", x32_dram[r0:r0 + 128, :], b.x32st[:],
                                 waits=[ev])
            acc_free = em.ev("spe")
            ln_ev = x32d
        else:
            acc_free = None
        ln_evs.append(ln_ev)
    return ln_evs




def build_program(cfg, parts):
    c = cfg
    nc = bass.Bass("TRN2", target_bir_lowering=False)
    x_in = nc.dram_tensor("x_in", [c.NSEQ, c.SEQ, c.D], F32, kind="ExternalInput").ap()
    x_out = nc.dram_tensor("x_out", [c.NSEQ, c.SEQ, c.D], F32, kind="ExternalOutput").ap()

    fused = len(parts) > 1 or parts == ["moe"]
    with ExitStack() as ctx:
        if fused:
            b = alloc_bufs_fused(nc, ctx, cfg)
        else:
            b = alloc_bufs(nc, ctx, cfg, parts)
        if "l0a" in parts:
            b.l0a_w = tuple(
                nc.dram_tensor(n, [c.D, c.D], F32, kind="ExternalInput").ap()
                for n in ("l0_wq", "l0_wk", "l0_wv", "l0_wo"))
        if "l1a" in parts:
            b.l1a_w = tuple(
                nc.dram_tensor(n, [c.D, c.D], F32, kind="ExternalInput").ap()
                for n in ("l1_wq", "l1_wk", "l1_wv"))
            b.l1a_wT = tuple(
                nc.dram_tensor(n, [128, c.NC, c.D], BF16).ap()
                for n in ("l1_wqT", "l1_wkT", "l1_wvT"))
        if "moe" in parts:
            b.moe_w = (
                nc.dram_tensor("l1_wg", [c.E, c.D], F32, kind="ExternalInput").ap(),
                nc.dram_tensor("l1_w1", [c.E, c.F, c.D], F32, kind="ExternalInput").ap(),
                nc.dram_tensor("l1_w2", [c.E, c.F, c.D], F32, kind="ExternalInput").ap(),
                nc.dram_tensor("l1_w3", [c.E, c.D, c.F], F32, kind="ExternalInput").ap(),
            )
            b.moe_w1T = nc.dram_tensor("m_w1T", [c.E, 128, c.NC, c.F], BF16).ap()
            b.moe_w2T = nc.dram_tensor("m_w2T", [c.E, 128, c.NC, c.F], BF16).ap()
            b.moe_w3T = nc.dram_tensor("m_w3T", [c.E, 128, c.NF, c.D], BF16).ap()
            b.accm = nc.dram_tensor("m_acc", [128, c.NC, c.NT], F32).ap()
        if "l0f" in parts:
            b.l0f_w = (
                nc.dram_tensor("l0_wgate", [c.I, c.D], F32, kind="ExternalInput").ap(),
                nc.dram_tensor("l0_wup", [c.I, c.D], F32, kind="ExternalInput").ap(),
                nc.dram_tensor("l0_wdown", [c.D, c.I], F32, kind="ExternalInput").ap(),
            )
            b.l0f_wgT = nc.dram_tensor("l0f_wgT", [128, c.NC, c.I], BF16).ap()
            b.l0f_wupT = nc.dram_tensor("l0f_wupT", [128, c.NC, c.I], BF16).ap()
            b.l0f_wdT = nc.dram_tensor("l0f_wdT", [128, c.NI, c.D], BF16).ap()
        if fused:
            b.x32s = nc.dram_tensor("x32s", [c.NT, c.D], F32).ap()
        if "moe" in parts:
            b.g_dram = nc.dram_tensor("g_dram", [8, c.NT], BF16).ap()
        em = Em(nc)

        bp = phase_bufs(b, "l0a") if fused else b
        ce = emit_consts(em, bp, cfg)
        tile_evs = emit_ingest(em, bp, cfg, x_in)
        final_evs = tile_evs
        if "l0a" in parts:
            final_evs = emit_l0_attn(em, bp, cfg, ce, tile_evs,
                                     x32_dram=b.x32s if fused else None)
        if "l0f" in parts:
            if fused:
                em.phase_barrier()
                bp = phase_bufs(b, "l0f")
                x32 = b.x32s
            else:
                x32 = x_in if parts[0] == "l0f" else None
            final_evs = emit_l0_ffn(em, bp, cfg, ce, final_evs, x32_dram=x32)
        if "l1a" in parts:
            if fused:
                em.phase_barrier()
                bp = phase_bufs(b, "l1a")
            final_evs = emit_l1_attn(em, bp, cfg, ce, final_evs)
        if "moe" in parts:
            if fused:
                em.phase_barrier()
                bp = phase_bufs(b, "moe")
            out_dmas = emit_l1_moe(em, bp, cfg, ce, final_evs, x_out)
        else:
            out_dmas = emit_egress(em, bp, cfg, x_out, [final_evs[-1]] if final_evs else [])
        finals = {}
        for s, v in out_dmas:
            finals[s] = max(finals.get(s, 0), v)
        em.emit("sy", lambda e: e.nop(), waits=sorted(finals.items()))

        for s in sorted(em.cnt.keys()):
            em.sems[s] = ctx.enter_context(nc.semaphore(s))
        block = ctx.enter_context(nc.Block())
        em.play(block)
    return nc




def emit_quant(em, b, cfg, src, dst, W, waits, qs=None, dt16=False, dst_war=()):
    c = cfg
    q_xn, q_g, q_a, q_c, q_sc = qs if qs is not None else b.qsets[0]
    LWE = 2 * c.LW if dt16 else c.LW
    ev = None
    for lo in range(0, W, LWE):
        wsub = min(LWE, W - lo)
        nb = wsub // 16
        s3 = src[:, lo:lo + wsub].rearrange("p (a b) -> p a b", b=16)
        sc16 = q_sc[:, 0:nb]
        scl = q_sc[:, nb:2 * nb]
        rscl = q_sc[:, 2 * nb:3 * nb]
        if dt16:
            xn = q_xn.bitcast(BF16)[:, 0:wsub]
            a = q_a.bitcast(BF16)[:, 0:wsub]
        else:
            xn = q_xn[:, 0:wsub]
            a = q_a[:, 0:wsub]
        g = q_g.bitcast(BF16)[:, 0:wsub]
        cc = q_c.bitcast(BF16)[:, 0:wsub]
        u = b.qu[:, 0:wsub] if dt16 else q_c.bitcast(BF16)[:, c.LW:c.LW + wsub]
        xn3 = xn.rearrange("p (a b) -> p a b", b=16)
        rev = em.v(lambda e, s3=s3, sc16=sc16: e.reduce_max(
            sc16, s3, axis=AX.X, apply_absolute_value=True),
            waits=list(waits) + ([ev] if ev is not None else []))
        sev2 = em.v(lambda e, scl=scl, sc16=sc16: e.tensor_scalar(
            scl, sc16, 1.0 / 6.0, 1e-12, ALU.mult, ALU.max), waits=[rev])
        rcev = em.v(lambda e, rscl=rscl, scl=scl: e.reciprocal(rscl, scl))
        vev = em.v(lambda e, xn3=xn3, s3=s3, rscl=rscl, nb=nb: e.tensor_tensor(
            xn3, s3, rscl.unsqueeze(2).broadcast_to((128, nb, 16)), ALU.mult),
            waits=[rcev])
        em.sc(lambda e, g=g, xn=xn: e.sign(g, xn), waits=[vev])
        sev = em.sc(lambda e, a=a, xn=xn: e.activation(a, xn, AF.Abs, bias=0.0, scale=1.0))
        em.v(lambda e, cc=cc, a=a: e.tensor_scalar(cc, a, 0.25, 0.5, ALU.is_gt, ALU.mult),
             waits=[sev])
        for mid, inc in ((0.75, 0.5), (1.25, 0.5), (1.75, 0.5),
                         (2.5, 1.0), (3.5, 1.0), (5.0, 2.0)):
            em.v(lambda e, uu=u, a=a, mid=mid, inc=inc: e.tensor_scalar(
                uu, a, mid, inc, ALU.is_gt, ALU.mult))
            em.v(lambda e, cc=cc, uu=u: e.tensor_tensor(cc, cc, uu, ALU.add))
        gev = em.v(lambda e, cc=cc, g=g: e.tensor_tensor(cc, cc, g, ALU.mult))
        cc3 = cc.rearrange("p (a b) -> p a b", b=16)
        d3 = dst[:, lo:lo + wsub].rearrange("p (a b) -> p a b", b=16)
        ev = em.v(lambda e, d3=d3, cc3=cc3, scl=scl, nb=nb: e.tensor_tensor(
            d3, cc3, scl.unsqueeze(2).broadcast_to((128, nb, 16)), ALU.mult),
            waits=[gev] + list(dst_war))
    return ev

def emit_l0f_weight_prep(em, b, cfg, ce, after=()):
    c = cfg
    jobs = (
        (b.l0f_w[0], b.l0f_wgT, c.I, c.D),
        (b.l0f_w[1], b.l0f_wupT, c.I, c.D),
        (b.l0f_w[2], b.l0f_wdT, c.D, c.I),
    )
    in_free = [list(after), list(after)]
    out_free = [None, None]
    pT_free = [em.ev("ssc"), em.ev("ssc")]
    rc_idx = 0
    for (w_d, wT_d, R, W) in jobs:
        nrc = R // 128
        nwc = W // 128
        wre = w_d.rearrange("(rc p) d -> p rc d", p=128)
        for rc in range(nrc):
            sl = rc_idx % 2
            ld = em.dma_gp(f"dwqin{sl}", b.wq_in2[sl][:, 0:W], wre[:, rc, :],
                           waits=in_free[sl] + [ce])
            qev = emit_quant(em, b, cfg, b.wq_in2[sl][:, 0:W],
                             b.wq_out2[sl][:, 0:W], W, [ld],
                             dst_war=[out_free[sl]])
            in_free[sl] = [qev]
            ssl = rc_idx % 2
            sev = None
            te = None
            for wc in range(nwc):
                slot = wc % 2
                te = em.pe(lambda e, wc=wc, slot=slot, sl=sl: e.transpose(
                    b.pT[slot][:, 0:128],
                    b.wq_out2[sl][:, wc * 128:(wc + 1) * 128], b.ident[:]),
                    waits=[qev, ce, pT_free[slot]], inc=True)
                sev = em.sc(lambda e, wc=wc, slot=slot, ssl=ssl: e.copy(
                    b.wstg[ssl][:, wc * 128:(wc + 1) * 128],
                    b.pT[slot][:, 0:128]),
                    waits=[te, b.wstg_free[ssl]], inc=True)
                pT_free[slot] = sev
            out_free[sl] = te
            dma = em.dma_gp(f"dstg{ssl}",
                            wT_d[:, :, rc * 128:(rc + 1) * 128],
                            b.wstg[ssl][:, 0:nwc * 128].rearrange(
                                "p (wc m) -> p wc m", m=128),
                            waits=[sev])
            b.wstg_free[ssl] = dma
            rc_idx += 1
    return [em.ev("dstg0"), em.ev("dstg1")]

def emit_l0_ffn(em, b, cfg, ce, x_tile_evs, prev_ln_ev=None, x32_dram=None):
    c = cfg
    T = c.TT
    if x32_dram is None:
        xflat32 = None
    elif len(x32_dram.shape) == 3:
        xflat32 = x32_dram.rearrange("s t d -> (s t) d")
    else:
        xflat32 = x32_dram
    prep_evs = emit_l0f_weight_prep(em, b, cfg, ce, after=[x_tile_evs[-1]])
    NFB = (c.I + 511) // 512
    ln_evs = []
    pa_free = [None, None, None, None]
    pT_free = [em.ev("ssc"), em.ev("ssc")]
    qi = [0]
    xq_rd = None
    hb_rd = [None] * 4
    hqfm_rd = None
    wg_free = [None, None]
    wd_free = [None, None]
    silu_free = None
    xtm_free = [None, None]
    xqtm_free = [None, None]
    qset_free = [None, None]
    qsh_free = [None, None]
    hqtm_free = None

    for ti in range(c.NTILE):
        xev = x_tile_evs[ti]
        xq_ev = None
        xq_sub_ev = [None] * 4
        for sub in range(4):
            sl = sub % 2
            r0 = ti * T + sub * 128
            ev32 = em.dma_gp(f"dxq32_{sl}", b.xtm32_2[sl], xflat32[r0:r0 + 128, :],
                             waits=[xev, xtm_free[sl]] + prep_evs)
            qev = emit_quant(em, b, cfg, b.xtm32_2[sl], b.xqtm2[sl], c.D,
                             [ev32], dst_war=[xqtm_free[sl]])
            xtm_free[sl] = qev
            te = None
            for ch in range(c.NC):
                slot = ch % 2
                te = em.pe(lambda e, ch=ch, slot=slot, sl=sl: e.transpose(
                    b.pT[slot][:, 0:128],
                    b.xqtm2[sl][:, ch * 128:(ch + 1) * 128], b.ident[:]),
                    waits=[qev, pT_free[slot]], inc=True)
                xq_ev = em.sc(lambda e, ch=ch, sub=sub, slot=slot: e.copy(
                    b.xq_fm[:, ch * T + sub * 128: ch * T + sub * 128 + 128],
                    b.pT[slot][:, 0:128]),
                    waits=[te, xq_rd], inc=True)
                pT_free[slot] = xq_ev
            xq_sub_ev[sub] = xq_ev
            xqtm_free[sl] = te
        h_ev = None
        h_sub_ev = [None] * 4
        for fb in range(NFB):
            fw = min(512, c.I - fb * 512)
            gsl = 0
            wgl = em.dma_gp(f"dwg{gsl}",
                            b.wg_s[gsl][:, 0:c.NC * fw].rearrange("p (dc f) -> p dc f", f=fw),
                            b.l0f_wgT[:, :, fb * 512: fb * 512 + fw],
                            waits=[wg_free[gsl]] + prep_evs)
            wul = em.dma_gp(f"dwu{gsl}",
                            b.wup_s[gsl][:, 0:c.NC * fw].rearrange("p (dc f) -> p dc f", f=fw),
                            b.l0f_wupT[:, :, fb * 512: fb * 512 + fw],
                            waits=[wg_free[gsl]] + prep_evs)
            last_mm = [None, None]
            for sub in range(4):
                bsel = (fb * 4 + sub) % 2
                bg, bu = (0, 1) if bsel == 0 else (2, 3)
                mmg = None
                mmu = None
                for dc in range(c.NC):
                    w = [xq_sub_ev[sub], wgl]
                    if dc == 0:
                        w.append(pa_free[bg])
                    mmg = em.pe(lambda e, dc=dc, sub=sub, fw=fw, gsl=gsl, bg=bg: e.matmul(
                        b.pA[bg][:, 0:fw],
                        b.xq_fm[:, dc * T + sub * 128: dc * T + sub * 128 + 128],
                        b.wg_s[gsl][:, dc * fw: (dc + 1) * fw],
                        start=(dc == 0), stop=(dc == c.NC - 1)),
                        waits=w, inc=(dc == c.NC - 1))
                for dc in range(c.NC):
                    w = [xq_sub_ev[sub], wul]
                    if dc == 0:
                        w.append(pa_free[bu])
                    mmu = em.pe(lambda e, dc=dc, sub=sub, fw=fw, gsl=gsl, bu=bu: e.matmul(
                        b.pA[bu][:, 0:fw],
                        b.xq_fm[:, dc * T + sub * 128: dc * T + sub * 128 + 128],
                        b.wup_s[gsl][:, dc * fw: (dc + 1) * fw],
                        start=(dc == 0), stop=(dc == c.NC - 1)),
                        waits=w, inc=(dc == c.NC - 1))
                sev = em.sc(lambda e, fw=fw, bg=bg: e.activation(
                    b.silu[:, 0:fw], b.pA[bg][:, 0:fw], AF.Silu, bias=0.0, scale=1.0),
                    waits=[mmg, silu_free], inc=True)
                pa_free[bg] = sev
                h_ev = em.v(lambda e, sub=sub, fb=fb, fw=fw, bu=bu: e.tensor_tensor(
                    b.hb[:, sub * c.I + fb * 512: sub * c.I + fb * 512 + fw],
                    b.silu[:, 0:fw], b.pA[bu][:, 0:fw], ALU.mult),
                    waits=[mmu, sev, hb_rd[sub]], inc=True)
                if fb == NFB - 1:
                    h_sub_ev[sub] = h_ev
                pa_free[bu] = h_ev
                silu_free = h_ev
                last_mm[0] = mmg
                last_mm[1] = mmu
            wg_free[gsl] = last_mm[1]
        hq_ev = None
        for sub in range(4):
            qv = emit_quant(em, b, cfg, b.hb[:, sub * c.I:(sub + 1) * c.I],
                            b.hq_tm[:], c.I, [h_sub_ev[sub]],
                            dt16=True, dst_war=[hqtm_free])
            qevs = [qv]
            hb_rd[sub] = qv
            te = None
            for ic in range(c.NI):
                slot = ic % 2
                qev = qevs[(ic * 128) // c.LW if (ic * 128) // c.LW < len(qevs) else -1]
                te = em.pe(lambda e, ic=ic, slot=slot: e.transpose(
                    b.pT[slot][:, 0:128],
                    b.hq_tm[:, ic * 128:(ic + 1) * 128], b.ident[:]),
                    waits=qevs + [pT_free[slot]], inc=True)
                hq_ev = em.sc(lambda e, ic=ic, sub=sub, slot=slot: e.copy(
                    b.hq_fm[:, ic * T + sub * 128: ic * T + sub * 128 + 128],
                    b.pT[slot][:, 0:128]),
                    waits=[te, hqfm_rd], inc=True)
                pT_free[slot] = hq_ev
            hqtm_free = te
        acc_ev = None
        down_mm = None
        wdls = [None] * c.NC
        for oc in range(2):
            wdls[oc] = em.dma_gp(f"dwd{oc % 2}",
                                 _ap(b.wd_s[oc % 2]).rearrange("p (ic o) -> p ic o", o=128),
                                 b.l0f_wdT[:, :, oc * 128:(oc + 1) * 128],
                                 waits=[wd_free[oc % 2]] + prep_evs)
        for oc in range(c.NC):
            dsl = oc % 2
            wdl = wdls[oc]
            bank = oc % 2
            for ic in range(c.NI):
                w = [hq_ev, wdl]
                if ic == 0:
                    w.append(pa_free[bank])
                down_mm = em.pe(lambda e, oc=oc, ic=ic, dsl=dsl, bank=bank: e.matmul(
                    b.pA[bank][:, 0:T],
                    _ap(b.wd_s[dsl])[:, ic * 128:(ic + 1) * 128],
                    b.hq_fm[:, ic * T:(ic + 1) * T],
                    start=(ic == 0), stop=(ic == c.NI - 1)),
                    waits=w, inc=(ic == c.NI - 1))
            wd_free[dsl] = down_mm
            if oc + 2 < c.NC:
                wdls[oc + 2] = em.dma_gp(f"dwd{dsl}",
                    _ap(b.wd_s[dsl]).rearrange("p (ic o) -> p ic o", o=128),
                    b.l0f_wdT[:, :, (oc + 2) * 128:(oc + 3) * 128],
                    waits=[down_mm] + prep_evs)
            acc_ev = em.v(lambda e, oc=oc, bank=bank, ti=ti: e.tensor_tensor(
                b.acc[:, oc * T:(oc + 1) * T], b.pA[bank][:, 0:T],
                b.x[:, oc * c.NT + ti * T: oc * c.NT + (ti + 1) * T], ALU.add),
                waits=[down_mm], inc=True)
            pa_free[bank] = acc_ev
        hqfm_rd = down_mm
        xq_rd = last_mm[1]
        ln_ev = emit_ln_fm(em, b, cfg, ti, acc_ev, prev_ln_ev,
                           extra_waits=[pa_free[2], pa_free[3]])
        pa_free[2] = ln_ev
        pa_free[3] = ln_ev
        prev_ln_ev = ln_ev
        ln_evs.append(ln_ev)
    return ln_evs


class PrepJob:

    def __init__(self, em, b, cfg, w_dram, wT_dram, ce, after=()):
        self.em, self.b, self.cfg = em, b, cfg
        self.wT = wT_dram
        self.ce = ce
        self.O = w_dram.shape[0]
        self.D = w_dram.shape[1]
        self.noc = self.O // 128
        self.ndc = self.D // 128
        self.hoc = max(1, (cfg.NC // 4 * cfg.D) // self.D)
        self.wre = w_dram.rearrange("(oc p) d -> p oc d", p=128)
        self.extra = list(after)
        self.og = 0

    def done(self):
        return self.og >= self.noc

    def load_chunk(self):
        em, b = self.em, self.b
        og = self.og
        og_n = min(self.hoc, self.noc - og)
        self.og += og_n
        D = self.D
        par = b.wload_par
        b.wload_par ^= 1
        ld = em.dma_gp(f"dwl{par}",
                       b.wload2[par][:, 0:og_n * D].rearrange("p (oc d) -> p oc d", d=D),
                       self.wre[:, og:og + og_n, :],
                       waits=self.extra + [b.wload_free[par], self.ce])
        self.extra = []
        return (og, og_n, ld, par)

    def tail_chunk(self, tok):
        em, b = self.em, self.b
        og, og_n, ld, par = tok
        D = self.D
        te = None
        for oc in range(og, og + og_n):
            ssl = oc % 2
            evs = None
            for dc in range(self.ndc):
                slot = dc % 2
                te = em.pe(lambda e, oc=oc, dc=dc, slot=slot, og=og, D=D, par=par: e.transpose(
                    b.pT[slot][:, 0:128],
                    b.wload2[par][:, (oc - og) * D + dc * 128: (oc - og) * D + (dc + 1) * 128],
                    b.ident[:]),
                    waits=[ld, self.ce, b.pT_free[slot]], inc=True)
                evs = em.sc(lambda e, dc=dc, slot=slot, ssl=ssl: e.copy(
                    b.wstg[ssl][:, dc * 128:(dc + 1) * 128],
                    b.pT[slot][:, 0:128]),
                    waits=[te, b.wstg_free[ssl]], inc=True)
                b.pT_free[slot] = evs
            dma = em.dma_gp(f"dstg{ssl}",
                            self.wT[:, :, oc * 128:(oc + 1) * 128],
                            b.wstg[ssl][:, 0:self.ndc * 128].rearrange(
                                "p (dc m) -> p dc m", m=128),
                            waits=[evs])
            b.wstg_free[ssl] = dma
        b.wload_free[par] = te


class PrepQueue:

    def __init__(self, jobs):
        self.jobs = list(jobs)
        self.ji = 0
        self.pending = None

    def done(self):
        return self.ji >= len(self.jobs) and self.pending is None

    def step(self):
        tok = None
        while self.ji < len(self.jobs):
            if not self.jobs[self.ji].done():
                tok = (self.jobs[self.ji], self.jobs[self.ji].load_chunk())
                break
            self.ji += 1
        if self.pending is not None:
            j, t = self.pending
            j.tail_chunk(t)
        self.pending = tok

    def run(self, k=-1):
        n = 0
        while not self.done() and (k < 0 or n < k):
            self.step()
            n += 1


def emit_weight_prep_dram(em, b, cfg, w_dram, wT_dram, ce, after=()):
    q = PrepQueue([PrepJob(em, b, cfg, w_dram, wT_dram, ce, after)])
    q.run()
    return [em.ev("dstg0"), em.ev("dstg1")]


def emit_l1_attn(em, b, cfg, ce, x_tile_evs, prev_ln_ev=None):
    c = cfg
    T = c.TT
    pq = emit_weight_prep_dram(em, b, cfg, b.l1a_w[0], b.l1a_wT[0], ce)
    pk = emit_weight_prep_dram(em, b, cfg, b.l1a_w[1], b.l1a_wT[1], ce)
    pv = emit_weight_prep_dram(em, b, cfg, b.l1a_w[2], b.l1a_wT[2], ce)
    prep_evs = pv

    mz = em.gp(lambda e: e.memset(b.cmask[:], 0.0), waits=[ce], inc=True)
    mask_ev = mz
    for o in range(4):
        mask_ev = em.gp(lambda e, o=o: e.affine_select(
            out=b.cmask[:, o * 512:(o + 1) * 512], in_=b.cmask[:, o * 512:(o + 1) * 512],
            pattern=[[-1, 512]], compare_op=ALU.is_ge, fill=NEG,
            base=128 * o, channel_multiplier=1), waits=[mz], inc=True)

    ngrp = c.SEQ // T
    ntpg = T // 128
    ln_evs = []
    pa_free = [None, None, None, None]
    pT_free = [em.ev("ssc"), em.ev("ssc")]
    pCtx_free = None
    probsT_free = [None, None]
    probs_free = None
    kv_free = None
    qg_free = None
    vtmp_free = None
    wstr_free = [None, None]

    def proj_stream(wT_dram, oc, wsl, extra):
        return em.dma_gp(f"dws{wsl}",
                         b.wstr[wsl][:].rearrange("p (dc m) -> p dc m", m=128),
                         wT_dram[:, :, oc * 128:(oc + 1) * 128],
                         waits=[wstr_free[wsl]] + prep_evs + list(extra))

    def proj_loop(wT_dram, emit_mm, emit_evac):
        wls = [None] * c.NC
        for oc in range(2):
            wls[oc] = proj_stream(wT_dram, oc, oc % 2, [])
        for oc in range(c.NC):
            wsl = oc % 2
            mm = emit_mm(oc, oc % 2, wsl, wls[oc])
            wstr_free[wsl] = mm
            if oc + 2 < c.NC:
                wls[oc + 2] = proj_stream(wT_dram, oc + 2, wsl, [mm])
            emit_evac(oc, oc % 2, mm)

    for s in range(c.NSEQ):
        base = s * c.SEQ
        kv_ev = None
        for tg in range(ngrp):
            ti_glob = (base + tg * T) // T
            xev = x_tile_evs[ti_glob]

            def mk_mm(tg=tg, base=base, xev=xev, extra=None):
                def emit_mm(oc, bank, wsl, wl):
                    mm = None
                    for dc in range(c.NC):
                        w = [xev, wl]
                        if dc == 0:
                            w += [pa_free[bank], kv_free] + ([extra] if extra else [])
                        mm = em.pe(lambda e, oc=oc, dc=dc, bank=bank, tg=tg,
                                   base=base, wsl=wsl: e.matmul(
                            b.pA[bank][:, 0:T],
                            b.wstr[wsl][:, dc * 128:(dc + 1) * 128],
                            b.x[:, dc * c.NT + base + tg * T: dc * c.NT + base + (tg + 1) * T],
                            start=(dc == 0), stop=(dc == c.NC - 1)),
                            waits=w, inc=(dc == c.NC - 1))
                    return mm
                return emit_mm

            def k_evac(oc, bank, mm, tg=tg):
                nonlocal kv_ev
                kv_ev = em.sc(lambda e, oc=oc, bank=bank, tg=tg: e.copy(
                    b.kS[:, oc * c.SEQ + tg * T: oc * c.SEQ + (tg + 1) * T],
                    b.pA[bank][:, 0:T]), waits=[mm], inc=True)
                pa_free[bank] = kv_ev

            def v_evac(oc, bank, mm, tg=tg):
                nonlocal kv_ev, vtmp_free
                vt = em.v(lambda e, bank=bank: e.tensor_copy(
                    b.vtmp[:], b.pA[bank][:, 0:T]), waits=[mm, vtmp_free], inc=True)
                pa_free[bank] = vt
                for sj in range(4):
                    slot = sj % 2
                    te = em.pe(lambda e, sj=sj, slot=slot: e.transpose(
                        b.pT[slot][:, 0:128], b.vtmp[:, sj * 128:(sj + 1) * 128],
                        b.ident[:]), waits=[vt, pT_free[slot] or em.ev("ssc")], inc=True)
                    ev = em.sc(lambda e, oc=oc, tg=tg, sj=sj, slot=slot: e.copy(
                        b.vS[:, (tg * 4 + sj) * c.D + oc * 128: (tg * 4 + sj) * c.D + (oc + 1) * 128],
                        b.pT[slot][:, 0:128]), waits=[te], inc=True)
                    pT_free[slot] = ev
                    vtmp_free = ev
                kv_ev = vtmp_free

            proj_loop(b.l1a_wT[1], mk_mm(), k_evac)
            proj_loop(b.l1a_wT[2], mk_mm(), v_evac)

        for tg in range(ngrp):
            ti_glob = (base + tg * T) // T
            xev = x_tile_evs[ti_glob]
            q_ev = None

            def mk_qmm(tg=tg, base=base, xev=xev):
                def emit_mm(oc, bank, wsl, wl):
                    mm = None
                    for dc in range(c.NC):
                        w = [xev, wl]
                        if dc == 0:
                            w += [pa_free[bank], qg_free]
                        mm = em.pe(lambda e, oc=oc, dc=dc, bank=bank, tg=tg,
                                   base=base, wsl=wsl: e.matmul(
                            b.pA[bank][:, 0:T],
                            b.wstr[wsl][:, dc * 128:(dc + 1) * 128],
                            b.x[:, dc * c.NT + base + tg * T: dc * c.NT + base + (tg + 1) * T],
                            start=(dc == 0), stop=(dc == c.NC - 1)),
                            waits=w, inc=(dc == c.NC - 1))
                    return mm
                return emit_mm

            def q_evac(oc, bank, mm):
                nonlocal q_ev
                q_ev = em.v(lambda e, oc=oc, bank=bank: e.tensor_scalar(
                    b.qg[:, oc * T:(oc + 1) * T], b.pA[bank][:, 0:T],
                    1.0 / 32.0, None, ALU.mult), waits=[mm], inc=True)
                pa_free[bank] = q_ev

            proj_loop(b.l1a_wT[0], mk_qmm(), q_evac)

            for qt in range(ntpg):
                i = tg * ntpg + qt
                nblk = i // 4 + 1
                exp_evs = []
                for j in range(nblk):
                    diag = (j == nblk - 1)
                    mmS = None
                    for dc in range(c.NC):
                        w = [q_ev, kv_ev]
                        if dc == 0:
                            w.append(pa_free[j])
                        mmS = em.pe(lambda e, dc=dc, j=j, qt=qt: e.matmul(
                            b.pA[j][:, 0:T],
                            b.qg[:, dc * T + qt * 128: dc * T + qt * 128 + 128],
                            b.kS[:, dc * c.SEQ + j * T: dc * c.SEQ + (j + 1) * T],
                            start=(dc == 0), stop=(dc == c.NC - 1)),
                            waits=w, inc=(dc == c.NC - 1))
                    if diag:
                        mmS = em.v(lambda e, j=j, i=i: e.tensor_tensor(
                            b.pA[j][:, 0:T], b.pA[j][:, 0:T],
                            b.cmask[:, (i % 4) * 512:(i % 4) * 512 + 512], ALU.add),
                            waits=[mmS, mask_ev], inc=True)
                    em.v(lambda e, j=j: e.reduce_max(
                        b.nm1[:, j:j + 1], b.pA[j][:, 0:T], axis=AX.X, negate=True),
                        waits=[mmS], inc=True)
                cm = em.v(lambda e: e.tensor_copy(b.nm1[:, 4:5], b.nm1[:, 0:1]), inc=True)
                for j in range(1, nblk):
                    cm = em.v(lambda e, j=j: e.tensor_tensor(
                        b.nm1[:, 4:5], b.nm1[:, 4:5], b.nm1[:, j:j + 1], ALU.min),
                        inc=True)
                for j in range(nblk):
                    ex = em.sc(lambda e, j=j: e.activation(
                        b.probs1[:, j * T:(j + 1) * T], b.pA[j][:, 0:T], AF.Exp,
                        bias=b.nm1[:, 4:5], scale=1.0,
                        accum_out=b.nm1[:, 5 + j:6 + j]),
                        waits=[cm, probs_free], inc=True)
                    pa_free[j] = ex
                    exp_evs.append(ex)
                sm = em.v(lambda e: e.tensor_copy(b.nm1[:, 9:10], b.nm1[:, 5:6]),
                          waits=[exp_evs[0]], inc=True)
                for j in range(1, nblk):
                    sm = em.v(lambda e, j=j: e.tensor_tensor(
                        b.nm1[:, 9:10], b.nm1[:, 9:10], b.nm1[:, 5 + j:6 + j], ALU.add),
                        waits=[exp_evs[j]], inc=True)
                rc = em.v(lambda e: e.reciprocal(b.nm1[:, 10:11], b.nm1[:, 9:10]),
                          waits=[sm], inc=True)
                pt_ev = None
                for j in range(nblk):
                    pn = em.v(lambda e, j=j: e.tensor_scalar(
                        b.probs1[:, j * T:(j + 1) * T], b.probs1[:, j * T:(j + 1) * T],
                        b.nm1[:, 10:11], None, ALU.mult),
                        waits=[rc, exp_evs[j]], inc=True)
                    for sj in range(4):
                        slot = sj % 2
                        te = em.pe(lambda e, j=j, sj=sj, slot=slot: e.transpose(
                            b.pT[slot][:, 0:128],
                            b.probs1[:, j * T + sj * 128: j * T + sj * 128 + 128],
                            b.ident[:]),
                            waits=[pn, pT_free[slot] or em.ev("ssc"), probsT_free[0]], inc=True)
                        pT_free[slot] = em.sc(lambda e, j=j, sj=sj, slot=slot: e.copy(
                            b.probsT1[:, (j * 4 + sj) * 128:(j * 4 + sj) * 128 + 128],
                            b.pT[slot][:, 0:128]),
                            waits=[te], inc=True)
                pt_ev = pT_free[(nblk * 4 - 1) % 2]
                ctx_mm = None
                for ch in range(c.NC):
                    for j in range(nblk):
                        for sj in range(4):
                            w = [pt_ev, kv_ev]
                            if ch == 0 and j == 0 and sj == 0:
                                w.append(pCtx_free)
                            ctx_mm = em.pe(lambda e, ch=ch, sj=sj, j=j, nblk=nblk: e.matmul(
                                b.pCtx[:, ch * 128:(ch + 1) * 128],
                                b.vS[:, (j * 4 + sj) * c.D + ch * 128: (j * 4 + sj) * c.D + (ch + 1) * 128],
                                b.probsT1[:, (j * 4 + sj) * 128:(j * 4 + sj) * 128 + 128],
                                start=(j == 0 and sj == 0), stop=(j == nblk - 1 and sj == 3)),
                                waits=w, inc=(ch == c.NC - 1 and j == nblk - 1 and sj == 3))
                probsT_free = [ctx_mm, ctx_mm]
                probs_free = pt_ev
                acc_ev = None
                for ch in range(c.NC):
                    acc_ev = em.v(lambda e, ch=ch, qt=qt, i=i, base=base: e.tensor_tensor(
                        b.acc[:, ch * T + qt * 128: ch * T + qt * 128 + 128],
                        b.pCtx[:, ch * 128:(ch + 1) * 128],
                        b.x[:, ch * c.NT + base + i * 128: ch * c.NT + base + i * 128 + 128],
                        ALU.add), waits=[ctx_mm], inc=True)
                pCtx_free = acc_ev
            ln_ev = emit_ln_fm(em, b, cfg, ti_glob, acc_ev, prev_ln_ev,
                               extra_waits=[pa_free[2], pa_free[3]])
            pa_free[2] = ln_ev
            pa_free[3] = ln_ev
            prev_ln_ev = ln_ev
            ln_evs.append(ln_ev)
            qg_free = em.ev("spe")
        kv_free = em.ev("spe")
    return ln_evs




def emit_l1_moe(em, b, cfg, ce, x_tile_evs, x_out):
    c = cfg
    T = c.TT
    F = c.F
    xoflat = x_out.rearrange("s t d -> (s t) d")

    wgl = em.dma_gp("dwg_in", b.wg_sb, b.moe_w[0], waits=[ce])
    wgc = em.v(lambda e: e.tensor_copy(b.scores[0:8, 0:c.D], b.wg_sb),
               waits=[wgl], inc=True)
    wgt_ev = None
    for dc in range(c.NC):
        slot = dc % 2
        te = em.pe(lambda e, dc=dc, slot=slot: e.transpose(
            b.pT[slot][:, 0:8], b.scores[0:8, dc * 128:(dc + 1) * 128],
            b.ident[0:8, 0:8]),
            waits=[wgc, ce, em.ev("ssc"), b.pT_free[slot]], inc=True)
        wgt_ev = em.sc(lambda e, dc=dc, slot=slot: e.copy(
            b.wgT[:, dc * 8:(dc + 1) * 8], b.pT[slot][0:128, 0:8]),
            waits=[te], inc=True)
        b.pT_free[slot] = wgt_ev

    pa_free = [None, None, None, None]
    sc_ev = None
    for ti in range(c.NTILE):
        bank = ti % 2
        mm = None
        for dc in range(c.NC):
            w = [x_tile_evs[ti], wgt_ev]
            if dc == 0:
                w.append(pa_free[bank])
            mm = em.pe(lambda e, ti=ti, dc=dc, bank=bank: e.matmul(
                b.pA[bank][0:8, 0:T],
                b.wgT[:, dc * 8:(dc + 1) * 8],
                b.x[:, dc * c.NT + ti * T: dc * c.NT + (ti + 1) * T],
                start=(dc == 0), stop=(dc == c.NC - 1)),
                waits=w, inc=(dc == c.NC - 1))
        sc_ev = em.sc(lambda e, ti=ti, bank=bank: e.copy(
            b.scores[0:8, ti * T:(ti + 1) * T], b.pA[bank][0:8, 0:T]),
            waits=[mm], inc=True)
        pa_free[bank] = sc_ev

    g_ev = None
    for t in range(c.NT // 128):
        slot = t % 2
        te = em.pe(lambda e, t=t, slot=slot: e.transpose(
            b.pT[slot][:, 0:8], b.scores[0:8, t * 128:(t + 1) * 128],
            b.ident[0:8, 0:8]), waits=[sc_ev, b.pT_free[slot]], inc=True)
        ev = em.sc(lambda e, slot=slot: e.copy(
            b.sctm[:], b.pT[slot][0:128, 0:8]), waits=[te, g_ev], inc=True)
        b.pT_free[slot] = ev
        em.v(lambda e: e.max(out=b.top8[:], in_=b.sctm[:]), waits=[ev])
        em.v(lambda e: e.memset(b.rep[:], NEG))
        em.v(lambda e: e.tensor_copy(b.rep[:, 0:1], b.top8[:, 0:1]))
        em.v(lambda e: e.match_replace(b.ms2[:], b.rep[:], b.sctm[:], -2e30))
        em.v(lambda e: e.tensor_tensor(b.mk1[:], b.sctm[:], b.ms2[:], ALU.not_equal))
        em.v(lambda e: e.tensor_copy(b.rep[:, 0:1], b.top8[:, 1:2]))
        em.v(lambda e: e.match_replace(b.ms3[:], b.rep[:], b.ms2[:], -2e30))
        em.v(lambda e: e.tensor_tensor(b.mk2[:], b.ms2[:], b.ms3[:], ALU.not_equal))
        dv = em.v(lambda e: e.tensor_sub(b.mst[:, 0:1], b.top8[:, 0:1], b.top8[:, 1:2]))
        p1 = em.sc(lambda e: e.activation(b.mst[:, 1:2], b.mst[:, 0:1], AF.Sigmoid,
                                          bias=0.0, scale=1.0), waits=[dv], inc=True)
        p2 = em.sc(lambda e: e.activation(b.mst[:, 2:3], b.mst[:, 0:1], AF.Sigmoid,
                                          bias=0.0, scale=-1.0), inc=True)
        em.v(lambda e: e.tensor_scalar(b.gtm[:], b.mk1[:], b.mst[:, 1:2], None, ALU.mult),
             waits=[p1, p2])
        em.v(lambda e: e.scalar_tensor_tensor(b.gtm[:], b.mk2[:], b.mst[:, 2:3],
                                              b.gtm[:], ALU.mult, ALU.add))
        gc = em.v(lambda e: e.tensor_copy(b.gbf[:], b.gtm[:]))
        tb = em.pe(lambda e, t=t, slot=slot: e.transpose(
            b.pT[slot][0:8, 128:256], b.gbf[:], b.ident[:]),
            waits=[gc, b.pT_free[slot]], inc=True)
        g_ev = em.sc(lambda e, t=t, slot=slot: e.copy(
            b.g_fm[0:8, t * 128:(t + 1) * 128], b.pT[slot][0:8, 128:256]),
            waits=[tb], inc=True)
        b.pT_free[slot] = g_ev
    g_dma = em.dma_gp("dgout", b.g_dram, b.g_fm[0:8, :], waits=[g_ev])

    sel_ev = None
    for e_ in range(c.E):
        sel_ev = em.gp(lambda e, e_=e_: e.affine_select(
            out=b.sel[0:8, e_ * 128:(e_ + 1) * 128], in_=b.ones_bf[0:8, 0:128],
            pattern=[[0, 128]], compare_op=ALU.is_equal, fill=0.0,
            base=-e_, channel_multiplier=1), waits=[ce], inc=True)

    queues = []
    for e in range(c.E):
        queues.append(PrepQueue([
            PrepJob(em, b, cfg, b.moe_w[1][e], b.moe_w1T[e], ce,
                    after=[g_dma] if e == 0 else ()),
            PrepJob(em, b, cfg, b.moe_w[2][e], b.moe_w2T[e], ce),
            PrepJob(em, b, cfg, b.moe_w[3][e], b.moe_w3T[e], ce),
        ]))
    prep_done = {}

    def emit_prep(e, n_chunks):
        queues[e].run(n_chunks)
        if queues[e].done() and e not in prep_done:
            prep_done[e] = [em.ev("dstg0"), em.ev("dstg1")]

    emit_prep(0, -1)

    w_rd = [None, None]
    w3_free = [None, None]
    pctx_free = None
    hid_rd = None
    stage_dma = None
    msil_free = None
    mt1_free = None
    gB_free = None
    gld_free = None
    acc_dmas = [None] * c.NTILE
    for e in range(c.E):
        pe_evs = prep_done[e]
        NSL = 4
        slw = F // NSL
        l1s, l2s = [], []
        for sl_ in range(NSL):
            l1s.append(em.dma_gp(f"dw1e{sl_}",
                b.w1e[:].rearrange("p (dc f) -> p dc f", f=F)[:, :, sl_ * slw:(sl_ + 1) * slw],
                b.moe_w1T[e][:, :, sl_ * slw:(sl_ + 1) * slw],
                waits=[w_rd[0], x_tile_evs[-1]] + pe_evs))
            l2s.append(em.dma_gp(f"dw2e{sl_}",
                b.w2e[:].rearrange("p (dc f) -> p dc f", f=F)[:, :, sl_ * slw:(sl_ + 1) * slw],
                b.moe_w2T[e][:, :, sl_ * slw:(sl_ + 1) * slw],
                waits=[w_rd[1]] + pe_evs))
        for ti in range(c.NTILE):
            gl = em.dma_gp("dgld", b.gld, b.g_dram[0:8, ti * T:(ti + 1) * T],
                           waits=[gld_free, g_dma])
            gmm = em.pe(lambda e_, e=e, ti=ti: e_.matmul(
                b.pCtx[:, 0:T], b.sel[0:8, e * 128:(e + 1) * 128],
                b.gld, start=True, stop=True),
                waits=[gl, sel_ev, pctx_free], inc=True)
            gld_free = gmm
            gbe = em.v(lambda e_: e_.tensor_copy(b.gB[:], b.pCtx[:, 0:T]),
                       waits=[gmm, gB_free], inc=True)
            pctx_free = gbe
            h_ev = None
            h_evs = [None] * c.NF
            mmh1_last = None
            for fc in range(c.NF):
                bs = fc % 2
                bh1, bh2 = (0, 1) if bs == 0 else (2, 3)
                mmh1 = None
                mmh2 = None
                for dc in range(c.NC):
                    w = [l1s[fc * 128 // slw], x_tile_evs[ti]]
                    if dc == 0:
                        w.append(pa_free[bh1])
                    mmh1 = em.pe(lambda e_, fc=fc, dc=dc, ti=ti, bh1=bh1: e_.matmul(
                        b.pA[bh1][:, 0:T],
                        b.w1e[:, dc * F + fc * 128: dc * F + fc * 128 + 128],
                        b.x[:, dc * c.NT + ti * T: dc * c.NT + (ti + 1) * T],
                        start=(dc == 0), stop=(dc == c.NC - 1)),
                        waits=w, inc=(dc == c.NC - 1))
                for dc in range(c.NC):
                    w = [l2s[fc * 128 // slw]]
                    if dc == 0:
                        w.append(pa_free[bh2])
                    mmh2 = em.pe(lambda e_, fc=fc, dc=dc, ti=ti, bh2=bh2: e_.matmul(
                        b.pA[bh2][:, 0:T],
                        b.w2e[:, dc * F + fc * 128: dc * F + fc * 128 + 128],
                        b.x[:, dc * c.NT + ti * T: dc * c.NT + (ti + 1) * T],
                        start=(dc == 0), stop=(dc == c.NC - 1)),
                        waits=w, inc=(dc == c.NC - 1))
                sev = em.sc(lambda e_, bh1=bh1: e_.activation(
                    b.msil[:], b.pA[bh1][:, 0:T], AF.Silu, bias=0.0, scale=1.0),
                    waits=[mmh1, msil_free], inc=True)
                pa_free[bh1] = sev
                h_ev = em.v(lambda e_, fc=fc, bh2=bh2: e_.tensor_tensor(
                    b.hid[:, fc * T:(fc + 1) * T], b.msil[:], b.pA[bh2][:, 0:T],
                    ALU.mult), waits=[sev, mmh2, hid_rd], inc=True)
                h_evs[fc] = h_ev
                pa_free[bh2] = h_ev
                msil_free = h_ev
                mmh1_last = mmh1
            down_last = None
            stg_ev = None
            l3s = [None] * c.NC
            for oc in range(2):
                w3sl = oc % 2
                l3s[oc] = em.dma_gp(f"dw3s{w3sl}",
                                    b.w3s[w3sl].rearrange("p (fc m) -> p fc m", m=128),
                                    b.moe_w3T[e][:, :, oc * 128:(oc + 1) * 128],
                                    waits=[w3_free[w3sl]] + pe_evs)
            for oc in range(c.NC):
                bank = oc % 4
                w3sl = oc % 2
                l3 = l3s[oc]
                mm = None
                for fc in range(c.NF):
                    w = [h_evs[fc], l3]
                    if fc == 0:
                        w.append(pa_free[bank])
                    mm = em.pe(lambda e_, oc=oc, fc=fc, bank=bank, w3sl=w3sl: e_.matmul(
                        b.pA[bank][:, 0:T],
                        b.w3s[w3sl][:, fc * 128:(fc + 1) * 128],
                        b.hid[:, fc * T:(fc + 1) * T],
                        start=(fc == 0), stop=(fc == c.NF - 1)),
                        waits=w, inc=(fc == c.NF - 1))
                w3_free[w3sl] = mm
                down_last = mm
                if oc + 2 < c.NC:
                    l3s[oc + 2] = em.dma_gp(f"dw3s{w3sl}",
                        b.w3s[w3sl].rearrange("p (fc m) -> p fc m", m=128),
                        b.moe_w3T[e][:, :, (oc + 2) * 128:(oc + 3) * 128],
                        waits=[mm] + pe_evs)
                ge = em.v(lambda e_, oc=oc, bank=bank: e_.tensor_tensor(
                    b.mt1[:], b.pA[bank][:, 0:T], b.gB[:], ALU.mult),
                    waits=[mm, gbe, stage_dma, mt1_free], inc=True)
                pa_free[bank] = ge
                if e == 0:
                    stg_ev = em.v(lambda e_, oc=oc, ti=ti: e_.tensor_tensor(
                        b.acc[:, oc * T:(oc + 1) * T], b.mt1[:],
                        b.x[:, oc * c.NT + ti * T: oc * c.NT + (ti + 1) * T], ALU.add),
                        waits=[ge], inc=True)
                else:
                    stg_ev = em.v(lambda e_, oc=oc: e_.tensor_copy(
                        b.acc[:, oc * T:(oc + 1) * T], b.mt1[:]),
                        waits=[ge], inc=True)
                mt1_free = stg_ev
            hid_rd = down_last
            gB_free = stg_ev
            dma = em.dma_gp(f"dacc{ti}",
                            b.accm[:, :, ti * T:(ti + 1) * T],
                            b.acc[:, 0:c.NC * T].rearrange("p (ch t) -> p ch t", t=T),
                            waits=[stg_ev, acc_dmas[ti]],
                            accum_op=ALU.add if e > 0 else ALU.bypass)
            acc_dmas[ti] = dma
            stage_dma = dma
            if e + 1 < c.E:
                emit_prep(e + 1, 3)
        if e + 1 < c.E:
            emit_prep(e + 1, -1)
        w_rd = [mmh1_last, h_ev]

    out_dmas = []
    rb_free = None
    otm_dma = None
    for ti in range(c.NTILE):
        rb = em.dma_gp("drb", b.acc[:, 0:c.NC * T].rearrange("p (ch t) -> p ch t", t=T),
                       b.accm[:, :, ti * T:(ti + 1) * T],
                       waits=list(acc_dmas) + [rb_free])
        for sub in range(4):
            ev = None
            for ch in range(c.NC):
                slot = ch % 2
                ptv = b.pT[slot][:].bitcast(F32)
                te = em.pe(lambda e_, ch=ch, sub=sub, slot=slot, ptv=ptv: e_.transpose(
                    ptv[:, 0:128],
                    b.acc[:, ch * T + sub * 128: ch * T + sub * 128 + 128],
                    b.ident32[:]),
                    waits=[rb, em.ev("ssc"), ce, b.pT_free[slot]], inc=True)
                ev = em.sc(lambda e_, ch=ch, slot=slot, ptv=ptv: e_.copy(
                    b.otm[:, ch * 128:(ch + 1) * 128], ptv[:, 0:128]),
                    waits=[te, otm_dma], inc=True)
                b.pT_free[slot] = ev
            rb_free = ev
            s1 = em.v(lambda e_: e_.reduce_sum(b.mst[:, 3:4], b.otm, axis=AX.X),
                      waits=[ev], inc=True)
            s2 = em.sc(lambda e_: e_.activation(b.sqf, b.otm, AF.Square,
                                                bias=0.0, scale=1.0,
                                                accum_out=b.mst[:, 4:5]),
                       waits=[ev], inc=True)
            em.v(lambda e_: e_.tensor_scalar(b.mst[:, 5:6], b.mst[:, 3:4],
                                             1.0 / c.D, None, ALU.mult), waits=[s1])
            em.v(lambda e_: e_.tensor_scalar(b.mst[:, 6:7], b.mst[:, 4:5],
                                             1.0 / c.D, None, ALU.mult), waits=[s2])
            em.v(lambda e_: e_.tensor_tensor(b.mst[:, 7:8], b.mst[:, 5:6],
                                             b.mst[:, 5:6], ALU.mult))
            em.v(lambda e_: e_.tensor_sub(b.mst[:, 6:7], b.mst[:, 6:7], b.mst[:, 7:8]))
            vv = em.v(lambda e_: e_.tensor_scalar(b.mst[:, 6:7], b.mst[:, 6:7],
                                                  1e-5, None, ALU.add))
            sq = em.sc(lambda e_: e_.activation(b.mst[:, 7:8], b.mst[:, 6:7], AF.Sqrt,
                                                bias=0.0, scale=1.0), waits=[vv], inc=True)
            rr = em.v(lambda e_: e_.reciprocal(b.mst[:, 8:9], b.mst[:, 7:8]),
                      waits=[sq])
            ap = em.v(lambda e_: e_.tensor_scalar(
                b.otm, b.otm, b.mst[:, 5:6], b.mst[:, 8:9],
                ALU.subtract, ALU.mult), waits=[rr])
            r0 = ti * T + sub * 128
            otm_dma = em.dma_sy("dout0", xoflat[r0:r0 + 128, :], b.otm, waits=[ap])
            out_dmas.append(otm_dma)
    return out_dmas



from concourse.bass_utils import run_bass_kernel_spmd

_PROG_CACHE = {}


def _get_prog():
    if "fused" not in _PROG_CACHE:
        cfg = Cfg(SEQ=2048, NSEQ=2)
        _PROG_CACHE["fused"] = build_program(cfg, parts=["l0a", "l0f", "l1a", "moe"])
    return _PROG_CACHE["fused"]


def kernel(L0_attn_x, L0_attn_Wq, L0_attn_Wk, L0_attn_Wv, L0_attn_Wo,
           L0_ffn_w_gate, L0_ffn_w_up, L0_ffn_w_down,
           L1_attn_Wq, L1_attn_Wk, L1_attn_Wv,
           L1_ffn_Wg, L1_ffn_W1, L1_ffn_W2, L1_ffn_W3):
    x = np.ascontiguousarray(np.asarray(L0_attn_x, dtype=np.float32))
    f32 = lambda a: np.ascontiguousarray(np.asarray(a, dtype=np.float32))
    wmap = dict(
        l0_wq=f32(L0_attn_Wq), l0_wk=f32(L0_attn_Wk),
        l0_wv=f32(L0_attn_Wv), l0_wo=f32(L0_attn_Wo),
        l0_wgate=f32(L0_ffn_w_gate), l0_wup=f32(L0_ffn_w_up),
        l0_wdown=f32(L0_ffn_w_down),
        l1_wq=f32(L1_attn_Wq), l1_wk=f32(L1_attn_Wk), l1_wv=f32(L1_attn_Wv),
        l1_wg=f32(L1_ffn_Wg), l1_w1=f32(L1_ffn_W1),
        l1_w2=f32(L1_ffn_W2), l1_w3=f32(L1_ffn_W3))
    nc = _get_prog()
    in_maps = []
    for c in range(8):
        m = dict(x_in=x[2 * c:2 * c + 2])
        m.update(wmap)
        in_maps.append(m)
    res = run_bass_kernel_spmd(nc, in_maps, core_ids=list(range(8)))
    return np.concatenate([np.asarray(r["x_out"]) for r in res.results], axis=0)
